# version 06
# baseline (speedup 1.0000x reference)
import math
import numpy as np
import ml_dtypes
import concourse.bass as bass
import concourse.mybir as mybir
from concourse.bass_utils import run_bass_kernel_spmd

F32 = mybir.dt.float32
BF16 = mybir.dt.bfloat16
I32 = mybir.dt.int32
AF = mybir.ActivationFunctionType
ALU = mybir.AluOpType
AX = mybir.AxisListType

S = 2048
DM = 2048
H = 16
NOPE = 128
RD = 64
VD = 128
QL = 768
KVL = 512
HID = 5632
NQ = 1024
HG = 4
NGRP = 4
SLOT_K = (8, 16)
SCALE = float(1.0 / math.sqrt(192.0))
LN_EPS = 1e-5
RMS_EPS = 1e-6
MAGIC = 8388608.0

C_HI = 6.28125
C_MID = float(np.float32(2 * np.pi - C_HI))
C_LO = float(np.float64(2 * np.pi) - np.float64(C_HI) - np.float64(np.float32(C_MID)))
INV2PI = float(1.0 / (2 * np.pi))
LOG1E4_32 = float(np.log(10000.0) / 32.0)


class Ctr:
    def __init__(self, handle):
        self.h = handle
        self.n = 0

    def inc(self, k=1):
        self.n += k
        return self.n


class EngQ:
    def __init__(self, ctr, chain):
        self.ops = []
        self.ctr = ctr
        self.chain = chain

    def wait(self, ctr, val):
        if val > 0:
            self.ops.append(lambda e, s=ctr.h, v=val: e.wait_ge(s, v))

    def op(self, fn):
        if self.chain and self.ctr.n > 0:
            self.ops.append(lambda e, s=self.ctr.h, v=self.ctr.n: e.wait_ge(s, v))
        val = self.ctr.inc()
        def run(e, fn=fn, s=self.ctr.h):
            fn(e).then_inc(s, 1)
        self.ops.append(run)
        return val

    def op_noinc(self, fn):
        self.ops.append(lambda e, fn=fn: fn(e))

    def dma(self, ctr, out, in_, transpose=False):
        val = ctr.inc(16)
        def run(e, out=out, in_=in_, s=ctr.h, tr=transpose):
            e.dma_start(out=out, in_=in_, transpose=tr).then_inc(s, 16)
        self.ops.append(run)
        return val

    def raw_wait(self, ctr, val):
        if val > 0:
            self.ops.append(lambda e, s=ctr.h, v=val: e.wait_ge(s, v))


def build_nc(debug=False):
    nc = bass.Bass("TRN2", target_bir_lowering=False)

    x_full = nc.dram_tensor("x_full", [S, DM], F32, kind="ExternalInput").ap()
    x_q = nc.dram_tensor("x_q", [NQ, DM], F32, kind="ExternalInput").ap()
    pos_q = nc.dram_tensor("pos_q", [1, NQ], F32, kind="ExternalInput").ap()
    gmeta = nc.dram_tensor("gmeta", [1, 2], F32, kind="ExternalInput").ap()
    w_dq = nc.dram_tensor("w_dq", [DM, QL], F32, kind="ExternalInput").ap()
    g_q_t = nc.dram_tensor("g_q_t", [128, 6], F32, kind="ExternalInput").ap()
    w_uq = nc.dram_tensor("w_uq", [QL, H * 192], F32, kind="ExternalInput").ap()
    w_dkv = nc.dram_tensor("w_dkv", [DM, KVL], F32, kind="ExternalInput").ap()
    w_kr = nc.dram_tensor("w_kr", [DM, RD], F32, kind="ExternalInput").ap()
    g_kv_t = nc.dram_tensor("g_kv_t", [128, 4], F32, kind="ExternalInput").ap()
    w_uk = nc.dram_tensor("w_uk", [KVL, H * NOPE], F32, kind="ExternalInput").ap()
    w_uv = nc.dram_tensor("w_uv", [KVL, H * VD], F32, kind="ExternalInput").ap()
    w_o = nc.dram_tensor("w_o", [H * VD, DM], F32, kind="ExternalInput").ap()
    wq_gate = nc.dram_tensor("wq_gate", [HID, DM], I32, kind="ExternalInput").ap()
    s_gate_t = nc.dram_tensor("s_gate_t", [128, 44], F32, kind="ExternalInput").ap()
    wq_up = nc.dram_tensor("wq_up", [HID, DM], I32, kind="ExternalInput").ap()
    s_up_t = nc.dram_tensor("s_up_t", [128, 44], F32, kind="ExternalInput").ap()
    wq_down = nc.dram_tensor("wq_down", [DM, HID], I32, kind="ExternalInput").ap()
    s_down_r = nc.dram_tensor("s_down_r", [1, DM], F32, kind="ExternalInput").ap()
    y_out = nc.dram_tensor("y", [NQ, DM], F32, kind="ExternalOutput").ap()

    wdq_scr = nc.dram_tensor("wdq_scr", [DM, QL], BF16, kind="Internal").ap()
    wuq_scr = nc.dram_tensor("wuq_scr", [QL, H * 192], BF16, kind="Internal").ap()
    wdkv_scr = nc.dram_tensor("wdkv_scr", [DM, KVL], BF16, kind="Internal").ap()
    wkr_scr = nc.dram_tensor("wkr_scr", [DM, RD], BF16, kind="Internal").ap()
    wuk_scr = nc.dram_tensor("wuk_scr", [KVL, H * NOPE], BF16, kind="Internal").ap()
    wuv_scr = nc.dram_tensor("wuv_scr", [KVL, H * VD], BF16, kind="Internal").ap()
    wo_scr = nc.dram_tensor("wo_scr", [H * VD, DM], BF16, kind="Internal").ap()
    g_scr = nc.dram_tensor("g_scr", [HID, DM], BF16, kind="Internal").ap()
    u_scr = nc.dram_tensor("u_scr", [HID, DM], BF16, kind="Internal").ap()
    d_scr = nc.dram_tensor("d_scr", [DM, HID], BF16, kind="Internal").ap()
    x1_scr = nc.dram_tensor("x1_scr", [NQ, DM], F32, kind="Internal").ap()

    dbg = {}
    if debug:
        def dbg_t(name, shape, dt=F32):
            dbg[name] = nc.dram_tensor("dbg_" + name, shape, dt, kind="ExternalOutput").ap()
        dbg_t("cq", [128, 6, NQ], BF16)
        dbg_t("kvT", [128, 4, S], BF16)
        dbg_t("kpe", [65, S], BF16)
        dbg_t("qno", [128, HG, NQ], BF16)
        dbg_t("qpe", [65, HG * NQ], BF16)
        dbg_t("kno", [128, HG, S], BF16)
        dbg_t("vtok", [128, 16, HG * VD], BF16)
        dbg_t("ctx", [128, 16, NQ], BF16)
        dbg_t("x1", [NQ, DM], F32)
        dbg_t("h2T", [128, 16, NQ], BF16)
        dbg_t("masks", [128, 16, 512], BF16)
        dbg_t("cosq", [32, NQ], F32)
        dbg_t("kmaxn", [1, 16], F32)

    sem_guards = []
    def mksem(name):
        g = nc.semaphore(name)
        h = g.__enter__()
        sem_guards.append(g)
        return Ctr(h)

    dIn = mksem("dIn")
    dX = [mksem("dX0"), mksem("dX1")]
    dT = [mksem("dT0"), mksem("dT1")]
    dW = mksem("dW")
    dD2 = mksem("dD2")
    dB = mksem("dB")
    dSt = [mksem("dSt0"), mksem("dSt1")]
    dSl = [mksem("dSl0"), mksem("dSl1")]
    dDl = [mksem("dDl0"), mksem("dDl1")]
    dR = [mksem("dR0"), mksem("dR1")]
    dY = [mksem("dY0"), mksem("dY1")]
    dDbg = mksem("dDbg")
    dC = [mksem("dC0"), mksem("dC1")]
    wsem = {n: mksem("dw_" + n) for n in
            ("wdq", "wuq", "wdkv", "wkr", "wuk", "wuv", "wo", "g", "u", "d")}
    tS = mksem("tS")
    vS = mksem("vS")
    aS = mksem("aS")
    gS = mksem("gS")

    QT = EngQ(tS, chain=False)
    QV = EngQ(vS, chain=True)
    QA = EngQ(aS, chain=True)
    QG = EngQ(gS, chain=True)
    QS = EngQ(None, chain=False)

    class Alloc:
        def __init__(self):
            self.guards = {}
            self.order = {}
            self.k = 0

        def sb(self, name, shape, dt, side=None):
            g = nc.sbuf_tensor("%s__%d" % (name, self.k), shape, dt, side=side)
            t = g.__enter__()
            self.guards[name] = g
            self.order[name] = self.k
            self.k += 1
            return t

        def ps(self, name, shape):
            g = nc.psum_tensor("%s__%d" % (name, self.k), shape, F32)
            t = g.__enter__()
            self.guards[name] = g
            self.order[name] = self.k
            self.k += 1
            return t

        def free(self, *names):
            for name in sorted(names, key=lambda n: -self.order[n]):
                self.guards.pop(name).__exit__(None, None, None)
                self.order.pop(name)

    al = Alloc()

    ones_col = al.sb("ones_col", [128, 1], BF16)
    ones_cf = al.sb("ones_cf", [128, 1], F32)
    epsln = al.sb("epsln", [128, 1], F32)
    epsrms = al.sb("epsrms", [128, 1], F32)
    ones_mat = al.sb("ones_mat", [128, 128], BF16)
    ones_rf = al.sb("ones_rf", [1, 128], F32)
    ones_rb = al.sb("ones_rb", [1, 512], BF16)
    sgate_c = al.sb("sgate_c", [128, 44], F32)
    sup_c = al.sb("sup_c", [128, 44], F32)
    gq_c = al.sb("gq_c", [128, 6], F32)
    gkv_c = al.sb("gkv_c", [128, 4], F32)
    gmeta_sb = al.sb("gmeta_sb", [1, 2], F32)
    gcol = al.sb("gcol", [128, 2], F32)
    iota_pq = al.sb("iota_pq", [128, 512], F32)
    kmaxn = al.sb("kmaxn", [1, 16], F32)
    cosq = al.sb("cosq", [32, NQ], F32)
    sinq = al.sb("sinq", [32, NQ], F32)
    masks = al.sb("masks", [128, 16, 512], BF16)
    cq_sb = al.sb("cq_sb", [128, 6, NQ], BF16)
    kvT = al.sb("kvT", [128, 4, S], BF16)
    kpe_ext = al.sb("kpe_ext", [65, S], BF16)
    kpe_sq = al.sb("kpe_sq", [64, S], BF16)

    stg32 = al.sb("stg32", [128, 2, 1024], F32, side="right")
    stgb = al.sb("stgb", [128, 2, 1024], BF16, side="right")

    def weight_pipeline():
        cast_use = [0, 0]
        last_store = [None, 0]
        slot = [0]
        plan = []
        plan.append(("wdkv", w_dkv, wdkv_scr, DM, KVL))
        plan.append(("wkr", w_kr, wkr_scr, DM, RD))
        plan.append(("wdq", w_dq, wdq_scr, DM, QL))
        plan.append(("wuq", w_uq, wuq_scr, QL, H * 192))
        plan.append(("wuk", w_uk, wuk_scr, KVL, H * NOPE))
        plan.append(("wuv", w_uv, wuv_scr, KVL, H * VD))
        plan.append(("g", wq_gate, g_scr, HID, DM))
        plan.append(("u", wq_up, u_scr, HID, DM))
        plan.append(("d", wq_down, d_scr, DM, HID))
        plan.append(("wo", w_o, wo_scr, H * VD, DM))
        marks = {}
        for name, src, dst, R, C in plan:
            sem = wsem[name]
            for r0 in range(0, R, 128):
                for c0 in range(0, C, 1024):
                    cw = min(1024, C - c0)
                    sl = slot[0]
                    slot[0] ^= 1
                    QG.wait(gS, cast_use[sl])
                    dL = QG.dma(dC[sl], stg32[:, sl:sl + 1, 0:cw],
                                src[r0:r0 + 128, c0:c0 + cw])
                    QG.wait(dC[sl], dL)
                    if last_store[0] is not None:
                        QG.wait(last_store[0], last_store[1])
                    g = QG.op(lambda gg, sl=sl, cw=cw: gg.tensor_copy(
                        stgb[:, sl:sl + 1, 0:cw], stg32[:, sl:sl + 1, 0:cw]))
                    cast_use[sl] = g
                    QG.wait(gS, g)
                    dSt_ = QG.dma(sem, dst[r0:r0 + 128, c0:c0 + cw], stgb[:, sl:sl + 1, 0:cw])
                    last_store[:] = [sem, dSt_]
            marks[name] = sem.n
        return marks

    wmark = weight_pipeline()

    Wp = S + NQ
    cosk = al.sb("cosk", [32, S], F32, side="right")
    sink = al.sb("sink", [32, S], F32, side="right")
    posq_sb = al.sb("posq_sb", [1, NQ], F32, side="right")
    rt_ang = al.sb("rt_ang", [32, Wp], F32, side="right")
    rt_k = al.sb("rt_k", [32, Wp], F32, side="right")
    rt_t1 = al.sb("rt_t1", [32, Wp], F32, side="right")
    rt_t2 = al.sb("rt_t2", [32, Wp], F32, side="right")
    invf = al.sb("invf", [32, 1], F32, side="right")
    invfr = al.sb("invfr", [32, 1], F32, side="right")
    em = al.sb("em", [128, 2, 512], F32, side="right")

    for out_t, in_t in ((gmeta_sb[:, :], gmeta), (posq_sb[:, :], pos_q),
                        (sgate_c[:, :], s_gate_t), (sup_c[:, :], s_up_t),
                        (gq_c[:, :], g_q_t), (gkv_c[:, :], g_kv_t)):
        QS.dma(dIn, out_t, in_t)
    d_small = dIn.n

    QG.op(lambda g: g.iota(iota_pq[:, :], [[-1, 512]], base=0, channel_multiplier=1,
                           allow_small_or_imprecise_dtypes=True))
    QG.op(lambda g: g.iota(invfr[:, :], [[0, 1]], base=0, channel_multiplier=1,
                           allow_small_or_imprecise_dtypes=True))
    g_iota = QG.op(lambda g: g.iota(rt_ang[:, 0:S], [[1, S]], base=0, channel_multiplier=0,
                                    allow_small_or_imprecise_dtypes=True))

    QV.op(lambda v: v.memset(ones_col[:, :], 1.0))
    QV.op(lambda v: v.memset(ones_cf[:, :], 1.0))
    QV.op(lambda v: v.memset(epsln[:, :], LN_EPS))
    QV.op(lambda v: v.memset(epsrms[:, :], RMS_EPS))
    QV.op(lambda v: v.memset(kmaxn[:, :], 0.0))
    QV.op(lambda v: v.memset(ones_mat[:, :], 1.0))
    QV.op(lambda v: v.memset(ones_rb[:, :], 1.0))
    v_ones = QV.op(lambda v: v.memset(ones_rf[:, :], 1.0))

    QA.wait(gS, g_iota)
    a_invf = QA.op(lambda sc: sc.activation(invf[:, :], invfr[:, :], AF.Exp, scale=-LOG1E4_32))

    ps_b32 = al.ps("ps_b32", [32, 512])
    ps_b128 = al.ps("ps_b128", [128, 2])
    QT.wait(dIn, d_small)
    QT.wait(vS, v_ones)
    for i in range(NQ // 512):
        QT.wait(vS, vS.n)
        t_b = QT.op(lambda t, i=i: t.matmul(ps_b32[:, :], ones_rf[0:1, 0:32],
                                            posq_sb[0:1, i * 512:(i + 1) * 512],
                                            start=True, stop=True))
        QV.wait(tS, t_b)
        QV.wait(gS, g_iota)
        QV.op(lambda v, i=i: v.tensor_copy(rt_ang[:, S + i * 512:S + (i + 1) * 512],
                                           ps_b32[:, :]))
    t_g = QT.op(lambda t: t.matmul(ps_b128[:, :], ones_rf[0:1, 0:128], gmeta_sb[0:1, 0:2],
                                   start=True, stop=True))
    QV.wait(tS, t_g)
    QV.op(lambda v: v.tensor_copy(gcol[:, :], ps_b128[:, :]))

    QV.wait(aS, a_invf)
    QV.op(lambda v: v.tensor_scalar(rt_ang[:, :], rt_ang[:, :], invf[:, :], None, ALU.mult))
    QV.op(lambda v: v.tensor_scalar(rt_k[:, :], rt_ang[:, :], INV2PI, None, ALU.mult))
    QV.op(lambda v: v.tensor_scalar(rt_k[:, :], rt_k[:, :], MAGIC, None, ALU.add))
    QV.op(lambda v: v.tensor_scalar(rt_k[:, :], rt_k[:, :], -MAGIC, None, ALU.add))
    QV.op(lambda v: v.tensor_scalar(rt_t1[:, :], rt_k[:, :], C_HI, None, ALU.mult))
    QV.op(lambda v: v.tensor_sub(rt_ang[:, :], rt_ang[:, :], rt_t1[:, :]))
    QV.op(lambda v: v.tensor_scalar(rt_t1[:, :], rt_k[:, :], C_MID, None, ALU.mult))
    QV.op(lambda v: v.tensor_sub(rt_ang[:, :], rt_ang[:, :], rt_t1[:, :]))
    QV.op(lambda v: v.tensor_scalar(rt_t1[:, :], rt_k[:, :], C_LO, None, ALU.mult))
    v_red = QV.op(lambda v: v.tensor_sub(rt_ang[:, :], rt_ang[:, :], rt_t1[:, :]))
    QA.wait(vS, v_red)
    a_s4 = QA.op(lambda sc: sc.activation(rt_k[:, :], rt_ang[:, :], AF.Sin, scale=0.25))
    QV.wait(aS, a_s4)
    v_t4 = QV.op(lambda v: v.tensor_mul(rt_t1[:, :], rt_k[:, :], rt_k[:, :]))
    QA.wait(vS, v_t4)
    a_c4 = QA.op(lambda sc: sc.activation(rt_t2[:, :], rt_t1[:, :], AF.Sqrt, bias=1.0, scale=-1.0))
    QV.wait(aS, a_c4)
    QV.op(lambda v: v.tensor_mul(rt_k[:, :], rt_k[:, :], rt_t2[:, :]))
    QV.op(lambda v: v.tensor_scalar(rt_k[:, :], rt_k[:, :], 2.0, None, ALU.mult))
    QV.op(lambda v: v.tensor_scalar(rt_t2[:, :], rt_t1[:, :], -2.0, 1.0, ALU.mult, ALU.add))
    QV.op(lambda v: v.tensor_mul(rt_t1[:, :], rt_k[:, :], rt_t2[:, :]))
    QV.op(lambda v: v.tensor_scalar(rt_t1[:, :], rt_t1[:, :], 2.0, None, ALU.mult))
    QV.op(lambda v: v.tensor_copy(sink[:, :], rt_t1[:, 0:S]))
    QV.op(lambda v: v.tensor_copy(sinq[:, :], rt_t1[:, S:Wp]))
    QV.op(lambda v: v.tensor_mul(rt_t1[:, :], rt_k[:, :], rt_k[:, :]))
    QV.op(lambda v: v.tensor_scalar(rt_t1[:, :], rt_t1[:, :], -2.0, 1.0, ALU.mult, ALU.add))
    QV.op(lambda v: v.tensor_copy(cosk[:, :], rt_t1[:, 0:S]))
    v_trig = QV.op(lambda v: v.tensor_copy(cosq[:, :], rt_t1[:, S:Wp]))
    if debug:
        QS.wait(vS, v_trig)
        QS.dma(dDbg, dbg["cosq"], cosq[:, :])

    QV.op(lambda v: v.tensor_scalar(em[:, 0:1, :], iota_pq[:, :], gcol[:, 0:1], None, ALU.subtract))
    QV.op(lambda v: v.tensor_scalar(em[:, 1:2, :], iota_pq[:, :], gcol[:, 1:2], None, ALU.subtract))
    for s in range(2):
        for j in range(8):
            kc = j if s == 0 else 8 + j
            QV.op(lambda v, s=s, j=j, kc=kc: v.tensor_scalar(
                masks[:, s * 8 + j:s * 8 + j + 1, :], em[:, s:s + 1, :],
                float(-128.0 * kc), None, ALU.is_le))
    v_masks = vS.n
    if debug:
        QS.wait(vS, v_masks)
        QS.dma(dDbg, dbg["masks"], masks[:, :, :])
    al.free("em", "invfr", "invf", "rt_t2", "rt_t1", "rt_k", "rt_ang", "posq_sb")
    al.free("ps_b32", "ps_b128")

    xst = al.sb("xst", [128, 2, DM], F32, side="right")
    hst = al.sb("hst", [128, 2, DM], BF16, side="right")
    sqst = al.sb("sqst", [128, DM], F32, side="right")
    st1 = al.sb("st1", [128, 8], F32, side="right")
    hwin = al.sb("hwin", [128, 16, 512], BF16, side="right")
    rka = al.sb("rka", [32, 512], F32, side="right")
    kst = al.sb("kst", [32, 2, 512], F32, side="right")
    rkb = al.sb("rkb", [32, 512], F32, side="right")
    rkbf = al.sb("rkbf", [32, 512], BF16, side="right")
    sqw = al.sb("sqw", [128, 512], F32, side="right")
    rrow = al.sb("rrow", [1, 512], F32, side="right")
    rbc = al.sb("rbc", [128, 512], F32, side="right")

    x_last_rd = [0, 0]
    h_last_rd = [0, 0]

    def ln_load(src_rows, buf):
        QA.raw_wait(aS, x_last_rd[buf])
        return QA.dma(dX[buf], xst[:, buf:buf + 1, :], src_rows)

    def ln_tile(d_x, dst_col, buf):
        xa = xst[:, buf:buf + 1, :]
        ha = hst[:, buf:buf + 1, :]
        QA.wait(dX[buf], d_x)
        QA.op(lambda sc: sc.activation(sqst[:, :], xa, AF.Square, accum_out=st1[:, 1:2]))
        QA.op(lambda sc: sc.activation(sqst[:, :], xa, AF.Identity, accum_out=st1[:, 0:1]))
        a_m2 = QA.op(lambda sc: sc.activation(st1[:, 4:5], st1[:, 0:1], AF.Square,
                                              scale=1.0 / DM))
        QV.wait(aS, a_m2)
        v_v = QV.op(lambda v: v.scalar_tensor_tensor(st1[:, 3:4], st1[:, 1:2], 1.0 / DM,
                                                     st1[:, 4:5], ALU.mult, ALU.subtract))
        QA.wait(vS, v_v)
        a_std = QA.op(lambda sc: sc.activation(st1[:, 5:6], st1[:, 3:4], AF.Sqrt,
                                               bias=epsln[:, 0:1]))
        QV.wait(aS, a_std)
        QV.op(lambda v: v.reciprocal(st1[:, 6:7], st1[:, 5:6]))
        v_nb = QV.op(lambda v: v.scalar_tensor_tensor(st1[:, 7:8], st1[:, 0:1], -1.0 / DM,
                                                      st1[:, 6:7], ALU.mult, ALU.mult))
        QA.wait(vS, v_nb)
        if h_last_rd[buf]:
            QA.wait(dT[buf], h_last_rd[buf])
        a_h = QA.op(lambda sc: sc.activation(ha, xa, AF.Identity, bias=st1[:, 7:8],
                                             scale=st1[:, 6:7]))
        x_last_rd[buf] = a_h
        QS.wait(aS, a_h)
        d_tr = QS.dma(dT[buf], hwin[:, :, dst_col:dst_col + 128], ha, transpose=True)
        h_last_rd[buf] = d_tr
        return d_tr

    wdkv_sb = al.sb("wdkv_sb", [128, 16, KVL], BF16, side="right")
    wkr_sb = al.sb("wkr_sb", [128, 16, RD], BF16, side="right")
    QS.wait(vS, vS.n)
    QS.wait(aS, aS.n)
    QS.wait(wsem["wdkv"], wmark["wdkv"])
    QS.wait(wsem["wkr"], wmark["wkr"])
    QS.dma(dW, wdkv_sb[:, :, :], wdkv_scr.rearrange("(c p) f -> p c f", p=128))
    g_kr1 = QS.dma(dW, wkr_sb[:, :, :], wkr_scr.rearrange("(c p) f -> p c f", p=128))
    wdq_sb = None
    g_dq1 = None
    QA.wait(vS, vS.n)

    ps_r = al.ps("ps_r", [1, 512])
    ps_bw = al.ps("ps_bw", [128, 512])
    ps_c = [al.ps("ps_c%d" % i, [128, 512]) for i in range(6)]
    ps_kp1 = ps_c[4][0:32, :]
    ps_kp2 = ps_c[5][0:32, :]

    def tile_src(gi):
        if gi >= 24:
            return None
        if gi < 16:
            return x_full[gi * 128:(gi + 1) * 128, :]
        return x_q[(gi - 16) * 128:(gi - 15) * 128, :]

    hwin_rd = 0
    pending = {0: ln_load(tile_src(0), 0)}
    for wdx in range(6):
        is_kv = wdx < 4
        if wdx == 4:
            al.free("wkr_sb", "wdkv_sb")
            wdq_sb = al.sb("wdq_sb", [128, 16, QL], BF16, side="right")
            QS.wait(tS, hwin_rd)
            QS.wait(wsem["wdq"], wmark["wdq"])
            g_dq1 = QS.dma(dW, wdq_sb[:, :, :],
                           wdq_scr.rearrange("(c p) f -> p c f", p=128))
        wc = (wdx if is_kv else wdx - 4) * 512
        QS.wait(tS, hwin_rd)
        for j in range(4):
            gi = wdx * 4 + j
            if gi + 1 < 24:
                pending[gi + 1] = ln_load(tile_src(gi + 1), (gi + 1) % 2)
            ln_tile(pending.pop(gi), j * 128, gi % 2)
        QT.wait(dT[0], dT[0].n)
        QT.wait(dT[1], dT[1].n)
        QT.wait(dW, g_kr1 if is_kv else g_dq1)
        QT.wait(vS, vS.n)
        nm = 4 if is_kv else 6
        wsb = wdkv_sb if is_kv else wdq_sb
        for m in range(nm):
            for c in range(16):
                def mm(t, m=m, c=c, wsb=wsb, nm=nm):
                    return t.matmul(ps_c[m][:, :],
                                    wsb[:, c:c + 1, m * 128:(m + 1) * 128],
                                    hwin[:, c:c + 1, :],
                                    start=(c == 0), stop=(c == 15))
                QT.op(mm) if c == 15 else QT.op_noinc(mm)
        if is_kv:
            for c in range(16):
                def mm1(t, c=c):
                    return t.matmul(ps_kp1, wkr_sb[:, c:c + 1, 0:32],
                                    hwin[:, c:c + 1, :],
                                    start=(c == 0), stop=(c == 15))
                def mm2(t, c=c):
                    return t.matmul(ps_kp2, wkr_sb[:, c:c + 1, 32:64],
                                    hwin[:, c:c + 1, :],
                                    start=(c == 0), stop=(c == 15))
                QT.op_noinc(mm1)
                QT.op(mm2) if c == 15 else QT.op_noinc(mm2)
        hwin_rd = tS.n
        QA.wait(tS, hwin_rd)
        for m in range(nm):
            a_s = QA.op(lambda sc, m=m: sc.activation(sqw[:, :], ps_c[m][:, :], AF.Square))
            QT.wait(aS, a_s)
            def mm(t, m=m, nm=nm):
                return t.matmul(ps_r[0:1, :], ones_cf[:, 0:1], sqw[:, :],
                                start=(m == 0), stop=(m == nm - 1))
            t_sq = QT.op(mm)
            QA.wait(tS, t_sq)
        a_r = QA.op(lambda sc, nm=nm: sc.activation(rrow[0:1, :], ps_r[0:1, :], AF.Sqrt,
                                                    scale=1.0 / (nm * 128),
                                                    bias=epsrms[0:1, 0:1]))
        QV.wait(aS, a_r)
        QV.op(lambda v: v.reciprocal(rrow[0:1, :], rrow[0:1, :]))
        QT.wait(vS, vS.n)
        t_b = QT.op(lambda t: t.matmul(ps_bw[:, :], ones_rf[0:1, 0:128], rrow[0:1, :],
                                       start=True, stop=True))
        QV.wait(tS, t_b)
        QV.op(lambda v: v.tensor_copy(rbc[:, :], ps_bw[:, :]))
        gvec = gkv_c if is_kv else gq_c
        dst = kvT if is_kv else cq_sb
        for m in range(nm):
            QV.op(lambda v, m=m, wc=wc, gvec=gvec, dst=dst: v.scalar_tensor_tensor(
                dst[:, m:m + 1, wc:wc + 512], ps_c[m][:, :],
                gvec[:, m:m + 1], rbc[:, :], ALU.mult, ALU.mult))
        if is_kv:
            csl = slice(wc, wc + 512)
            QV.op(lambda v: v.tensor_copy(kst[:, 0:1, :], ps_kp1))
            QV.op(lambda v: v.tensor_copy(kst[:, 1:2, :], ps_kp2))
            QV.op(lambda v, csl=csl: v.tensor_mul(rka[0:32, :], kst[:, 0:1, :], cosk[:, csl]))
            QV.op(lambda v, csl=csl: v.tensor_mul(rkb[0:32, :], kst[:, 1:2, :], sink[:, csl]))
            QV.op(lambda v: v.tensor_sub(rka[0:32, :], rka[0:32, :], rkb[0:32, :]))
            QV.op(lambda v, csl=csl: v.tensor_copy(kpe_ext[0:32, csl], rka[0:32, :]))
            QV.op(lambda v, csl=csl: v.tensor_mul(rka[0:32, :], kst[:, 0:1, :], sink[:, csl]))
            QV.op(lambda v, csl=csl: v.tensor_mul(rkb[0:32, :], kst[:, 1:2, :], cosk[:, csl]))
            QV.op(lambda v: v.tensor_add(rka[0:32, :], rka[0:32, :], rkb[0:32, :]))
            QV.wait(dB, dB.n)
            v_k2 = QV.op(lambda v: v.tensor_copy(rkbf[0:32, :], rka[0:32, :]))
            QS.wait(vS, v_k2)
            QS.dma(dB, kpe_ext[32:64, csl], rkbf[0:32, :])
    v_kv = vS.n
    v_cq = vS.n
    QV.wait(dB, dB.n)
    QV.op(lambda v: v.memset(kpe_ext[64:65, :], 1.0))
    QA.wait(vS, vS.n)
    a_kpsq = QA.op(lambda sc: sc.activation(kpe_sq[0:64, :], kpe_ext[0:64, :], AF.Square))
    if debug:
        QS.wait(vS, vS.n)
        QS.dma(dDbg, dbg["kvT"], kvT[:, :, :])
        QS.dma(dDbg, dbg["kpe"], kpe_ext[:, :])
        QS.dma(dDbg, dbg["cq"], cq_sb[:, :, :])
        QV.wait(dDbg, dDbg.n)
        QT.wait(dDbg, dDbg.n)
        QA.wait(dDbg, dDbg.n)
    al.free("ps_c5", "ps_c4", "ps_c3", "ps_c2", "ps_c1", "ps_c0", "ps_bw", "ps_r")
    al.free("wdq_sb")
    al.free("rbc", "rrow", "sqw", "rkbf", "rkb", "kst", "rka", "hwin", "st1", "sqst",
            "hst", "xst")
    al.free("sink", "cosk")


    q_no = al.sb("q_no", [128, HG, NQ], BF16)
    qpe_ext = al.sb("qpe_ext", [65, HG * NQ], BF16)
    kno = al.sb("kno", [128, HG, S], BF16)
    vtok = al.sb("vtok", [128, 16, HG * VD], BF16)
    sqs = al.sb("sqs", [128, S], BF16)
    qn_row = al.sb("qn_row", [1, NQ], F32)
    bnd_row = al.sb("bnd_row", [1, NQ], BF16)
    kmx = al.sb("kmx", [1, 8], F32)
    p_sb = al.sb("p_sb", [128, 3, 512], BF16)
    rec_sb = al.sb("rec_sb", [128, 512], F32)
    rpq1 = al.sb("rpq1", [32, NQ], F32)
    rpq2 = al.sb("rpq2", [32, NQ], F32)
    rps1 = al.sb("rps1", [32, NQ], F32)
    rps2 = al.sb("rps2", [32, NQ], F32)
    rpqbf = al.sb("rpqbf", [32, NQ], BF16)
    ctx = al.sb("ctx", [128, 16, NQ], BF16, side="right")

    ctx_evac = {}
    last_exp_bank = [0, 0]
    last_av_pbuf = [0, 0, 0]
    d_wo = 0
    wo_sb = None

    for grp in range(NGRP):
        wuq_sb = al.sb("wuq_sb", [128, 6, HG * 192], BF16, side="right")
        wuk_sb = al.sb("wuk_sb", [128, 4, HG * NOPE], BF16, side="right")
        wuv_sb = al.sb("wuv_sb", [128, 4, HG * VD], BF16, side="right")
        QS.wait(tS, tS.n)
        QS.wait(wsem["wuq"], wmark["wuq"])
        QS.wait(wsem["wuk"], wmark["wuk"])
        QS.wait(wsem["wuv"], wmark["wuv"])
        QS.dma(dW, wuq_sb[:, :, :],
               wuq_scr[:, grp * HG * 192:(grp + 1) * HG * 192]
               .rearrange("(c p) f -> p c f", p=128))
        QS.dma(dW, wuk_sb[:, :, :],
               wuk_scr[:, grp * HG * NOPE:(grp + 1) * HG * NOPE]
               .rearrange("(c p) f -> p c f", p=128))
        d_wg = QS.dma(dW, wuv_sb[:, :, :],
                      wuv_scr[:, grp * HG * VD:(grp + 1) * HG * VD]
                      .rearrange("(c p) f -> p c f", p=128))

        ps_a = [al.ps("ps_a0", [128, 512]), al.ps("ps_a1", [128, 512])]
        ps_p1 = al.ps("ps_p1", [32, 1024])
        ps_p2 = al.ps("ps_p2", [32, 1024])

        QT.wait(dW, d_wg)
        QT.wait(vS, vS.n)
        QT.wait(aS, aS.n)
        evac_done = [vS.n, vS.n]
        for hl in range(HG):
            for nt in range(2):
                pb = (hl * 2 + nt) % 2
                QT.wait(vS, evac_done[pb])
                for c in range(6):
                    def mm(t, hl=hl, nt=nt, c=c, pa=ps_a[pb], w=wuq_sb):
                        return t.matmul(pa[:, :],
                                        w[:, c:c + 1, hl * 192:hl * 192 + 128],
                                        cq_sb[:, c:c + 1, nt * 512:(nt + 1) * 512],
                                        start=(c == 0), stop=(c == 5))
                    QT.op(mm) if c == 5 else QT.op_noinc(mm)
                QV.wait(tS, tS.n)
                evac_done[pb] = QV.op(lambda v, hl=hl, nt=nt, pa=ps_a[pb]: v.tensor_copy(
                    q_no[:, hl:hl + 1, nt * 512:(nt + 1) * 512], pa[:, :]))

        v_st = 0
        for hl in range(HG):
            co = hl * NQ
            QT.wait(vS, v_st)
            for c in range(6):
                for nt in range(2):
                    def mm1(t, hl=hl, c=c, nt=nt, pp=ps_p1, w=wuq_sb):
                        return t.matmul(pp[:, nt * 512:(nt + 1) * 512],
                                        w[:, c:c + 1,
                                          hl * 192 + 128:hl * 192 + 160],
                                        cq_sb[:, c:c + 1, nt * 512:(nt + 1) * 512],
                                        start=(c == 0), stop=(c == 5))
                    def mm2(t, hl=hl, c=c, nt=nt, pp=ps_p2, w=wuq_sb):
                        return t.matmul(pp[:, nt * 512:(nt + 1) * 512],
                                        w[:, c:c + 1,
                                          hl * 192 + 160:hl * 192 + 192],
                                        cq_sb[:, c:c + 1, nt * 512:(nt + 1) * 512],
                                        start=(c == 0), stop=(c == 5))
                    QT.op_noinc(mm1)
                    QT.op(mm2) if (c == 5 and nt == 1) else QT.op_noinc(mm2)
            t_pe = tS.n
            QV.wait(tS, t_pe)
            QV.op(lambda v, pp=ps_p1: v.tensor_copy(rps1[:, :], pp[:, :]))
            v_st = QV.op(lambda v, pp=ps_p2: v.tensor_copy(rps2[:, :], pp[:, :]))
            QV.op(lambda v: v.tensor_mul(rpq1[:, :], rps1[:, :], cosq[:, :]))
            QV.op(lambda v: v.tensor_mul(rpq2[:, :], rps2[:, :], sinq[:, :]))
            QV.op(lambda v: v.tensor_sub(rpq1[:, :], rpq1[:, :], rpq2[:, :]))
            QV.op(lambda v, co=co: v.tensor_copy(qpe_ext[0:32, co:co + NQ], rpq1[:, :]))
            QV.op(lambda v: v.tensor_mul(rpq1[:, :], rps1[:, :], sinq[:, :]))
            QV.op(lambda v: v.tensor_mul(rpq2[:, :], rps2[:, :], cosq[:, :]))
            QV.op(lambda v: v.tensor_add(rpq1[:, :], rpq1[:, :], rpq2[:, :]))
            QV.wait(dB, dB.n)
            v_q2 = QV.op(lambda v: v.tensor_copy(rpqbf[:, :], rpq1[:, :]))
            QS.wait(vS, v_q2)
            QS.dma(dB, qpe_ext[32:64, co:co + NQ], rpqbf[:, :])

        QT.wait(vS, v_kv)
        for hl in range(HG):
            for nt in range(4):
                pb = (hl * 4 + nt) % 2
                QT.wait(vS, evac_done[pb])
                for c in range(4):
                    def mm(t, hl=hl, nt=nt, c=c, pa=ps_a[pb], w=wuk_sb):
                        return t.matmul(pa[:, :],
                                        w[:, c:c + 1, hl * 128:(hl + 1) * 128],
                                        kvT[:, c:c + 1, nt * 512:(nt + 1) * 512],
                                        start=(c == 0), stop=(c == 3))
                    QT.op(mm) if c == 3 else QT.op_noinc(mm)
                QV.wait(tS, tS.n)
                evac_done[pb] = QV.op(lambda v, hl=hl, nt=nt, pa=ps_a[pb]: v.tensor_copy(
                    kno[:, hl:hl + 1, nt * 512:(nt + 1) * 512], pa[:, :]))

        for kt in range(16):
            pb = kt % 2
            QT.wait(vS, evac_done[pb])
            for c in range(4):
                def mm(t, kt=kt, c=c, pa=ps_a[pb], w=wuv_sb):
                    return t.matmul(pa[:, :],
                                    kvT[:, c:c + 1, kt * 128:(kt + 1) * 128],
                                    w[:, c:c + 1, 0:HG * VD],
                                    start=(c == 0), stop=(c == 3))
                QT.op(mm) if c == 3 else QT.op_noinc(mm)
            QV.wait(tS, tS.n)
            evac_done[pb] = QV.op(lambda v, kt=kt, pa=ps_a[pb]: v.tensor_copy(
                vtok[:, kt:kt + 1, :], pa[:, :]))

        last_red = 0
        for hl in range(HG):
            QA.wait(vS, vS.n)
            QA.wait(tS, tS.n)
            a_sq = QA.op(lambda sc, hl=hl: sc.activation(sqs[:, :], kno[:, hl:hl + 1, :],
                                                         AF.Square))
            for half in range(2):
                QT.wait(aS, a_sq)
                QT.wait(aS, a_kpsq)
                QT.wait(vS, last_red)
                for j in range(2):
                    sl = slice(half * 1024 + j * 512, half * 1024 + (j + 1) * 512)
                    def mma(t, j=j, sl=sl, pp=ps_p1):
                        return t.matmul(pp[0:1, j * 512:(j + 1) * 512],
                                        ones_col[:, 0:1], sqs[:, sl],
                                        start=True, stop=False)
                    def mmb(t, j=j, sl=sl, pp=ps_p1):
                        return t.matmul(pp[0:1, j * 512:(j + 1) * 512],
                                        ones_col[0:64, 0:1], kpe_sq[0:64, sl],
                                        start=False, stop=True)
                    QT.op_noinc(mma)
                    QT.op(mmb)
                QV.wait(tS, tS.n)
                last_red = QV.op(lambda v, half=half, pp=ps_p1: v.reduce_max(
                    kmx[0:1, half:half + 1], pp[0:1, 0:1024], AX.X))
            QV.op(lambda v: v.tensor_tensor(kmx[0:1, 2:3], kmx[0:1, 0:1], kmx[0:1, 1:2],
                                            ALU.max))
            v_kmx = vS.n
            QA.wait(vS, v_kmx)
            QA.op(lambda sc, hl=hl, grp=grp: sc.activation(
                kmaxn[0:1, grp * HG + hl:grp * HG + hl + 1], kmx[0:1, 2:3],
                AF.Sqrt, scale=1.0404))

        a_qn_prev = 0
        for hl in range(HG):
            QA.wait(tS, tS.n)
            QA.wait(vS, vS.n)
            a_q1 = QA.op(lambda sc, hl=hl: sc.activation(sqs[:, 0:NQ],
                                                         q_no[:, hl:hl + 1, :], AF.Square))
            QT.wait(aS, a_q1)
            QT.wait(aS, a_qn_prev)
            for j in range(2):
                def mmq(t, j=j, pp=ps_p1):
                    return t.matmul(pp[0:1, j * 512:(j + 1) * 512], ones_col[:, 0:1],
                                    sqs[:, j * 512:(j + 1) * 512], start=True, stop=False)
                QT.op(mmq) if j == 1 else QT.op_noinc(mmq)
            QA.wait(tS, tS.n)
            a_q2 = QA.op(lambda sc, hl=hl: sc.activation(
                sqs[0:64, 0:NQ], qpe_ext[0:64, hl * NQ:(hl + 1) * NQ], AF.Square))
            QT.wait(aS, a_q2)
            for j in range(2):
                def mmq2(t, j=j, pp=ps_p1):
                    return t.matmul(pp[0:1, j * 512:(j + 1) * 512], ones_col[0:64, 0:1],
                                    sqs[0:64, j * 512:(j + 1) * 512], start=False, stop=True)
                QT.op(mmq2) if j == 1 else QT.op_noinc(mmq2)
            QA.wait(tS, tS.n)
            a_qn_prev = QA.op(lambda sc, pp=ps_p1: sc.activation(qn_row[0:1, :], pp[0:1, 0:1024],
                                                                 AF.Sqrt))
            QV.wait(aS, a_qn_prev)
            QV.wait(dB, dB.n)
            QV.op(lambda v, hl=hl, grp=grp: v.tensor_scalar(
                bnd_row[0:1, :], qn_row[0:1, :],
                kmaxn[0:1, grp * HG + hl:grp * HG + hl + 1], None, ALU.mult))
            v_bnd = QV.op(lambda v: v.tensor_scalar(bnd_row[0:1, :], bnd_row[0:1, :],
                                                    -1.0, None, ALU.mult))
            QS.wait(vS, v_bnd)
            QS.dma(dB, qpe_ext[64:65, hl * NQ:(hl + 1) * NQ], bnd_row[0:1, :])

        if debug and grp == 0:
            QS.wait(vS, vS.n)
            QS.wait(aS, aS.n)
            QS.wait(dB, dB.n)
            QS.dma(dDbg, dbg["qno"], q_no[:, :, :])
            QS.dma(dDbg, dbg["qpe"], qpe_ext[:, :])
            QS.dma(dDbg, dbg["kno"], kno[:, :, :])
            QS.dma(dDbg, dbg["vtok"], vtok[:, :, :])
            QS.dma(dDbg, dbg["kmaxn"], kmaxn[:, :])
            QV.wait(dDbg, dDbg.n)
            QT.wait(dDbg, dDbg.n)
            QA.wait(dDbg, dDbg.n)

        al.free("ps_p2", "ps_p1", "ps_a1", "ps_a0")
        al.free("wuv_sb", "wuk_sb", "wuq_sb")

        QT.wait(dB, dB.n)
        ps_sT = [al.ps("ps_sT0", [128, 512]), al.ps("ps_sT1", [128, 512])]
        ps_cx = [al.ps("ps_cx0", [128, 512]), al.ps("ps_cx1", [128, 512])]
        ps_sm = [al.ps("ps_sm0", [128, 512]), al.ps("ps_sm1", [128, 512])]

        QT.wait(vS, vS.n)
        QT.wait(aS, aS.n)
        for sl_i in range(2):
            nk = SLOT_K[sl_i]
            qs = sl_i * 512
            for hl in range(HG):
                gh = grp * HG + hl
                cb = gh % 2
                prev_evac = ctx_evac.get(gh - 2, 0)
                def emit_av(kc, dep_v, a_ex):
                    pbuf = kc % 3
                    if dep_v:
                        QT.wait(vS, dep_v)
                    else:
                        QT.wait(aS, a_ex)
                    if kc == 0:
                        QT.wait(vS, prev_evac)
                    def mm_av(t, hl=hl, kc=kc, pbuf=pbuf, nk=nk, pp=ps_cx[cb]):
                        return t.matmul(pp[:, :],
                                        vtok[:, kc:kc + 1, hl * VD:(hl + 1) * VD],
                                        p_sb[:, pbuf:pbuf + 1, :],
                                        start=(kc == 0), stop=(kc == nk - 1))
                    QT.op_noinc(mm_av)
                    def mm_sm(t, kc=kc, pbuf=pbuf, nk=nk, pp=ps_sm[cb]):
                        return t.matmul(pp[:, :], ones_mat[:, :],
                                        p_sb[:, pbuf:pbuf + 1, :],
                                        start=(kc == 0), stop=(kc == nk - 1))
                    last_av_pbuf[pbuf] = QT.op(mm_sm)

                pend = None
                for kc in range(nk):
                    pbuf = kc % 3
                    sbk = kc % 2
                    QT.wait(aS, last_exp_bank[sbk])
                    def mm_n(t, hl=hl, kc=kc, qs=qs, pp=ps_sT[sbk]):
                        return t.matmul(pp[:, :],
                                        kno[:, hl:hl + 1, kc * 128:(kc + 1) * 128],
                                        q_no[:, hl:hl + 1, qs:qs + 512],
                                        start=True, stop=False)
                    QT.op_noinc(mm_n)
                    def mm_p(t, hl=hl, kc=kc, qs=qs, pp=ps_sT[sbk]):
                        return t.matmul(pp[:, :],
                                        kpe_ext[0:65, kc * 128:(kc + 1) * 128],
                                        qpe_ext[0:65, hl * NQ + qs:hl * NQ + qs + 512],
                                        start=False, stop=True)
                    t_sc = QT.op(mm_p)
                    QA.wait(tS, t_sc)
                    QA.wait(tS, last_av_pbuf[pbuf])
                    a_ex = QA.op(lambda sc, pbuf=pbuf, pp=ps_sT[sbk]: sc.activation(
                        p_sb[:, pbuf:pbuf + 1, :], pp[:, :], AF.Exp, scale=SCALE))
                    last_exp_bank[sbk] = a_ex
                    midx = kc if sl_i == 0 else (8 + (kc - 8) if kc >= 8 else None)
                    dep_v = 0
                    if midx is not None:
                        QV.wait(aS, a_ex)
                        dep_v = QV.op(lambda v, pbuf=pbuf, midx=midx: v.tensor_mul(
                            p_sb[:, pbuf:pbuf + 1, :], p_sb[:, pbuf:pbuf + 1, :],
                            masks[:, midx:midx + 1, :]))
                    if pend is not None:
                        emit_av(*pend)
                    pend = (kc, dep_v, a_ex)
                emit_av(*pend)
                QV.wait(tS, last_av_pbuf[(nk - 1) % 3])
                QV.op(lambda v, pp=ps_sm[cb]: v.reciprocal(rec_sb[:, :], pp[:, :]))
                ctx_evac[gh] = QV.op(lambda v, gh=gh, qs=qs, pp=ps_cx[cb]: v.scalar_tensor_tensor(
                    ctx[:, gh:gh + 1, qs:qs + 512], pp[:, :], 1.0,
                    rec_sb[:, :], ALU.mult, ALU.mult))
        al.free("ps_sm1", "ps_sm0", "ps_cx1", "ps_cx0", "ps_sT1", "ps_sT0")

    v_ctx = vS.n
    if debug:
        QS.wait(vS, v_ctx)
        QS.dma(dDbg, dbg["ctx"], ctx[:, :, :])
        QV.wait(dDbg, dDbg.n)
        QT.wait(dDbg, dDbg.n)
        QA.wait(dDbg, dDbg.n)
    al.free("rpqbf", "rps2", "rps1", "rpq2", "rpq1", "rec_sb", "p_sb", "kmx",
            "bnd_row", "qn_row", "sqs", "vtok", "kno", "qpe_ext", "q_no")
    al.free("kpe_sq", "kpe_ext", "kvT", "cq_sb", "masks")

    wo_sb = al.sb("wo_sb", [128, 16, DM], BF16, side="right")
    QS.wait(tS, tS.n)
    QS.wait(vS, vS.n)
    QS.wait(aS, aS.n)
    QS.wait(wsem["wo"], wmark["wo"])
    d_wo = QS.dma(dW, wo_sb[:, :, :], wo_scr.rearrange("(c p) f -> p c f", p=128))
    h2T = al.sb("h2T", [128, 16, NQ], BF16)
    xqst = al.sb("xqst", [128, 2, DM], F32, side="right")
    x1st = al.sb("x1st", [128, 2, DM], F32, side="right")
    h2st = al.sb("h2st", [128, 2, DM], BF16, side="right")
    sqstf = al.sb("sqstf", [128, DM], F32, side="right")
    st1f = al.sb("st1f", [128, 8], F32, side="right")
    ps_f = [al.ps("ps_f%d" % i, [128, 512]) for i in range(4)]

    f_evac = {}
    x1_wr = [0, 0]
    h2_rd = [0, 0]
    x_rd = [0, 0]
    QS.wait(tS, tS.n)
    QS.wait(vS, vS.n)
    QS.wait(aS, aS.n)
    for tt in range(8):
        buf = tt % 2
        QS.wait(vS, x_rd[buf])
        d_xq = QS.dma(dX[buf], xqst[:, buf:buf + 1, :], x_q[tt * 128:(tt + 1) * 128, :])
        QT.wait(vS, v_ctx)
        QT.wait(dW, d_wo)
        for ds in range(4):
            pb = ds
            QT.wait(vS, f_evac.get((tt - 1, ds), 0))
            for c in range(16):
                def mm(t, tt=tt, ds=ds, c=c, pb=pb):
                    return t.matmul(ps_f[pb][:, :],
                                    ctx[:, c:c + 1, tt * 128:(tt + 1) * 128],
                                    wo_sb[:, c:c + 1, ds * 512:(ds + 1) * 512],
                                    start=(c == 0), stop=(c == 15))
                QT.op(mm) if c == 15 else QT.op_noinc(mm)
            QV.wait(tS, tS.n)
            QV.wait(dX[buf], d_xq)
            QV.wait(dSt[buf], x1_wr[buf])
            f_evac[(tt, ds)] = QV.op(lambda v, tt=tt, ds=ds, pb=pb, buf=buf: v.tensor_add(
                x1st[:, buf:buf + 1, ds * 512:(ds + 1) * 512], ps_f[pb][:, :],
                xqst[:, buf:buf + 1, ds * 512:(ds + 1) * 512]))
        x_rd[buf] = f_evac[(tt, 3)]
        xa = x1st[:, buf:buf + 1, :]
        ha = h2st[:, buf:buf + 1, :]
        QS.wait(vS, f_evac[(tt, 3)])
        x1_wr[buf] = QS.dma(dSt[buf], x1_scr[tt * 128:(tt + 1) * 128, :], xa)
        QV.op(lambda v, xa=xa: v.reduce_sum(st1f[:, 0:1], xa, AX.X))
        QA.wait(vS, vS.n)
        QA.op(lambda sc, xa=xa: sc.activation(sqstf[:, :], xa, AF.Square, accum_out=st1f[:, 1:2]))
        QV.wait(aS, aS.n)
        QV.op(lambda v: v.tensor_scalar(st1f[:, 2:3], st1f[:, 0:1], 1.0 / DM, None, ALU.mult))
        QV.op(lambda v: v.tensor_scalar(st1f[:, 3:4], st1f[:, 1:2], 1.0 / DM, None, ALU.mult))
        QV.op(lambda v: v.tensor_mul(st1f[:, 4:5], st1f[:, 2:3], st1f[:, 2:3]))
        v_var = QV.op(lambda v: v.tensor_sub(st1f[:, 3:4], st1f[:, 3:4], st1f[:, 4:5]))
        QA.wait(vS, v_var)
        a_std = QA.op(lambda sc: sc.activation(st1f[:, 5:6], st1f[:, 3:4], AF.Sqrt, bias=epsln[:, 0:1]))
        QV.wait(aS, a_std)
        QV.op(lambda v: v.reciprocal(st1f[:, 6:7], st1f[:, 5:6]))
        v_nb = QV.op(lambda v: v.tensor_scalar(st1f[:, 7:8], st1f[:, 2:3], st1f[:, 6:7], -1.0,
                                               ALU.mult, ALU.mult))
        QA.wait(vS, v_nb)
        if h2_rd[buf]:
            QA.wait(dT[buf], h2_rd[buf])
        a_h = QA.op(lambda sc, ha=ha, xa=xa: sc.activation(ha, xa, AF.Identity,
                                                           bias=st1f[:, 7:8], scale=st1f[:, 6:7]))
        QS.wait(aS, a_h)
        h2_rd[buf] = QS.dma(dT[buf], h2T[:, :, tt * 128:(tt + 1) * 128], ha, transpose=True)
    d_h2T = (dT[0].n, dT[1].n)
    if debug:
        QS.wait(dT[0], d_h2T[0])
        QS.wait(dT[1], d_h2T[1])
        QS.wait(dSt[0], dSt[0].n)
        QS.wait(dSt[1], dSt[1].n)
        QS.dma(dDbg, dbg["h2T"], h2T[:, :, :])
        QS.dma(dDbg, dbg["x1"], x1_scr)
        QV.wait(dDbg, dDbg.n)
        QT.wait(dDbg, dDbg.n)
        QA.wait(dDbg, dDbg.n)
    al.free("ps_f3", "ps_f2", "ps_f1", "ps_f0")
    al.free("st1f", "sqstf", "h2st", "x1st", "xqst")
    al.free("wo_sb")
    al.free("ctx")
    al.free("stgb", "stg32")

    sdbc = al.sb("sdbc", [128, DM], F32)
    nact = al.sb("nact", [1, NQ], BF16)
    x1r = al.sb("x1r", [128, 2, 512], F32)
    y1 = al.sb("y1", [128, 2, 512], F32)
    sdrow = al.sb("sdrow", [1, DM], F32)
    act_sb = al.sb("act_sb", [128, 44, NQ], BF16, side="right")
    slab = [al.sb("slab0", [128, 16, 512], BF16, side="right"),
            al.sb("slab1", [128, 16, 512], BF16, side="right")]
    gate_sb = al.sb("gate_sb", [128, 8, 512], BF16, side="right")
    dslab = [al.sb("dslab0", [128, 4, 512], BF16, side="right"),
             al.sb("dslab1", [128, 4, 512], BF16, side="right")]

    ps_g = [al.ps("ps_g0", [128, 512]), al.ps("ps_g1", [128, 512])]
    ps_u = [al.ps("ps_u0", [128, 512]), al.ps("ps_u1", [128, 512])]
    ps_as = al.ps("ps_as", [1, 1024])
    ps_sd = al.ps("ps_sd", [128, 512])

    d_sdr = QS.dma(dIn, sdrow[:, :], s_down_r)
    for j in range(4):
        QT.wait(dIn, d_sdr)
        QT.wait(vS, vS.n)
        t_b = QT.op(lambda t, j=j: t.matmul(ps_sd[:, :], ones_rf[0:1, 0:128],
                                            sdrow[0:1, j * 512:(j + 1) * 512],
                                            start=True, stop=True))
        QV.wait(tS, t_b)
        QV.op(lambda v, j=j: v.tensor_copy(sdbc[:, j * 512:(j + 1) * 512], ps_sd[:, :]))
    v_sdbc = vS.n
    QS.wait(vS, v_sdbc)
    al.free("sdrow")

    slab_user = [0, 0]
    silu_done = {}
    act_done = {}
    for sec in range(11):
        for which in range(2):
            l = 2 * sec + which
            sb_i = l % 2
            src = g_scr if which == 0 else u_scr
            QS.wait(wsem["g"] if which == 0 else wsem["u"],
                    wmark["g"] if which == 0 else wmark["u"])
            QS.wait(tS, slab_user[sb_i])
            d_sl = QS.dma(dSl[sb_i], slab[sb_i][:, :, :],
                          src[sec * 512:(sec + 1) * 512, :], transpose=True)
            for hcl in range(4):
                hc = sec * 4 + hcl
                for nt in range(2):
                    pp = (ps_g if which == 0 else ps_u)[nt]
                    QT.wait(dSl[sb_i], d_sl)
                    QT.wait(dT[0], d_h2T[0])
                    QT.wait(dT[1], d_h2T[1])
                    if which == 0:
                        QT.wait(aS, silu_done.get((hc - 1, nt), 0))
                    else:
                        QT.wait(vS, act_done.get((hc - 1, nt), 0))
                    for c in range(16):
                        def mm(t, sb_i=sb_i, hcl=hcl, nt=nt, c=c, pp=pp):
                            return t.matmul(pp[:, :],
                                            slab[sb_i][:, c:c + 1, hcl * 128:(hcl + 1) * 128],
                                            h2T[:, c:c + 1, nt * 512:(nt + 1) * 512],
                                            start=(c == 0), stop=(c == 15))
                        t_mm = QT.op(mm) if c == 15 else QT.op_noinc(mm)
                    slab_user[sb_i] = tS.n
                    gi = hcl * 2 + nt
                    if which == 0:
                        QA.wait(tS, tS.n)
                        QA.wait(vS, act_done.get((hc - 4, nt), 0))
                        silu_done[(hc, nt)] = QA.op(lambda sc, hc=hc, gi=gi, pp=pp: sc.activation(
                            gate_sb[:, gi:gi + 1, :], pp[:, :], AF.Silu,
                            scale=sgate_c[:, hc:hc + 1]))
                    else:
                        QV.wait(tS, tS.n)
                        QV.wait(aS, silu_done[(hc, nt)])
                        a_d = QV.op(lambda v, hc=hc, nt=nt, gi=gi, pp=pp: v.scalar_tensor_tensor(
                            act_sb[:, hc:hc + 1, nt * 512:(nt + 1) * 512], pp[:, :],
                            sup_c[:, hc:hc + 1], gate_sb[:, gi:gi + 1, :],
                            ALU.mult, ALU.mult))
                        act_done[(hc, nt)] = a_d
                        QT.wait(vS, a_d)
                        def mm_as(t, hc=hc, nt=nt):
                            return t.matmul(ps_as[0:1, nt * 512:(nt + 1) * 512],
                                            ones_col[:, 0:1],
                                            act_sb[:, hc:hc + 1, nt * 512:(nt + 1) * 512],
                                            start=(hc == 0), stop=(hc == 43))
                        QT.op(mm_as) if hc == 43 else QT.op_noinc(mm_as)
    t_as = tS.n
    QA.wait(tS, t_as)
    a_nact = QA.op(lambda sc: sc.activation(nact[0:1, :], ps_as[0:1, :], AF.Copy,
                                            scale=-128.0))
    al.free("ps_sd", "ps_as", "ps_u1", "ps_u0", "ps_g1", "ps_g0")
    ps_d = [al.ps("ps_d%d" % i, [128, 512]) for i in range(8)]
    QT.wait(aS, aS.n)

    SECS = [(0, 4), (4, 4), (8, 4), (12, 4), (16, 4), (20, 4),
            (24, 4), (28, 4), (32, 4), (36, 4), (40, 4)]
    dslab_user = [0, 0]
    y_done = {}
    x1r_use = [0, 0]
    y1_use = [0, 0]
    QA.raw_wait(dSt[0], dSt[0].n)
    QA.raw_wait(dSt[1], dSt[1].n)
    for ds in range(4):
        for si, (h0, nch) in enumerate(SECS):
            sb_i = si % 2
            QS.wait(wsem["d"], wmark["d"])
            QS.wait(tS, dslab_user[sb_i])
            d_sl = QS.dma(dDl[sb_i], dslab[sb_i][:, 0:nch, :],
                          d_scr[ds * 512:(ds + 1) * 512, h0 * 128:(h0 + nch) * 128],
                          transpose=True)
            last = None
            for j in range(nch):
                hc = h0 + j
                for tt in range(8):
                    QT.wait(dDl[sb_i], d_sl)
                    if hc == 0:
                        QT.wait(vS, y_done.get((tt, ds - 1), 0))
                    def mm(t, sb_i=sb_i, j=j, hc=hc, tt=tt):
                        return t.matmul(ps_d[tt][:, :],
                                        act_sb[:, hc:hc + 1, tt * 128:(tt + 1) * 128],
                                        dslab[sb_i][:, j:j + 1, :],
                                        start=(hc == 0), stop=False)
                    if j == nch - 1 and tt == 7:
                        dslab_user[sb_i] = QT.op(mm)
                    else:
                        QT.op_noinc(mm)
        for tt in range(8):
            buf = tt % 2
            QT.wait(aS, a_nact)
            def mm_c(t, tt=tt):
                return t.matmul(ps_d[tt][:, :], nact[0:1, tt * 128:(tt + 1) * 128],
                                ones_rb[0:1, :], start=False, stop=True)
            t_c = QT.op(mm_c)
            QA.wait(vS, x1r_use[buf])
            d_x1 = QA.dma(dR[buf], x1r[:, buf:buf + 1, :],
                          x1_scr[tt * 128:(tt + 1) * 128, ds * 512:(ds + 1) * 512])
            QV.wait(tS, t_c)
            QV.wait(dR[buf], d_x1)
            QV.wait(vS, v_sdbc)
            QV.wait(dY[buf], y1_use[buf])
            QV.op(lambda v, ds=ds, buf=buf, tt=tt: v.tensor_mul(
                y1[:, buf:buf + 1, :], ps_d[tt][:, :],
                sdbc[:, ds * 512:(ds + 1) * 512]))
            y_done[(tt, ds)] = QV.op(lambda v, buf=buf: v.tensor_add(
                y1[:, buf:buf + 1, :], y1[:, buf:buf + 1, :], x1r[:, buf:buf + 1, :]))
            x1r_use[buf] = y_done[(tt, ds)]
            QS.wait(vS, y_done[(tt, ds)])
            y1_use[buf] = QS.dma(
                dY[buf], y_out[tt * 128:(tt + 1) * 128, ds * 512:(ds + 1) * 512],
                y1[:, buf:buf + 1, :])
    QS.wait(dY[0], dY[0].n)
    QS.wait(dY[1], dY[1].n)
    if debug:
        QS.wait(dDbg, dDbg.n)

    block_g = nc.Block()
    block = block_g.__enter__()

    def run_queue(q):
        def body(eng):
            for fn in q.ops:
                fn(eng)
        return body

    block.sync(run_queue(QS))
    block.tensor(run_queue(QT))
    block.vector(run_queue(QV))
    block.scalar(run_queue(QA))
    block.gpsimd(run_queue(QG))
    block_g.__exit__(None, None, None)

    for g in reversed(sem_guards):
        g.__exit__(None, None, None)
    return nc


_cache = {}


def kernel(**inputs):
    f = lambda k: np.ascontiguousarray(np.asarray(inputs[k], dtype=np.float32))
    x = f("L0_attn_x")
    w = {
        "w_dq": f("L0_attn_w_dq"),
        "w_uq": f("L0_attn_w_uq"),
        "w_dkv": f("L0_attn_w_dkv"),
        "w_kr": f("L0_attn_w_kr"),
        "w_uk": f("L0_attn_w_uk"),
        "w_uv": f("L0_attn_w_uv"),
        "w_o": f("L0_attn_w_o"),
        "g_q_t": np.ascontiguousarray(f("L0_attn_g_q").reshape(6, 128).T),
        "g_kv_t": np.ascontiguousarray(f("L0_attn_g_kv").reshape(4, 128).T),
        "s_gate_t": np.ascontiguousarray(f("L0_ffn_s_gate").reshape(44, 128).T),
        "s_up_t": np.ascontiguousarray(f("L0_ffn_s_up").reshape(44, 128).T),
        "s_down_r": np.ascontiguousarray(f("L0_ffn_s_down").reshape(1, DM)),
        "wq_gate": np.ascontiguousarray(np.asarray(inputs["L0_ffn_wq_gate"], dtype=np.int32)),
        "wq_up": np.ascontiguousarray(np.asarray(inputs["L0_ffn_wq_up"], dtype=np.int32)),
        "wq_down": np.ascontiguousarray(np.asarray(inputs["L0_ffn_wq_down"], dtype=np.int32)),
    }
    if "nc" not in _cache:
        _cache["nc"] = build_nc()
    nc = _cache["nc"]

    in_maps = []
    sels = []
    for core in range(8):
        b, half = core // 2, core % 2
        if half == 0:
            rows = np.r_[0:512, 1536:2048]
            g0, g1 = 0.0, 1536.0
        else:
            rows = np.r_[512:1536]
            g0, g1 = 512.0, 1024.0
        sels.append((b, rows))
        m = dict(w)
        m["x_full"] = x[b]
        m["x_q"] = np.ascontiguousarray(x[b][rows])
        m["pos_q"] = np.ascontiguousarray(rows.astype(np.float32).reshape(1, NQ))
        m["gmeta"] = np.array([[g0, g1]], dtype=np.float32)
        in_maps.append(m)

    res = run_bass_kernel_spmd(nc, in_maps, core_ids=list(range(8)))
    y = np.empty((4, S, DM), dtype=np.float32)
    for core in range(8):
        b, rows = sels[core]
        y[b][rows] = np.asarray(res.results[core]["y"], dtype=np.float32)
    return y


# version 07
# speedup vs baseline: 1.0563x; 1.0563x over previous
import math
import numpy as np
import ml_dtypes
import concourse.bass as bass
import concourse.mybir as mybir
from concourse.bass_utils import run_bass_kernel_spmd

F32 = mybir.dt.float32
BF16 = mybir.dt.bfloat16
I32 = mybir.dt.int32
AF = mybir.ActivationFunctionType
ALU = mybir.AluOpType
AX = mybir.AxisListType

S = 2048
DM = 2048
H = 16
NOPE = 128
RD = 64
VD = 128
QL = 768
KVL = 512
HID = 5632
NQ = 1024
HG = 4
NGRP = 4
SLOT_K = (8, 16)
SCALE = float(1.0 / math.sqrt(192.0))
LN_EPS = 1e-5
RMS_EPS = 1e-6
MAGIC = 8388608.0

C_HI = 6.28125
C_MID = float(np.float32(2 * np.pi - C_HI))
C_LO = float(np.float64(2 * np.pi) - np.float64(C_HI) - np.float64(np.float32(C_MID)))
INV2PI = float(1.0 / (2 * np.pi))
LOG1E4_32 = float(np.log(10000.0) / 32.0)


class Ctr:
    def __init__(self, handle):
        self.h = handle
        self.n = 0

    def inc(self, k=1):
        self.n += k
        return self.n


class EngQ:
    def __init__(self, ctr, chain):
        self.ops = []
        self.ctr = ctr
        self.chain = chain

    def wait(self, ctr, val):
        if val > 0:
            self.ops.append(lambda e, s=ctr.h, v=val: e.wait_ge(s, v))

    def op(self, fn):
        if self.chain and self.ctr.n > 0:
            self.ops.append(lambda e, s=self.ctr.h, v=self.ctr.n: e.wait_ge(s, v))
        val = self.ctr.inc()
        def run(e, fn=fn, s=self.ctr.h):
            fn(e).then_inc(s, 1)
        self.ops.append(run)
        return val

    def op_noinc(self, fn):
        self.ops.append(lambda e, fn=fn: fn(e))

    def dma(self, ctr, out, in_, transpose=False):
        val = ctr.inc(16)
        def run(e, out=out, in_=in_, s=ctr.h, tr=transpose):
            e.dma_start(out=out, in_=in_, transpose=tr).then_inc(s, 16)
        self.ops.append(run)
        return val

    def raw_wait(self, ctr, val):
        if val > 0:
            self.ops.append(lambda e, s=ctr.h, v=val: e.wait_ge(s, v))


def build_nc(debug=False):
    nc = bass.Bass("TRN2", target_bir_lowering=False)

    x_full = nc.dram_tensor("x_full", [S, DM], F32, kind="ExternalInput").ap()
    x_q = nc.dram_tensor("x_q", [NQ, DM], F32, kind="ExternalInput").ap()
    pos_q = nc.dram_tensor("pos_q", [1, NQ], F32, kind="ExternalInput").ap()
    gmeta = nc.dram_tensor("gmeta", [1, 2], F32, kind="ExternalInput").ap()
    w_dq = nc.dram_tensor("w_dq", [DM, QL], F32, kind="ExternalInput").ap()
    g_q_t = nc.dram_tensor("g_q_t", [128, 6], F32, kind="ExternalInput").ap()
    w_uq = nc.dram_tensor("w_uq", [QL, H * 192], F32, kind="ExternalInput").ap()
    w_dkv = nc.dram_tensor("w_dkv", [DM, KVL], F32, kind="ExternalInput").ap()
    w_kr = nc.dram_tensor("w_kr", [DM, RD], F32, kind="ExternalInput").ap()
    g_kv_t = nc.dram_tensor("g_kv_t", [128, 4], F32, kind="ExternalInput").ap()
    w_uk = nc.dram_tensor("w_uk", [KVL, H * NOPE], F32, kind="ExternalInput").ap()
    w_uv = nc.dram_tensor("w_uv", [KVL, H * VD], F32, kind="ExternalInput").ap()
    w_o = nc.dram_tensor("w_o", [H * VD, DM], F32, kind="ExternalInput").ap()
    wq_gate = nc.dram_tensor("wq_gate", [HID, DM], I32, kind="ExternalInput").ap()
    s_gate_t = nc.dram_tensor("s_gate_t", [128, 44], F32, kind="ExternalInput").ap()
    wq_up = nc.dram_tensor("wq_up", [HID, DM], I32, kind="ExternalInput").ap()
    s_up_t = nc.dram_tensor("s_up_t", [128, 44], F32, kind="ExternalInput").ap()
    wq_down = nc.dram_tensor("wq_down", [DM, HID], I32, kind="ExternalInput").ap()
    s_down_r = nc.dram_tensor("s_down_r", [1, DM], F32, kind="ExternalInput").ap()
    y_out = nc.dram_tensor("y", [NQ, DM], F32, kind="ExternalOutput").ap()

    wdq_scr = nc.dram_tensor("wdq_scr", [DM, QL], BF16, kind="Internal").ap()
    wuq_scr = nc.dram_tensor("wuq_scr", [QL, H * 192], BF16, kind="Internal").ap()
    wdkv_scr = nc.dram_tensor("wdkv_scr", [DM, KVL], BF16, kind="Internal").ap()
    wkr_scr = nc.dram_tensor("wkr_scr", [DM, RD], BF16, kind="Internal").ap()
    wuk_scr = nc.dram_tensor("wuk_scr", [KVL, H * NOPE], BF16, kind="Internal").ap()
    wuv_scr = nc.dram_tensor("wuv_scr", [KVL, H * VD], BF16, kind="Internal").ap()
    wo_scr = nc.dram_tensor("wo_scr", [H * VD, DM], BF16, kind="Internal").ap()
    g_scr = nc.dram_tensor("g_scr", [HID, DM], BF16, kind="Internal").ap()
    u_scr = nc.dram_tensor("u_scr", [HID, DM], BF16, kind="Internal").ap()
    d_scr = nc.dram_tensor("d_scr", [DM, HID], BF16, kind="Internal").ap()
    x1_scr = nc.dram_tensor("x1_scr", [NQ, DM], F32, kind="Internal").ap()

    dbg = {}
    if debug:
        def dbg_t(name, shape, dt=F32):
            dbg[name] = nc.dram_tensor("dbg_" + name, shape, dt, kind="ExternalOutput").ap()
        dbg_t("cq", [128, 6, NQ], BF16)
        dbg_t("kvT", [128, 4, S], BF16)
        dbg_t("kpe", [65, S], BF16)
        dbg_t("qno", [128, HG, NQ], BF16)
        dbg_t("qpe", [65, HG * NQ], BF16)
        dbg_t("kno", [128, HG, S], BF16)
        dbg_t("vtok", [128, 16, HG * VD], BF16)
        dbg_t("ctx", [128, 16, NQ], BF16)
        dbg_t("x1", [NQ, DM], F32)
        dbg_t("h2T", [128, 16, NQ], BF16)
        dbg_t("masks", [128, 16, 512], BF16)
        dbg_t("cosq", [32, NQ], F32)
        dbg_t("kmaxn", [1, 16], F32)

    sem_guards = []
    def mksem(name):
        g = nc.semaphore(name)
        h = g.__enter__()
        sem_guards.append(g)
        return Ctr(h)

    dIn = mksem("dIn")
    dX = [mksem("dX0"), mksem("dX1")]
    dT = [mksem("dT0"), mksem("dT1")]
    dW = mksem("dW")
    dD2 = mksem("dD2")
    dB = mksem("dB")
    dSt = [mksem("dSt0"), mksem("dSt1")]
    dSl = [mksem("dSl0"), mksem("dSl1")]
    dDl = [mksem("dDl0"), mksem("dDl1")]
    dR = [mksem("dR0"), mksem("dR1")]
    dY = [mksem("dY0"), mksem("dY1")]
    dDbg = mksem("dDbg")
    dC = [mksem("dC0"), mksem("dC1")]
    wsem = {n: mksem("dw_" + n) for n in
            ("wdq", "wuq", "wdkv", "wkr", "wuk", "wuv", "wo", "g", "u", "d")}
    tS = mksem("tS")
    vS = mksem("vS")
    aS = mksem("aS")
    gS = mksem("gS")

    QT = EngQ(tS, chain=False)
    QV = EngQ(vS, chain=True)
    QA = EngQ(aS, chain=True)
    QG = EngQ(gS, chain=True)
    QS = EngQ(None, chain=False)

    class Alloc:
        def __init__(self):
            self.guards = {}
            self.order = {}
            self.k = 0

        def sb(self, name, shape, dt, side=None):
            g = nc.sbuf_tensor("%s__%d" % (name, self.k), shape, dt, side=side)
            t = g.__enter__()
            self.guards[name] = g
            self.order[name] = self.k
            self.k += 1
            return t

        def ps(self, name, shape):
            g = nc.psum_tensor("%s__%d" % (name, self.k), shape, F32)
            t = g.__enter__()
            self.guards[name] = g
            self.order[name] = self.k
            self.k += 1
            return t

        def free(self, *names):
            for name in sorted(names, key=lambda n: -self.order[n]):
                self.guards.pop(name).__exit__(None, None, None)
                self.order.pop(name)

    al = Alloc()

    ones_col = al.sb("ones_col", [128, 1], BF16)
    ones_cf = al.sb("ones_cf", [128, 1], F32)
    epsln = al.sb("epsln", [128, 1], F32)
    epsrms = al.sb("epsrms", [128, 1], F32)
    ones_mat = al.sb("ones_mat", [128, 128], BF16)
    ones_rf = al.sb("ones_rf", [1, 128], F32)
    ones_rb = al.sb("ones_rb", [1, 512], BF16)
    sgate_c = al.sb("sgate_c", [128, 44], F32)
    sup_c = al.sb("sup_c", [128, 44], F32)
    gq_c = al.sb("gq_c", [128, 6], F32)
    gkv_c = al.sb("gkv_c", [128, 4], F32)
    gmeta_sb = al.sb("gmeta_sb", [1, 2], F32)
    gcol = al.sb("gcol", [128, 2], F32)
    iota_pq = al.sb("iota_pq", [128, 512], F32)
    kmaxn = al.sb("kmaxn", [1, 16], F32)
    cosq = al.sb("cosq", [32, NQ], F32)
    sinq = al.sb("sinq", [32, NQ], F32)
    masks = al.sb("masks", [128, 16, 512], BF16)
    cq_sb = al.sb("cq_sb", [128, 6, NQ], BF16)
    kvT = al.sb("kvT", [128, 4, S], BF16)
    kpe_ext = al.sb("kpe_ext", [65, S], BF16)
    kpe_sq = al.sb("kpe_sq", [64, S], BF16)

    stg32 = al.sb("stg32", [128, 2, 1024], F32)
    stgi = al.sb("stgi", [128, 2, 1024], I32)
    stgb = al.sb("stgb", [128, 2, 1024], BF16)

    def weight_pipeline():
        cast_use = [0, 0]
        last_store = [None, 0]
        slot = [0]
        plan = []
        plan.append(("wdkv", w_dkv, wdkv_scr, DM, KVL, False))
        plan.append(("wkr", w_kr, wkr_scr, DM, RD, False))
        plan.append(("wdq", w_dq, wdq_scr, DM, QL, False))
        plan.append(("wuq", w_uq, wuq_scr, QL, H * 192, False))
        plan.append(("wuk", w_uk, wuk_scr, KVL, H * NOPE, False))
        plan.append(("wuv", w_uv, wuv_scr, KVL, H * VD, False))
        plan.append(("g", wq_gate, g_scr, HID, DM, True))
        plan.append(("u", wq_up, u_scr, HID, DM, True))
        plan.append(("d", wq_down, d_scr, DM, HID, True))
        plan.append(("wo", w_o, wo_scr, H * VD, DM, False))
        marks = {}
        for name, src, dst, R, C, isint in plan:
            stg = stgi if isint else stg32
            sem = wsem[name]
            for r0 in range(0, R, 128):
                for c0 in range(0, C, 1024):
                    cw = min(1024, C - c0)
                    sl = slot[0]
                    slot[0] ^= 1
                    QS.wait(gS, cast_use[sl])
                    dL = QS.dma(dC[sl], stg[:, sl:sl + 1, 0:cw],
                                src[r0:r0 + 128, c0:c0 + cw])
                    QG.wait(dC[sl], dL)
                    if last_store[0] is not None:
                        QG.wait(last_store[0], last_store[1])
                    g = QG.op(lambda gg, sl=sl, cw=cw, stg=stg: gg.tensor_copy(
                        stgb[:, sl:sl + 1, 0:cw], stg[:, sl:sl + 1, 0:cw]))
                    cast_use[sl] = g
                    QS.wait(gS, g)
                    dSt_ = QS.dma(sem, dst[r0:r0 + 128, c0:c0 + cw], stgb[:, sl:sl + 1, 0:cw])
                    last_store[:] = [sem, dSt_]
            marks[name] = sem.n
        return marks

    Wp = S + NQ
    cosk = al.sb("cosk", [32, S], F32, side="right")
    sink = al.sb("sink", [32, S], F32, side="right")
    posq_sb = al.sb("posq_sb", [1, NQ], F32, side="right")
    rt_ang = al.sb("rt_ang", [32, Wp], F32, side="right")
    rt_k = al.sb("rt_k", [32, Wp], F32, side="right")
    rt_t1 = al.sb("rt_t1", [32, Wp], F32, side="right")
    rt_t2 = al.sb("rt_t2", [32, Wp], F32, side="right")
    invf = al.sb("invf", [32, 1], F32, side="right")
    invfr = al.sb("invfr", [32, 1], F32, side="right")
    em = al.sb("em", [128, 2, 512], F32, side="right")

    for out_t, in_t in ((gmeta_sb[:, :], gmeta), (posq_sb[:, :], pos_q),
                        (sgate_c[:, :], s_gate_t), (sup_c[:, :], s_up_t),
                        (gq_c[:, :], g_q_t), (gkv_c[:, :], g_kv_t)):
        QS.dma(dIn, out_t, in_t)
    d_small = dIn.n

    QG.op(lambda g: g.iota(iota_pq[:, :], [[-1, 512]], base=0, channel_multiplier=1,
                           allow_small_or_imprecise_dtypes=True))
    QG.op(lambda g: g.iota(invfr[:, :], [[0, 1]], base=0, channel_multiplier=1,
                           allow_small_or_imprecise_dtypes=True))
    g_iota = QG.op(lambda g: g.iota(rt_ang[:, 0:S], [[1, S]], base=0, channel_multiplier=0,
                                    allow_small_or_imprecise_dtypes=True))
    wmark = weight_pipeline()

    QV.op(lambda v: v.memset(ones_col[:, :], 1.0))
    QV.op(lambda v: v.memset(ones_cf[:, :], 1.0))
    QV.op(lambda v: v.memset(epsln[:, :], LN_EPS))
    QV.op(lambda v: v.memset(epsrms[:, :], RMS_EPS))
    QV.op(lambda v: v.memset(kmaxn[:, :], 0.0))
    QV.op(lambda v: v.memset(ones_mat[:, :], 1.0))
    QV.op(lambda v: v.memset(ones_rb[:, :], 1.0))
    v_ones = QV.op(lambda v: v.memset(ones_rf[:, :], 1.0))

    QA.wait(gS, g_iota)
    a_invf = QA.op(lambda sc: sc.activation(invf[:, :], invfr[:, :], AF.Exp, scale=-LOG1E4_32))

    ps_b32 = al.ps("ps_b32", [32, 512])
    ps_b128 = al.ps("ps_b128", [128, 2])
    QT.wait(dIn, d_small)
    QT.wait(vS, v_ones)
    for i in range(NQ // 512):
        QT.wait(vS, vS.n)
        t_b = QT.op(lambda t, i=i: t.matmul(ps_b32[:, :], ones_rf[0:1, 0:32],
                                            posq_sb[0:1, i * 512:(i + 1) * 512],
                                            start=True, stop=True))
        QV.wait(tS, t_b)
        QV.wait(gS, g_iota)
        QV.op(lambda v, i=i: v.tensor_copy(rt_ang[:, S + i * 512:S + (i + 1) * 512],
                                           ps_b32[:, :]))
    t_g = QT.op(lambda t: t.matmul(ps_b128[:, :], ones_rf[0:1, 0:128], gmeta_sb[0:1, 0:2],
                                   start=True, stop=True))
    QV.wait(tS, t_g)
    QV.op(lambda v: v.tensor_copy(gcol[:, :], ps_b128[:, :]))

    QV.wait(aS, a_invf)
    QV.op(lambda v: v.tensor_scalar(rt_ang[:, :], rt_ang[:, :], invf[:, :], None, ALU.mult))
    QV.op(lambda v: v.tensor_scalar(rt_k[:, :], rt_ang[:, :], INV2PI, None, ALU.mult))
    QV.op(lambda v: v.tensor_scalar(rt_k[:, :], rt_k[:, :], MAGIC, None, ALU.add))
    QV.op(lambda v: v.tensor_scalar(rt_k[:, :], rt_k[:, :], -MAGIC, None, ALU.add))
    QV.op(lambda v: v.tensor_scalar(rt_t1[:, :], rt_k[:, :], C_HI, None, ALU.mult))
    QV.op(lambda v: v.tensor_sub(rt_ang[:, :], rt_ang[:, :], rt_t1[:, :]))
    QV.op(lambda v: v.tensor_scalar(rt_t1[:, :], rt_k[:, :], C_MID, None, ALU.mult))
    QV.op(lambda v: v.tensor_sub(rt_ang[:, :], rt_ang[:, :], rt_t1[:, :]))
    QV.op(lambda v: v.tensor_scalar(rt_t1[:, :], rt_k[:, :], C_LO, None, ALU.mult))
    v_red = QV.op(lambda v: v.tensor_sub(rt_ang[:, :], rt_ang[:, :], rt_t1[:, :]))
    QA.wait(vS, v_red)
    a_s4 = QA.op(lambda sc: sc.activation(rt_k[:, :], rt_ang[:, :], AF.Sin, scale=0.25))
    QV.wait(aS, a_s4)
    v_t4 = QV.op(lambda v: v.tensor_mul(rt_t1[:, :], rt_k[:, :], rt_k[:, :]))
    QA.wait(vS, v_t4)
    a_c4 = QA.op(lambda sc: sc.activation(rt_t2[:, :], rt_t1[:, :], AF.Sqrt, bias=1.0, scale=-1.0))
    QV.wait(aS, a_c4)
    QV.op(lambda v: v.tensor_mul(rt_k[:, :], rt_k[:, :], rt_t2[:, :]))
    QV.op(lambda v: v.tensor_scalar(rt_k[:, :], rt_k[:, :], 2.0, None, ALU.mult))
    QV.op(lambda v: v.tensor_scalar(rt_t2[:, :], rt_t1[:, :], -2.0, 1.0, ALU.mult, ALU.add))
    QV.op(lambda v: v.tensor_mul(rt_t1[:, :], rt_k[:, :], rt_t2[:, :]))
    QV.op(lambda v: v.tensor_scalar(rt_t1[:, :], rt_t1[:, :], 2.0, None, ALU.mult))
    QV.op(lambda v: v.tensor_copy(sink[:, :], rt_t1[:, 0:S]))
    QV.op(lambda v: v.tensor_copy(sinq[:, :], rt_t1[:, S:Wp]))
    QV.op(lambda v: v.tensor_mul(rt_t1[:, :], rt_k[:, :], rt_k[:, :]))
    QV.op(lambda v: v.tensor_scalar(rt_t1[:, :], rt_t1[:, :], -2.0, 1.0, ALU.mult, ALU.add))
    QV.op(lambda v: v.tensor_copy(cosk[:, :], rt_t1[:, 0:S]))
    v_trig = QV.op(lambda v: v.tensor_copy(cosq[:, :], rt_t1[:, S:Wp]))
    if debug:
        QS.wait(vS, v_trig)
        QS.dma(dDbg, dbg["cosq"], cosq[:, :])

    QV.op(lambda v: v.tensor_scalar(em[:, 0:1, :], iota_pq[:, :], gcol[:, 0:1], None, ALU.subtract))
    QV.op(lambda v: v.tensor_scalar(em[:, 1:2, :], iota_pq[:, :], gcol[:, 1:2], None, ALU.subtract))
    for s in range(2):
        for j in range(8):
            kc = j if s == 0 else 8 + j
            QV.op(lambda v, s=s, j=j, kc=kc: v.tensor_scalar(
                masks[:, s * 8 + j:s * 8 + j + 1, :], em[:, s:s + 1, :],
                float(-128.0 * kc), None, ALU.is_le))
    v_masks = vS.n
    if debug:
        QS.wait(vS, v_masks)
        QS.dma(dDbg, dbg["masks"], masks[:, :, :])
    al.free("em", "invfr", "invf", "rt_t2", "rt_t1", "rt_k", "rt_ang", "posq_sb")
    al.free("ps_b32", "ps_b128")

    xst = al.sb("xst", [128, 2, DM], F32, side="right")
    hst = al.sb("hst", [128, 2, DM], BF16, side="right")
    sqst = al.sb("sqst", [128, DM], F32, side="right")
    st1 = al.sb("st1", [128, 8], F32, side="right")
    hwin = al.sb("hwin", [128, 16, 512], BF16, side="right")
    rka = al.sb("rka", [32, 512], F32, side="right")
    kst = al.sb("kst", [32, 2, 512], F32, side="right")
    rkb = al.sb("rkb", [32, 512], F32, side="right")
    rkbf = al.sb("rkbf", [32, 512], BF16, side="right")
    sqw = al.sb("sqw", [128, 512], F32, side="right")
    rrow = al.sb("rrow", [1, 512], F32, side="right")
    rbc = al.sb("rbc", [128, 512], F32, side="right")

    x_last_rd = [0, 0]
    h_last_rd = [0, 0]

    def ln_load(src_rows, buf):
        QA.raw_wait(aS, x_last_rd[buf])
        return QA.dma(dX[buf], xst[:, buf:buf + 1, :], src_rows)

    def ln_tile(d_x, dst_col, buf):
        xa = xst[:, buf:buf + 1, :]
        ha = hst[:, buf:buf + 1, :]
        QA.wait(dX[buf], d_x)
        QA.op(lambda sc: sc.activation(sqst[:, :], xa, AF.Square, accum_out=st1[:, 1:2]))
        QA.op(lambda sc: sc.activation(sqst[:, :], xa, AF.Identity, accum_out=st1[:, 0:1]))
        a_m2 = QA.op(lambda sc: sc.activation(st1[:, 4:5], st1[:, 0:1], AF.Square,
                                              scale=1.0 / DM))
        QV.wait(aS, a_m2)
        v_v = QV.op(lambda v: v.scalar_tensor_tensor(st1[:, 3:4], st1[:, 1:2], 1.0 / DM,
                                                     st1[:, 4:5], ALU.mult, ALU.subtract))
        QA.wait(vS, v_v)
        a_std = QA.op(lambda sc: sc.activation(st1[:, 5:6], st1[:, 3:4], AF.Sqrt,
                                               bias=epsln[:, 0:1]))
        QV.wait(aS, a_std)
        QV.op(lambda v: v.reciprocal(st1[:, 6:7], st1[:, 5:6]))
        v_nb = QV.op(lambda v: v.scalar_tensor_tensor(st1[:, 7:8], st1[:, 0:1], -1.0 / DM,
                                                      st1[:, 6:7], ALU.mult, ALU.mult))
        QA.wait(vS, v_nb)
        if h_last_rd[buf]:
            QA.wait(dT[buf], h_last_rd[buf])
        a_h = QA.op(lambda sc: sc.activation(ha, xa, AF.Identity, bias=st1[:, 7:8],
                                             scale=st1[:, 6:7]))
        x_last_rd[buf] = a_h
        QS.wait(aS, a_h)
        d_tr = QS.dma(dT[buf], hwin[:, :, dst_col:dst_col + 128], ha, transpose=True)
        h_last_rd[buf] = d_tr
        return d_tr

    wdkv_sb = al.sb("wdkv_sb", [128, 16, KVL], BF16, side="right")
    wkr_sb = al.sb("wkr_sb", [128, 16, RD], BF16, side="right")
    QS.wait(vS, vS.n)
    QS.wait(aS, aS.n)
    QS.wait(wsem["wdkv"], wmark["wdkv"])
    QS.wait(wsem["wkr"], wmark["wkr"])
    QS.dma(dW, wdkv_sb[:, :, :], wdkv_scr.rearrange("(c p) f -> p c f", p=128))
    g_kr1 = QS.dma(dW, wkr_sb[:, :, :], wkr_scr.rearrange("(c p) f -> p c f", p=128))
    wdq_sb = None
    g_dq1 = None
    QA.wait(vS, vS.n)

    ps_r = al.ps("ps_r", [1, 512])
    ps_bw = al.ps("ps_bw", [128, 512])
    ps_c = [al.ps("ps_c%d" % i, [128, 512]) for i in range(6)]
    ps_kp1 = ps_c[4][0:32, :]
    ps_kp2 = ps_c[5][0:32, :]

    def tile_src(gi):
        if gi >= 24:
            return None
        if gi < 16:
            return x_full[gi * 128:(gi + 1) * 128, :]
        return x_q[(gi - 16) * 128:(gi - 15) * 128, :]

    hwin_rd = 0
    pending = {0: ln_load(tile_src(0), 0)}
    for wdx in range(6):
        is_kv = wdx < 4
        if wdx == 4:
            al.free("wkr_sb", "wdkv_sb")
            wdq_sb = al.sb("wdq_sb", [128, 16, QL], BF16, side="right")
            QS.wait(tS, hwin_rd)
            QS.wait(wsem["wdq"], wmark["wdq"])
            g_dq1 = QS.dma(dW, wdq_sb[:, :, :],
                           wdq_scr.rearrange("(c p) f -> p c f", p=128))
        wc = (wdx if is_kv else wdx - 4) * 512
        QS.wait(tS, hwin_rd)
        for j in range(4):
            gi = wdx * 4 + j
            if gi + 1 < 24:
                pending[gi + 1] = ln_load(tile_src(gi + 1), (gi + 1) % 2)
            ln_tile(pending.pop(gi), j * 128, gi % 2)
        QT.wait(dT[0], dT[0].n)
        QT.wait(dT[1], dT[1].n)
        QT.wait(dW, g_kr1 if is_kv else g_dq1)
        QT.wait(vS, vS.n)
        nm = 4 if is_kv else 6
        wsb = wdkv_sb if is_kv else wdq_sb
        for m in range(nm):
            for c in range(16):
                def mm(t, m=m, c=c, wsb=wsb, nm=nm):
                    return t.matmul(ps_c[m][:, :],
                                    wsb[:, c:c + 1, m * 128:(m + 1) * 128],
                                    hwin[:, c:c + 1, :],
                                    start=(c == 0), stop=(c == 15))
                QT.op(mm) if c == 15 else QT.op_noinc(mm)
        if is_kv:
            for c in range(16):
                def mm1(t, c=c):
                    return t.matmul(ps_kp1, wkr_sb[:, c:c + 1, 0:32],
                                    hwin[:, c:c + 1, :],
                                    start=(c == 0), stop=(c == 15))
                def mm2(t, c=c):
                    return t.matmul(ps_kp2, wkr_sb[:, c:c + 1, 32:64],
                                    hwin[:, c:c + 1, :],
                                    start=(c == 0), stop=(c == 15))
                QT.op_noinc(mm1)
                QT.op(mm2) if c == 15 else QT.op_noinc(mm2)
        hwin_rd = tS.n
        QA.wait(tS, hwin_rd)
        for m in range(nm):
            a_s = QA.op(lambda sc, m=m: sc.activation(sqw[:, :], ps_c[m][:, :], AF.Square))
            QT.wait(aS, a_s)
            def mm(t, m=m, nm=nm):
                return t.matmul(ps_r[0:1, :], ones_cf[:, 0:1], sqw[:, :],
                                start=(m == 0), stop=(m == nm - 1))
            t_sq = QT.op(mm)
            QA.wait(tS, t_sq)
        a_r = QA.op(lambda sc, nm=nm: sc.activation(rrow[0:1, :], ps_r[0:1, :], AF.Sqrt,
                                                    scale=1.0 / (nm * 128),
                                                    bias=epsrms[0:1, 0:1]))
        QV.wait(aS, a_r)
        QV.op(lambda v: v.reciprocal(rrow[0:1, :], rrow[0:1, :]))
        QT.wait(vS, vS.n)
        t_b = QT.op(lambda t: t.matmul(ps_bw[:, :], ones_rf[0:1, 0:128], rrow[0:1, :],
                                       start=True, stop=True))
        QV.wait(tS, t_b)
        QV.op(lambda v: v.tensor_copy(rbc[:, :], ps_bw[:, :]))
        gvec = gkv_c if is_kv else gq_c
        dst = kvT if is_kv else cq_sb
        for m in range(nm):
            QV.op(lambda v, m=m, wc=wc, gvec=gvec, dst=dst: v.scalar_tensor_tensor(
                dst[:, m:m + 1, wc:wc + 512], ps_c[m][:, :],
                gvec[:, m:m + 1], rbc[:, :], ALU.mult, ALU.mult))
        if is_kv:
            csl = slice(wc, wc + 512)
            QV.op(lambda v: v.tensor_copy(kst[:, 0:1, :], ps_kp1))
            QV.op(lambda v: v.tensor_copy(kst[:, 1:2, :], ps_kp2))
            QV.op(lambda v, csl=csl: v.tensor_mul(rka[0:32, :], kst[:, 0:1, :], cosk[:, csl]))
            QV.op(lambda v, csl=csl: v.tensor_mul(rkb[0:32, :], kst[:, 1:2, :], sink[:, csl]))
            QV.op(lambda v: v.tensor_sub(rka[0:32, :], rka[0:32, :], rkb[0:32, :]))
            QV.op(lambda v, csl=csl: v.tensor_copy(kpe_ext[0:32, csl], rka[0:32, :]))
            QV.op(lambda v, csl=csl: v.tensor_mul(rka[0:32, :], kst[:, 0:1, :], sink[:, csl]))
            QV.op(lambda v, csl=csl: v.tensor_mul(rkb[0:32, :], kst[:, 1:2, :], cosk[:, csl]))
            QV.op(lambda v: v.tensor_add(rka[0:32, :], rka[0:32, :], rkb[0:32, :]))
            QV.wait(dB, dB.n)
            v_k2 = QV.op(lambda v: v.tensor_copy(rkbf[0:32, :], rka[0:32, :]))
            QS.wait(vS, v_k2)
            QS.dma(dB, kpe_ext[32:64, csl], rkbf[0:32, :])
    v_kv = vS.n
    v_cq = vS.n
    QV.wait(dB, dB.n)
    QV.op(lambda v: v.memset(kpe_ext[64:65, :], 1.0))
    QA.wait(vS, vS.n)
    a_kpsq = QA.op(lambda sc: sc.activation(kpe_sq[0:64, :], kpe_ext[0:64, :], AF.Square))
    if debug:
        QS.wait(vS, vS.n)
        QS.dma(dDbg, dbg["kvT"], kvT[:, :, :])
        QS.dma(dDbg, dbg["kpe"], kpe_ext[:, :])
        QS.dma(dDbg, dbg["cq"], cq_sb[:, :, :])
        QV.wait(dDbg, dDbg.n)
        QT.wait(dDbg, dDbg.n)
        QA.wait(dDbg, dDbg.n)
    al.free("ps_c5", "ps_c4", "ps_c3", "ps_c2", "ps_c1", "ps_c0", "ps_bw", "ps_r")
    al.free("wdq_sb")
    al.free("rbc", "rrow", "sqw", "rkbf", "rkb", "kst", "rka", "hwin", "st1", "sqst",
            "hst", "xst")
    al.free("sink", "cosk")


    q_no = al.sb("q_no", [128, HG, NQ], BF16)
    qpe_ext = al.sb("qpe_ext", [65, HG * NQ], BF16)
    kno = al.sb("kno", [128, HG, S], BF16)
    vtok = al.sb("vtok", [128, 16, HG * VD], BF16)
    sqs = al.sb("sqs", [128, S], BF16)
    bnd_row = al.sb("bnd_row", [1, NQ], BF16)
    kmx = al.sb("kmx", [1, 8], F32)
    p_sb = al.sb("p_sb", [128, 3, 512], BF16)
    rec_sb = al.sb("rec_sb", [128, 512], F32)
    rpq1 = al.sb("rpq1", [32, 512], F32)
    rpq2 = al.sb("rpq2", [32, 512], F32)
    rps1 = al.sb("rps1", [32, 512], F32)
    rps2 = al.sb("rps2", [32, 512], F32)
    rpqbf = al.sb("rpqbf", [32, NQ], BF16)
    qn_row = al.sb("qn_row", [1, NQ], F32)
    ctx = al.sb("ctx", [128, 16, NQ], BF16, side="right")

    ctx_evac = {}
    last_exp_bank = [0, 0]
    last_av_pbuf = [0, 0, 0]
    d_wo = 0
    wo_sb = None

    for grp in range(NGRP):
        wuq_sb = al.sb("wuq_sb", [128, 6, HG * 192], BF16, side="right")
        wuk_sb = al.sb("wuk_sb", [128, 4, HG * NOPE], BF16, side="right")
        wuv_sb = al.sb("wuv_sb", [128, 4, HG * VD], BF16, side="right")
        QS.wait(tS, tS.n)
        QS.wait(wsem["wuq"], wmark["wuq"])
        QS.wait(wsem["wuk"], wmark["wuk"])
        QS.wait(wsem["wuv"], wmark["wuv"])
        QS.dma(dW, wuq_sb[:, :, :],
               wuq_scr[:, grp * HG * 192:(grp + 1) * HG * 192]
               .rearrange("(c p) f -> p c f", p=128))
        QS.dma(dW, wuk_sb[:, :, :],
               wuk_scr[:, grp * HG * NOPE:(grp + 1) * HG * NOPE]
               .rearrange("(c p) f -> p c f", p=128))
        d_wg = QS.dma(dW, wuv_sb[:, :, :],
                      wuv_scr[:, grp * HG * VD:(grp + 1) * HG * VD]
                      .rearrange("(c p) f -> p c f", p=128))

        ps_a = [al.ps("ps_a0", [128, 512]), al.ps("ps_a1", [128, 512])]
        ps_p1 = al.ps("ps_p1", [32, 1024])
        ps_p2 = al.ps("ps_p2", [32, 1024])

        QT.wait(dW, d_wg)
        QT.wait(vS, vS.n)
        QT.wait(aS, aS.n)
        evac_done = [vS.n, vS.n]
        for hl in range(HG):
            for nt in range(2):
                pb = (hl * 2 + nt) % 2
                QT.wait(vS, evac_done[pb])
                for c in range(6):
                    def mm(t, hl=hl, nt=nt, c=c, pa=ps_a[pb], w=wuq_sb):
                        return t.matmul(pa[:, :],
                                        w[:, c:c + 1, hl * 192:hl * 192 + 128],
                                        cq_sb[:, c:c + 1, nt * 512:(nt + 1) * 512],
                                        start=(c == 0), stop=(c == 5))
                    QT.op(mm) if c == 5 else QT.op_noinc(mm)
                QV.wait(tS, tS.n)
                evac_done[pb] = QV.op(lambda v, hl=hl, nt=nt, pa=ps_a[pb]: v.tensor_copy(
                    q_no[:, hl:hl + 1, nt * 512:(nt + 1) * 512], pa[:, :]))

        v_st = 0
        for hl in range(HG):
            co = hl * NQ
            QT.wait(vS, v_st)
            for c in range(6):
                for nt in range(2):
                    def mm1(t, hl=hl, c=c, nt=nt, pp=ps_p1, w=wuq_sb):
                        return t.matmul(pp[:, nt * 512:(nt + 1) * 512],
                                        w[:, c:c + 1,
                                          hl * 192 + 128:hl * 192 + 160],
                                        cq_sb[:, c:c + 1, nt * 512:(nt + 1) * 512],
                                        start=(c == 0), stop=(c == 5))
                    def mm2(t, hl=hl, c=c, nt=nt, pp=ps_p2, w=wuq_sb):
                        return t.matmul(pp[:, nt * 512:(nt + 1) * 512],
                                        w[:, c:c + 1,
                                          hl * 192 + 160:hl * 192 + 192],
                                        cq_sb[:, c:c + 1, nt * 512:(nt + 1) * 512],
                                        start=(c == 0), stop=(c == 5))
                    QT.op_noinc(mm1)
                    QT.op(mm2) if (c == 5 and nt == 1) else QT.op_noinc(mm2)
            t_pe = tS.n
            QV.wait(tS, t_pe)
            QV.wait(dB, dB.n)
            for h2_ in range(2):
                hsl = slice(h2_ * 512, (h2_ + 1) * 512)
                QV.op(lambda v, pp=ps_p1, hsl=hsl: v.tensor_copy(rps1[:, :], pp[:, hsl]))
                v_st = QV.op(lambda v, pp=ps_p2, hsl=hsl: v.tensor_copy(rps2[:, :], pp[:, hsl]))
                QV.op(lambda v, hsl=hsl: v.tensor_mul(rpq1[:, :], rps1[:, :], cosq[:, hsl]))
                QV.op(lambda v, hsl=hsl: v.tensor_mul(rpq2[:, :], rps2[:, :], sinq[:, hsl]))
                QV.op(lambda v: v.tensor_sub(rpq1[:, :], rpq1[:, :], rpq2[:, :]))
                QV.op(lambda v, co=co, h2_=h2_: v.tensor_copy(
                    qpe_ext[0:32, co + h2_ * 512:co + (h2_ + 1) * 512], rpq1[:, :]))
                QV.op(lambda v, hsl=hsl: v.tensor_mul(rpq1[:, :], rps1[:, :], sinq[:, hsl]))
                QV.op(lambda v, hsl=hsl: v.tensor_mul(rpq2[:, :], rps2[:, :], cosq[:, hsl]))
                QV.op(lambda v: v.tensor_add(rpq1[:, :], rpq1[:, :], rpq2[:, :]))
                QV.op(lambda v, h2_=h2_: v.tensor_copy(rpqbf[:, h2_ * 512:(h2_ + 1) * 512],
                                                       rpq1[:, :]))
            v_q2 = vS.n
            QS.wait(vS, v_q2)
            QS.dma(dB, qpe_ext[32:64, co:co + NQ], rpqbf[:, :])
            import os
            if os.environ.get("DBGPE"): print("emit qpe-dma hl dB.n=", dB.n)

        import os
        if debug and grp == 0 and os.environ.get("DBGPE2"):
            QS.wait(dB, dB.n)
            QS.wait(vS, vS.n)
            QS.dma(dDbg, dbg["qpe"], qpe_ext[:, :])
        QT.wait(vS, v_kv)
        for hl in range(HG):
            for nt in range(4):
                pb = (hl * 4 + nt) % 2
                QT.wait(vS, evac_done[pb])
                for c in range(4):
                    def mm(t, hl=hl, nt=nt, c=c, pa=ps_a[pb], w=wuk_sb):
                        return t.matmul(pa[:, :],
                                        w[:, c:c + 1, hl * 128:(hl + 1) * 128],
                                        kvT[:, c:c + 1, nt * 512:(nt + 1) * 512],
                                        start=(c == 0), stop=(c == 3))
                    QT.op(mm) if c == 3 else QT.op_noinc(mm)
                QV.wait(tS, tS.n)
                evac_done[pb] = QV.op(lambda v, hl=hl, nt=nt, pa=ps_a[pb]: v.tensor_copy(
                    kno[:, hl:hl + 1, nt * 512:(nt + 1) * 512], pa[:, :]))

        for kt in range(16):
            pb = kt % 2
            QT.wait(vS, evac_done[pb])
            for c in range(4):
                def mm(t, kt=kt, c=c, pa=ps_a[pb], w=wuv_sb):
                    return t.matmul(pa[:, :],
                                    kvT[:, c:c + 1, kt * 128:(kt + 1) * 128],
                                    w[:, c:c + 1, 0:HG * VD],
                                    start=(c == 0), stop=(c == 3))
                QT.op(mm) if c == 3 else QT.op_noinc(mm)
            QV.wait(tS, tS.n)
            evac_done[pb] = QV.op(lambda v, kt=kt, pa=ps_a[pb]: v.tensor_copy(
                vtok[:, kt:kt + 1, :], pa[:, :]))

        last_red = 0
        for hl in range(HG):
            QA.wait(vS, vS.n)
            QA.wait(tS, tS.n)
            a_sq = QA.op(lambda sc, hl=hl: sc.activation(sqs[:, :], kno[:, hl:hl + 1, :],
                                                         AF.Square))
            for half in range(2):
                QT.wait(aS, a_sq)
                QT.wait(aS, a_kpsq)
                QT.wait(vS, last_red)
                for j in range(2):
                    sl = slice(half * 1024 + j * 512, half * 1024 + (j + 1) * 512)
                    def mma(t, j=j, sl=sl, pp=ps_p1):
                        return t.matmul(pp[0:1, j * 512:(j + 1) * 512],
                                        ones_col[:, 0:1], sqs[:, sl],
                                        start=True, stop=False)
                    def mmb(t, j=j, sl=sl, pp=ps_p1):
                        return t.matmul(pp[0:1, j * 512:(j + 1) * 512],
                                        ones_col[0:64, 0:1], kpe_sq[0:64, sl],
                                        start=False, stop=True)
                    QT.op_noinc(mma)
                    QT.op(mmb)
                QV.wait(tS, tS.n)
                last_red = QV.op(lambda v, half=half, pp=ps_p1: v.reduce_max(
                    kmx[0:1, half:half + 1], pp[0:1, 0:1024], AX.X))
            QV.op(lambda v: v.tensor_tensor(kmx[0:1, 2:3], kmx[0:1, 0:1], kmx[0:1, 1:2],
                                            ALU.max))
            v_kmx = vS.n
            QA.wait(vS, v_kmx)
            QA.op(lambda sc, hl=hl, grp=grp: sc.activation(
                kmaxn[0:1, grp * HG + hl:grp * HG + hl + 1], kmx[0:1, 2:3],
                AF.Sqrt, scale=1.0404))

        a_qn_prev = 0
        for hl in range(HG):
            QA.wait(tS, tS.n)
            QA.wait(vS, vS.n)
            a_q1 = QA.op(lambda sc, hl=hl: sc.activation(sqs[:, 0:NQ],
                                                         q_no[:, hl:hl + 1, :], AF.Square))
            QT.wait(aS, a_q1)
            QT.wait(aS, a_qn_prev)
            for j in range(2):
                def mmq(t, j=j, pp=ps_p1):
                    return t.matmul(pp[0:1, j * 512:(j + 1) * 512], ones_col[:, 0:1],
                                    sqs[:, j * 512:(j + 1) * 512], start=True, stop=False)
                QT.op(mmq) if j == 1 else QT.op_noinc(mmq)
            QA.wait(tS, tS.n)
            QA.wait(dB, dB.n)
            import os
            if os.environ.get("DBGPE"): print("emit qsq hl=", hl, "dB.n=", dB.n, "vSwait=", vS.n)
            a_q2 = QA.op(lambda sc, hl=hl: sc.activation(
                sqs[0:64, 0:NQ], qpe_ext[0:64, hl * NQ:(hl + 1) * NQ], AF.Square))
            QT.wait(aS, a_q2)
            for j in range(2):
                def mmq2(t, j=j, pp=ps_p1):
                    return t.matmul(pp[0:1, j * 512:(j + 1) * 512], ones_col[0:64, 0:1],
                                    sqs[0:64, j * 512:(j + 1) * 512], start=False, stop=True)
                QT.op(mmq2) if j == 1 else QT.op_noinc(mmq2)
            QA.wait(tS, tS.n)
            a_qn_prev = QA.op(lambda sc, pp=ps_p1: sc.activation(qn_row[0:1, :], pp[0:1, 0:1024],
                                                                 AF.Sqrt))
            QV.wait(aS, a_qn_prev)
            QV.wait(dB, dB.n)
            QV.op(lambda v, hl=hl, grp=grp: v.tensor_scalar(
                bnd_row[0:1, :], qn_row[0:1, :],
                kmaxn[0:1, grp * HG + hl:grp * HG + hl + 1], None, ALU.mult))
            v_bnd = QV.op(lambda v: v.tensor_scalar(bnd_row[0:1, :], bnd_row[0:1, :],
                                                    -1.0, None, ALU.mult))
            QS.wait(vS, v_bnd)
            QS.dma(dB, qpe_ext[64:65, hl * NQ:(hl + 1) * NQ], bnd_row[0:1, :])

        if debug and grp == 0:
            QS.wait(vS, vS.n)
            QS.wait(aS, aS.n)
            QS.wait(dB, dB.n)
            QS.dma(dDbg, dbg["qno"], q_no[:, :, :])
            QS.dma(dDbg, dbg["qpe"], qpe_ext[:, :])
            QS.dma(dDbg, dbg["kno"], kno[:, :, :])
            QS.dma(dDbg, dbg["vtok"], vtok[:, :, :])
            QS.dma(dDbg, dbg["kmaxn"], kmaxn[:, :])
            QV.wait(dDbg, dDbg.n)
            QT.wait(dDbg, dDbg.n)
            QA.wait(dDbg, dDbg.n)

        al.free("ps_p2", "ps_p1", "ps_a1", "ps_a0")
        al.free("wuv_sb", "wuk_sb", "wuq_sb")

        QT.wait(dB, dB.n)
        ps_sT = [al.ps("ps_sT0", [128, 512]), al.ps("ps_sT1", [128, 512])]
        ps_cx = [al.ps("ps_cx0", [128, 512]), al.ps("ps_cx1", [128, 512])]
        ps_sm = [al.ps("ps_sm0", [128, 512]), al.ps("ps_sm1", [128, 512])]

        QT.wait(vS, vS.n)
        QT.wait(aS, aS.n)
        for sl_i in range(2):
            nk = SLOT_K[sl_i]
            qs = sl_i * 512
            for hl in range(HG):
                gh = grp * HG + hl
                cb = gh % 2
                prev_evac = ctx_evac.get(gh - 2, 0)
                def emit_av(kc, dep_v, a_ex):
                    pbuf = kc % 3
                    if dep_v:
                        QT.wait(vS, dep_v)
                    else:
                        QT.wait(aS, a_ex)
                    if kc == 0:
                        QT.wait(vS, prev_evac)
                    def mm_av(t, hl=hl, kc=kc, pbuf=pbuf, nk=nk, pp=ps_cx[cb]):
                        return t.matmul(pp[:, :],
                                        vtok[:, kc:kc + 1, hl * VD:(hl + 1) * VD],
                                        p_sb[:, pbuf:pbuf + 1, :],
                                        start=(kc == 0), stop=(kc == nk - 1))
                    QT.op_noinc(mm_av)
                    def mm_sm(t, kc=kc, pbuf=pbuf, nk=nk, pp=ps_sm[cb]):
                        return t.matmul(pp[:, :], ones_mat[:, :],
                                        p_sb[:, pbuf:pbuf + 1, :],
                                        start=(kc == 0), stop=(kc == nk - 1))
                    last_av_pbuf[pbuf] = QT.op(mm_sm)

                pend = None
                for kc in range(nk):
                    pbuf = kc % 3
                    sbk = kc % 2
                    QT.wait(aS, last_exp_bank[sbk])
                    def mm_n(t, hl=hl, kc=kc, qs=qs, pp=ps_sT[sbk]):
                        return t.matmul(pp[:, :],
                                        kno[:, hl:hl + 1, kc * 128:(kc + 1) * 128],
                                        q_no[:, hl:hl + 1, qs:qs + 512],
                                        start=True, stop=False)
                    QT.op_noinc(mm_n)
                    def mm_p(t, hl=hl, kc=kc, qs=qs, pp=ps_sT[sbk]):
                        return t.matmul(pp[:, :],
                                        kpe_ext[0:65, kc * 128:(kc + 1) * 128],
                                        qpe_ext[0:65, hl * NQ + qs:hl * NQ + qs + 512],
                                        start=False, stop=True)
                    t_sc = QT.op(mm_p)
                    QA.wait(tS, t_sc)
                    QA.wait(tS, last_av_pbuf[pbuf])
                    a_ex = QA.op(lambda sc, pbuf=pbuf, pp=ps_sT[sbk]: sc.activation(
                        p_sb[:, pbuf:pbuf + 1, :], pp[:, :], AF.Exp, scale=SCALE))
                    last_exp_bank[sbk] = a_ex
                    midx = kc if sl_i == 0 else (8 + (kc - 8) if kc >= 8 else None)
                    dep_v = 0
                    if midx is not None:
                        QV.wait(aS, a_ex)
                        dep_v = QV.op(lambda v, pbuf=pbuf, midx=midx: v.tensor_mul(
                            p_sb[:, pbuf:pbuf + 1, :], p_sb[:, pbuf:pbuf + 1, :],
                            masks[:, midx:midx + 1, :]))
                    if pend is not None:
                        emit_av(*pend)
                    pend = (kc, dep_v, a_ex)
                emit_av(*pend)
                QV.wait(tS, last_av_pbuf[(nk - 1) % 3])
                QV.op(lambda v, pp=ps_sm[cb]: v.reciprocal(rec_sb[:, :], pp[:, :]))
                ctx_evac[gh] = QV.op(lambda v, gh=gh, qs=qs, pp=ps_cx[cb]: v.scalar_tensor_tensor(
                    ctx[:, gh:gh + 1, qs:qs + 512], pp[:, :], 1.0,
                    rec_sb[:, :], ALU.mult, ALU.mult))
        al.free("ps_sm1", "ps_sm0", "ps_cx1", "ps_cx0", "ps_sT1", "ps_sT0")

    v_ctx = vS.n
    if debug:
        QS.wait(vS, v_ctx)
        QS.dma(dDbg, dbg["ctx"], ctx[:, :, :])
        QV.wait(dDbg, dDbg.n)
        QT.wait(dDbg, dDbg.n)
        QA.wait(dDbg, dDbg.n)
    al.free("qn_row", "rpqbf", "rps2", "rps1", "rpq2", "rpq1", "rec_sb", "p_sb",
            "kmx", "bnd_row", "sqs", "vtok", "kno", "qpe_ext", "q_no")
    al.free("stgb", "stgi", "stg32")
    al.free("kpe_sq", "kpe_ext", "kvT", "cq_sb", "masks")

    wo_sb = al.sb("wo_sb", [128, 16, DM], BF16, side="right")
    QS.wait(tS, tS.n)
    QS.wait(vS, vS.n)
    QS.wait(aS, aS.n)
    QS.wait(wsem["wo"], wmark["wo"])
    QS.wait(gS, gS.n)
    d_wo = QS.dma(dW, wo_sb[:, :, :], wo_scr.rearrange("(c p) f -> p c f", p=128))
    h2T = al.sb("h2T", [128, 16, NQ], BF16)
    xqst = al.sb("xqst", [128, 2, DM], F32, side="right")
    x1st = al.sb("x1st", [128, 2, DM], F32, side="right")
    h2st = al.sb("h2st", [128, 2, DM], BF16, side="right")
    sqstf = al.sb("sqstf", [128, DM], F32, side="right")
    st1f = al.sb("st1f", [128, 8], F32, side="right")
    ps_f = [al.ps("ps_f%d" % i, [128, 512]) for i in range(4)]

    f_evac = {}
    x1_wr = [0, 0]
    h2_rd = [0, 0]
    x_rd = [0, 0]
    QS.wait(tS, tS.n)
    QS.wait(vS, vS.n)
    QS.wait(aS, aS.n)
    for tt in range(8):
        buf = tt % 2
        QS.wait(vS, x_rd[buf])
        d_xq = QS.dma(dX[buf], xqst[:, buf:buf + 1, :], x_q[tt * 128:(tt + 1) * 128, :])
        QT.wait(vS, v_ctx)
        QT.wait(dW, d_wo)
        for ds in range(4):
            pb = ds
            QT.wait(vS, f_evac.get((tt - 1, ds), 0))
            for c in range(16):
                def mm(t, tt=tt, ds=ds, c=c, pb=pb):
                    return t.matmul(ps_f[pb][:, :],
                                    ctx[:, c:c + 1, tt * 128:(tt + 1) * 128],
                                    wo_sb[:, c:c + 1, ds * 512:(ds + 1) * 512],
                                    start=(c == 0), stop=(c == 15))
                QT.op(mm) if c == 15 else QT.op_noinc(mm)
            QV.wait(tS, tS.n)
            QV.wait(dX[buf], d_xq)
            QV.wait(dSt[buf], x1_wr[buf])
            f_evac[(tt, ds)] = QV.op(lambda v, tt=tt, ds=ds, pb=pb, buf=buf: v.tensor_add(
                x1st[:, buf:buf + 1, ds * 512:(ds + 1) * 512], ps_f[pb][:, :],
                xqst[:, buf:buf + 1, ds * 512:(ds + 1) * 512]))
        x_rd[buf] = f_evac[(tt, 3)]
        xa = x1st[:, buf:buf + 1, :]
        ha = h2st[:, buf:buf + 1, :]
        QS.wait(vS, f_evac[(tt, 3)])
        x1_wr[buf] = QS.dma(dSt[buf], x1_scr[tt * 128:(tt + 1) * 128, :], xa)
        QV.op(lambda v, xa=xa: v.reduce_sum(st1f[:, 0:1], xa, AX.X))
        QA.wait(vS, vS.n)
        QA.op(lambda sc, xa=xa: sc.activation(sqstf[:, :], xa, AF.Square, accum_out=st1f[:, 1:2]))
        QV.wait(aS, aS.n)
        QV.op(lambda v: v.tensor_scalar(st1f[:, 2:3], st1f[:, 0:1], 1.0 / DM, None, ALU.mult))
        QV.op(lambda v: v.tensor_scalar(st1f[:, 3:4], st1f[:, 1:2], 1.0 / DM, None, ALU.mult))
        QV.op(lambda v: v.tensor_mul(st1f[:, 4:5], st1f[:, 2:3], st1f[:, 2:3]))
        v_var = QV.op(lambda v: v.tensor_sub(st1f[:, 3:4], st1f[:, 3:4], st1f[:, 4:5]))
        QA.wait(vS, v_var)
        a_std = QA.op(lambda sc: sc.activation(st1f[:, 5:6], st1f[:, 3:4], AF.Sqrt, bias=epsln[:, 0:1]))
        QV.wait(aS, a_std)
        QV.op(lambda v: v.reciprocal(st1f[:, 6:7], st1f[:, 5:6]))
        v_nb = QV.op(lambda v: v.tensor_scalar(st1f[:, 7:8], st1f[:, 2:3], st1f[:, 6:7], -1.0,
                                               ALU.mult, ALU.mult))
        QA.wait(vS, v_nb)
        if h2_rd[buf]:
            QA.wait(dT[buf], h2_rd[buf])
        a_h = QA.op(lambda sc, ha=ha, xa=xa: sc.activation(ha, xa, AF.Identity,
                                                           bias=st1f[:, 7:8], scale=st1f[:, 6:7]))
        QS.wait(aS, a_h)
        h2_rd[buf] = QS.dma(dT[buf], h2T[:, :, tt * 128:(tt + 1) * 128], ha, transpose=True)
    d_h2T = (dT[0].n, dT[1].n)
    if debug:
        QS.wait(dT[0], d_h2T[0])
        QS.wait(dT[1], d_h2T[1])
        QS.wait(dSt[0], dSt[0].n)
        QS.wait(dSt[1], dSt[1].n)
        QS.dma(dDbg, dbg["h2T"], h2T[:, :, :])
        QS.dma(dDbg, dbg["x1"], x1_scr)
        QV.wait(dDbg, dDbg.n)
        QT.wait(dDbg, dDbg.n)
        QA.wait(dDbg, dDbg.n)
    al.free("ps_f3", "ps_f2", "ps_f1", "ps_f0")
    al.free("st1f", "sqstf", "h2st", "x1st", "xqst")
    al.free("wo_sb")
    al.free("ctx")

    sdbc = al.sb("sdbc", [128, DM], F32)
    nact = al.sb("nact", [1, NQ], BF16)
    x1r = al.sb("x1r", [128, 2, 512], F32)
    y1 = al.sb("y1", [128, 2, 512], F32)
    sdrow = al.sb("sdrow", [1, DM], F32)
    act_sb = al.sb("act_sb", [128, 44, NQ], BF16, side="right")
    slab = [al.sb("slab0", [128, 16, 512], BF16, side="right"),
            al.sb("slab1", [128, 16, 512], BF16, side="right")]
    gate_sb = al.sb("gate_sb", [128, 8, 512], BF16, side="right")
    dslab = [al.sb("dslab0", [128, 4, 512], BF16, side="right"),
             al.sb("dslab1", [128, 4, 512], BF16, side="right")]

    ps_g = [al.ps("ps_g0", [128, 512]), al.ps("ps_g1", [128, 512])]
    ps_u = [al.ps("ps_u0", [128, 512]), al.ps("ps_u1", [128, 512])]
    ps_as = al.ps("ps_as", [1, 1024])
    ps_sd = al.ps("ps_sd", [128, 512])

    d_sdr = QS.dma(dIn, sdrow[:, :], s_down_r)
    for j in range(4):
        QT.wait(dIn, d_sdr)
        QT.wait(vS, vS.n)
        t_b = QT.op(lambda t, j=j: t.matmul(ps_sd[:, :], ones_rf[0:1, 0:128],
                                            sdrow[0:1, j * 512:(j + 1) * 512],
                                            start=True, stop=True))
        QV.wait(tS, t_b)
        QV.op(lambda v, j=j: v.tensor_copy(sdbc[:, j * 512:(j + 1) * 512], ps_sd[:, :]))
    v_sdbc = vS.n
    QS.wait(vS, v_sdbc)
    al.free("sdrow")

    slab_user = [0, 0]
    silu_done = {}
    act_done = {}
    for sec in range(11):
        for which in range(2):
            l = 2 * sec + which
            sb_i = l % 2
            src = g_scr if which == 0 else u_scr
            QS.wait(wsem["g"] if which == 0 else wsem["u"],
                    wmark["g"] if which == 0 else wmark["u"])
            QS.wait(tS, slab_user[sb_i])
            d_sl = QS.dma(dSl[sb_i], slab[sb_i][:, :, :],
                          src[sec * 512:(sec + 1) * 512, :], transpose=True)
            for hcl in range(4):
                hc = sec * 4 + hcl
                for nt in range(2):
                    pp = (ps_g if which == 0 else ps_u)[nt]
                    QT.wait(dSl[sb_i], d_sl)
                    QT.wait(dT[0], d_h2T[0])
                    QT.wait(dT[1], d_h2T[1])
                    if which == 0:
                        QT.wait(aS, silu_done.get((hc - 1, nt), 0))
                    else:
                        QT.wait(vS, act_done.get((hc - 1, nt), 0))
                    for c in range(16):
                        def mm(t, sb_i=sb_i, hcl=hcl, nt=nt, c=c, pp=pp):
                            return t.matmul(pp[:, :],
                                            slab[sb_i][:, c:c + 1, hcl * 128:(hcl + 1) * 128],
                                            h2T[:, c:c + 1, nt * 512:(nt + 1) * 512],
                                            start=(c == 0), stop=(c == 15))
                        t_mm = QT.op(mm) if c == 15 else QT.op_noinc(mm)
                    slab_user[sb_i] = tS.n
                    gi = hcl * 2 + nt
                    if which == 0:
                        QA.wait(tS, tS.n)
                        QA.wait(vS, act_done.get((hc - 4, nt), 0))
                        silu_done[(hc, nt)] = QA.op(lambda sc, hc=hc, gi=gi, pp=pp: sc.activation(
                            gate_sb[:, gi:gi + 1, :], pp[:, :], AF.Silu,
                            scale=sgate_c[:, hc:hc + 1]))
                    else:
                        QV.wait(tS, tS.n)
                        QV.wait(aS, silu_done[(hc, nt)])
                        a_d = QV.op(lambda v, hc=hc, nt=nt, gi=gi, pp=pp: v.scalar_tensor_tensor(
                            act_sb[:, hc:hc + 1, nt * 512:(nt + 1) * 512], pp[:, :],
                            sup_c[:, hc:hc + 1], gate_sb[:, gi:gi + 1, :],
                            ALU.mult, ALU.mult))
                        act_done[(hc, nt)] = a_d
                        QT.wait(vS, a_d)
                        def mm_as(t, hc=hc, nt=nt):
                            return t.matmul(ps_as[0:1, nt * 512:(nt + 1) * 512],
                                            ones_col[:, 0:1],
                                            act_sb[:, hc:hc + 1, nt * 512:(nt + 1) * 512],
                                            start=(hc == 0), stop=(hc == 43))
                        QT.op(mm_as) if hc == 43 else QT.op_noinc(mm_as)
    t_as = tS.n
    QA.wait(tS, t_as)
    a_nact = QA.op(lambda sc: sc.activation(nact[0:1, :], ps_as[0:1, :], AF.Copy,
                                            scale=-128.0))
    al.free("ps_sd", "ps_as", "ps_u1", "ps_u0", "ps_g1", "ps_g0")
    ps_d = [al.ps("ps_d%d" % i, [128, 512]) for i in range(8)]
    QT.wait(aS, aS.n)

    SECS = [(0, 4), (4, 4), (8, 4), (12, 4), (16, 4), (20, 4),
            (24, 4), (28, 4), (32, 4), (36, 4), (40, 4)]
    dslab_user = [0, 0]
    y_done = {}
    x1r_use = [0, 0]
    y1_use = [0, 0]
    QA.raw_wait(dSt[0], dSt[0].n)
    QA.raw_wait(dSt[1], dSt[1].n)
    for ds in range(4):
        for si, (h0, nch) in enumerate(SECS):
            sb_i = si % 2
            QS.wait(wsem["d"], wmark["d"])
            QS.wait(tS, dslab_user[sb_i])
            d_sl = QS.dma(dDl[sb_i], dslab[sb_i][:, 0:nch, :],
                          d_scr[ds * 512:(ds + 1) * 512, h0 * 128:(h0 + nch) * 128],
                          transpose=True)
            last = None
            for j in range(nch):
                hc = h0 + j
                for tt in range(8):
                    QT.wait(dDl[sb_i], d_sl)
                    if hc == 0:
                        QT.wait(vS, y_done.get((tt, ds - 1), 0))
                    def mm(t, sb_i=sb_i, j=j, hc=hc, tt=tt):
                        return t.matmul(ps_d[tt][:, :],
                                        act_sb[:, hc:hc + 1, tt * 128:(tt + 1) * 128],
                                        dslab[sb_i][:, j:j + 1, :],
                                        start=(hc == 0), stop=False)
                    if j == nch - 1 and tt == 7:
                        dslab_user[sb_i] = QT.op(mm)
                    else:
                        QT.op_noinc(mm)
        for tt in range(8):
            buf = tt % 2
            QT.wait(aS, a_nact)
            def mm_c(t, tt=tt):
                return t.matmul(ps_d[tt][:, :], nact[0:1, tt * 128:(tt + 1) * 128],
                                ones_rb[0:1, :], start=False, stop=True)
            t_c = QT.op(mm_c)
            QA.wait(vS, x1r_use[buf])
            d_x1 = QA.dma(dR[buf], x1r[:, buf:buf + 1, :],
                          x1_scr[tt * 128:(tt + 1) * 128, ds * 512:(ds + 1) * 512])
            QV.wait(tS, t_c)
            QV.wait(dR[buf], d_x1)
            QV.wait(vS, v_sdbc)
            QV.wait(dY[buf], y1_use[buf])
            QV.op(lambda v, ds=ds, buf=buf, tt=tt: v.tensor_mul(
                y1[:, buf:buf + 1, :], ps_d[tt][:, :],
                sdbc[:, ds * 512:(ds + 1) * 512]))
            y_done[(tt, ds)] = QV.op(lambda v, buf=buf: v.tensor_add(
                y1[:, buf:buf + 1, :], y1[:, buf:buf + 1, :], x1r[:, buf:buf + 1, :]))
            x1r_use[buf] = y_done[(tt, ds)]
            QS.wait(vS, y_done[(tt, ds)])
            y1_use[buf] = QS.dma(
                dY[buf], y_out[tt * 128:(tt + 1) * 128, ds * 512:(ds + 1) * 512],
                y1[:, buf:buf + 1, :])
    QS.wait(dY[0], dY[0].n)
    QS.wait(dY[1], dY[1].n)
    if debug:
        QS.wait(dDbg, dDbg.n)

    block_g = nc.Block()
    block = block_g.__enter__()

    def run_queue(q):
        def body(eng):
            for fn in q.ops:
                fn(eng)
        return body

    block.sync(run_queue(QS))
    block.tensor(run_queue(QT))
    block.vector(run_queue(QV))
    block.scalar(run_queue(QA))
    block.gpsimd(run_queue(QG))
    block_g.__exit__(None, None, None)

    for g in reversed(sem_guards):
        g.__exit__(None, None, None)
    return nc


_cache = {}


def kernel(**inputs):
    f = lambda k: np.ascontiguousarray(np.asarray(inputs[k], dtype=np.float32))
    x = f("L0_attn_x")
    w = {
        "w_dq": f("L0_attn_w_dq"),
        "w_uq": f("L0_attn_w_uq"),
        "w_dkv": f("L0_attn_w_dkv"),
        "w_kr": f("L0_attn_w_kr"),
        "w_uk": f("L0_attn_w_uk"),
        "w_uv": f("L0_attn_w_uv"),
        "w_o": f("L0_attn_w_o"),
        "g_q_t": np.ascontiguousarray(f("L0_attn_g_q").reshape(6, 128).T),
        "g_kv_t": np.ascontiguousarray(f("L0_attn_g_kv").reshape(4, 128).T),
        "s_gate_t": np.ascontiguousarray(f("L0_ffn_s_gate").reshape(44, 128).T),
        "s_up_t": np.ascontiguousarray(f("L0_ffn_s_up").reshape(44, 128).T),
        "s_down_r": np.ascontiguousarray(f("L0_ffn_s_down").reshape(1, DM)),
        "wq_gate": np.ascontiguousarray(np.asarray(inputs["L0_ffn_wq_gate"], dtype=np.int32)),
        "wq_up": np.ascontiguousarray(np.asarray(inputs["L0_ffn_wq_up"], dtype=np.int32)),
        "wq_down": np.ascontiguousarray(np.asarray(inputs["L0_ffn_wq_down"], dtype=np.int32)),
    }
    if "nc" not in _cache:
        _cache["nc"] = build_nc()
    nc = _cache["nc"]

    in_maps = []
    sels = []
    for core in range(8):
        b, half = core // 2, core % 2
        if half == 0:
            rows = np.r_[0:512, 1536:2048]
            g0, g1 = 0.0, 1536.0
        else:
            rows = np.r_[512:1536]
            g0, g1 = 512.0, 1024.0
        sels.append((b, rows))
        m = dict(w)
        m["x_full"] = x[b]
        m["x_q"] = np.ascontiguousarray(x[b][rows])
        m["pos_q"] = np.ascontiguousarray(rows.astype(np.float32).reshape(1, NQ))
        m["gmeta"] = np.array([[g0, g1]], dtype=np.float32)
        in_maps.append(m)

    res = run_bass_kernel_spmd(nc, in_maps, core_ids=list(range(8)))
    y = np.empty((4, S, DM), dtype=np.float32)
    for core in range(8):
        b, rows = sels[core]
        y[b][rows] = np.asarray(res.results[core]["y"], dtype=np.float32)
    return y


# version 08
# speedup vs baseline: 1.7511x; 1.6578x over previous
import math
import numpy as np
import ml_dtypes
import concourse.bass as bass
import concourse.mybir as mybir
from concourse.bass_utils import run_bass_kernel_spmd

F32 = mybir.dt.float32
BF16 = mybir.dt.bfloat16
I32 = mybir.dt.int32
AF = mybir.ActivationFunctionType
ALU = mybir.AluOpType
AX = mybir.AxisListType

S = 2048
DM = 2048
H = 16
NOPE = 128
RD = 64
VD = 128
QL = 768
KVL = 512
HID = 5632
NQ = 1024
HG = 4
NGRP = 4
SLOT_K = (8, 16)
SCALE = float(1.0 / math.sqrt(192.0))
LN_EPS = 1e-5
RMS_EPS = 1e-6
MAGIC = 8388608.0

C_HI = 6.28125
C_MID = float(np.float32(2 * np.pi - C_HI))
C_LO = float(np.float64(2 * np.pi) - np.float64(C_HI) - np.float64(np.float32(C_MID)))
INV2PI = float(1.0 / (2 * np.pi))
LOG1E4_32 = float(np.log(10000.0) / 32.0)


class Ctr:
    def __init__(self, handle):
        self.h = handle
        self.n = 0

    def inc(self, k=1):
        self.n += k
        return self.n


class EngQ:
    def __init__(self, ctr, chain):
        self.ops = []
        self.ctr = ctr
        self.chain = chain

    def wait(self, ctr, val):
        if val > 0:
            self.ops.append(lambda e, s=ctr.h, v=val: e.wait_ge(s, v))

    def op(self, fn):
        if self.chain and self.ctr.n > 0:
            self.ops.append(lambda e, s=self.ctr.h, v=self.ctr.n: e.wait_ge(s, v))
        val = self.ctr.inc()
        def run(e, fn=fn, s=self.ctr.h):
            fn(e).then_inc(s, 1)
        self.ops.append(run)
        return val

    def op_noinc(self, fn):
        self.ops.append(lambda e, fn=fn: fn(e))

    def dma(self, ctr, out, in_, transpose=False):
        val = ctr.inc(16)
        def run(e, out=out, in_=in_, s=ctr.h, tr=transpose):
            e.dma_start(out=out, in_=in_, transpose=tr).then_inc(s, 16)
        self.ops.append(run)
        return val


def build_nc(debug=False):
    nc = bass.Bass("TRN2", target_bir_lowering=False)

    x_full = nc.dram_tensor("x_full", [S, DM], F32, kind="ExternalInput").ap()
    x_q = nc.dram_tensor("x_q", [NQ, DM], F32, kind="ExternalInput").ap()
    pos_q = nc.dram_tensor("pos_q", [1, NQ], F32, kind="ExternalInput").ap()
    gmeta = nc.dram_tensor("gmeta", [1, 2], F32, kind="ExternalInput").ap()
    w_dq = nc.dram_tensor("w_dq", [DM, QL], F32, kind="ExternalInput").ap()
    g_q_t = nc.dram_tensor("g_q_t", [128, 6], F32, kind="ExternalInput").ap()
    w_uq = nc.dram_tensor("w_uq", [QL, H * 192], F32, kind="ExternalInput").ap()
    w_dkv = nc.dram_tensor("w_dkv", [DM, KVL], F32, kind="ExternalInput").ap()
    w_kr = nc.dram_tensor("w_kr", [DM, RD], F32, kind="ExternalInput").ap()
    g_kv_t = nc.dram_tensor("g_kv_t", [128, 4], F32, kind="ExternalInput").ap()
    w_uk = nc.dram_tensor("w_uk", [KVL, H * NOPE], F32, kind="ExternalInput").ap()
    w_uv = nc.dram_tensor("w_uv", [KVL, H * VD], F32, kind="ExternalInput").ap()
    w_o = nc.dram_tensor("w_o", [H * VD, DM], F32, kind="ExternalInput").ap()
    wq_gate = nc.dram_tensor("wq_gate", [HID, DM], I32, kind="ExternalInput").ap()
    s_gate_t = nc.dram_tensor("s_gate_t", [128, 44], F32, kind="ExternalInput").ap()
    wq_up = nc.dram_tensor("wq_up", [HID, DM], I32, kind="ExternalInput").ap()
    s_up_t = nc.dram_tensor("s_up_t", [128, 44], F32, kind="ExternalInput").ap()
    wq_down = nc.dram_tensor("wq_down", [DM, HID], I32, kind="ExternalInput").ap()
    s_down_r = nc.dram_tensor("s_down_r", [1, DM], F32, kind="ExternalInput").ap()
    y_out = nc.dram_tensor("y", [NQ, DM], F32, kind="ExternalOutput").ap()

    g_scr = nc.dram_tensor("g_scr", [HID, DM], BF16, kind="Internal").ap()
    u_scr = nc.dram_tensor("u_scr", [HID, DM], BF16, kind="Internal").ap()
    d_scr = nc.dram_tensor("d_scr", [DM, HID], BF16, kind="Internal").ap()
    x1_scr = nc.dram_tensor("x1_scr", [NQ, DM], F32, kind="Internal").ap()

    dbg = {}
    if debug:
        def dbg_t(name, shape, dt=F32):
            dbg[name] = nc.dram_tensor("dbg_" + name, shape, dt, kind="ExternalOutput").ap()
        dbg_t("cq", [128, 6, NQ], BF16)
        dbg_t("kvT", [128, 4, S], BF16)
        dbg_t("kpe", [65, S], BF16)
        dbg_t("qno", [128, HG, NQ], BF16)
        dbg_t("qpe", [65, HG * NQ], BF16)
        dbg_t("kno", [128, HG, S], BF16)
        dbg_t("vtok", [128, 16, HG * VD], BF16)
        dbg_t("ctx", [128, 16, NQ], BF16)
        dbg_t("x1", [NQ, DM], F32)
        dbg_t("h2T", [128, 16, NQ], BF16)
        dbg_t("masks", [128, 16, 512], BF16)
        dbg_t("cosq", [32, NQ], F32)
        dbg_t("kmaxn", [1, 16], F32)

    sem_guards = []
    def mksem(name):
        g = nc.semaphore(name)
        h = g.__enter__()
        sem_guards.append(g)
        return Ctr(h)

    dIn = mksem("dIn")
    dX = [mksem("dX0"), mksem("dX1")]
    dT = [mksem("dT0"), mksem("dT1")]
    dW = mksem("dW")
    dD2 = mksem("dD2")
    dB = mksem("dB")
    dSt = [mksem("dSt0"), mksem("dSt1")]
    dSl = [mksem("dSl0"), mksem("dSl1")]
    dDl = [mksem("dDl0"), mksem("dDl1")]
    dR = [mksem("dR0"), mksem("dR1")]
    dY = [mksem("dY0"), mksem("dY1")]
    dDbg = mksem("dDbg")
    tS = mksem("tS")
    vS = mksem("vS")
    aS = mksem("aS")
    gS = mksem("gS")

    QT = EngQ(tS, chain=False)
    QV = EngQ(vS, chain=True)
    QA = EngQ(aS, chain=True)
    QG = EngQ(gS, chain=True)
    QS = EngQ(None, chain=False)

    class Alloc:
        def __init__(self):
            self.guards = {}
            self.order = {}
            self.k = 0

        def sb(self, name, shape, dt, side=None):
            g = nc.sbuf_tensor("%s__%d" % (name, self.k), shape, dt, side=side)
            t = g.__enter__()
            self.guards[name] = g
            self.order[name] = self.k
            self.k += 1
            return t

        def ps(self, name, shape):
            g = nc.psum_tensor("%s__%d" % (name, self.k), shape, F32)
            t = g.__enter__()
            self.guards[name] = g
            self.order[name] = self.k
            self.k += 1
            return t

        def free(self, *names):
            for name in sorted(names, key=lambda n: -self.order[n]):
                self.guards.pop(name).__exit__(None, None, None)
                self.order.pop(name)

    al = Alloc()

    ones_col = al.sb("ones_col", [128, 1], BF16)
    ones_cf = al.sb("ones_cf", [128, 1], F32)
    epsln = al.sb("epsln", [128, 1], F32)
    epsrms = al.sb("epsrms", [128, 1], F32)
    ones_mat = al.sb("ones_mat", [128, 128], BF16)
    ones_rf = al.sb("ones_rf", [1, 128], F32)
    ones_rb = al.sb("ones_rb", [1, 512], BF16)
    sgate_c = al.sb("sgate_c", [128, 44], F32)
    sup_c = al.sb("sup_c", [128, 44], F32)
    gq_c = al.sb("gq_c", [128, 6], F32)
    gkv_c = al.sb("gkv_c", [128, 4], F32)
    gmeta_sb = al.sb("gmeta_sb", [1, 2], F32)
    gcol = al.sb("gcol", [128, 2], F32)
    iota_pq = al.sb("iota_pq", [128, 512], F32)
    kmaxn = al.sb("kmaxn", [1, 16], F32)
    cosq = al.sb("cosq", [32, NQ], F32)
    sinq = al.sb("sinq", [32, NQ], F32)
    masks = al.sb("masks", [128, 16, 512], BF16)
    cq_sb = al.sb("cq_sb", [128, 6, NQ], BF16)
    kvT = al.sb("kvT", [128, 4, S], BF16)
    kpe_ext = al.sb("kpe_ext", [65, S], BF16)
    kpe_sq = al.sb("kpe_sq", [64, S], BF16)

    Wp = S + NQ
    cosk = al.sb("cosk", [32, S], F32, side="right")
    sink = al.sb("sink", [32, S], F32, side="right")
    posq_sb = al.sb("posq_sb", [1, NQ], F32, side="right")
    rt_ang = al.sb("rt_ang", [32, Wp], F32, side="right")
    rt_k = al.sb("rt_k", [32, Wp], F32, side="right")
    rt_t1 = al.sb("rt_t1", [32, Wp], F32, side="right")
    rt_t2 = al.sb("rt_t2", [32, Wp], F32, side="right")
    invf = al.sb("invf", [32, 1], F32, side="right")
    invfr = al.sb("invfr", [32, 1], F32, side="right")
    em = al.sb("em", [128, 2, 512], F32, side="right")

    for out_t, in_t in ((gmeta_sb[:, :], gmeta), (posq_sb[:, :], pos_q),
                        (sgate_c[:, :], s_gate_t), (sup_c[:, :], s_up_t),
                        (gq_c[:, :], g_q_t), (gkv_c[:, :], g_kv_t)):
        QS.dma(dIn, out_t, in_t)
    d_small = dIn.n

    QG.op(lambda g: g.iota(iota_pq[:, :], [[-1, 512]], base=0, channel_multiplier=1,
                           allow_small_or_imprecise_dtypes=True))
    QG.op(lambda g: g.iota(invfr[:, :], [[0, 1]], base=0, channel_multiplier=1,
                           allow_small_or_imprecise_dtypes=True))
    g_iota = QG.op(lambda g: g.iota(rt_ang[:, 0:S], [[1, S]], base=0, channel_multiplier=0,
                                    allow_small_or_imprecise_dtypes=True))

    QV.op(lambda v: v.memset(ones_col[:, :], 1.0))
    QV.op(lambda v: v.memset(ones_cf[:, :], 1.0))
    QV.op(lambda v: v.memset(epsln[:, :], LN_EPS))
    QV.op(lambda v: v.memset(epsrms[:, :], RMS_EPS))
    QV.op(lambda v: v.memset(kmaxn[:, :], 0.0))
    QV.op(lambda v: v.memset(ones_mat[:, :], 1.0))
    QV.op(lambda v: v.memset(ones_rb[:, :], 1.0))
    v_ones = QV.op(lambda v: v.memset(ones_rf[:, :], 1.0))

    QA.wait(gS, g_iota)
    a_invf = QA.op(lambda sc: sc.activation(invf[:, :], invfr[:, :], AF.Exp, scale=-LOG1E4_32))

    ps_b32 = al.ps("ps_b32", [32, 512])
    ps_b128 = al.ps("ps_b128", [128, 2])
    QT.wait(dIn, d_small)
    QT.wait(vS, v_ones)
    for i in range(NQ // 512):
        QT.wait(vS, vS.n)
        t_b = QT.op(lambda t, i=i: t.matmul(ps_b32[:, :], ones_rf[0:1, 0:32],
                                            posq_sb[0:1, i * 512:(i + 1) * 512],
                                            start=True, stop=True))
        QV.wait(tS, t_b)
        QV.wait(gS, g_iota)
        QV.op(lambda v, i=i: v.tensor_copy(rt_ang[:, S + i * 512:S + (i + 1) * 512],
                                           ps_b32[:, :]))
    t_g = QT.op(lambda t: t.matmul(ps_b128[:, :], ones_rf[0:1, 0:128], gmeta_sb[0:1, 0:2],
                                   start=True, stop=True))
    QV.wait(tS, t_g)
    QV.op(lambda v: v.tensor_copy(gcol[:, :], ps_b128[:, :]))

    QV.wait(aS, a_invf)
    QV.op(lambda v: v.tensor_scalar(rt_ang[:, :], rt_ang[:, :], invf[:, :], None, ALU.mult))
    QV.op(lambda v: v.tensor_scalar(rt_k[:, :], rt_ang[:, :], INV2PI, None, ALU.mult))
    QV.op(lambda v: v.tensor_scalar(rt_k[:, :], rt_k[:, :], MAGIC, None, ALU.add))
    QV.op(lambda v: v.tensor_scalar(rt_k[:, :], rt_k[:, :], -MAGIC, None, ALU.add))
    QV.op(lambda v: v.tensor_scalar(rt_t1[:, :], rt_k[:, :], C_HI, None, ALU.mult))
    QV.op(lambda v: v.tensor_sub(rt_ang[:, :], rt_ang[:, :], rt_t1[:, :]))
    QV.op(lambda v: v.tensor_scalar(rt_t1[:, :], rt_k[:, :], C_MID, None, ALU.mult))
    QV.op(lambda v: v.tensor_sub(rt_ang[:, :], rt_ang[:, :], rt_t1[:, :]))
    QV.op(lambda v: v.tensor_scalar(rt_t1[:, :], rt_k[:, :], C_LO, None, ALU.mult))
    v_red = QV.op(lambda v: v.tensor_sub(rt_ang[:, :], rt_ang[:, :], rt_t1[:, :]))
    QA.wait(vS, v_red)
    a_s4 = QA.op(lambda sc: sc.activation(rt_k[:, :], rt_ang[:, :], AF.Sin, scale=0.25))
    QV.wait(aS, a_s4)
    v_t4 = QV.op(lambda v: v.tensor_mul(rt_t1[:, :], rt_k[:, :], rt_k[:, :]))
    QA.wait(vS, v_t4)
    a_c4 = QA.op(lambda sc: sc.activation(rt_t2[:, :], rt_t1[:, :], AF.Sqrt, bias=1.0, scale=-1.0))
    QV.wait(aS, a_c4)
    QV.op(lambda v: v.tensor_mul(rt_k[:, :], rt_k[:, :], rt_t2[:, :]))
    QV.op(lambda v: v.tensor_scalar(rt_k[:, :], rt_k[:, :], 2.0, None, ALU.mult))
    QV.op(lambda v: v.tensor_scalar(rt_t2[:, :], rt_t1[:, :], -2.0, 1.0, ALU.mult, ALU.add))
    QV.op(lambda v: v.tensor_mul(rt_t1[:, :], rt_k[:, :], rt_t2[:, :]))
    QV.op(lambda v: v.tensor_scalar(rt_t1[:, :], rt_t1[:, :], 2.0, None, ALU.mult))
    QV.op(lambda v: v.tensor_copy(sink[:, :], rt_t1[:, 0:S]))
    QV.op(lambda v: v.tensor_copy(sinq[:, :], rt_t1[:, S:Wp]))
    QV.op(lambda v: v.tensor_mul(rt_t1[:, :], rt_k[:, :], rt_k[:, :]))
    QV.op(lambda v: v.tensor_scalar(rt_t1[:, :], rt_t1[:, :], -2.0, 1.0, ALU.mult, ALU.add))
    QV.op(lambda v: v.tensor_copy(cosk[:, :], rt_t1[:, 0:S]))
    v_trig = QV.op(lambda v: v.tensor_copy(cosq[:, :], rt_t1[:, S:Wp]))
    if debug:
        QS.wait(vS, v_trig)
        QS.dma(dDbg, dbg["cosq"], cosq[:, :])

    QV.op(lambda v: v.tensor_scalar(em[:, 0:1, :], iota_pq[:, :], gcol[:, 0:1], None, ALU.subtract))
    QV.op(lambda v: v.tensor_scalar(em[:, 1:2, :], iota_pq[:, :], gcol[:, 1:2], None, ALU.subtract))
    for s in range(2):
        for j in range(8):
            kc = j if s == 0 else 8 + j
            QV.op(lambda v, s=s, j=j, kc=kc: v.tensor_scalar(
                masks[:, s * 8 + j:s * 8 + j + 1, :], em[:, s:s + 1, :],
                float(-128.0 * kc), None, ALU.is_le))
    v_masks = vS.n
    if debug:
        QS.wait(vS, v_masks)
        QS.dma(dDbg, dbg["masks"], masks[:, :, :])
    al.free("em", "invfr", "invf", "rt_t2", "rt_t1", "rt_k", "rt_ang", "posq_sb")
    al.free("ps_b32", "ps_b128")

    xst = al.sb("xst", [128, 2, DM], F32, side="right")
    hst = al.sb("hst", [128, 2, DM], BF16, side="right")
    sqst = al.sb("sqst", [128, DM], F32, side="right")
    st1 = al.sb("st1", [128, 8], F32, side="right")
    hwin = al.sb("hwin", [128, 16, 512], BF16, side="right")
    rka = al.sb("rka", [32, 512], F32, side="right")
    rkb = al.sb("rkb", [32, 512], F32, side="right")
    rkbf = al.sb("rkbf", [32, 512], BF16, side="right")
    sqw = al.sb("sqw", [128, 512], F32, side="right")
    rrow = al.sb("rrow", [1, 512], F32, side="right")
    rbc = al.sb("rbc", [128, 512], F32, side="right")

    x_last_rd = [0, 0]
    h_last_rd = [0, 0]

    def ln_tile(src_rows, dst_col, buf):
        xa = xst[:, buf:buf + 1, :]
        ha = hst[:, buf:buf + 1, :]
        if x_last_rd[buf]:
            QS.wait(aS, x_last_rd[buf])
        d_x = QS.dma(dX[buf], xa, src_rows)
        QV.wait(dX[buf], d_x)
        QV.op(lambda v: v.reduce_sum(st1[:, 0:1], xa, AX.X))
        QA.wait(dX[buf], d_x)
        QA.op(lambda sc: sc.activation(sqst[:, :], xa, AF.Square, accum_out=st1[:, 1:2]))
        QV.wait(aS, aS.n)
        QV.op(lambda v: v.tensor_scalar(st1[:, 2:3], st1[:, 0:1], 1.0 / DM, None, ALU.mult))
        QV.op(lambda v: v.tensor_scalar(st1[:, 3:4], st1[:, 1:2], 1.0 / DM, None, ALU.mult))
        QV.op(lambda v: v.tensor_mul(st1[:, 4:5], st1[:, 2:3], st1[:, 2:3]))
        v_var = QV.op(lambda v: v.tensor_sub(st1[:, 3:4], st1[:, 3:4], st1[:, 4:5]))
        QA.wait(vS, v_var)
        a_std = QA.op(lambda sc: sc.activation(st1[:, 5:6], st1[:, 3:4], AF.Sqrt, bias=epsln[:, 0:1]))
        QV.wait(aS, a_std)
        QV.op(lambda v: v.reciprocal(st1[:, 6:7], st1[:, 5:6]))
        v_nb = QV.op(lambda v: v.tensor_scalar(st1[:, 7:8], st1[:, 2:3], st1[:, 6:7], -1.0,
                                               ALU.mult, ALU.mult))
        QA.wait(vS, v_nb)
        if h_last_rd[buf]:
            QA.wait(dT[buf], h_last_rd[buf])
        a_h = QA.op(lambda sc: sc.activation(ha, xa, AF.Identity, bias=st1[:, 7:8], scale=st1[:, 6:7]))
        x_last_rd[buf] = a_h
        QS.wait(aS, a_h)
        d_tr = QS.dma(dT[buf], hwin[:, :, dst_col:dst_col + 128], ha, transpose=True)
        h_last_rd[buf] = d_tr
        return d_tr

    wdkv_sb = al.sb("wdkv_sb", [128, 16, KVL], BF16, side="right")
    wkr_sb = al.sb("wkr_sb", [128, 16, RD], BF16, side="right")
    QG.wait(vS, vS.n)
    QG.wait(aS, aS.n)
    QG.dma(dW, wdkv_sb[:, :, :], w_dkv.rearrange("(c p) f -> p c f", p=128))
    d_wkr = QG.dma(dW, wkr_sb[:, :, :], w_kr.rearrange("(c p) f -> p c f", p=128))
    QS.wait(vS, vS.n)

    ps_r = al.ps("ps_r", [1, 512])
    ps_bw = al.ps("ps_bw", [128, 512])
    ps_c = [al.ps("ps_c%d" % i, [128, 512]) for i in range(4)]
    ps_kp1 = al.ps("ps_kp1", [32, 512])
    ps_kp2 = al.ps("ps_kp2", [32, 512])

    hwin_rd = 0
    for wdx in range(4):
        wc = wdx * 512
        QS.wait(tS, hwin_rd)
        for j in range(4):
            d_hw = ln_tile(x_full[(4 * wdx + j) * 128:(4 * wdx + j + 1) * 128, :],
                           j * 128, j % 2)
        QT.wait(dT[0], dT[0].n)
        QT.wait(dT[1], dT[1].n)
        QT.wait(dW, d_wkr)
        QT.wait(vS, vS.n)
        for m in range(4):
            for c in range(16):
                def mm(t, m=m, c=c):
                    return t.matmul(ps_c[m][:, :],
                                    wdkv_sb[:, c:c + 1, m * 128:(m + 1) * 128],
                                    hwin[:, c:c + 1, :],
                                    start=(c == 0), stop=(c == 15))
                QT.op(mm) if c == 15 else QT.op_noinc(mm)
        for c in range(16):
            def mm1(t, c=c):
                return t.matmul(ps_kp1[:, :], wkr_sb[:, c:c + 1, 0:32],
                                hwin[:, c:c + 1, :],
                                start=(c == 0), stop=(c == 15))
            def mm2(t, c=c):
                return t.matmul(ps_kp2[:, :], wkr_sb[:, c:c + 1, 32:64],
                                hwin[:, c:c + 1, :],
                                start=(c == 0), stop=(c == 15))
            QT.op_noinc(mm1)
            QT.op(mm2) if c == 15 else QT.op_noinc(mm2)
        hwin_rd = tS.n
        QA.wait(tS, hwin_rd)
        for m in range(4):
            a_s = QA.op(lambda sc, m=m: sc.activation(sqw[:, :], ps_c[m][:, :], AF.Square))
            QT.wait(aS, a_s)
            def mm(t, m=m):
                return t.matmul(ps_r[0:1, :], ones_cf[:, 0:1], sqw[:, :],
                                start=(m == 0), stop=(m == 3))
            t_sq = QT.op(mm)
            QA.wait(tS, t_sq)
        QA.wait(tS, tS.n)
        a_r = QA.op(lambda sc: sc.activation(rrow[0:1, :], ps_r[0:1, :], AF.Sqrt,
                                             scale=1.0 / KVL, bias=epsrms[0:1, 0:1]))
        QV.wait(aS, a_r)
        QV.op(lambda v: v.reciprocal(rrow[0:1, :], rrow[0:1, :]))
        QT.wait(vS, vS.n)
        t_b = QT.op(lambda t: t.matmul(ps_bw[:, :], ones_rf[0:1, 0:128], rrow[0:1, :],
                                       start=True, stop=True))
        QV.wait(tS, t_b)
        QV.op(lambda v: v.tensor_copy(rbc[:, :], ps_bw[:, :]))
        for m in range(4):
            QV.op(lambda v, m=m, wc=wc: v.scalar_tensor_tensor(
                kvT[:, m:m + 1, wc:wc + 512], ps_c[m][:, :],
                gkv_c[:, m:m + 1], rbc[:, :], ALU.mult, ALU.mult))
        csl = slice(wc, wc + 512)
        QV.op(lambda v, csl=csl: v.tensor_mul(rka[0:32, :], ps_kp1[:, :], cosk[:, csl]))
        QV.op(lambda v, csl=csl: v.tensor_mul(rkb[0:32, :], ps_kp2[:, :], sink[:, csl]))
        QV.op(lambda v: v.tensor_sub(rka[0:32, :], rka[0:32, :], rkb[0:32, :]))
        QV.op(lambda v, csl=csl: v.tensor_copy(kpe_ext[0:32, csl], rka[0:32, :]))
        QV.op(lambda v, csl=csl: v.tensor_mul(rka[0:32, :], ps_kp1[:, :], sink[:, csl]))
        QV.op(lambda v, csl=csl: v.tensor_mul(rkb[0:32, :], ps_kp2[:, :], cosk[:, csl]))
        QV.op(lambda v: v.tensor_add(rka[0:32, :], rka[0:32, :], rkb[0:32, :]))
        QV.wait(dB, dB.n)
        v_k2 = QV.op(lambda v: v.tensor_copy(rkbf[0:32, :], rka[0:32, :]))
        QS.wait(vS, v_k2)
        QS.dma(dB, kpe_ext[32:64, csl], rkbf[0:32, :])
    QV.wait(dB, dB.n)
    QV.op(lambda v: v.memset(kpe_ext[64:65, :], 1.0))
    v_kv = vS.n
    QA.wait(vS, v_kv)
    a_kpsq = QA.op(lambda sc: sc.activation(kpe_sq[0:64, :], kpe_ext[0:64, :], AF.Square))
    if debug:
        QS.wait(vS, v_kv)
        QS.dma(dDbg, dbg["kvT"], kvT[:, :, :])
        QS.dma(dDbg, dbg["kpe"], kpe_ext[:, :])
        QV.wait(dDbg, dDbg.n)
        QT.wait(dDbg, dDbg.n)
        QA.wait(dDbg, dDbg.n)
    al.free("ps_kp2", "ps_kp1", "ps_c3", "ps_c2", "ps_c1", "ps_c0")
    al.free("wkr_sb", "wdkv_sb")

    wdq_sb = al.sb("wdq_sb", [128, 16, QL], BF16, side="right")
    QG.wait(tS, tS.n)
    d_wdq = QG.dma(dW, wdq_sb[:, :, :], w_dq.rearrange("(c p) f -> p c f", p=128))
    ps_c2 = [al.ps("ps_c2%d" % i, [128, 512]) for i in range(6)]

    for wdx in range(2):
        wc = wdx * 512
        QS.wait(tS, hwin_rd)
        for j in range(4):
            d_hw = ln_tile(x_q[(4 * wdx + j) * 128:(4 * wdx + j + 1) * 128, :],
                           j * 128, j % 2)
        QT.wait(dT[0], dT[0].n)
        QT.wait(dT[1], dT[1].n)
        QT.wait(dW, d_wdq)
        QT.wait(vS, vS.n)
        for m in range(6):
            for c in range(16):
                def mm(t, m=m, c=c):
                    return t.matmul(ps_c2[m][:, :],
                                    wdq_sb[:, c:c + 1, m * 128:(m + 1) * 128],
                                    hwin[:, c:c + 1, :],
                                    start=(c == 0), stop=(c == 15))
                QT.op(mm) if c == 15 else QT.op_noinc(mm)
        hwin_rd = tS.n
        QA.wait(tS, hwin_rd)
        for m in range(6):
            a_s = QA.op(lambda sc, m=m: sc.activation(sqw[:, :], ps_c2[m][:, :], AF.Square))
            QT.wait(aS, a_s)
            def mm(t, m=m):
                return t.matmul(ps_r[0:1, :], ones_cf[:, 0:1], sqw[:, :],
                                start=(m == 0), stop=(m == 5))
            t_sq = QT.op(mm)
            QA.wait(tS, t_sq)
        QA.wait(tS, tS.n)
        a_r = QA.op(lambda sc: sc.activation(rrow[0:1, :], ps_r[0:1, :], AF.Sqrt,
                                             scale=1.0 / QL, bias=epsrms[0:1, 0:1]))
        QV.wait(aS, a_r)
        QV.op(lambda v: v.reciprocal(rrow[0:1, :], rrow[0:1, :]))
        QT.wait(vS, vS.n)
        t_b = QT.op(lambda t: t.matmul(ps_bw[:, :], ones_rf[0:1, 0:128], rrow[0:1, :],
                                       start=True, stop=True))
        QV.wait(tS, t_b)
        QV.op(lambda v: v.tensor_copy(rbc[:, :], ps_bw[:, :]))
        for m in range(6):
            QV.op(lambda v, m=m, wc=wc: v.scalar_tensor_tensor(
                cq_sb[:, m:m + 1, wc:wc + 512], ps_c2[m][:, :],
                gq_c[:, m:m + 1], rbc[:, :], ALU.mult, ALU.mult))
    v_cq = vS.n
    if debug:
        QS.wait(vS, v_cq)
        QS.dma(dDbg, dbg["cq"], cq_sb[:, :, :])
        QV.wait(dDbg, dDbg.n)
        QT.wait(dDbg, dDbg.n)
        QA.wait(dDbg, dDbg.n)
    al.free("ps_c25", "ps_c24", "ps_c23", "ps_c22", "ps_c21", "ps_c20", "ps_bw", "ps_r")
    al.free("wdq_sb")
    al.free("rbc", "rrow", "sqw", "rkbf", "rkb", "rka", "hwin", "st1", "sqst", "hst", "xst")
    al.free("sink", "cosk")

    for j in range(4):
        QG.dma(dD2, g_scr[j * 1408:(j + 1) * 1408, :], wq_gate[j * 1408:(j + 1) * 1408, :])
    for j in range(4):
        QG.dma(dD2, u_scr[j * 1408:(j + 1) * 1408, :], wq_up[j * 1408:(j + 1) * 1408, :])
    for j in range(4):
        QG.dma(dD2, d_scr[j * 512:(j + 1) * 512, :], wq_down[j * 512:(j + 1) * 512, :])
    d_d2d = dD2.n

    q_no = al.sb("q_no", [128, HG, NQ], BF16)
    qpe_ext = al.sb("qpe_ext", [65, HG * NQ], BF16)
    kno = al.sb("kno", [128, HG, S], BF16)
    vtok = al.sb("vtok", [128, 16, HG * VD], BF16)
    sqs = al.sb("sqs", [128, S], BF16)
    qn_row = al.sb("qn_row", [1, NQ], F32)
    bnd_row = al.sb("bnd_row", [1, NQ], BF16)
    kmx = al.sb("kmx", [1, 8], F32)
    p_sb = al.sb("p_sb", [128, 3, 512], BF16)
    rec_sb = al.sb("rec_sb", [128, 512], F32)
    rpq1 = al.sb("rpq1", [32, NQ], F32)
    rpq2 = al.sb("rpq2", [32, NQ], F32)
    rpqbf = al.sb("rpqbf", [32, NQ], BF16)
    ctx = al.sb("ctx", [128, 16, NQ], BF16, side="right")

    ctx_evac = {}
    last_exp_bank = [0, 0]
    last_av_pbuf = [0, 0, 0]
    d_wo = 0
    wo_sb = None

    for grp in range(NGRP):
        wuq_sb = al.sb("wuq_sb", [128, 6, HG * 192], BF16, side="right")
        wuk_sb = al.sb("wuk_sb", [128, 4, HG * NOPE], BF16, side="right")
        wuv_sb = al.sb("wuv_sb", [128, 4, HG * VD], BF16, side="right")
        QG.wait(tS, tS.n)
        QG.dma(dW, wuq_sb[:, :, :],
               w_uq[:, grp * HG * 192:(grp + 1) * HG * 192]
               .rearrange("(c p) f -> p c f", p=128))
        QG.dma(dW, wuk_sb[:, :, :],
               w_uk[:, grp * HG * NOPE:(grp + 1) * HG * NOPE]
               .rearrange("(c p) f -> p c f", p=128))
        d_wg = QG.dma(dW, wuv_sb[:, :, :],
                      w_uv[:, grp * HG * VD:(grp + 1) * HG * VD]
                      .rearrange("(c p) f -> p c f", p=128))

        ps_a = [al.ps("ps_a0", [128, 512]), al.ps("ps_a1", [128, 512])]
        ps_p1 = al.ps("ps_p1", [32, 1024])
        ps_p2 = al.ps("ps_p2", [32, 1024])

        QT.wait(dW, d_wg)
        QT.wait(vS, vS.n)
        QT.wait(aS, aS.n)
        evac_done = [vS.n, vS.n]
        for hl in range(HG):
            for nt in range(2):
                pb = (hl * 2 + nt) % 2
                QT.wait(vS, evac_done[pb])
                for c in range(6):
                    def mm(t, hl=hl, nt=nt, c=c, pa=ps_a[pb], w=wuq_sb):
                        return t.matmul(pa[:, :],
                                        w[:, c:c + 1, hl * 192:hl * 192 + 128],
                                        cq_sb[:, c:c + 1, nt * 512:(nt + 1) * 512],
                                        start=(c == 0), stop=(c == 5))
                    QT.op(mm) if c == 5 else QT.op_noinc(mm)
                QV.wait(tS, tS.n)
                evac_done[pb] = QV.op(lambda v, hl=hl, nt=nt, pa=ps_a[pb]: v.tensor_copy(
                    q_no[:, hl:hl + 1, nt * 512:(nt + 1) * 512], pa[:, :]))

        for hl in range(HG):
            co = hl * NQ
            QT.wait(vS, vS.n)
            for c in range(6):
                for nt in range(2):
                    def mm1(t, hl=hl, c=c, nt=nt, pp=ps_p1, w=wuq_sb):
                        return t.matmul(pp[:, nt * 512:(nt + 1) * 512],
                                        w[:, c:c + 1,
                                          hl * 192 + 128:hl * 192 + 160],
                                        cq_sb[:, c:c + 1, nt * 512:(nt + 1) * 512],
                                        start=(c == 0), stop=(c == 5))
                    def mm2(t, hl=hl, c=c, nt=nt, pp=ps_p2, w=wuq_sb):
                        return t.matmul(pp[:, nt * 512:(nt + 1) * 512],
                                        w[:, c:c + 1,
                                          hl * 192 + 160:hl * 192 + 192],
                                        cq_sb[:, c:c + 1, nt * 512:(nt + 1) * 512],
                                        start=(c == 0), stop=(c == 5))
                    QT.op_noinc(mm1)
                    QT.op(mm2) if (c == 5 and nt == 1) else QT.op_noinc(mm2)
            t_pe = tS.n
            QV.wait(tS, t_pe)
            QV.op(lambda v, pp=ps_p1: v.tensor_mul(rpq1[:, :], pp[:, :], cosq[:, :]))
            QV.op(lambda v, pp=ps_p2: v.tensor_mul(rpq2[:, :], pp[:, :], sinq[:, :]))
            QV.op(lambda v: v.tensor_sub(rpq1[:, :], rpq1[:, :], rpq2[:, :]))
            QV.op(lambda v, co=co: v.tensor_copy(qpe_ext[0:32, co:co + NQ], rpq1[:, :]))
            QV.op(lambda v, pp=ps_p1: v.tensor_mul(rpq1[:, :], pp[:, :], sinq[:, :]))
            QV.op(lambda v, pp=ps_p2: v.tensor_mul(rpq2[:, :], pp[:, :], cosq[:, :]))
            QV.op(lambda v: v.tensor_add(rpq1[:, :], rpq1[:, :], rpq2[:, :]))
            QV.wait(dB, dB.n)
            v_q2 = QV.op(lambda v: v.tensor_copy(rpqbf[:, :], rpq1[:, :]))
            QS.wait(vS, v_q2)
            QS.dma(dB, qpe_ext[32:64, co:co + NQ], rpqbf[:, :])

        QT.wait(vS, v_kv)
        for hl in range(HG):
            for nt in range(4):
                pb = (hl * 4 + nt) % 2
                QT.wait(vS, evac_done[pb])
                for c in range(4):
                    def mm(t, hl=hl, nt=nt, c=c, pa=ps_a[pb], w=wuk_sb):
                        return t.matmul(pa[:, :],
                                        w[:, c:c + 1, hl * 128:(hl + 1) * 128],
                                        kvT[:, c:c + 1, nt * 512:(nt + 1) * 512],
                                        start=(c == 0), stop=(c == 3))
                    QT.op(mm) if c == 3 else QT.op_noinc(mm)
                QV.wait(tS, tS.n)
                evac_done[pb] = QV.op(lambda v, hl=hl, nt=nt, pa=ps_a[pb]: v.tensor_copy(
                    kno[:, hl:hl + 1, nt * 512:(nt + 1) * 512], pa[:, :]))

        for kt in range(16):
            pb = kt % 2
            QT.wait(vS, evac_done[pb])
            for c in range(4):
                def mm(t, kt=kt, c=c, pa=ps_a[pb], w=wuv_sb):
                    return t.matmul(pa[:, :],
                                    kvT[:, c:c + 1, kt * 128:(kt + 1) * 128],
                                    w[:, c:c + 1, 0:HG * VD],
                                    start=(c == 0), stop=(c == 3))
                QT.op(mm) if c == 3 else QT.op_noinc(mm)
            QV.wait(tS, tS.n)
            evac_done[pb] = QV.op(lambda v, kt=kt, pa=ps_a[pb]: v.tensor_copy(
                vtok[:, kt:kt + 1, :], pa[:, :]))

        last_red = 0
        for hl in range(HG):
            QA.wait(vS, vS.n)
            QA.wait(tS, tS.n)
            a_sq = QA.op(lambda sc, hl=hl: sc.activation(sqs[:, :], kno[:, hl:hl + 1, :],
                                                         AF.Square))
            for half in range(2):
                QT.wait(aS, a_sq)
                QT.wait(aS, a_kpsq)
                QT.wait(vS, last_red)
                for j in range(2):
                    sl = slice(half * 1024 + j * 512, half * 1024 + (j + 1) * 512)
                    def mma(t, j=j, sl=sl, pp=ps_p1):
                        return t.matmul(pp[0:1, j * 512:(j + 1) * 512],
                                        ones_col[:, 0:1], sqs[:, sl],
                                        start=True, stop=False)
                    def mmb(t, j=j, sl=sl, pp=ps_p1):
                        return t.matmul(pp[0:1, j * 512:(j + 1) * 512],
                                        ones_col[0:64, 0:1], kpe_sq[0:64, sl],
                                        start=False, stop=True)
                    QT.op_noinc(mma)
                    QT.op(mmb)
                QV.wait(tS, tS.n)
                last_red = QV.op(lambda v, half=half, pp=ps_p1: v.reduce_max(
                    kmx[0:1, half:half + 1], pp[0:1, 0:1024], AX.X))
            QV.op(lambda v: v.tensor_tensor(kmx[0:1, 2:3], kmx[0:1, 0:1], kmx[0:1, 1:2],
                                            ALU.max))
            v_kmx = vS.n
            QA.wait(vS, v_kmx)
            QA.op(lambda sc, hl=hl, grp=grp: sc.activation(
                kmaxn[0:1, grp * HG + hl:grp * HG + hl + 1], kmx[0:1, 2:3],
                AF.Sqrt, scale=1.0404))

        a_qn_prev = 0
        for hl in range(HG):
            QA.wait(tS, tS.n)
            QA.wait(vS, vS.n)
            a_q1 = QA.op(lambda sc, hl=hl: sc.activation(sqs[:, 0:NQ],
                                                         q_no[:, hl:hl + 1, :], AF.Square))
            QT.wait(aS, a_q1)
            QT.wait(aS, a_qn_prev)
            for j in range(2):
                def mmq(t, j=j, pp=ps_p1):
                    return t.matmul(pp[0:1, j * 512:(j + 1) * 512], ones_col[:, 0:1],
                                    sqs[:, j * 512:(j + 1) * 512], start=True, stop=False)
                QT.op(mmq) if j == 1 else QT.op_noinc(mmq)
            QA.wait(tS, tS.n)
            a_q2 = QA.op(lambda sc, hl=hl: sc.activation(
                sqs[0:64, 0:NQ], qpe_ext[0:64, hl * NQ:(hl + 1) * NQ], AF.Square))
            QT.wait(aS, a_q2)
            for j in range(2):
                def mmq2(t, j=j, pp=ps_p1):
                    return t.matmul(pp[0:1, j * 512:(j + 1) * 512], ones_col[0:64, 0:1],
                                    sqs[0:64, j * 512:(j + 1) * 512], start=False, stop=True)
                QT.op(mmq2) if j == 1 else QT.op_noinc(mmq2)
            QA.wait(tS, tS.n)
            a_qn_prev = QA.op(lambda sc, pp=ps_p1: sc.activation(qn_row[0:1, :], pp[0:1, 0:1024],
                                                                 AF.Sqrt))
            QV.wait(aS, a_qn_prev)
            QV.wait(dB, dB.n)
            QV.op(lambda v, hl=hl, grp=grp: v.tensor_scalar(
                bnd_row[0:1, :], qn_row[0:1, :],
                kmaxn[0:1, grp * HG + hl:grp * HG + hl + 1], None, ALU.mult))
            v_bnd = QV.op(lambda v: v.tensor_scalar(bnd_row[0:1, :], bnd_row[0:1, :],
                                                    -1.0, None, ALU.mult))
            QS.wait(vS, v_bnd)
            QS.dma(dB, qpe_ext[64:65, hl * NQ:(hl + 1) * NQ], bnd_row[0:1, :])

        if debug and grp == 0:
            QS.wait(vS, vS.n)
            QS.wait(aS, aS.n)
            QS.wait(dB, dB.n)
            QS.dma(dDbg, dbg["qno"], q_no[:, :, :])
            QS.dma(dDbg, dbg["qpe"], qpe_ext[:, :])
            QS.dma(dDbg, dbg["kno"], kno[:, :, :])
            QS.dma(dDbg, dbg["vtok"], vtok[:, :, :])
            QS.dma(dDbg, dbg["kmaxn"], kmaxn[:, :])
            QV.wait(dDbg, dDbg.n)
            QT.wait(dDbg, dDbg.n)
            QA.wait(dDbg, dDbg.n)

        al.free("ps_p2", "ps_p1", "ps_a1", "ps_a0")
        al.free("wuv_sb", "wuk_sb", "wuq_sb")

        QT.wait(dB, dB.n)
        ps_sT = [al.ps("ps_sT0", [128, 512]), al.ps("ps_sT1", [128, 512])]
        ps_cx = [al.ps("ps_cx0", [128, 512]), al.ps("ps_cx1", [128, 512])]
        ps_sm = [al.ps("ps_sm0", [128, 512]), al.ps("ps_sm1", [128, 512])]

        QT.wait(vS, vS.n)
        QT.wait(aS, aS.n)
        for sl_i in range(2):
            nk = SLOT_K[sl_i]
            qs = sl_i * 512
            for hl in range(HG):
                gh = grp * HG + hl
                cb = gh % 2
                prev_evac = ctx_evac.get(gh - 2, 0)
                def emit_av(kc, dep_v, a_ex):
                    pbuf = kc % 3
                    if dep_v:
                        QT.wait(vS, dep_v)
                    else:
                        QT.wait(aS, a_ex)
                    if kc == 0:
                        QT.wait(vS, prev_evac)
                    def mm_av(t, hl=hl, kc=kc, pbuf=pbuf, nk=nk, pp=ps_cx[cb]):
                        return t.matmul(pp[:, :],
                                        vtok[:, kc:kc + 1, hl * VD:(hl + 1) * VD],
                                        p_sb[:, pbuf:pbuf + 1, :],
                                        start=(kc == 0), stop=(kc == nk - 1))
                    QT.op_noinc(mm_av)
                    def mm_sm(t, kc=kc, pbuf=pbuf, nk=nk, pp=ps_sm[cb]):
                        return t.matmul(pp[:, :], ones_mat[:, :],
                                        p_sb[:, pbuf:pbuf + 1, :],
                                        start=(kc == 0), stop=(kc == nk - 1))
                    last_av_pbuf[pbuf] = QT.op(mm_sm)

                pend = None
                for kc in range(nk):
                    pbuf = kc % 3
                    sbk = kc % 2
                    QT.wait(aS, last_exp_bank[sbk])
                    def mm_n(t, hl=hl, kc=kc, qs=qs, pp=ps_sT[sbk]):
                        return t.matmul(pp[:, :],
                                        kno[:, hl:hl + 1, kc * 128:(kc + 1) * 128],
                                        q_no[:, hl:hl + 1, qs:qs + 512],
                                        start=True, stop=False)
                    QT.op_noinc(mm_n)
                    def mm_p(t, hl=hl, kc=kc, qs=qs, pp=ps_sT[sbk]):
                        return t.matmul(pp[:, :],
                                        kpe_ext[0:65, kc * 128:(kc + 1) * 128],
                                        qpe_ext[0:65, hl * NQ + qs:hl * NQ + qs + 512],
                                        start=False, stop=True)
                    t_sc = QT.op(mm_p)
                    QA.wait(tS, t_sc)
                    QA.wait(tS, last_av_pbuf[pbuf])
                    a_ex = QA.op(lambda sc, pbuf=pbuf, pp=ps_sT[sbk]: sc.activation(
                        p_sb[:, pbuf:pbuf + 1, :], pp[:, :], AF.Exp, scale=SCALE))
                    last_exp_bank[sbk] = a_ex
                    midx = kc if sl_i == 0 else (8 + (kc - 8) if kc >= 8 else None)
                    dep_v = 0
                    if midx is not None:
                        QV.wait(aS, a_ex)
                        dep_v = QV.op(lambda v, pbuf=pbuf, midx=midx: v.tensor_mul(
                            p_sb[:, pbuf:pbuf + 1, :], p_sb[:, pbuf:pbuf + 1, :],
                            masks[:, midx:midx + 1, :]))
                    if pend is not None:
                        emit_av(*pend)
                    pend = (kc, dep_v, a_ex)
                emit_av(*pend)
                QV.wait(tS, last_av_pbuf[(nk - 1) % 3])
                QV.op(lambda v, pp=ps_sm[cb]: v.reciprocal(rec_sb[:, :], pp[:, :]))
                ctx_evac[gh] = QV.op(lambda v, gh=gh, qs=qs, pp=ps_cx[cb]: v.scalar_tensor_tensor(
                    ctx[:, gh:gh + 1, qs:qs + 512], pp[:, :], 1.0,
                    rec_sb[:, :], ALU.mult, ALU.mult))
        al.free("ps_sm1", "ps_sm0", "ps_cx1", "ps_cx0", "ps_sT1", "ps_sT0")

    v_ctx = vS.n
    if debug:
        QS.wait(vS, v_ctx)
        QS.dma(dDbg, dbg["ctx"], ctx[:, :, :])
        QV.wait(dDbg, dDbg.n)
        QT.wait(dDbg, dDbg.n)
        QA.wait(dDbg, dDbg.n)
    al.free("rpqbf", "rpq2", "rpq1", "rec_sb", "p_sb", "kmx", "bnd_row", "qn_row",
            "sqs", "vtok", "kno", "qpe_ext", "q_no")
    al.free("kpe_sq", "kpe_ext", "kvT", "cq_sb", "masks")

    wo_sb = al.sb("wo_sb", [128, 16, DM], BF16, side="right")
    QG.wait(tS, tS.n)
    QG.wait(vS, vS.n)
    QG.wait(aS, aS.n)
    QG.dma(dW, wo_sb[:, 0:8, :], w_o[0:1024, :].rearrange("(c p) f -> p c f", p=128))
    d_wo = QG.dma(dW, wo_sb[:, 8:16, :],
                  w_o[1024:2048, :].rearrange("(c p) f -> p c f", p=128))
    h2T = al.sb("h2T", [128, 16, NQ], BF16)
    xqst = al.sb("xqst", [128, 2, DM], F32, side="right")
    x1st = al.sb("x1st", [128, 2, DM], F32, side="right")
    h2st = al.sb("h2st", [128, 2, DM], BF16, side="right")
    sqstf = al.sb("sqstf", [128, DM], F32, side="right")
    st1f = al.sb("st1f", [128, 8], F32, side="right")
    ps_f = [al.ps("ps_f%d" % i, [128, 512]) for i in range(4)]

    f_evac = {}
    x1_wr = [0, 0]
    h2_rd = [0, 0]
    x_rd = [0, 0]
    QS.wait(tS, tS.n)
    QS.wait(vS, vS.n)
    QS.wait(aS, aS.n)
    for tt in range(8):
        buf = tt % 2
        QS.wait(vS, x_rd[buf])
        d_xq = QS.dma(dX[buf], xqst[:, buf:buf + 1, :], x_q[tt * 128:(tt + 1) * 128, :])
        QT.wait(vS, v_ctx)
        QT.wait(dW, d_wo)
        for ds in range(4):
            pb = ds
            QT.wait(vS, f_evac.get((tt - 1, ds), 0))
            for c in range(16):
                def mm(t, tt=tt, ds=ds, c=c, pb=pb):
                    return t.matmul(ps_f[pb][:, :],
                                    ctx[:, c:c + 1, tt * 128:(tt + 1) * 128],
                                    wo_sb[:, c:c + 1, ds * 512:(ds + 1) * 512],
                                    start=(c == 0), stop=(c == 15))
                QT.op(mm) if c == 15 else QT.op_noinc(mm)
            QV.wait(tS, tS.n)
            QV.wait(dX[buf], d_xq)
            QV.wait(dSt[buf], x1_wr[buf])
            f_evac[(tt, ds)] = QV.op(lambda v, tt=tt, ds=ds, pb=pb, buf=buf: v.tensor_add(
                x1st[:, buf:buf + 1, ds * 512:(ds + 1) * 512], ps_f[pb][:, :],
                xqst[:, buf:buf + 1, ds * 512:(ds + 1) * 512]))
        x_rd[buf] = f_evac[(tt, 3)]
        xa = x1st[:, buf:buf + 1, :]
        ha = h2st[:, buf:buf + 1, :]
        QS.wait(vS, f_evac[(tt, 3)])
        x1_wr[buf] = QS.dma(dSt[buf], x1_scr[tt * 128:(tt + 1) * 128, :], xa)
        QV.op(lambda v, xa=xa: v.reduce_sum(st1f[:, 0:1], xa, AX.X))
        QA.wait(vS, vS.n)
        QA.op(lambda sc, xa=xa: sc.activation(sqstf[:, :], xa, AF.Square, accum_out=st1f[:, 1:2]))
        QV.wait(aS, aS.n)
        QV.op(lambda v: v.tensor_scalar(st1f[:, 2:3], st1f[:, 0:1], 1.0 / DM, None, ALU.mult))
        QV.op(lambda v: v.tensor_scalar(st1f[:, 3:4], st1f[:, 1:2], 1.0 / DM, None, ALU.mult))
        QV.op(lambda v: v.tensor_mul(st1f[:, 4:5], st1f[:, 2:3], st1f[:, 2:3]))
        v_var = QV.op(lambda v: v.tensor_sub(st1f[:, 3:4], st1f[:, 3:4], st1f[:, 4:5]))
        QA.wait(vS, v_var)
        a_std = QA.op(lambda sc: sc.activation(st1f[:, 5:6], st1f[:, 3:4], AF.Sqrt, bias=epsln[:, 0:1]))
        QV.wait(aS, a_std)
        QV.op(lambda v: v.reciprocal(st1f[:, 6:7], st1f[:, 5:6]))
        v_nb = QV.op(lambda v: v.tensor_scalar(st1f[:, 7:8], st1f[:, 2:3], st1f[:, 6:7], -1.0,
                                               ALU.mult, ALU.mult))
        QA.wait(vS, v_nb)
        if h2_rd[buf]:
            QA.wait(dT[buf], h2_rd[buf])
        a_h = QA.op(lambda sc, ha=ha, xa=xa: sc.activation(ha, xa, AF.Identity,
                                                           bias=st1f[:, 7:8], scale=st1f[:, 6:7]))
        QS.wait(aS, a_h)
        h2_rd[buf] = QS.dma(dT[buf], h2T[:, :, tt * 128:(tt + 1) * 128], ha, transpose=True)
    d_h2T = (dT[0].n, dT[1].n)
    if debug:
        QS.wait(dT[0], d_h2T[0])
        QS.wait(dT[1], d_h2T[1])
        QS.wait(dSt[0], dSt[0].n)
        QS.wait(dSt[1], dSt[1].n)
        QS.dma(dDbg, dbg["h2T"], h2T[:, :, :])
        QS.dma(dDbg, dbg["x1"], x1_scr)
        QV.wait(dDbg, dDbg.n)
        QT.wait(dDbg, dDbg.n)
        QA.wait(dDbg, dDbg.n)
    al.free("ps_f3", "ps_f2", "ps_f1", "ps_f0")
    al.free("st1f", "sqstf", "h2st", "x1st", "xqst")
    al.free("wo_sb")
    al.free("ctx")

    sdbc = al.sb("sdbc", [128, DM], F32)
    nact = al.sb("nact", [1, NQ], BF16)
    x1r = al.sb("x1r", [128, 2, 512], F32)
    y1 = al.sb("y1", [128, 2, 512], F32)
    sdrow = al.sb("sdrow", [1, DM], F32)
    act_sb = al.sb("act_sb", [128, 44, NQ], BF16, side="right")
    slab = [al.sb("slab0", [128, 16, 512], BF16, side="right"),
            al.sb("slab1", [128, 16, 512], BF16, side="right")]
    gate_sb = al.sb("gate_sb", [128, 8, 512], BF16, side="right")
    dslab = [al.sb("dslab0", [128, 4, 512], BF16, side="right"),
             al.sb("dslab1", [128, 4, 512], BF16, side="right")]

    ps_g = [al.ps("ps_g0", [128, 512]), al.ps("ps_g1", [128, 512])]
    ps_u = [al.ps("ps_u0", [128, 512]), al.ps("ps_u1", [128, 512])]
    ps_as = al.ps("ps_as", [1, 1024])
    ps_sd = al.ps("ps_sd", [128, 512])

    d_sdr = QS.dma(dIn, sdrow[:, :], s_down_r)
    for j in range(4):
        QT.wait(dIn, d_sdr)
        QT.wait(vS, vS.n)
        t_b = QT.op(lambda t, j=j: t.matmul(ps_sd[:, :], ones_rf[0:1, 0:128],
                                            sdrow[0:1, j * 512:(j + 1) * 512],
                                            start=True, stop=True))
        QV.wait(tS, t_b)
        QV.op(lambda v, j=j: v.tensor_copy(sdbc[:, j * 512:(j + 1) * 512], ps_sd[:, :]))
    v_sdbc = vS.n
    QS.wait(vS, v_sdbc)
    al.free("sdrow")

    QS.wait(dSt[0], dSt[0].n)
    QS.wait(dSt[1], dSt[1].n)
    QS.wait(dT[0], d_h2T[0])
    QS.wait(dT[1], d_h2T[1])
    QV.wait(dT[0], d_h2T[0])
    QV.wait(dT[1], d_h2T[1])
    QA.wait(dT[0], d_h2T[0])
    QA.wait(dT[1], d_h2T[1])
    QV.wait(dSt[0], dSt[0].n)
    QV.wait(dSt[1], dSt[1].n)
    QA.wait(dSt[0], dSt[0].n)
    QA.wait(dSt[1], dSt[1].n)
    slab_user = [0, 0]
    silu_done = {}
    act_done = {}
    for sec in range(11):
        for which in range(2):
            l = 2 * sec + which
            sb_i = l % 2
            src = g_scr if which == 0 else u_scr
            QS.wait(dD2, d_d2d)
            QS.wait(tS, slab_user[sb_i])
            d_sl = QS.dma(dSl[sb_i], slab[sb_i][:, :, :],
                          src[sec * 512:(sec + 1) * 512, :], transpose=True)
            for hcl in range(4):
                hc = sec * 4 + hcl
                for nt in range(2):
                    pp = (ps_g if which == 0 else ps_u)[nt]
                    QT.wait(dSl[sb_i], d_sl)
                    QT.wait(dT[0], d_h2T[0])
                    QT.wait(dT[1], d_h2T[1])
                    if which == 0:
                        QT.wait(aS, silu_done.get((hc - 1, nt), 0))
                    else:
                        QT.wait(vS, act_done.get((hc - 1, nt), 0))
                    for c in range(16):
                        def mm(t, sb_i=sb_i, hcl=hcl, nt=nt, c=c, pp=pp):
                            return t.matmul(pp[:, :],
                                            slab[sb_i][:, c:c + 1, hcl * 128:(hcl + 1) * 128],
                                            h2T[:, c:c + 1, nt * 512:(nt + 1) * 512],
                                            start=(c == 0), stop=(c == 15))
                        t_mm = QT.op(mm) if c == 15 else QT.op_noinc(mm)
                    slab_user[sb_i] = tS.n
                    gi = hcl * 2 + nt
                    if which == 0:
                        QA.wait(tS, tS.n)
                        QA.wait(vS, act_done.get((hc - 4, nt), 0))
                        silu_done[(hc, nt)] = QA.op(lambda sc, hc=hc, gi=gi, pp=pp: sc.activation(
                            gate_sb[:, gi:gi + 1, :], pp[:, :], AF.Silu,
                            scale=sgate_c[:, hc:hc + 1]))
                    else:
                        QV.wait(tS, tS.n)
                        QV.wait(aS, silu_done[(hc, nt)])
                        a_d = QV.op(lambda v, hc=hc, nt=nt, gi=gi, pp=pp: v.scalar_tensor_tensor(
                            act_sb[:, hc:hc + 1, nt * 512:(nt + 1) * 512], pp[:, :],
                            sup_c[:, hc:hc + 1], gate_sb[:, gi:gi + 1, :],
                            ALU.mult, ALU.mult))
                        act_done[(hc, nt)] = a_d
                        QT.wait(vS, a_d)
                        def mm_as(t, hc=hc, nt=nt):
                            return t.matmul(ps_as[0:1, nt * 512:(nt + 1) * 512],
                                            ones_col[:, 0:1],
                                            act_sb[:, hc:hc + 1, nt * 512:(nt + 1) * 512],
                                            start=(hc == 0), stop=(hc == 43))
                        QT.op(mm_as) if hc == 43 else QT.op_noinc(mm_as)
    t_as = tS.n
    QA.wait(tS, t_as)
    a_nact = QA.op(lambda sc: sc.activation(nact[0:1, :], ps_as[0:1, :], AF.Copy,
                                            scale=-128.0))
    al.free("ps_sd", "ps_as", "ps_u1", "ps_u0", "ps_g1", "ps_g0")
    ps_d = [al.ps("ps_d%d" % i, [128, 512]) for i in range(8)]
    QT.wait(aS, aS.n)

    SECS = [(0, 4), (4, 4), (8, 4), (12, 4), (16, 4), (20, 4),
            (24, 4), (28, 4), (32, 4), (36, 4), (40, 4)]
    dslab_user = [0, 0]
    y_done = {}
    x1r_use = [0, 0]
    y1_use = [0, 0]
    QS.wait(dSt[0], dSt[0].n)
    QS.wait(dSt[1], dSt[1].n)
    for ds in range(4):
        for si, (h0, nch) in enumerate(SECS):
            sb_i = si % 2
            QS.wait(dD2, d_d2d)
            QS.wait(tS, dslab_user[sb_i])
            d_sl = QS.dma(dDl[sb_i], dslab[sb_i][:, 0:nch, :],
                          d_scr[ds * 512:(ds + 1) * 512, h0 * 128:(h0 + nch) * 128],
                          transpose=True)
            last = None
            for j in range(nch):
                hc = h0 + j
                for tt in range(8):
                    QT.wait(dDl[sb_i], d_sl)
                    if hc == 0:
                        QT.wait(vS, y_done.get((tt, ds - 1), 0))
                    def mm(t, sb_i=sb_i, j=j, hc=hc, tt=tt):
                        return t.matmul(ps_d[tt][:, :],
                                        act_sb[:, hc:hc + 1, tt * 128:(tt + 1) * 128],
                                        dslab[sb_i][:, j:j + 1, :],
                                        start=(hc == 0), stop=False)
                    if j == nch - 1 and tt == 7:
                        dslab_user[sb_i] = QT.op(mm)
                    else:
                        QT.op_noinc(mm)
        for tt in range(8):
            buf = tt % 2
            QT.wait(aS, a_nact)
            def mm_c(t, tt=tt):
                return t.matmul(ps_d[tt][:, :], nact[0:1, tt * 128:(tt + 1) * 128],
                                ones_rb[0:1, :], start=False, stop=True)
            t_c = QT.op(mm_c)
            QS.wait(vS, x1r_use[buf])
            d_x1 = QS.dma(dR[buf], x1r[:, buf:buf + 1, :],
                          x1_scr[tt * 128:(tt + 1) * 128, ds * 512:(ds + 1) * 512])
            QV.wait(tS, t_c)
            QV.wait(dR[buf], d_x1)
            QV.wait(vS, v_sdbc)
            QV.wait(dY[buf], y1_use[buf])
            QV.op(lambda v, ds=ds, buf=buf, tt=tt: v.tensor_mul(
                y1[:, buf:buf + 1, :], ps_d[tt][:, :],
                sdbc[:, ds * 512:(ds + 1) * 512]))
            y_done[(tt, ds)] = QV.op(lambda v, buf=buf: v.tensor_add(
                y1[:, buf:buf + 1, :], y1[:, buf:buf + 1, :], x1r[:, buf:buf + 1, :]))
            x1r_use[buf] = y_done[(tt, ds)]
            QS.wait(vS, y_done[(tt, ds)])
            y1_use[buf] = QS.dma(
                dY[buf], y_out[tt * 128:(tt + 1) * 128, ds * 512:(ds + 1) * 512],
                y1[:, buf:buf + 1, :])
    QS.wait(dY[0], dY[0].n)
    QS.wait(dY[1], dY[1].n)
    if debug:
        QS.wait(dDbg, dDbg.n)

    block_g = nc.Block()
    block = block_g.__enter__()

    def run_queue(q):
        def body(eng):
            for fn in q.ops:
                fn(eng)
        return body

    block.sync(run_queue(QS))
    block.tensor(run_queue(QT))
    block.vector(run_queue(QV))
    block.scalar(run_queue(QA))
    block.gpsimd(run_queue(QG))
    block_g.__exit__(None, None, None)

    for g in reversed(sem_guards):
        g.__exit__(None, None, None)
    return nc


_cache = {}


def kernel(**inputs):
    f = lambda k: np.ascontiguousarray(np.asarray(inputs[k], dtype=np.float32))
    x = f("L0_attn_x")
    w = {
        "w_dq": f("L0_attn_w_dq"),
        "w_uq": f("L0_attn_w_uq"),
        "w_dkv": f("L0_attn_w_dkv"),
        "w_kr": f("L0_attn_w_kr"),
        "w_uk": f("L0_attn_w_uk"),
        "w_uv": f("L0_attn_w_uv"),
        "w_o": f("L0_attn_w_o"),
        "g_q_t": np.ascontiguousarray(f("L0_attn_g_q").reshape(6, 128).T),
        "g_kv_t": np.ascontiguousarray(f("L0_attn_g_kv").reshape(4, 128).T),
        "s_gate_t": np.ascontiguousarray(f("L0_ffn_s_gate").reshape(44, 128).T),
        "s_up_t": np.ascontiguousarray(f("L0_ffn_s_up").reshape(44, 128).T),
        "s_down_r": np.ascontiguousarray(f("L0_ffn_s_down").reshape(1, DM)),
        "wq_gate": np.ascontiguousarray(np.asarray(inputs["L0_ffn_wq_gate"], dtype=np.int32)),
        "wq_up": np.ascontiguousarray(np.asarray(inputs["L0_ffn_wq_up"], dtype=np.int32)),
        "wq_down": np.ascontiguousarray(np.asarray(inputs["L0_ffn_wq_down"], dtype=np.int32)),
    }
    if "nc" not in _cache:
        _cache["nc"] = build_nc()
    nc = _cache["nc"]

    in_maps = []
    sels = []
    for core in range(8):
        b, half = core // 2, core % 2
        if half == 0:
            rows = np.r_[0:512, 1536:2048]
            g0, g1 = 0.0, 1536.0
        else:
            rows = np.r_[512:1536]
            g0, g1 = 512.0, 1024.0
        sels.append((b, rows))
        m = dict(w)
        m["x_full"] = x[b]
        m["x_q"] = np.ascontiguousarray(x[b][rows])
        m["pos_q"] = np.ascontiguousarray(rows.astype(np.float32).reshape(1, NQ))
        m["gmeta"] = np.array([[g0, g1]], dtype=np.float32)
        in_maps.append(m)

    res = run_bass_kernel_spmd(nc, in_maps, core_ids=list(range(8)))
    y = np.empty((4, S, DM), dtype=np.float32)
    for core in range(8):
        b, rows = sels[core]
        y[b][rows] = np.asarray(res.results[core]["y"], dtype=np.float32)
    return y


# version 09
# speedup vs baseline: 1.7617x; 1.0060x over previous
import math
import numpy as np
import ml_dtypes
import concourse.bass as bass
import concourse.mybir as mybir
from concourse.bass_utils import run_bass_kernel_spmd

F32 = mybir.dt.float32
BF16 = mybir.dt.bfloat16
I32 = mybir.dt.int32
AF = mybir.ActivationFunctionType
ALU = mybir.AluOpType
AX = mybir.AxisListType

S = 2048
DM = 2048
H = 16
NOPE = 128
RD = 64
VD = 128
QL = 768
KVL = 512
HID = 5632
NQ = 1024
HG = 4
NGRP = 4
SLOT_K = (8, 16)
SCALE = float(1.0 / math.sqrt(192.0))
LN_EPS = 1e-5
RMS_EPS = 1e-6
MAGIC = 8388608.0

C_HI = 6.28125
C_MID = float(np.float32(2 * np.pi - C_HI))
C_LO = float(np.float64(2 * np.pi) - np.float64(C_HI) - np.float64(np.float32(C_MID)))
INV2PI = float(1.0 / (2 * np.pi))
LOG1E4_32 = float(np.log(10000.0) / 32.0)


class Ctr:
    def __init__(self, handle):
        self.h = handle
        self.n = 0

    def inc(self, k=1):
        self.n += k
        return self.n


class EngQ:
    def __init__(self, ctr, chain):
        self.ops = []
        self.ctr = ctr
        self.chain = chain

    def wait(self, ctr, val):
        if val > 0:
            self.ops.append(lambda e, s=ctr.h, v=val: e.wait_ge(s, v))

    def op(self, fn):
        if self.chain and self.ctr.n > 0:
            self.ops.append(lambda e, s=self.ctr.h, v=self.ctr.n: e.wait_ge(s, v))
        val = self.ctr.inc()
        def run(e, fn=fn, s=self.ctr.h):
            fn(e).then_inc(s, 1)
        self.ops.append(run)
        return val

    def op_noinc(self, fn):
        self.ops.append(lambda e, fn=fn: fn(e))

    def dma(self, ctr, out, in_, transpose=False):
        val = ctr.inc(16)
        def run(e, out=out, in_=in_, s=ctr.h, tr=transpose):
            e.dma_start(out=out, in_=in_, transpose=tr).then_inc(s, 16)
        self.ops.append(run)
        return val


def build_nc(debug=False):
    nc = bass.Bass("TRN2", target_bir_lowering=False)

    x_full = nc.dram_tensor("x_full", [S, DM], F32, kind="ExternalInput").ap()
    x_q = nc.dram_tensor("x_q", [NQ, DM], F32, kind="ExternalInput").ap()
    pos_q = nc.dram_tensor("pos_q", [1, NQ], F32, kind="ExternalInput").ap()
    gmeta = nc.dram_tensor("gmeta", [1, 2], F32, kind="ExternalInput").ap()
    w_dq = nc.dram_tensor("w_dq", [DM, QL], F32, kind="ExternalInput").ap()
    g_q_t = nc.dram_tensor("g_q_t", [128, 6], F32, kind="ExternalInput").ap()
    w_uq = nc.dram_tensor("w_uq", [QL, H * 192], F32, kind="ExternalInput").ap()
    w_dkv = nc.dram_tensor("w_dkv", [DM, KVL], F32, kind="ExternalInput").ap()
    w_kr = nc.dram_tensor("w_kr", [DM, RD], F32, kind="ExternalInput").ap()
    g_kv_t = nc.dram_tensor("g_kv_t", [128, 4], F32, kind="ExternalInput").ap()
    w_uk = nc.dram_tensor("w_uk", [KVL, H * NOPE], F32, kind="ExternalInput").ap()
    w_uv = nc.dram_tensor("w_uv", [KVL, H * VD], F32, kind="ExternalInput").ap()
    w_o = nc.dram_tensor("w_o", [H * VD, DM], F32, kind="ExternalInput").ap()
    wq_gate = nc.dram_tensor("wq_gate", [HID, DM], I32, kind="ExternalInput").ap()
    s_gate_t = nc.dram_tensor("s_gate_t", [128, 44], F32, kind="ExternalInput").ap()
    wq_up = nc.dram_tensor("wq_up", [HID, DM], I32, kind="ExternalInput").ap()
    s_up_t = nc.dram_tensor("s_up_t", [128, 44], F32, kind="ExternalInput").ap()
    wq_down = nc.dram_tensor("wq_down", [DM, HID], I32, kind="ExternalInput").ap()
    s_down_r = nc.dram_tensor("s_down_r", [1, DM], F32, kind="ExternalInput").ap()
    y_out = nc.dram_tensor("y", [NQ, DM], F32, kind="ExternalOutput").ap()

    g_scr = nc.dram_tensor("g_scr", [HID, DM], BF16, kind="Internal").ap()
    u_scr = nc.dram_tensor("u_scr", [HID, DM], BF16, kind="Internal").ap()
    d_scr = nc.dram_tensor("d_scr", [DM, HID], BF16, kind="Internal").ap()
    x1_scr = nc.dram_tensor("x1_scr", [NQ, DM], F32, kind="Internal").ap()

    dbg = {}
    if debug:
        def dbg_t(name, shape, dt=F32):
            dbg[name] = nc.dram_tensor("dbg_" + name, shape, dt, kind="ExternalOutput").ap()
        dbg_t("cq", [128, 6, NQ], BF16)
        dbg_t("kvT", [128, 4, S], BF16)
        dbg_t("kpe", [65, S], BF16)
        dbg_t("qno", [128, HG, NQ], BF16)
        dbg_t("qpe", [65, HG * NQ], BF16)
        dbg_t("kno", [128, HG, S], BF16)
        dbg_t("vtok", [128, 16, HG * VD], BF16)
        dbg_t("ctx", [128, 16, NQ], BF16)
        dbg_t("x1", [NQ, DM], F32)
        dbg_t("h2T", [128, 16, NQ], BF16)
        dbg_t("masks", [128, 16, 512], BF16)
        dbg_t("cosq", [32, NQ], F32)
        dbg_t("kmaxn", [1, 16], F32)

    sem_guards = []
    def mksem(name):
        g = nc.semaphore(name)
        h = g.__enter__()
        sem_guards.append(g)
        return Ctr(h)

    dIn = mksem("dIn")
    dX = [mksem("dX0"), mksem("dX1")]
    dT = [mksem("dT0"), mksem("dT1")]
    dW = mksem("dW")
    dW0 = mksem("dW0")
    dD2 = mksem("dD2")
    dB = mksem("dB")
    dSt = [mksem("dSt0"), mksem("dSt1")]
    dSl = [mksem("dSl0"), mksem("dSl1")]
    dDl = [mksem("dDl0"), mksem("dDl1")]
    dR = [mksem("dR0"), mksem("dR1")]
    dY = [mksem("dY0"), mksem("dY1")]
    dDbg = mksem("dDbg")
    tS = mksem("tS")
    vS = mksem("vS")
    aS = mksem("aS")
    gS = mksem("gS")

    QT = EngQ(tS, chain=False)
    QV = EngQ(vS, chain=True)
    QA = EngQ(aS, chain=True)
    QG = EngQ(gS, chain=True)
    QS = EngQ(None, chain=False)

    class Alloc:
        def __init__(self):
            self.guards = {}
            self.order = {}
            self.k = 0

        def sb(self, name, shape, dt, side=None):
            g = nc.sbuf_tensor("%s__%d" % (name, self.k), shape, dt, side=side)
            t = g.__enter__()
            self.guards[name] = g
            self.order[name] = self.k
            self.k += 1
            return t

        def ps(self, name, shape):
            g = nc.psum_tensor("%s__%d" % (name, self.k), shape, F32)
            t = g.__enter__()
            self.guards[name] = g
            self.order[name] = self.k
            self.k += 1
            return t

        def free(self, *names):
            for name in sorted(names, key=lambda n: -self.order[n]):
                self.guards.pop(name).__exit__(None, None, None)
                self.order.pop(name)

    al = Alloc()

    ones_col = al.sb("ones_col", [128, 1], BF16)
    ones_cf = al.sb("ones_cf", [128, 1], F32)
    epsln = al.sb("epsln", [128, 1], F32)
    epsrms = al.sb("epsrms", [128, 1], F32)
    ones_mat = al.sb("ones_mat", [128, 128], BF16)
    ones_rf = al.sb("ones_rf", [1, 128], F32)
    ones_rb = al.sb("ones_rb", [1, 512], BF16)
    sgate_c = al.sb("sgate_c", [128, 44], F32)
    sup_c = al.sb("sup_c", [128, 44], F32)
    gq_c = al.sb("gq_c", [128, 6], F32)
    gkv_c = al.sb("gkv_c", [128, 4], F32)
    gmeta_sb = al.sb("gmeta_sb", [1, 2], F32)
    gcol = al.sb("gcol", [128, 2], F32)
    iota_pq = al.sb("iota_pq", [128, 512], F32)
    kmaxn = al.sb("kmaxn", [1, 16], F32)
    cosq = al.sb("cosq", [32, NQ], F32)
    sinq = al.sb("sinq", [32, NQ], F32)
    masks = al.sb("masks", [128, 16, 512], BF16)
    cq_sb = al.sb("cq_sb", [128, 6, NQ], BF16)
    kvT = al.sb("kvT", [128, 4, S], BF16)
    kpe_ext = al.sb("kpe_ext", [65, S], BF16)
    kpe_sq = al.sb("kpe_sq", [64, S], BF16)

    Wp = S + NQ
    cosk = al.sb("cosk", [32, S], F32, side="right")
    sink = al.sb("sink", [32, S], F32, side="right")
    posq_sb = al.sb("posq_sb", [1, NQ], F32, side="right")
    rt_ang = al.sb("rt_ang", [32, Wp], F32, side="right")
    rt_k = al.sb("rt_k", [32, Wp], F32, side="right")
    rt_t1 = al.sb("rt_t1", [32, Wp], F32, side="right")
    rt_t2 = al.sb("rt_t2", [32, Wp], F32, side="right")
    invf = al.sb("invf", [32, 1], F32, side="right")
    invfr = al.sb("invfr", [32, 1], F32, side="right")
    em = al.sb("em", [128, 2, 512], F32, side="right")

    for out_t, in_t in ((gmeta_sb[:, :], gmeta), (posq_sb[:, :], pos_q),
                        (sgate_c[:, :], s_gate_t), (sup_c[:, :], s_up_t),
                        (gq_c[:, :], g_q_t), (gkv_c[:, :], g_kv_t)):
        QS.dma(dIn, out_t, in_t)
    d_small = dIn.n

    QG.op(lambda g: g.iota(iota_pq[:, :], [[-1, 512]], base=0, channel_multiplier=1,
                           allow_small_or_imprecise_dtypes=True))
    QG.op(lambda g: g.iota(invfr[:, :], [[0, 1]], base=0, channel_multiplier=1,
                           allow_small_or_imprecise_dtypes=True))
    g_iota = QG.op(lambda g: g.iota(rt_ang[:, 0:S], [[1, S]], base=0, channel_multiplier=0,
                                    allow_small_or_imprecise_dtypes=True))

    QV.op(lambda v: v.memset(ones_col[:, :], 1.0))
    QV.op(lambda v: v.memset(ones_cf[:, :], 1.0))
    QV.op(lambda v: v.memset(epsln[:, :], LN_EPS))
    QV.op(lambda v: v.memset(epsrms[:, :], RMS_EPS))
    QV.op(lambda v: v.memset(kmaxn[:, :], 0.0))
    QV.op(lambda v: v.memset(ones_mat[:, :], 1.0))
    QV.op(lambda v: v.memset(ones_rb[:, :], 1.0))
    v_ones = QV.op(lambda v: v.memset(ones_rf[:, :], 1.0))

    QA.wait(gS, g_iota)
    a_invf = QA.op(lambda sc: sc.activation(invf[:, :], invfr[:, :], AF.Exp, scale=-LOG1E4_32))

    ps_b32 = al.ps("ps_b32", [32, 512])
    ps_b128 = al.ps("ps_b128", [128, 2])
    QT.wait(dIn, d_small)
    QT.wait(vS, v_ones)
    for i in range(NQ // 512):
        QT.wait(vS, vS.n)
        t_b = QT.op(lambda t, i=i: t.matmul(ps_b32[:, :], ones_rf[0:1, 0:32],
                                            posq_sb[0:1, i * 512:(i + 1) * 512],
                                            start=True, stop=True))
        QV.wait(tS, t_b)
        QV.wait(gS, g_iota)
        QV.op(lambda v, i=i: v.tensor_copy(rt_ang[:, S + i * 512:S + (i + 1) * 512],
                                           ps_b32[:, :]))
    t_g = QT.op(lambda t: t.matmul(ps_b128[:, :], ones_rf[0:1, 0:128], gmeta_sb[0:1, 0:2],
                                   start=True, stop=True))
    QV.wait(tS, t_g)
    QV.op(lambda v: v.tensor_copy(gcol[:, :], ps_b128[:, :]))

    QV.wait(aS, a_invf)
    QV.op(lambda v: v.tensor_scalar(rt_ang[:, :], rt_ang[:, :], invf[:, :], None, ALU.mult))
    QV.op(lambda v: v.tensor_scalar(rt_k[:, :], rt_ang[:, :], INV2PI, None, ALU.mult))
    QV.op(lambda v: v.tensor_scalar(rt_k[:, :], rt_k[:, :], MAGIC, None, ALU.add))
    QV.op(lambda v: v.tensor_scalar(rt_k[:, :], rt_k[:, :], -MAGIC, None, ALU.add))
    QV.op(lambda v: v.tensor_scalar(rt_t1[:, :], rt_k[:, :], C_HI, None, ALU.mult))
    QV.op(lambda v: v.tensor_sub(rt_ang[:, :], rt_ang[:, :], rt_t1[:, :]))
    QV.op(lambda v: v.tensor_scalar(rt_t1[:, :], rt_k[:, :], C_MID, None, ALU.mult))
    QV.op(lambda v: v.tensor_sub(rt_ang[:, :], rt_ang[:, :], rt_t1[:, :]))
    QV.op(lambda v: v.tensor_scalar(rt_t1[:, :], rt_k[:, :], C_LO, None, ALU.mult))
    v_red = QV.op(lambda v: v.tensor_sub(rt_ang[:, :], rt_ang[:, :], rt_t1[:, :]))
    QA.wait(vS, v_red)
    a_s4 = QA.op(lambda sc: sc.activation(rt_k[:, :], rt_ang[:, :], AF.Sin, scale=0.25))
    QV.wait(aS, a_s4)
    v_t4 = QV.op(lambda v: v.tensor_mul(rt_t1[:, :], rt_k[:, :], rt_k[:, :]))
    QA.wait(vS, v_t4)
    a_c4 = QA.op(lambda sc: sc.activation(rt_t2[:, :], rt_t1[:, :], AF.Sqrt, bias=1.0, scale=-1.0))
    QV.wait(aS, a_c4)
    QV.op(lambda v: v.tensor_mul(rt_k[:, :], rt_k[:, :], rt_t2[:, :]))
    QV.op(lambda v: v.tensor_scalar(rt_k[:, :], rt_k[:, :], 2.0, None, ALU.mult))
    QV.op(lambda v: v.tensor_scalar(rt_t2[:, :], rt_t1[:, :], -2.0, 1.0, ALU.mult, ALU.add))
    QV.op(lambda v: v.tensor_mul(rt_t1[:, :], rt_k[:, :], rt_t2[:, :]))
    QV.op(lambda v: v.tensor_scalar(rt_t1[:, :], rt_t1[:, :], 2.0, None, ALU.mult))
    QV.op(lambda v: v.tensor_copy(sink[:, :], rt_t1[:, 0:S]))
    QV.op(lambda v: v.tensor_copy(sinq[:, :], rt_t1[:, S:Wp]))
    QV.op(lambda v: v.tensor_mul(rt_t1[:, :], rt_k[:, :], rt_k[:, :]))
    QV.op(lambda v: v.tensor_scalar(rt_t1[:, :], rt_t1[:, :], -2.0, 1.0, ALU.mult, ALU.add))
    QV.op(lambda v: v.tensor_copy(cosk[:, :], rt_t1[:, 0:S]))
    v_trig = QV.op(lambda v: v.tensor_copy(cosq[:, :], rt_t1[:, S:Wp]))
    if debug:
        QS.wait(vS, v_trig)
        QS.dma(dDbg, dbg["cosq"], cosq[:, :])

    QV.op(lambda v: v.tensor_scalar(em[:, 0:1, :], iota_pq[:, :], gcol[:, 0:1], None, ALU.subtract))
    QV.op(lambda v: v.tensor_scalar(em[:, 1:2, :], iota_pq[:, :], gcol[:, 1:2], None, ALU.subtract))
    for s in range(2):
        for j in range(8):
            kc = j if s == 0 else 8 + j
            QV.op(lambda v, s=s, j=j, kc=kc: v.tensor_scalar(
                masks[:, s * 8 + j:s * 8 + j + 1, :], em[:, s:s + 1, :],
                float(-128.0 * kc), None, ALU.is_le))
    v_masks = vS.n
    if debug:
        QS.wait(vS, v_masks)
        QS.dma(dDbg, dbg["masks"], masks[:, :, :])
    al.free("em", "invfr", "invf", "rt_t2", "rt_t1", "rt_k", "rt_ang", "posq_sb")
    al.free("ps_b32", "ps_b128")

    xst = al.sb("xst", [128, 2, DM], F32, side="right")
    hst = al.sb("hst", [128, 2, DM], BF16, side="right")
    sqst = al.sb("sqst", [128, DM], F32, side="right")
    st1 = al.sb("st1", [128, 8], F32, side="right")
    hwin = al.sb("hwin", [128, 16, 512], BF16, side="right")
    rka = al.sb("rka", [32, 512], F32, side="right")
    rkb = al.sb("rkb", [32, 512], F32, side="right")
    rkbf = al.sb("rkbf", [32, 512], BF16, side="right")
    sqw = al.sb("sqw", [128, 512], F32, side="right")
    rrow = al.sb("rrow", [1, 512], F32, side="right")
    rbc = al.sb("rbc", [128, 512], F32, side="right")

    x_last_rd = [0, 0]
    h_last_rd = [0, 0]

    def ln_tile(src_rows, dst_col, buf):
        xa = xst[:, buf:buf + 1, :]
        ha = hst[:, buf:buf + 1, :]
        if x_last_rd[buf]:
            QS.wait(aS, x_last_rd[buf])
        d_x = QS.dma(dX[buf], xa, src_rows)
        QV.wait(dX[buf], d_x)
        QV.op(lambda v: v.reduce_sum(st1[:, 0:1], xa, AX.X))
        QA.wait(dX[buf], d_x)
        QA.op(lambda sc: sc.activation(sqst[:, :], xa, AF.Square, accum_out=st1[:, 1:2]))
        QV.wait(aS, aS.n)
        QV.op(lambda v: v.tensor_scalar(st1[:, 2:3], st1[:, 0:1], 1.0 / DM, None, ALU.mult))
        QV.op(lambda v: v.tensor_scalar(st1[:, 3:4], st1[:, 1:2], 1.0 / DM, None, ALU.mult))
        QV.op(lambda v: v.tensor_mul(st1[:, 4:5], st1[:, 2:3], st1[:, 2:3]))
        v_var = QV.op(lambda v: v.tensor_sub(st1[:, 3:4], st1[:, 3:4], st1[:, 4:5]))
        QA.wait(vS, v_var)
        a_std = QA.op(lambda sc: sc.activation(st1[:, 5:6], st1[:, 3:4], AF.Sqrt, bias=epsln[:, 0:1]))
        QV.wait(aS, a_std)
        QV.op(lambda v: v.reciprocal(st1[:, 6:7], st1[:, 5:6]))
        v_nb = QV.op(lambda v: v.tensor_scalar(st1[:, 7:8], st1[:, 2:3], st1[:, 6:7], -1.0,
                                               ALU.mult, ALU.mult))
        QA.wait(vS, v_nb)
        if h_last_rd[buf]:
            QA.wait(dT[buf], h_last_rd[buf])
        a_h = QA.op(lambda sc: sc.activation(ha, xa, AF.Identity, bias=st1[:, 7:8], scale=st1[:, 6:7]))
        x_last_rd[buf] = a_h
        QS.wait(aS, a_h)
        d_tr = QS.dma(dT[buf], hwin[:, :, dst_col:dst_col + 128], ha, transpose=True)
        h_last_rd[buf] = d_tr
        return d_tr

    wdkv_sb = al.sb("wdkv_sb", [128, 16, KVL], BF16, side="right")
    wkr_sb = al.sb("wkr_sb", [128, 16, RD], BF16, side="right")
    QG.wait(vS, vS.n)
    QG.wait(aS, aS.n)
    QG.dma(dW, wdkv_sb[:, :, :], w_dkv.rearrange("(c p) f -> p c f", p=128))
    d_wkr = QG.dma(dW, wkr_sb[:, :, :], w_kr.rearrange("(c p) f -> p c f", p=128))
    QS.wait(vS, vS.n)

    ps_r = al.ps("ps_r", [1, 512])
    ps_bw = al.ps("ps_bw", [128, 512])
    ps_c = [al.ps("ps_c%d" % i, [128, 512]) for i in range(4)]
    ps_kp1 = al.ps("ps_kp1", [32, 512])
    ps_kp2 = al.ps("ps_kp2", [32, 512])

    hwin_rd = 0
    for wdx in range(4):
        wc = wdx * 512
        QS.wait(tS, hwin_rd)
        for j in range(4):
            d_hw = ln_tile(x_full[(4 * wdx + j) * 128:(4 * wdx + j + 1) * 128, :],
                           j * 128, j % 2)
        QT.wait(dT[0], dT[0].n)
        QT.wait(dT[1], dT[1].n)
        QT.wait(dW, d_wkr)
        QT.wait(vS, vS.n)
        for m in range(4):
            for c in range(16):
                def mm(t, m=m, c=c):
                    return t.matmul(ps_c[m][:, :],
                                    wdkv_sb[:, c:c + 1, m * 128:(m + 1) * 128],
                                    hwin[:, c:c + 1, :],
                                    start=(c == 0), stop=(c == 15))
                QT.op(mm) if c == 15 else QT.op_noinc(mm)
        for c in range(16):
            def mm1(t, c=c):
                return t.matmul(ps_kp1[:, :], wkr_sb[:, c:c + 1, 0:32],
                                hwin[:, c:c + 1, :],
                                start=(c == 0), stop=(c == 15))
            def mm2(t, c=c):
                return t.matmul(ps_kp2[:, :], wkr_sb[:, c:c + 1, 32:64],
                                hwin[:, c:c + 1, :],
                                start=(c == 0), stop=(c == 15))
            QT.op_noinc(mm1)
            QT.op(mm2) if c == 15 else QT.op_noinc(mm2)
        hwin_rd = tS.n
        QA.wait(tS, hwin_rd)
        for m in range(4):
            a_s = QA.op(lambda sc, m=m: sc.activation(sqw[:, :], ps_c[m][:, :], AF.Square))
            QT.wait(aS, a_s)
            def mm(t, m=m):
                return t.matmul(ps_r[0:1, :], ones_cf[:, 0:1], sqw[:, :],
                                start=(m == 0), stop=(m == 3))
            t_sq = QT.op(mm)
            QA.wait(tS, t_sq)
        QA.wait(tS, tS.n)
        a_r = QA.op(lambda sc: sc.activation(rrow[0:1, :], ps_r[0:1, :], AF.Sqrt,
                                             scale=1.0 / KVL, bias=epsrms[0:1, 0:1]))
        QV.wait(aS, a_r)
        QV.op(lambda v: v.reciprocal(rrow[0:1, :], rrow[0:1, :]))
        QT.wait(vS, vS.n)
        t_b = QT.op(lambda t: t.matmul(ps_bw[:, :], ones_rf[0:1, 0:128], rrow[0:1, :],
                                       start=True, stop=True))
        QV.wait(tS, t_b)
        QV.op(lambda v: v.tensor_copy(rbc[:, :], ps_bw[:, :]))
        for m in range(4):
            QV.op(lambda v, m=m, wc=wc: v.scalar_tensor_tensor(
                kvT[:, m:m + 1, wc:wc + 512], ps_c[m][:, :],
                gkv_c[:, m:m + 1], rbc[:, :], ALU.mult, ALU.mult))
        csl = slice(wc, wc + 512)
        QV.op(lambda v, csl=csl: v.tensor_mul(rka[0:32, :], ps_kp1[:, :], cosk[:, csl]))
        QV.op(lambda v, csl=csl: v.tensor_mul(rkb[0:32, :], ps_kp2[:, :], sink[:, csl]))
        QV.op(lambda v: v.tensor_sub(rka[0:32, :], rka[0:32, :], rkb[0:32, :]))
        QV.op(lambda v, csl=csl: v.tensor_copy(kpe_ext[0:32, csl], rka[0:32, :]))
        QV.op(lambda v, csl=csl: v.tensor_mul(rka[0:32, :], ps_kp1[:, :], sink[:, csl]))
        QV.op(lambda v, csl=csl: v.tensor_mul(rkb[0:32, :], ps_kp2[:, :], cosk[:, csl]))
        QV.op(lambda v: v.tensor_add(rka[0:32, :], rka[0:32, :], rkb[0:32, :]))
        QV.wait(dB, dB.n)
        v_k2 = QV.op(lambda v: v.tensor_copy(rkbf[0:32, :], rka[0:32, :]))
        QS.wait(vS, v_k2)
        QS.dma(dB, kpe_ext[32:64, csl], rkbf[0:32, :])
    QV.wait(dB, dB.n)
    QV.op(lambda v: v.memset(kpe_ext[64:65, :], 1.0))
    v_kv = vS.n
    QA.wait(vS, v_kv)
    a_kpsq = QA.op(lambda sc: sc.activation(kpe_sq[0:64, :], kpe_ext[0:64, :], AF.Square))
    if debug:
        QS.wait(vS, v_kv)
        QS.dma(dDbg, dbg["kvT"], kvT[:, :, :])
        QS.dma(dDbg, dbg["kpe"], kpe_ext[:, :])
        QV.wait(dDbg, dDbg.n)
        QT.wait(dDbg, dDbg.n)
        QA.wait(dDbg, dDbg.n)
    al.free("ps_kp2", "ps_kp1", "ps_c3", "ps_c2", "ps_c1", "ps_c0")
    al.free("wkr_sb", "wdkv_sb")

    wdq_sb = al.sb("wdq_sb", [128, 16, QL], BF16, side="right")
    QG.wait(tS, tS.n)
    d_wdq = QG.dma(dW, wdq_sb[:, :, :], w_dq.rearrange("(c p) f -> p c f", p=128))
    ps_c2 = [al.ps("ps_c2%d" % i, [128, 512]) for i in range(6)]

    for wdx in range(2):
        wc = wdx * 512
        QS.wait(tS, hwin_rd)
        for j in range(4):
            d_hw = ln_tile(x_q[(4 * wdx + j) * 128:(4 * wdx + j + 1) * 128, :],
                           j * 128, j % 2)
        QT.wait(dT[0], dT[0].n)
        QT.wait(dT[1], dT[1].n)
        QT.wait(dW, d_wdq)
        QT.wait(vS, vS.n)
        for m in range(6):
            for c in range(16):
                def mm(t, m=m, c=c):
                    return t.matmul(ps_c2[m][:, :],
                                    wdq_sb[:, c:c + 1, m * 128:(m + 1) * 128],
                                    hwin[:, c:c + 1, :],
                                    start=(c == 0), stop=(c == 15))
                QT.op(mm) if c == 15 else QT.op_noinc(mm)
        hwin_rd = tS.n
        QA.wait(tS, hwin_rd)
        for m in range(6):
            a_s = QA.op(lambda sc, m=m: sc.activation(sqw[:, :], ps_c2[m][:, :], AF.Square))
            QT.wait(aS, a_s)
            def mm(t, m=m):
                return t.matmul(ps_r[0:1, :], ones_cf[:, 0:1], sqw[:, :],
                                start=(m == 0), stop=(m == 5))
            t_sq = QT.op(mm)
            QA.wait(tS, t_sq)
        QA.wait(tS, tS.n)
        a_r = QA.op(lambda sc: sc.activation(rrow[0:1, :], ps_r[0:1, :], AF.Sqrt,
                                             scale=1.0 / QL, bias=epsrms[0:1, 0:1]))
        QV.wait(aS, a_r)
        QV.op(lambda v: v.reciprocal(rrow[0:1, :], rrow[0:1, :]))
        QT.wait(vS, vS.n)
        t_b = QT.op(lambda t: t.matmul(ps_bw[:, :], ones_rf[0:1, 0:128], rrow[0:1, :],
                                       start=True, stop=True))
        QV.wait(tS, t_b)
        QV.op(lambda v: v.tensor_copy(rbc[:, :], ps_bw[:, :]))
        for m in range(6):
            QV.op(lambda v, m=m, wc=wc: v.scalar_tensor_tensor(
                cq_sb[:, m:m + 1, wc:wc + 512], ps_c2[m][:, :],
                gq_c[:, m:m + 1], rbc[:, :], ALU.mult, ALU.mult))
    v_cq = vS.n
    if debug:
        QS.wait(vS, v_cq)
        QS.dma(dDbg, dbg["cq"], cq_sb[:, :, :])
        QV.wait(dDbg, dDbg.n)
        QT.wait(dDbg, dDbg.n)
        QA.wait(dDbg, dDbg.n)
    al.free("ps_c25", "ps_c24", "ps_c23", "ps_c22", "ps_c21", "ps_c20", "ps_bw", "ps_r")
    al.free("wdq_sb")
    al.free("rbc", "rrow", "sqw", "rkbf", "rkb", "rka", "hwin", "st1", "sqst", "hst", "xst")
    al.free("sink", "cosk")

    wuq0_sb = al.sb("wuq0_sb", [128, 6, HG * 192], BF16)
    wuk0_sb = al.sb("wuk0_sb", [128, 4, HG * NOPE], BF16)
    wuv0_sb = al.sb("wuv0_sb", [128, 4, HG * VD], BF16)
    QG.dma(dW0, wuq0_sb[:, :, :],
           w_uq[:, 0:HG * 192].rearrange("(c p) f -> p c f", p=128))
    QG.dma(dW0, wuk0_sb[:, :, :],
           w_uk[:, 0:HG * NOPE].rearrange("(c p) f -> p c f", p=128))
    d_wg0 = QG.dma(dW0, wuv0_sb[:, :, :],
                   w_uv[:, 0:HG * VD].rearrange("(c p) f -> p c f", p=128))

    for j in range(4):
        QG.dma(dD2, g_scr[j * 1408:(j + 1) * 1408, :], wq_gate[j * 1408:(j + 1) * 1408, :])
    for j in range(4):
        QG.dma(dD2, u_scr[j * 1408:(j + 1) * 1408, :], wq_up[j * 1408:(j + 1) * 1408, :])
    for j in range(4):
        QG.dma(dD2, d_scr[j * 512:(j + 1) * 512, :], wq_down[j * 512:(j + 1) * 512, :])
    d_d2d = dD2.n

    q_no = al.sb("q_no", [128, HG, NQ], BF16)
    qpe_ext = al.sb("qpe_ext", [65, HG * NQ], BF16)
    kno = al.sb("kno", [128, HG, S], BF16)
    vtok = al.sb("vtok", [128, 16, HG * VD], BF16)
    sqs = al.sb("sqs", [128, S], BF16)
    qn_row = al.sb("qn_row", [1, NQ], F32)
    bnd_row = al.sb("bnd_row", [1, NQ], BF16)
    kmx = al.sb("kmx", [1, 8], F32)
    p_sb = al.sb("p_sb", [128, 3, 512], BF16)
    rec_sb = al.sb("rec_sb", [128, 512], F32)
    rpq1 = al.sb("rpq1", [32, NQ], F32)
    rpq2 = al.sb("rpq2", [32, NQ], F32)
    rpqbf = al.sb("rpqbf", [32, NQ], BF16)
    ctx = al.sb("ctx", [128, 16, NQ], BF16, side="right")

    ctx_evac = {}
    last_exp_bank = [0, 0]
    last_av_pbuf = [0, 0, 0]
    d_wo = 0
    wo_sb = None

    for grp in range(NGRP):
        if grp == 0:
            wuq_sb, wuk_sb, wuv_sb = wuq0_sb, wuk0_sb, wuv0_sb
            d_wg, s_wg = d_wg0, dW0
        else:
            wuq_sb = al.sb("wuq_sb", [128, 6, HG * 192], BF16, side="right")
            wuk_sb = al.sb("wuk_sb", [128, 4, HG * NOPE], BF16, side="right")
            wuv_sb = al.sb("wuv_sb", [128, 4, HG * VD], BF16, side="right")
            QG.wait(tS, t_dend)
            QG.dma(dW, wuq_sb[:, :, :],
                   w_uq[:, grp * HG * 192:(grp + 1) * HG * 192]
                   .rearrange("(c p) f -> p c f", p=128))
            QG.dma(dW, wuk_sb[:, :, :],
                   w_uk[:, grp * HG * NOPE:(grp + 1) * HG * NOPE]
                   .rearrange("(c p) f -> p c f", p=128))
            d_wg = QG.dma(dW, wuv_sb[:, :, :],
                          w_uv[:, grp * HG * VD:(grp + 1) * HG * VD]
                          .rearrange("(c p) f -> p c f", p=128))
            s_wg = dW

        ps_a = [al.ps("ps_a0", [128, 512]), al.ps("ps_a1", [128, 512])]
        ps_p1 = al.ps("ps_p1", [32, 1024])
        ps_p2 = al.ps("ps_p2", [32, 1024])

        QT.wait(s_wg, d_wg)
        QT.wait(vS, vS.n)
        QT.wait(aS, aS.n)
        evac_done = [vS.n, vS.n]
        for hl in range(HG):
            for nt in range(2):
                pb = (hl * 2 + nt) % 2
                QT.wait(vS, evac_done[pb])
                for c in range(6):
                    def mm(t, hl=hl, nt=nt, c=c, pa=ps_a[pb], w=wuq_sb):
                        return t.matmul(pa[:, :],
                                        w[:, c:c + 1, hl * 192:hl * 192 + 128],
                                        cq_sb[:, c:c + 1, nt * 512:(nt + 1) * 512],
                                        start=(c == 0), stop=(c == 5))
                    QT.op(mm) if c == 5 else QT.op_noinc(mm)
                QV.wait(tS, tS.n)
                evac_done[pb] = QV.op(lambda v, hl=hl, nt=nt, pa=ps_a[pb]: v.tensor_copy(
                    q_no[:, hl:hl + 1, nt * 512:(nt + 1) * 512], pa[:, :]))

        for hl in range(HG):
            co = hl * NQ
            QT.wait(vS, vS.n)
            for c in range(6):
                for nt in range(2):
                    def mm1(t, hl=hl, c=c, nt=nt, pp=ps_p1, w=wuq_sb):
                        return t.matmul(pp[:, nt * 512:(nt + 1) * 512],
                                        w[:, c:c + 1,
                                          hl * 192 + 128:hl * 192 + 160],
                                        cq_sb[:, c:c + 1, nt * 512:(nt + 1) * 512],
                                        start=(c == 0), stop=(c == 5))
                    def mm2(t, hl=hl, c=c, nt=nt, pp=ps_p2, w=wuq_sb):
                        return t.matmul(pp[:, nt * 512:(nt + 1) * 512],
                                        w[:, c:c + 1,
                                          hl * 192 + 160:hl * 192 + 192],
                                        cq_sb[:, c:c + 1, nt * 512:(nt + 1) * 512],
                                        start=(c == 0), stop=(c == 5))
                    QT.op_noinc(mm1)
                    QT.op(mm2) if (c == 5 and nt == 1) else QT.op_noinc(mm2)
            t_pe = tS.n
            QV.wait(tS, t_pe)
            QV.op(lambda v, pp=ps_p1: v.tensor_mul(rpq1[:, :], pp[:, :], cosq[:, :]))
            QV.op(lambda v, pp=ps_p2: v.tensor_mul(rpq2[:, :], pp[:, :], sinq[:, :]))
            QV.op(lambda v: v.tensor_sub(rpq1[:, :], rpq1[:, :], rpq2[:, :]))
            QV.op(lambda v, co=co: v.tensor_copy(qpe_ext[0:32, co:co + NQ], rpq1[:, :]))
            QV.op(lambda v, pp=ps_p1: v.tensor_mul(rpq1[:, :], pp[:, :], sinq[:, :]))
            QV.op(lambda v, pp=ps_p2: v.tensor_mul(rpq2[:, :], pp[:, :], cosq[:, :]))
            QV.op(lambda v: v.tensor_add(rpq1[:, :], rpq1[:, :], rpq2[:, :]))
            QV.wait(dB, dB.n)
            v_q2 = QV.op(lambda v: v.tensor_copy(rpqbf[:, :], rpq1[:, :]))
            QS.wait(vS, v_q2)
            QS.dma(dB, qpe_ext[32:64, co:co + NQ], rpqbf[:, :])

        QT.wait(vS, v_kv)
        for hl in range(HG):
            for nt in range(4):
                pb = (hl * 4 + nt) % 2
                QT.wait(vS, evac_done[pb])
                for c in range(4):
                    def mm(t, hl=hl, nt=nt, c=c, pa=ps_a[pb], w=wuk_sb):
                        return t.matmul(pa[:, :],
                                        w[:, c:c + 1, hl * 128:(hl + 1) * 128],
                                        kvT[:, c:c + 1, nt * 512:(nt + 1) * 512],
                                        start=(c == 0), stop=(c == 3))
                    QT.op(mm) if c == 3 else QT.op_noinc(mm)
                QV.wait(tS, tS.n)
                evac_done[pb] = QV.op(lambda v, hl=hl, nt=nt, pa=ps_a[pb]: v.tensor_copy(
                    kno[:, hl:hl + 1, nt * 512:(nt + 1) * 512], pa[:, :]))

        for kt in range(16):
            pb = kt % 2
            QT.wait(vS, evac_done[pb])
            for c in range(4):
                def mm(t, kt=kt, c=c, pa=ps_a[pb], w=wuv_sb):
                    return t.matmul(pa[:, :],
                                    kvT[:, c:c + 1, kt * 128:(kt + 1) * 128],
                                    w[:, c:c + 1, 0:HG * VD],
                                    start=(c == 0), stop=(c == 3))
                QT.op(mm) if c == 3 else QT.op_noinc(mm)
            QV.wait(tS, tS.n)
            evac_done[pb] = QV.op(lambda v, kt=kt, pa=ps_a[pb]: v.tensor_copy(
                vtok[:, kt:kt + 1, :], pa[:, :]))
        t_dend = tS.n

        last_red = 0
        for hl in range(HG):
            QA.wait(vS, vS.n)
            QA.wait(tS, tS.n)
            a_sq = QA.op(lambda sc, hl=hl: sc.activation(sqs[:, :], kno[:, hl:hl + 1, :],
                                                         AF.Square))
            for half in range(2):
                QT.wait(aS, a_sq)
                QT.wait(aS, a_kpsq)
                QT.wait(vS, last_red)
                for j in range(2):
                    sl = slice(half * 1024 + j * 512, half * 1024 + (j + 1) * 512)
                    def mma(t, j=j, sl=sl, pp=ps_p1):
                        return t.matmul(pp[0:1, j * 512:(j + 1) * 512],
                                        ones_col[:, 0:1], sqs[:, sl],
                                        start=True, stop=False)
                    def mmb(t, j=j, sl=sl, pp=ps_p1):
                        return t.matmul(pp[0:1, j * 512:(j + 1) * 512],
                                        ones_col[0:64, 0:1], kpe_sq[0:64, sl],
                                        start=False, stop=True)
                    QT.op_noinc(mma)
                    QT.op(mmb)
                QV.wait(tS, tS.n)
                last_red = QV.op(lambda v, half=half, pp=ps_p1: v.reduce_max(
                    kmx[0:1, half:half + 1], pp[0:1, 0:1024], AX.X))
            QV.op(lambda v: v.tensor_tensor(kmx[0:1, 2:3], kmx[0:1, 0:1], kmx[0:1, 1:2],
                                            ALU.max))
            v_kmx = vS.n
            QA.wait(vS, v_kmx)
            QA.op(lambda sc, hl=hl, grp=grp: sc.activation(
                kmaxn[0:1, grp * HG + hl:grp * HG + hl + 1], kmx[0:1, 2:3],
                AF.Sqrt, scale=1.0404))

        a_qn_prev = 0
        for hl in range(HG):
            QA.wait(tS, tS.n)
            QA.wait(vS, vS.n)
            a_q1 = QA.op(lambda sc, hl=hl: sc.activation(sqs[:, 0:NQ],
                                                         q_no[:, hl:hl + 1, :], AF.Square))
            QT.wait(aS, a_q1)
            QT.wait(aS, a_qn_prev)
            for j in range(2):
                def mmq(t, j=j, pp=ps_p1):
                    return t.matmul(pp[0:1, j * 512:(j + 1) * 512], ones_col[:, 0:1],
                                    sqs[:, j * 512:(j + 1) * 512], start=True, stop=False)
                QT.op(mmq) if j == 1 else QT.op_noinc(mmq)
            QA.wait(tS, tS.n)
            a_q2 = QA.op(lambda sc, hl=hl: sc.activation(
                sqs[0:64, 0:NQ], qpe_ext[0:64, hl * NQ:(hl + 1) * NQ], AF.Square))
            QT.wait(aS, a_q2)
            for j in range(2):
                def mmq2(t, j=j, pp=ps_p1):
                    return t.matmul(pp[0:1, j * 512:(j + 1) * 512], ones_col[0:64, 0:1],
                                    sqs[0:64, j * 512:(j + 1) * 512], start=False, stop=True)
                QT.op(mmq2) if j == 1 else QT.op_noinc(mmq2)
            QA.wait(tS, tS.n)
            a_qn_prev = QA.op(lambda sc, pp=ps_p1: sc.activation(qn_row[0:1, :], pp[0:1, 0:1024],
                                                                 AF.Sqrt))
            QV.wait(aS, a_qn_prev)
            QV.wait(dB, dB.n)
            QV.op(lambda v, hl=hl, grp=grp: v.tensor_scalar(
                bnd_row[0:1, :], qn_row[0:1, :],
                kmaxn[0:1, grp * HG + hl:grp * HG + hl + 1], None, ALU.mult))
            v_bnd = QV.op(lambda v: v.tensor_scalar(bnd_row[0:1, :], bnd_row[0:1, :],
                                                    -1.0, None, ALU.mult))
            QS.wait(vS, v_bnd)
            QS.dma(dB, qpe_ext[64:65, hl * NQ:(hl + 1) * NQ], bnd_row[0:1, :])

        if debug and grp == 0:
            QS.wait(vS, vS.n)
            QS.wait(aS, aS.n)
            QS.wait(dB, dB.n)
            QS.dma(dDbg, dbg["qno"], q_no[:, :, :])
            QS.dma(dDbg, dbg["qpe"], qpe_ext[:, :])
            QS.dma(dDbg, dbg["kno"], kno[:, :, :])
            QS.dma(dDbg, dbg["vtok"], vtok[:, :, :])
            QS.dma(dDbg, dbg["kmaxn"], kmaxn[:, :])
            QV.wait(dDbg, dDbg.n)
            QT.wait(dDbg, dDbg.n)
            QA.wait(dDbg, dDbg.n)

        al.free("ps_p2", "ps_p1", "ps_a1", "ps_a0")
        if grp != 0:
            al.free("wuv_sb", "wuk_sb", "wuq_sb")

        QT.wait(dB, dB.n)
        ps_sT = [al.ps("ps_sT0", [128, 512]), al.ps("ps_sT1", [128, 512])]
        ps_cx = [al.ps("ps_cx0", [128, 512]), al.ps("ps_cx1", [128, 512])]
        ps_sm = [al.ps("ps_sm0", [128, 512]), al.ps("ps_sm1", [128, 512])]

        QT.wait(vS, vS.n)
        QT.wait(aS, aS.n)
        for sl_i in range(2):
            nk = SLOT_K[sl_i]
            qs = sl_i * 512
            for hl in range(HG):
                gh = grp * HG + hl
                cb = gh % 2
                prev_evac = ctx_evac.get(gh - 2, 0)
                def emit_av(kc, dep_v, a_ex):
                    pbuf = kc % 3
                    if dep_v:
                        QT.wait(vS, dep_v)
                    else:
                        QT.wait(aS, a_ex)
                    if kc == 0:
                        QT.wait(vS, prev_evac)
                    def mm_av(t, hl=hl, kc=kc, pbuf=pbuf, nk=nk, pp=ps_cx[cb]):
                        return t.matmul(pp[:, :],
                                        vtok[:, kc:kc + 1, hl * VD:(hl + 1) * VD],
                                        p_sb[:, pbuf:pbuf + 1, :],
                                        start=(kc == 0), stop=(kc == nk - 1))
                    QT.op_noinc(mm_av)
                    def mm_sm(t, kc=kc, pbuf=pbuf, nk=nk, pp=ps_sm[cb]):
                        return t.matmul(pp[:, :], ones_mat[:, :],
                                        p_sb[:, pbuf:pbuf + 1, :],
                                        start=(kc == 0), stop=(kc == nk - 1))
                    last_av_pbuf[pbuf] = QT.op(mm_sm)

                pend = None
                for kc in range(nk):
                    pbuf = kc % 3
                    sbk = kc % 2
                    QT.wait(aS, last_exp_bank[sbk])
                    def mm_n(t, hl=hl, kc=kc, qs=qs, pp=ps_sT[sbk]):
                        return t.matmul(pp[:, :],
                                        kno[:, hl:hl + 1, kc * 128:(kc + 1) * 128],
                                        q_no[:, hl:hl + 1, qs:qs + 512],
                                        start=True, stop=False)
                    QT.op_noinc(mm_n)
                    def mm_p(t, hl=hl, kc=kc, qs=qs, pp=ps_sT[sbk]):
                        return t.matmul(pp[:, :],
                                        kpe_ext[0:65, kc * 128:(kc + 1) * 128],
                                        qpe_ext[0:65, hl * NQ + qs:hl * NQ + qs + 512],
                                        start=False, stop=True)
                    t_sc = QT.op(mm_p)
                    QA.wait(tS, t_sc)
                    QA.wait(tS, last_av_pbuf[pbuf])
                    a_ex = QA.op(lambda sc, pbuf=pbuf, pp=ps_sT[sbk]: sc.activation(
                        p_sb[:, pbuf:pbuf + 1, :], pp[:, :], AF.Exp, scale=SCALE))
                    last_exp_bank[sbk] = a_ex
                    midx = kc if sl_i == 0 else (8 + (kc - 8) if kc >= 8 else None)
                    dep_v = 0
                    if midx is not None:
                        QV.wait(aS, a_ex)
                        dep_v = QV.op(lambda v, pbuf=pbuf, midx=midx: v.tensor_mul(
                            p_sb[:, pbuf:pbuf + 1, :], p_sb[:, pbuf:pbuf + 1, :],
                            masks[:, midx:midx + 1, :]))
                    if pend is not None:
                        emit_av(*pend)
                    pend = (kc, dep_v, a_ex)
                emit_av(*pend)
                QV.wait(tS, last_av_pbuf[(nk - 1) % 3])
                QV.op(lambda v, pp=ps_sm[cb]: v.reciprocal(rec_sb[:, :], pp[:, :]))
                ctx_evac[gh] = QV.op(lambda v, gh=gh, qs=qs, pp=ps_cx[cb]: v.scalar_tensor_tensor(
                    ctx[:, gh:gh + 1, qs:qs + 512], pp[:, :], 1.0,
                    rec_sb[:, :], ALU.mult, ALU.mult))
        al.free("ps_sm1", "ps_sm0", "ps_cx1", "ps_cx0", "ps_sT1", "ps_sT0")

    v_ctx = vS.n
    if debug:
        QS.wait(vS, v_ctx)
        QS.dma(dDbg, dbg["ctx"], ctx[:, :, :])
        QV.wait(dDbg, dDbg.n)
        QT.wait(dDbg, dDbg.n)
        QA.wait(dDbg, dDbg.n)
    al.free("rpqbf", "rpq2", "rpq1", "rec_sb", "p_sb", "kmx", "bnd_row", "qn_row",
            "sqs", "vtok", "kno", "qpe_ext", "q_no")
    al.free("wuv0_sb", "wuk0_sb", "wuq0_sb")
    al.free("kpe_sq", "kpe_ext", "kvT", "cq_sb", "masks")

    wo_sb = al.sb("wo_sb", [128, 16, DM], BF16, side="right")
    QG.wait(tS, tS.n)
    QG.wait(vS, vS.n)
    QG.wait(aS, aS.n)
    QG.dma(dW, wo_sb[:, 0:8, :], w_o[0:1024, :].rearrange("(c p) f -> p c f", p=128))
    d_wo = QG.dma(dW, wo_sb[:, 8:16, :],
                  w_o[1024:2048, :].rearrange("(c p) f -> p c f", p=128))
    h2T = al.sb("h2T", [128, 16, NQ], BF16)
    xqst = al.sb("xqst", [128, 2, DM], F32, side="right")
    x1st = al.sb("x1st", [128, 2, DM], F32, side="right")
    h2st = al.sb("h2st", [128, 2, DM], BF16, side="right")
    sqstf = al.sb("sqstf", [128, DM], F32, side="right")
    st1f = al.sb("st1f", [128, 8], F32, side="right")
    ps_f = [al.ps("ps_f%d" % i, [128, 512]) for i in range(4)]

    f_evac = {}
    x1_wr = [0, 0]
    h2_rd = [0, 0]
    x_rd = [0, 0]
    QS.wait(tS, tS.n)
    QS.wait(vS, vS.n)
    QS.wait(aS, aS.n)
    for tt in range(8):
        buf = tt % 2
        QS.wait(vS, x_rd[buf])
        d_xq = QS.dma(dX[buf], xqst[:, buf:buf + 1, :], x_q[tt * 128:(tt + 1) * 128, :])
        QT.wait(vS, v_ctx)
        QT.wait(dW, d_wo)
        for ds in range(4):
            pb = ds
            QT.wait(vS, f_evac.get((tt - 1, ds), 0))
            for c in range(16):
                def mm(t, tt=tt, ds=ds, c=c, pb=pb):
                    return t.matmul(ps_f[pb][:, :],
                                    ctx[:, c:c + 1, tt * 128:(tt + 1) * 128],
                                    wo_sb[:, c:c + 1, ds * 512:(ds + 1) * 512],
                                    start=(c == 0), stop=(c == 15))
                QT.op(mm) if c == 15 else QT.op_noinc(mm)
            QV.wait(tS, tS.n)
            QV.wait(dX[buf], d_xq)
            QV.wait(dSt[buf], x1_wr[buf])
            f_evac[(tt, ds)] = QV.op(lambda v, tt=tt, ds=ds, pb=pb, buf=buf: v.tensor_add(
                x1st[:, buf:buf + 1, ds * 512:(ds + 1) * 512], ps_f[pb][:, :],
                xqst[:, buf:buf + 1, ds * 512:(ds + 1) * 512]))
        x_rd[buf] = f_evac[(tt, 3)]
        xa = x1st[:, buf:buf + 1, :]
        ha = h2st[:, buf:buf + 1, :]
        QS.wait(vS, f_evac[(tt, 3)])
        x1_wr[buf] = QS.dma(dSt[buf], x1_scr[tt * 128:(tt + 1) * 128, :], xa)
        QV.op(lambda v, xa=xa: v.reduce_sum(st1f[:, 0:1], xa, AX.X))
        QA.wait(vS, vS.n)
        QA.op(lambda sc, xa=xa: sc.activation(sqstf[:, :], xa, AF.Square, accum_out=st1f[:, 1:2]))
        QV.wait(aS, aS.n)
        QV.op(lambda v: v.tensor_scalar(st1f[:, 2:3], st1f[:, 0:1], 1.0 / DM, None, ALU.mult))
        QV.op(lambda v: v.tensor_scalar(st1f[:, 3:4], st1f[:, 1:2], 1.0 / DM, None, ALU.mult))
        QV.op(lambda v: v.tensor_mul(st1f[:, 4:5], st1f[:, 2:3], st1f[:, 2:3]))
        v_var = QV.op(lambda v: v.tensor_sub(st1f[:, 3:4], st1f[:, 3:4], st1f[:, 4:5]))
        QA.wait(vS, v_var)
        a_std = QA.op(lambda sc: sc.activation(st1f[:, 5:6], st1f[:, 3:4], AF.Sqrt, bias=epsln[:, 0:1]))
        QV.wait(aS, a_std)
        QV.op(lambda v: v.reciprocal(st1f[:, 6:7], st1f[:, 5:6]))
        v_nb = QV.op(lambda v: v.tensor_scalar(st1f[:, 7:8], st1f[:, 2:3], st1f[:, 6:7], -1.0,
                                               ALU.mult, ALU.mult))
        QA.wait(vS, v_nb)
        if h2_rd[buf]:
            QA.wait(dT[buf], h2_rd[buf])
        a_h = QA.op(lambda sc, ha=ha, xa=xa: sc.activation(ha, xa, AF.Identity,
                                                           bias=st1f[:, 7:8], scale=st1f[:, 6:7]))
        QS.wait(aS, a_h)
        h2_rd[buf] = QS.dma(dT[buf], h2T[:, :, tt * 128:(tt + 1) * 128], ha, transpose=True)
    d_h2T = (dT[0].n, dT[1].n)
    if debug:
        QS.wait(dT[0], d_h2T[0])
        QS.wait(dT[1], d_h2T[1])
        QS.wait(dSt[0], dSt[0].n)
        QS.wait(dSt[1], dSt[1].n)
        QS.dma(dDbg, dbg["h2T"], h2T[:, :, :])
        QS.dma(dDbg, dbg["x1"], x1_scr)
        QV.wait(dDbg, dDbg.n)
        QT.wait(dDbg, dDbg.n)
        QA.wait(dDbg, dDbg.n)
    al.free("ps_f3", "ps_f2", "ps_f1", "ps_f0")
    al.free("st1f", "sqstf", "h2st", "x1st", "xqst")
    al.free("wo_sb")
    al.free("ctx")

    sdbc = al.sb("sdbc", [128, DM], F32)
    nact = al.sb("nact", [1, NQ], BF16)
    x1r = al.sb("x1r", [128, 2, 512], F32)
    y1 = al.sb("y1", [128, 2, 512], F32)
    sdrow = al.sb("sdrow", [1, DM], F32)
    act_sb = al.sb("act_sb", [128, 44, NQ], BF16, side="right")
    slab = [al.sb("slab0", [128, 16, 512], BF16, side="right"),
            al.sb("slab1", [128, 16, 512], BF16, side="right")]
    gate_sb = al.sb("gate_sb", [128, 8, 512], BF16, side="right")
    dslab = [al.sb("dslab0", [128, 4, 512], BF16, side="right"),
             al.sb("dslab1", [128, 4, 512], BF16, side="right")]

    ps_g = [al.ps("ps_g0", [128, 512]), al.ps("ps_g1", [128, 512])]
    ps_u = [al.ps("ps_u0", [128, 512]), al.ps("ps_u1", [128, 512])]
    ps_as = al.ps("ps_as", [1, 1024])
    ps_sd = al.ps("ps_sd", [128, 512])

    d_sdr = QS.dma(dIn, sdrow[:, :], s_down_r)
    for j in range(4):
        QT.wait(dIn, d_sdr)
        QT.wait(vS, vS.n)
        t_b = QT.op(lambda t, j=j: t.matmul(ps_sd[:, :], ones_rf[0:1, 0:128],
                                            sdrow[0:1, j * 512:(j + 1) * 512],
                                            start=True, stop=True))
        QV.wait(tS, t_b)
        QV.op(lambda v, j=j: v.tensor_copy(sdbc[:, j * 512:(j + 1) * 512], ps_sd[:, :]))
    v_sdbc = vS.n
    QS.wait(vS, v_sdbc)
    al.free("sdrow")

    QS.wait(dSt[0], dSt[0].n)
    QS.wait(dSt[1], dSt[1].n)
    QS.wait(dT[0], d_h2T[0])
    QS.wait(dT[1], d_h2T[1])
    QV.wait(dT[0], d_h2T[0])
    QV.wait(dT[1], d_h2T[1])
    QA.wait(dT[0], d_h2T[0])
    QA.wait(dT[1], d_h2T[1])
    QV.wait(dSt[0], dSt[0].n)
    QV.wait(dSt[1], dSt[1].n)
    QA.wait(dSt[0], dSt[0].n)
    QA.wait(dSt[1], dSt[1].n)
    slab_user = [0, 0]
    silu_done = {}
    act_done = {}
    for sec in range(11):
        for which in range(2):
            l = 2 * sec + which
            sb_i = l % 2
            src = g_scr if which == 0 else u_scr
            QS.wait(dD2, d_d2d)
            QS.wait(tS, slab_user[sb_i])
            d_sl = QS.dma(dSl[sb_i], slab[sb_i][:, :, :],
                          src[sec * 512:(sec + 1) * 512, :], transpose=True)
            for hcl in range(4):
                hc = sec * 4 + hcl
                for nt in range(2):
                    pp = (ps_g if which == 0 else ps_u)[nt]
                    QT.wait(dSl[sb_i], d_sl)
                    QT.wait(dT[0], d_h2T[0])
                    QT.wait(dT[1], d_h2T[1])
                    if which == 0:
                        QT.wait(aS, silu_done.get((hc - 1, nt), 0))
                    else:
                        QT.wait(vS, act_done.get((hc - 1, nt), 0))
                    for c in range(16):
                        def mm(t, sb_i=sb_i, hcl=hcl, nt=nt, c=c, pp=pp):
                            return t.matmul(pp[:, :],
                                            slab[sb_i][:, c:c + 1, hcl * 128:(hcl + 1) * 128],
                                            h2T[:, c:c + 1, nt * 512:(nt + 1) * 512],
                                            start=(c == 0), stop=(c == 15))
                        t_mm = QT.op(mm) if c == 15 else QT.op_noinc(mm)
                    slab_user[sb_i] = tS.n
                    gi = hcl * 2 + nt
                    if which == 0:
                        QA.wait(tS, tS.n)
                        QA.wait(vS, act_done.get((hc - 4, nt), 0))
                        silu_done[(hc, nt)] = QA.op(lambda sc, hc=hc, gi=gi, pp=pp: sc.activation(
                            gate_sb[:, gi:gi + 1, :], pp[:, :], AF.Silu,
                            scale=sgate_c[:, hc:hc + 1]))
                    else:
                        QV.wait(tS, tS.n)
                        QV.wait(aS, silu_done[(hc, nt)])
                        a_d = QV.op(lambda v, hc=hc, nt=nt, gi=gi, pp=pp: v.scalar_tensor_tensor(
                            act_sb[:, hc:hc + 1, nt * 512:(nt + 1) * 512], pp[:, :],
                            sup_c[:, hc:hc + 1], gate_sb[:, gi:gi + 1, :],
                            ALU.mult, ALU.mult))
                        act_done[(hc, nt)] = a_d
                        QT.wait(vS, a_d)
                        def mm_as(t, hc=hc, nt=nt):
                            return t.matmul(ps_as[0:1, nt * 512:(nt + 1) * 512],
                                            ones_col[:, 0:1],
                                            act_sb[:, hc:hc + 1, nt * 512:(nt + 1) * 512],
                                            start=(hc == 0), stop=(hc == 43))
                        QT.op(mm_as) if hc == 43 else QT.op_noinc(mm_as)
    t_as = tS.n
    QA.wait(tS, t_as)
    a_nact = QA.op(lambda sc: sc.activation(nact[0:1, :], ps_as[0:1, :], AF.Copy,
                                            scale=-128.0))
    al.free("ps_sd", "ps_as", "ps_u1", "ps_u0", "ps_g1", "ps_g0")
    ps_d = [al.ps("ps_d%d" % i, [128, 512]) for i in range(8)]
    QT.wait(aS, aS.n)

    SECS = [(0, 4), (4, 4), (8, 4), (12, 4), (16, 4), (20, 4),
            (24, 4), (28, 4), (32, 4), (36, 4), (40, 4)]
    dslab_user = [0, 0]
    y_done = {}
    x1r_use = [0, 0]
    y1_use = [0, 0]
    QS.wait(dSt[0], dSt[0].n)
    QS.wait(dSt[1], dSt[1].n)
    for ds in range(4):
        for si, (h0, nch) in enumerate(SECS):
            sb_i = si % 2
            QS.wait(dD2, d_d2d)
            QS.wait(tS, dslab_user[sb_i])
            d_sl = QS.dma(dDl[sb_i], dslab[sb_i][:, 0:nch, :],
                          d_scr[ds * 512:(ds + 1) * 512, h0 * 128:(h0 + nch) * 128],
                          transpose=True)
            last = None
            for j in range(nch):
                hc = h0 + j
                for tt in range(8):
                    QT.wait(dDl[sb_i], d_sl)
                    if hc == 0:
                        QT.wait(vS, y_done.get((tt, ds - 1), 0))
                    def mm(t, sb_i=sb_i, j=j, hc=hc, tt=tt):
                        return t.matmul(ps_d[tt][:, :],
                                        act_sb[:, hc:hc + 1, tt * 128:(tt + 1) * 128],
                                        dslab[sb_i][:, j:j + 1, :],
                                        start=(hc == 0), stop=False)
                    if j == nch - 1 and tt == 7:
                        dslab_user[sb_i] = QT.op(mm)
                    else:
                        QT.op_noinc(mm)
        for tt in range(8):
            buf = tt % 2
            QT.wait(aS, a_nact)
            def mm_c(t, tt=tt):
                return t.matmul(ps_d[tt][:, :], nact[0:1, tt * 128:(tt + 1) * 128],
                                ones_rb[0:1, :], start=False, stop=True)
            t_c = QT.op(mm_c)
            QS.wait(vS, x1r_use[buf])
            d_x1 = QS.dma(dR[buf], x1r[:, buf:buf + 1, :],
                          x1_scr[tt * 128:(tt + 1) * 128, ds * 512:(ds + 1) * 512])
            QV.wait(tS, t_c)
            QV.wait(dR[buf], d_x1)
            QV.wait(vS, v_sdbc)
            QV.wait(dY[buf], y1_use[buf])
            QV.op(lambda v, ds=ds, buf=buf, tt=tt: v.tensor_mul(
                y1[:, buf:buf + 1, :], ps_d[tt][:, :],
                sdbc[:, ds * 512:(ds + 1) * 512]))
            y_done[(tt, ds)] = QV.op(lambda v, buf=buf: v.tensor_add(
                y1[:, buf:buf + 1, :], y1[:, buf:buf + 1, :], x1r[:, buf:buf + 1, :]))
            x1r_use[buf] = y_done[(tt, ds)]
            QS.wait(vS, y_done[(tt, ds)])
            y1_use[buf] = QS.dma(
                dY[buf], y_out[tt * 128:(tt + 1) * 128, ds * 512:(ds + 1) * 512],
                y1[:, buf:buf + 1, :])
    QS.wait(dY[0], dY[0].n)
    QS.wait(dY[1], dY[1].n)
    if debug:
        QS.wait(dDbg, dDbg.n)

    block_g = nc.Block()
    block = block_g.__enter__()

    def run_queue(q):
        def body(eng):
            for fn in q.ops:
                fn(eng)
        return body

    block.sync(run_queue(QS))
    block.tensor(run_queue(QT))
    block.vector(run_queue(QV))
    block.scalar(run_queue(QA))
    block.gpsimd(run_queue(QG))
    block_g.__exit__(None, None, None)

    for g in reversed(sem_guards):
        g.__exit__(None, None, None)
    return nc


_cache = {}


def kernel(**inputs):
    f = lambda k: np.ascontiguousarray(np.asarray(inputs[k], dtype=np.float32))
    x = f("L0_attn_x")
    w = {
        "w_dq": f("L0_attn_w_dq"),
        "w_uq": f("L0_attn_w_uq"),
        "w_dkv": f("L0_attn_w_dkv"),
        "w_kr": f("L0_attn_w_kr"),
        "w_uk": f("L0_attn_w_uk"),
        "w_uv": f("L0_attn_w_uv"),
        "w_o": f("L0_attn_w_o"),
        "g_q_t": np.ascontiguousarray(f("L0_attn_g_q").reshape(6, 128).T),
        "g_kv_t": np.ascontiguousarray(f("L0_attn_g_kv").reshape(4, 128).T),
        "s_gate_t": np.ascontiguousarray(f("L0_ffn_s_gate").reshape(44, 128).T),
        "s_up_t": np.ascontiguousarray(f("L0_ffn_s_up").reshape(44, 128).T),
        "s_down_r": np.ascontiguousarray(f("L0_ffn_s_down").reshape(1, DM)),
        "wq_gate": np.ascontiguousarray(np.asarray(inputs["L0_ffn_wq_gate"], dtype=np.int32)),
        "wq_up": np.ascontiguousarray(np.asarray(inputs["L0_ffn_wq_up"], dtype=np.int32)),
        "wq_down": np.ascontiguousarray(np.asarray(inputs["L0_ffn_wq_down"], dtype=np.int32)),
    }
    if "nc" not in _cache:
        _cache["nc"] = build_nc()
    nc = _cache["nc"]

    in_maps = []
    sels = []
    for core in range(8):
        b, half = core // 2, core % 2
        if half == 0:
            rows = np.r_[0:512, 1536:2048]
            g0, g1 = 0.0, 1536.0
        else:
            rows = np.r_[512:1536]
            g0, g1 = 512.0, 1024.0
        sels.append((b, rows))
        m = dict(w)
        m["x_full"] = x[b]
        m["x_q"] = np.ascontiguousarray(x[b][rows])
        m["pos_q"] = np.ascontiguousarray(rows.astype(np.float32).reshape(1, NQ))
        m["gmeta"] = np.array([[g0, g1]], dtype=np.float32)
        in_maps.append(m)

    res = run_bass_kernel_spmd(nc, in_maps, core_ids=list(range(8)))
    y = np.empty((4, S, DM), dtype=np.float32)
    for core in range(8):
        b, rows = sels[core]
        y[b][rows] = np.asarray(res.results[core]["y"], dtype=np.float32)
    return y


# version 10
# speedup vs baseline: 1.7659x; 1.0024x over previous
import math
import numpy as np
import ml_dtypes
import concourse.bass as bass
import concourse.mybir as mybir
from concourse.bass_utils import run_bass_kernel_spmd

F32 = mybir.dt.float32
BF16 = mybir.dt.bfloat16
I32 = mybir.dt.int32
AF = mybir.ActivationFunctionType
ALU = mybir.AluOpType
AX = mybir.AxisListType

S = 2048
DM = 2048
H = 16
NOPE = 128
RD = 64
VD = 128
QL = 768
KVL = 512
HID = 5632
NQ = 1024
HG = 4
NGRP = 4
SLOT_K = (8, 16)
SCALE = float(1.0 / math.sqrt(192.0))
LN_EPS = 1e-5
RMS_EPS = 1e-6
MAGIC = 8388608.0

C_HI = 6.28125
C_MID = float(np.float32(2 * np.pi - C_HI))
C_LO = float(np.float64(2 * np.pi) - np.float64(C_HI) - np.float64(np.float32(C_MID)))
INV2PI = float(1.0 / (2 * np.pi))
LOG1E4_32 = float(np.log(10000.0) / 32.0)


class Ctr:
    def __init__(self, handle):
        self.h = handle
        self.n = 0

    def inc(self, k=1):
        self.n += k
        return self.n


class EngQ:
    def __init__(self, ctr, chain):
        self.ops = []
        self.ctr = ctr
        self.chain = chain

    def wait(self, ctr, val):
        if val > 0:
            self.ops.append(lambda e, s=ctr.h, v=val: e.wait_ge(s, v))

    def op(self, fn):
        if self.chain and self.ctr.n > 0:
            self.ops.append(lambda e, s=self.ctr.h, v=self.ctr.n: e.wait_ge(s, v))
        val = self.ctr.inc()
        def run(e, fn=fn, s=self.ctr.h):
            fn(e).then_inc(s, 1)
        self.ops.append(run)
        return val

    def op_noinc(self, fn):
        self.ops.append(lambda e, fn=fn: fn(e))

    def dma(self, ctr, out, in_, transpose=False):
        val = ctr.inc(16)
        def run(e, out=out, in_=in_, s=ctr.h, tr=transpose):
            e.dma_start(out=out, in_=in_, transpose=tr).then_inc(s, 16)
        self.ops.append(run)
        return val


def build_nc(debug=False):
    nc = bass.Bass("TRN2", target_bir_lowering=False)

    x_full = nc.dram_tensor("x_full", [S, DM], F32, kind="ExternalInput").ap()
    x_q = nc.dram_tensor("x_q", [NQ, DM], F32, kind="ExternalInput").ap()
    pos_q = nc.dram_tensor("pos_q", [1, NQ], F32, kind="ExternalInput").ap()
    gmeta = nc.dram_tensor("gmeta", [1, 2], F32, kind="ExternalInput").ap()
    w_dq = nc.dram_tensor("w_dq", [DM, QL], F32, kind="ExternalInput").ap()
    g_q_t = nc.dram_tensor("g_q_t", [128, 6], F32, kind="ExternalInput").ap()
    w_uq = nc.dram_tensor("w_uq", [QL, H * 192], F32, kind="ExternalInput").ap()
    w_dkv = nc.dram_tensor("w_dkv", [DM, KVL], F32, kind="ExternalInput").ap()
    w_kr = nc.dram_tensor("w_kr", [DM, RD], F32, kind="ExternalInput").ap()
    g_kv_t = nc.dram_tensor("g_kv_t", [128, 4], F32, kind="ExternalInput").ap()
    w_uk = nc.dram_tensor("w_uk", [KVL, H * NOPE], F32, kind="ExternalInput").ap()
    w_uv = nc.dram_tensor("w_uv", [KVL, H * VD], F32, kind="ExternalInput").ap()
    w_o = nc.dram_tensor("w_o", [H * VD, DM], F32, kind="ExternalInput").ap()
    wq_gate = nc.dram_tensor("wq_gate", [HID, DM], I32, kind="ExternalInput").ap()
    s_gate_t = nc.dram_tensor("s_gate_t", [128, 44], F32, kind="ExternalInput").ap()
    wq_up = nc.dram_tensor("wq_up", [HID, DM], I32, kind="ExternalInput").ap()
    s_up_t = nc.dram_tensor("s_up_t", [128, 44], F32, kind="ExternalInput").ap()
    wq_down = nc.dram_tensor("wq_down", [DM, HID], I32, kind="ExternalInput").ap()
    s_down_r = nc.dram_tensor("s_down_r", [1, DM], F32, kind="ExternalInput").ap()
    y_out = nc.dram_tensor("y", [NQ, DM], F32, kind="ExternalOutput").ap()

    g_scr = nc.dram_tensor("g_scr", [HID, DM], BF16, kind="Internal").ap()
    u_scr = nc.dram_tensor("u_scr", [HID, DM], BF16, kind="Internal").ap()
    d_scr = nc.dram_tensor("d_scr", [DM, HID], BF16, kind="Internal").ap()
    x1_scr = nc.dram_tensor("x1_scr", [NQ, DM], F32, kind="Internal").ap()

    dbg = {}
    if debug:
        def dbg_t(name, shape, dt=F32):
            dbg[name] = nc.dram_tensor("dbg_" + name, shape, dt, kind="ExternalOutput").ap()
        dbg_t("cq", [128, 6, NQ], BF16)
        dbg_t("kvT", [128, 4, S], BF16)
        dbg_t("kpe", [65, S], BF16)
        dbg_t("qno", [128, HG, NQ], BF16)
        dbg_t("qpe", [65, HG * NQ], BF16)
        dbg_t("kno", [128, HG, S], BF16)
        dbg_t("vtok", [128, 16, HG * VD], BF16)
        dbg_t("ctx", [128, 16, NQ], BF16)
        dbg_t("x1", [NQ, DM], F32)
        dbg_t("h2T", [128, 16, NQ], BF16)
        dbg_t("masks", [128, 16, 512], BF16)
        dbg_t("cosq", [32, NQ], F32)
        dbg_t("kmaxn", [1, 16], F32)

    sem_guards = []
    def mksem(name):
        g = nc.semaphore(name)
        h = g.__enter__()
        sem_guards.append(g)
        return Ctr(h)

    dIn = mksem("dIn")
    dX = [mksem("dX0"), mksem("dX1")]
    dT = [mksem("dT0"), mksem("dT1")]
    dW = mksem("dW")
    dW0 = mksem("dW0")
    dD2 = mksem("dD2")
    dB = mksem("dB")
    dSt = [mksem("dSt0"), mksem("dSt1")]
    dSl = [mksem("dSl0"), mksem("dSl1")]
    dDl = [mksem("dDl0"), mksem("dDl1")]
    dR = [mksem("dR0"), mksem("dR1")]
    dY = [mksem("dY0"), mksem("dY1")]
    dDbg = mksem("dDbg")
    tS = mksem("tS")
    vS = mksem("vS")
    aS = mksem("aS")
    gS = mksem("gS")

    QT = EngQ(tS, chain=False)
    QV = EngQ(vS, chain=True)
    QA = EngQ(aS, chain=True)
    QG = EngQ(gS, chain=True)
    QS = EngQ(None, chain=False)

    class Alloc:
        def __init__(self):
            self.guards = {}
            self.order = {}
            self.k = 0

        def sb(self, name, shape, dt, side=None):
            g = nc.sbuf_tensor("%s__%d" % (name, self.k), shape, dt, side=side)
            t = g.__enter__()
            self.guards[name] = g
            self.order[name] = self.k
            self.k += 1
            return t

        def ps(self, name, shape):
            g = nc.psum_tensor("%s__%d" % (name, self.k), shape, F32)
            t = g.__enter__()
            self.guards[name] = g
            self.order[name] = self.k
            self.k += 1
            return t

        def free(self, *names):
            for name in sorted(names, key=lambda n: -self.order[n]):
                self.guards.pop(name).__exit__(None, None, None)
                self.order.pop(name)

    al = Alloc()

    ones_col = al.sb("ones_col", [128, 1], BF16)
    ones_cf = al.sb("ones_cf", [128, 1], F32)
    epsln = al.sb("epsln", [128, 1], F32)
    epsrms = al.sb("epsrms", [128, 1], F32)
    ones_mat = al.sb("ones_mat", [128, 128], BF16)
    ones_rf = al.sb("ones_rf", [1, 128], F32)
    ones_rb = al.sb("ones_rb", [1, 512], BF16)
    sgate_c = al.sb("sgate_c", [128, 44], F32)
    sup_c = al.sb("sup_c", [128, 44], F32)
    gq_c = al.sb("gq_c", [128, 6], F32)
    gkv_c = al.sb("gkv_c", [128, 4], F32)
    gmeta_sb = al.sb("gmeta_sb", [1, 2], F32)
    gcol = al.sb("gcol", [128, 2], F32)
    iota_pq = al.sb("iota_pq", [128, 512], F32)
    kmaxn = al.sb("kmaxn", [1, 16], F32)
    cosq = al.sb("cosq", [32, NQ], F32)
    sinq = al.sb("sinq", [32, NQ], F32)
    masks = al.sb("masks", [128, 16, 512], BF16)
    cq_sb = al.sb("cq_sb", [128, 6, NQ], BF16)
    kvT = al.sb("kvT", [128, 4, S], BF16)
    kpe_ext = al.sb("kpe_ext", [65, S], BF16)
    kpe_sq = al.sb("kpe_sq", [64, S], BF16)

    Wp = S + NQ
    cosk = al.sb("cosk", [32, S], F32, side="right")
    sink = al.sb("sink", [32, S], F32, side="right")
    posq_sb = al.sb("posq_sb", [1, NQ], F32, side="right")
    rt_ang = al.sb("rt_ang", [32, Wp], F32, side="right")
    rt_k = al.sb("rt_k", [32, Wp], F32, side="right")
    rt_t1 = al.sb("rt_t1", [32, Wp], F32, side="right")
    rt_t2 = al.sb("rt_t2", [32, Wp], F32, side="right")
    invf = al.sb("invf", [32, 1], F32, side="right")
    invfr = al.sb("invfr", [32, 1], F32, side="right")
    em = al.sb("em", [128, 2, 512], F32, side="right")

    for out_t, in_t in ((gmeta_sb[:, :], gmeta), (posq_sb[:, :], pos_q),
                        (sgate_c[:, :], s_gate_t), (sup_c[:, :], s_up_t),
                        (gq_c[:, :], g_q_t), (gkv_c[:, :], g_kv_t)):
        QS.dma(dIn, out_t, in_t)
    d_small = dIn.n

    QG.op(lambda g: g.iota(iota_pq[:, :], [[-1, 512]], base=0, channel_multiplier=1,
                           allow_small_or_imprecise_dtypes=True))
    QG.op(lambda g: g.iota(invfr[:, :], [[0, 1]], base=0, channel_multiplier=1,
                           allow_small_or_imprecise_dtypes=True))
    g_iota = QG.op(lambda g: g.iota(rt_ang[:, 0:S], [[1, S]], base=0, channel_multiplier=0,
                                    allow_small_or_imprecise_dtypes=True))

    QV.op(lambda v: v.memset(ones_col[:, :], 1.0))
    QV.op(lambda v: v.memset(ones_cf[:, :], 1.0))
    QV.op(lambda v: v.memset(epsln[:, :], LN_EPS))
    QV.op(lambda v: v.memset(epsrms[:, :], RMS_EPS))
    QV.op(lambda v: v.memset(kmaxn[:, :], 0.0))
    QV.op(lambda v: v.memset(ones_mat[:, :], 1.0))
    QV.op(lambda v: v.memset(ones_rb[:, :], 1.0))
    v_ones = QV.op(lambda v: v.memset(ones_rf[:, :], 1.0))

    QA.wait(gS, g_iota)
    a_invf = QA.op(lambda sc: sc.activation(invf[:, :], invfr[:, :], AF.Exp, scale=-LOG1E4_32))

    ps_b32 = al.ps("ps_b32", [32, 512])
    ps_b128 = al.ps("ps_b128", [128, 2])
    QT.wait(dIn, d_small)
    QT.wait(vS, v_ones)
    for i in range(NQ // 512):
        QT.wait(vS, vS.n)
        t_b = QT.op(lambda t, i=i: t.matmul(ps_b32[:, :], ones_rf[0:1, 0:32],
                                            posq_sb[0:1, i * 512:(i + 1) * 512],
                                            start=True, stop=True))
        QV.wait(tS, t_b)
        QV.wait(gS, g_iota)
        QV.op(lambda v, i=i: v.tensor_copy(rt_ang[:, S + i * 512:S + (i + 1) * 512],
                                           ps_b32[:, :]))
    t_g = QT.op(lambda t: t.matmul(ps_b128[:, :], ones_rf[0:1, 0:128], gmeta_sb[0:1, 0:2],
                                   start=True, stop=True))
    QV.wait(tS, t_g)
    QV.op(lambda v: v.tensor_copy(gcol[:, :], ps_b128[:, :]))

    QV.wait(aS, a_invf)
    QV.op(lambda v: v.tensor_scalar(rt_ang[:, :], rt_ang[:, :], invf[:, :], None, ALU.mult))
    QV.op(lambda v: v.tensor_scalar(rt_k[:, :], rt_ang[:, :], INV2PI, None, ALU.mult))
    QV.op(lambda v: v.tensor_scalar(rt_k[:, :], rt_k[:, :], MAGIC, None, ALU.add))
    QV.op(lambda v: v.tensor_scalar(rt_k[:, :], rt_k[:, :], -MAGIC, None, ALU.add))
    QV.op(lambda v: v.tensor_scalar(rt_t1[:, :], rt_k[:, :], C_HI, None, ALU.mult))
    QV.op(lambda v: v.tensor_sub(rt_ang[:, :], rt_ang[:, :], rt_t1[:, :]))
    QV.op(lambda v: v.tensor_scalar(rt_t1[:, :], rt_k[:, :], C_MID, None, ALU.mult))
    QV.op(lambda v: v.tensor_sub(rt_ang[:, :], rt_ang[:, :], rt_t1[:, :]))
    QV.op(lambda v: v.tensor_scalar(rt_t1[:, :], rt_k[:, :], C_LO, None, ALU.mult))
    v_red = QV.op(lambda v: v.tensor_sub(rt_ang[:, :], rt_ang[:, :], rt_t1[:, :]))
    QA.wait(vS, v_red)
    a_s4 = QA.op(lambda sc: sc.activation(rt_k[:, :], rt_ang[:, :], AF.Sin, scale=0.25))
    QV.wait(aS, a_s4)
    v_t4 = QV.op(lambda v: v.tensor_mul(rt_t1[:, :], rt_k[:, :], rt_k[:, :]))
    QA.wait(vS, v_t4)
    a_c4 = QA.op(lambda sc: sc.activation(rt_t2[:, :], rt_t1[:, :], AF.Sqrt, bias=1.0, scale=-1.0))
    QV.wait(aS, a_c4)
    QV.op(lambda v: v.tensor_mul(rt_k[:, :], rt_k[:, :], rt_t2[:, :]))
    QV.op(lambda v: v.tensor_scalar(rt_k[:, :], rt_k[:, :], 2.0, None, ALU.mult))
    QV.op(lambda v: v.tensor_scalar(rt_t2[:, :], rt_t1[:, :], -2.0, 1.0, ALU.mult, ALU.add))
    QV.op(lambda v: v.tensor_mul(rt_t1[:, :], rt_k[:, :], rt_t2[:, :]))
    QV.op(lambda v: v.tensor_scalar(rt_t1[:, :], rt_t1[:, :], 2.0, None, ALU.mult))
    QV.op(lambda v: v.tensor_copy(sink[:, :], rt_t1[:, 0:S]))
    QV.op(lambda v: v.tensor_copy(sinq[:, :], rt_t1[:, S:Wp]))
    QV.op(lambda v: v.tensor_mul(rt_t1[:, :], rt_k[:, :], rt_k[:, :]))
    QV.op(lambda v: v.tensor_scalar(rt_t1[:, :], rt_t1[:, :], -2.0, 1.0, ALU.mult, ALU.add))
    QV.op(lambda v: v.tensor_copy(cosk[:, :], rt_t1[:, 0:S]))
    v_trig = QV.op(lambda v: v.tensor_copy(cosq[:, :], rt_t1[:, S:Wp]))
    if debug:
        QS.wait(vS, v_trig)
        QS.dma(dDbg, dbg["cosq"], cosq[:, :])

    QV.op(lambda v: v.tensor_scalar(em[:, 0:1, :], iota_pq[:, :], gcol[:, 0:1], None, ALU.subtract))
    QV.op(lambda v: v.tensor_scalar(em[:, 1:2, :], iota_pq[:, :], gcol[:, 1:2], None, ALU.subtract))
    for s in range(2):
        for j in range(8):
            kc = j if s == 0 else 8 + j
            QV.op(lambda v, s=s, j=j, kc=kc: v.tensor_scalar(
                masks[:, s * 8 + j:s * 8 + j + 1, :], em[:, s:s + 1, :],
                float(-128.0 * kc), None, ALU.is_le))
    v_masks = vS.n
    if debug:
        QS.wait(vS, v_masks)
        QS.dma(dDbg, dbg["masks"], masks[:, :, :])
    al.free("em", "invfr", "invf", "rt_t2", "rt_t1", "rt_k", "rt_ang", "posq_sb")
    al.free("ps_b32", "ps_b128")

    xst = al.sb("xst", [128, 2, DM], F32, side="right")
    hst = al.sb("hst", [128, 2, DM], BF16, side="right")
    sqst = al.sb("sqst", [128, DM], F32, side="right")
    st1 = al.sb("st1", [128, 8], F32, side="right")
    hwin = al.sb("hwin", [128, 16, 512], BF16, side="right")
    rka = al.sb("rka", [32, 512], F32, side="right")
    rkb = al.sb("rkb", [32, 512], F32, side="right")
    rkbf = al.sb("rkbf", [32, 512], BF16, side="right")
    sqw = al.sb("sqw", [128, 512], F32, side="right")
    rrow = al.sb("rrow", [1, 512], F32, side="right")
    rbc = al.sb("rbc", [128, 512], F32, side="right")

    x_last_rd = [0, 0]
    h_last_rd = [0, 0]

    def ln_tile(src_rows, dst_col, buf):
        xa = xst[:, buf:buf + 1, :]
        ha = hst[:, buf:buf + 1, :]
        if x_last_rd[buf]:
            QS.wait(aS, x_last_rd[buf])
        d_x = QS.dma(dX[buf], xa, src_rows)
        QV.wait(dX[buf], d_x)
        QV.op(lambda v: v.reduce_sum(st1[:, 0:1], xa, AX.X))
        QA.wait(dX[buf], d_x)
        QA.op(lambda sc: sc.activation(sqst[:, :], xa, AF.Square, accum_out=st1[:, 1:2]))
        QV.wait(aS, aS.n)
        QV.op(lambda v: v.tensor_scalar(st1[:, 2:3], st1[:, 0:1], 1.0 / DM, None, ALU.mult))
        QV.op(lambda v: v.tensor_scalar(st1[:, 3:4], st1[:, 1:2], 1.0 / DM, None, ALU.mult))
        QV.op(lambda v: v.tensor_mul(st1[:, 4:5], st1[:, 2:3], st1[:, 2:3]))
        v_var = QV.op(lambda v: v.tensor_sub(st1[:, 3:4], st1[:, 3:4], st1[:, 4:5]))
        QA.wait(vS, v_var)
        a_std = QA.op(lambda sc: sc.activation(st1[:, 5:6], st1[:, 3:4], AF.Sqrt, bias=epsln[:, 0:1]))
        QV.wait(aS, a_std)
        QV.op(lambda v: v.reciprocal(st1[:, 6:7], st1[:, 5:6]))
        v_nb = QV.op(lambda v: v.tensor_scalar(st1[:, 7:8], st1[:, 2:3], st1[:, 6:7], -1.0,
                                               ALU.mult, ALU.mult))
        QA.wait(vS, v_nb)
        if h_last_rd[buf]:
            QA.wait(dT[buf], h_last_rd[buf])
        a_h = QA.op(lambda sc: sc.activation(ha, xa, AF.Identity, bias=st1[:, 7:8], scale=st1[:, 6:7]))
        x_last_rd[buf] = a_h
        QS.wait(aS, a_h)
        d_tr = QS.dma(dT[buf], hwin[:, :, dst_col:dst_col + 128], ha, transpose=True)
        h_last_rd[buf] = d_tr
        return d_tr

    wdkv_sb = al.sb("wdkv_sb", [128, 16, KVL], BF16, side="right")
    wkr_sb = al.sb("wkr_sb", [128, 16, RD], BF16, side="right")
    QG.wait(vS, vS.n)
    QG.wait(aS, aS.n)
    QG.dma(dW, wdkv_sb[:, :, :], w_dkv.rearrange("(c p) f -> p c f", p=128))
    d_wkr = QG.dma(dW, wkr_sb[:, :, :], w_kr.rearrange("(c p) f -> p c f", p=128))
    QS.wait(vS, vS.n)

    ps_r = al.ps("ps_r", [1, 512])
    ps_bw = al.ps("ps_bw", [128, 512])
    ps_c = [al.ps("ps_c%d" % i, [128, 512]) for i in range(4)]
    ps_kp1 = al.ps("ps_kp1", [32, 512])
    ps_kp2 = al.ps("ps_kp2", [32, 512])

    hwin_rd = 0
    for wdx in range(4):
        wc = wdx * 512
        QS.wait(tS, hwin_rd)
        for j in range(4):
            d_hw = ln_tile(x_full[(4 * wdx + j) * 128:(4 * wdx + j + 1) * 128, :],
                           j * 128, j % 2)
        QT.wait(dT[0], dT[0].n)
        QT.wait(dT[1], dT[1].n)
        QT.wait(dW, d_wkr)
        QT.wait(vS, vS.n)
        for m in range(4):
            for c in range(16):
                def mm(t, m=m, c=c):
                    return t.matmul(ps_c[m][:, :],
                                    wdkv_sb[:, c:c + 1, m * 128:(m + 1) * 128],
                                    hwin[:, c:c + 1, :],
                                    start=(c == 0), stop=(c == 15))
                QT.op(mm) if c == 15 else QT.op_noinc(mm)
        for c in range(16):
            def mm1(t, c=c):
                return t.matmul(ps_kp1[:, :], wkr_sb[:, c:c + 1, 0:32],
                                hwin[:, c:c + 1, :],
                                start=(c == 0), stop=(c == 15))
            def mm2(t, c=c):
                return t.matmul(ps_kp2[:, :], wkr_sb[:, c:c + 1, 32:64],
                                hwin[:, c:c + 1, :],
                                start=(c == 0), stop=(c == 15))
            QT.op_noinc(mm1)
            QT.op(mm2) if c == 15 else QT.op_noinc(mm2)
        hwin_rd = tS.n
        QA.wait(tS, hwin_rd)
        for m in range(4):
            a_s = QA.op(lambda sc, m=m: sc.activation(sqw[:, :], ps_c[m][:, :], AF.Square))
            QT.wait(aS, a_s)
            def mm(t, m=m):
                return t.matmul(ps_r[0:1, :], ones_cf[:, 0:1], sqw[:, :],
                                start=(m == 0), stop=(m == 3))
            t_sq = QT.op(mm)
            QA.wait(tS, t_sq)
        QA.wait(tS, tS.n)
        a_r = QA.op(lambda sc: sc.activation(rrow[0:1, :], ps_r[0:1, :], AF.Sqrt,
                                             scale=1.0 / KVL, bias=epsrms[0:1, 0:1]))
        QV.wait(aS, a_r)
        QV.op(lambda v: v.reciprocal(rrow[0:1, :], rrow[0:1, :]))
        QT.wait(vS, vS.n)
        t_b = QT.op(lambda t: t.matmul(ps_bw[:, :], ones_rf[0:1, 0:128], rrow[0:1, :],
                                       start=True, stop=True))
        QV.wait(tS, t_b)
        QV.op(lambda v: v.tensor_copy(rbc[:, :], ps_bw[:, :]))
        for m in range(4):
            QV.op(lambda v, m=m, wc=wc: v.scalar_tensor_tensor(
                kvT[:, m:m + 1, wc:wc + 512], ps_c[m][:, :],
                gkv_c[:, m:m + 1], rbc[:, :], ALU.mult, ALU.mult))
        csl = slice(wc, wc + 512)
        QV.op(lambda v, csl=csl: v.tensor_mul(rka[0:32, :], ps_kp1[:, :], cosk[:, csl]))
        QV.op(lambda v, csl=csl: v.tensor_mul(rkb[0:32, :], ps_kp2[:, :], sink[:, csl]))
        QV.op(lambda v: v.tensor_sub(rka[0:32, :], rka[0:32, :], rkb[0:32, :]))
        QV.op(lambda v, csl=csl: v.tensor_copy(kpe_ext[0:32, csl], rka[0:32, :]))
        QV.op(lambda v, csl=csl: v.tensor_mul(rka[0:32, :], ps_kp1[:, :], sink[:, csl]))
        QV.op(lambda v, csl=csl: v.tensor_mul(rkb[0:32, :], ps_kp2[:, :], cosk[:, csl]))
        QV.op(lambda v: v.tensor_add(rka[0:32, :], rka[0:32, :], rkb[0:32, :]))
        QV.wait(dB, dB.n)
        v_k2 = QV.op(lambda v: v.tensor_copy(rkbf[0:32, :], rka[0:32, :]))
        QS.wait(vS, v_k2)
        QS.dma(dB, kpe_ext[32:64, csl], rkbf[0:32, :])
    QV.wait(dB, dB.n)
    QV.op(lambda v: v.memset(kpe_ext[64:65, :], 1.0))
    v_kv = vS.n
    QA.wait(vS, v_kv)
    a_kpsq = QA.op(lambda sc: sc.activation(kpe_sq[0:64, :], kpe_ext[0:64, :], AF.Square))
    if debug:
        QS.wait(vS, v_kv)
        QS.dma(dDbg, dbg["kvT"], kvT[:, :, :])
        QS.dma(dDbg, dbg["kpe"], kpe_ext[:, :])
        QV.wait(dDbg, dDbg.n)
        QT.wait(dDbg, dDbg.n)
        QA.wait(dDbg, dDbg.n)
    al.free("ps_kp2", "ps_kp1", "ps_c3", "ps_c2", "ps_c1", "ps_c0")
    al.free("wkr_sb", "wdkv_sb")

    wdq_sb = al.sb("wdq_sb", [128, 16, QL], BF16, side="right")
    QG.wait(tS, tS.n)
    d_wdq = QG.dma(dW, wdq_sb[:, :, :], w_dq.rearrange("(c p) f -> p c f", p=128))
    ps_c2 = [al.ps("ps_c2%d" % i, [128, 512]) for i in range(6)]

    for wdx in range(2):
        wc = wdx * 512
        QS.wait(tS, hwin_rd)
        for j in range(4):
            d_hw = ln_tile(x_q[(4 * wdx + j) * 128:(4 * wdx + j + 1) * 128, :],
                           j * 128, j % 2)
        QT.wait(dT[0], dT[0].n)
        QT.wait(dT[1], dT[1].n)
        QT.wait(dW, d_wdq)
        QT.wait(vS, vS.n)
        for m in range(6):
            for c in range(16):
                def mm(t, m=m, c=c):
                    return t.matmul(ps_c2[m][:, :],
                                    wdq_sb[:, c:c + 1, m * 128:(m + 1) * 128],
                                    hwin[:, c:c + 1, :],
                                    start=(c == 0), stop=(c == 15))
                QT.op(mm) if c == 15 else QT.op_noinc(mm)
        hwin_rd = tS.n
        QA.wait(tS, hwin_rd)
        for m in range(6):
            a_s = QA.op(lambda sc, m=m: sc.activation(sqw[:, :], ps_c2[m][:, :], AF.Square))
            QT.wait(aS, a_s)
            def mm(t, m=m):
                return t.matmul(ps_r[0:1, :], ones_cf[:, 0:1], sqw[:, :],
                                start=(m == 0), stop=(m == 5))
            t_sq = QT.op(mm)
            QA.wait(tS, t_sq)
        QA.wait(tS, tS.n)
        a_r = QA.op(lambda sc: sc.activation(rrow[0:1, :], ps_r[0:1, :], AF.Sqrt,
                                             scale=1.0 / QL, bias=epsrms[0:1, 0:1]))
        QV.wait(aS, a_r)
        QV.op(lambda v: v.reciprocal(rrow[0:1, :], rrow[0:1, :]))
        QT.wait(vS, vS.n)
        t_b = QT.op(lambda t: t.matmul(ps_bw[:, :], ones_rf[0:1, 0:128], rrow[0:1, :],
                                       start=True, stop=True))
        QV.wait(tS, t_b)
        QV.op(lambda v: v.tensor_copy(rbc[:, :], ps_bw[:, :]))
        for m in range(6):
            QV.op(lambda v, m=m, wc=wc: v.scalar_tensor_tensor(
                cq_sb[:, m:m + 1, wc:wc + 512], ps_c2[m][:, :],
                gq_c[:, m:m + 1], rbc[:, :], ALU.mult, ALU.mult))
    v_cq = vS.n
    if debug:
        QS.wait(vS, v_cq)
        QS.dma(dDbg, dbg["cq"], cq_sb[:, :, :])
        QV.wait(dDbg, dDbg.n)
        QT.wait(dDbg, dDbg.n)
        QA.wait(dDbg, dDbg.n)
    al.free("ps_c25", "ps_c24", "ps_c23", "ps_c22", "ps_c21", "ps_c20", "ps_bw", "ps_r")
    al.free("wdq_sb")
    al.free("rbc", "rrow", "sqw", "rkbf", "rkb", "rka", "hwin", "st1", "sqst", "hst", "xst")
    al.free("sink", "cosk")

    wuq0_sb = al.sb("wuq0_sb", [128, 6, HG * 192], BF16)
    wuk0_sb = al.sb("wuk0_sb", [128, 4, HG * NOPE], BF16)
    wuv0_sb = al.sb("wuv0_sb", [128, 4, HG * VD], BF16)
    QG.wait(dX[0], dX[0].n)
    QG.wait(dX[1], dX[1].n)
    QG.dma(dW0, wuq0_sb[:, :, :],
           w_uq[:, 0:HG * 192].rearrange("(c p) f -> p c f", p=128))
    QG.dma(dW0, wuk0_sb[:, :, :],
           w_uk[:, 0:HG * NOPE].rearrange("(c p) f -> p c f", p=128))
    d_wg0 = QG.dma(dW0, wuv0_sb[:, :, :],
                   w_uv[:, 0:HG * VD].rearrange("(c p) f -> p c f", p=128))

    for j in range(4):
        QG.dma(dD2, g_scr[j * 1408:(j + 1) * 1408, :], wq_gate[j * 1408:(j + 1) * 1408, :])
    for j in range(4):
        QG.dma(dD2, u_scr[j * 1408:(j + 1) * 1408, :], wq_up[j * 1408:(j + 1) * 1408, :])
    for j in range(4):
        QG.dma(dD2, d_scr[j * 512:(j + 1) * 512, :], wq_down[j * 512:(j + 1) * 512, :])
    d_d2d = dD2.n

    q_no = al.sb("q_no", [128, HG, NQ], BF16)
    qpe_ext = al.sb("qpe_ext", [65, HG * NQ], BF16)
    kno = al.sb("kno", [128, HG, S], BF16)
    vtok = al.sb("vtok", [128, 16, HG * VD], BF16)
    sqs = al.sb("sqs", [128, S], BF16)
    qn_row = al.sb("qn_row", [1, NQ], F32)
    bnd_row = al.sb("bnd_row", [1, NQ], BF16)
    kmx = al.sb("kmx", [1, 8], F32)
    p_sb = al.sb("p_sb", [128, 3, 512], BF16)
    rec_sb = al.sb("rec_sb", [128, 512], F32)
    rpq1 = al.sb("rpq1", [32, NQ], F32)
    rpq2 = al.sb("rpq2", [32, NQ], F32)
    rpqbf = al.sb("rpqbf", [32, NQ], BF16)
    ctx = al.sb("ctx", [128, 16, NQ], BF16, side="right")

    ctx_evac = {}
    last_exp_bank = [0, 0]
    last_av_pbuf = [0, 0, 0]
    d_wo = 0
    wo_sb = None

    for grp in range(NGRP):
        if grp == 0:
            wuq_sb, wuk_sb, wuv_sb = wuq0_sb, wuk0_sb, wuv0_sb
            d_wg, s_wg = d_wg0, dW0
        else:
            wuq_sb = al.sb("wuq_sb", [128, 6, HG * 192], BF16, side="right")
            wuk_sb = al.sb("wuk_sb", [128, 4, HG * NOPE], BF16, side="right")
            wuv_sb = al.sb("wuv_sb", [128, 4, HG * VD], BF16, side="right")
            QG.wait(tS, t_dend)
            QG.dma(dW, wuq_sb[:, :, :],
                   w_uq[:, grp * HG * 192:(grp + 1) * HG * 192]
                   .rearrange("(c p) f -> p c f", p=128))
            QG.dma(dW, wuk_sb[:, :, :],
                   w_uk[:, grp * HG * NOPE:(grp + 1) * HG * NOPE]
                   .rearrange("(c p) f -> p c f", p=128))
            d_wg = QG.dma(dW, wuv_sb[:, :, :],
                          w_uv[:, grp * HG * VD:(grp + 1) * HG * VD]
                          .rearrange("(c p) f -> p c f", p=128))
            s_wg = dW

        ps_a = [al.ps("ps_a0", [128, 512]), al.ps("ps_a1", [128, 512])]
        ps_p1 = al.ps("ps_p1", [32, 1024])
        ps_p2 = al.ps("ps_p2", [32, 1024])

        QT.wait(s_wg, d_wg)
        QT.wait(vS, vS.n)
        QT.wait(aS, aS.n)
        evac_done = [vS.n, vS.n]
        for hl in range(HG):
            for nt in range(2):
                pb = (hl * 2 + nt) % 2
                QT.wait(vS, evac_done[pb])
                for c in range(6):
                    def mm(t, hl=hl, nt=nt, c=c, pa=ps_a[pb], w=wuq_sb):
                        return t.matmul(pa[:, :],
                                        w[:, c:c + 1, hl * 192:hl * 192 + 128],
                                        cq_sb[:, c:c + 1, nt * 512:(nt + 1) * 512],
                                        start=(c == 0), stop=(c == 5))
                    QT.op(mm) if c == 5 else QT.op_noinc(mm)
                QV.wait(tS, tS.n)
                evac_done[pb] = QV.op(lambda v, hl=hl, nt=nt, pa=ps_a[pb]: v.tensor_copy(
                    q_no[:, hl:hl + 1, nt * 512:(nt + 1) * 512], pa[:, :]))

        for hl in range(HG):
            co = hl * NQ
            QT.wait(vS, vS.n)
            for c in range(6):
                for nt in range(2):
                    def mm1(t, hl=hl, c=c, nt=nt, pp=ps_p1, w=wuq_sb):
                        return t.matmul(pp[:, nt * 512:(nt + 1) * 512],
                                        w[:, c:c + 1,
                                          hl * 192 + 128:hl * 192 + 160],
                                        cq_sb[:, c:c + 1, nt * 512:(nt + 1) * 512],
                                        start=(c == 0), stop=(c == 5))
                    def mm2(t, hl=hl, c=c, nt=nt, pp=ps_p2, w=wuq_sb):
                        return t.matmul(pp[:, nt * 512:(nt + 1) * 512],
                                        w[:, c:c + 1,
                                          hl * 192 + 160:hl * 192 + 192],
                                        cq_sb[:, c:c + 1, nt * 512:(nt + 1) * 512],
                                        start=(c == 0), stop=(c == 5))
                    QT.op_noinc(mm1)
                    QT.op(mm2) if (c == 5 and nt == 1) else QT.op_noinc(mm2)
            t_pe = tS.n
            QV.wait(tS, t_pe)
            QV.op(lambda v, pp=ps_p1: v.tensor_mul(rpq1[:, :], pp[:, :], cosq[:, :]))
            QV.op(lambda v, pp=ps_p2: v.tensor_mul(rpq2[:, :], pp[:, :], sinq[:, :]))
            QV.op(lambda v: v.tensor_sub(rpq1[:, :], rpq1[:, :], rpq2[:, :]))
            QV.op(lambda v, co=co: v.tensor_copy(qpe_ext[0:32, co:co + NQ], rpq1[:, :]))
            QV.op(lambda v, pp=ps_p1: v.tensor_mul(rpq1[:, :], pp[:, :], sinq[:, :]))
            QV.op(lambda v, pp=ps_p2: v.tensor_mul(rpq2[:, :], pp[:, :], cosq[:, :]))
            QV.op(lambda v: v.tensor_add(rpq1[:, :], rpq1[:, :], rpq2[:, :]))
            QV.wait(dB, dB.n)
            v_q2 = QV.op(lambda v: v.tensor_copy(rpqbf[:, :], rpq1[:, :]))
            QS.wait(vS, v_q2)
            QS.dma(dB, qpe_ext[32:64, co:co + NQ], rpqbf[:, :])

        QT.wait(vS, v_kv)
        for hl in range(HG):
            for nt in range(4):
                pb = (hl * 4 + nt) % 2
                QT.wait(vS, evac_done[pb])
                for c in range(4):
                    def mm(t, hl=hl, nt=nt, c=c, pa=ps_a[pb], w=wuk_sb):
                        return t.matmul(pa[:, :],
                                        w[:, c:c + 1, hl * 128:(hl + 1) * 128],
                                        kvT[:, c:c + 1, nt * 512:(nt + 1) * 512],
                                        start=(c == 0), stop=(c == 3))
                    QT.op(mm) if c == 3 else QT.op_noinc(mm)
                QV.wait(tS, tS.n)
                evac_done[pb] = QV.op(lambda v, hl=hl, nt=nt, pa=ps_a[pb]: v.tensor_copy(
                    kno[:, hl:hl + 1, nt * 512:(nt + 1) * 512], pa[:, :]))

        for kt in range(16):
            pb = kt % 2
            QT.wait(vS, evac_done[pb])
            for c in range(4):
                def mm(t, kt=kt, c=c, pa=ps_a[pb], w=wuv_sb):
                    return t.matmul(pa[:, :],
                                    kvT[:, c:c + 1, kt * 128:(kt + 1) * 128],
                                    w[:, c:c + 1, 0:HG * VD],
                                    start=(c == 0), stop=(c == 3))
                QT.op(mm) if c == 3 else QT.op_noinc(mm)
            QV.wait(tS, tS.n)
            evac_done[pb] = QV.op(lambda v, kt=kt, pa=ps_a[pb]: v.tensor_copy(
                vtok[:, kt:kt + 1, :], pa[:, :]))
        t_dend = tS.n

        last_red = 0
        for hl in range(HG):
            QA.wait(vS, vS.n)
            QA.wait(tS, tS.n)
            a_sq = QA.op(lambda sc, hl=hl: sc.activation(sqs[:, :], kno[:, hl:hl + 1, :],
                                                         AF.Square))
            for half in range(2):
                QT.wait(aS, a_sq)
                QT.wait(aS, a_kpsq)
                QT.wait(vS, last_red)
                for j in range(2):
                    sl = slice(half * 1024 + j * 512, half * 1024 + (j + 1) * 512)
                    def mma(t, j=j, sl=sl, pp=ps_p1):
                        return t.matmul(pp[0:1, j * 512:(j + 1) * 512],
                                        ones_col[:, 0:1], sqs[:, sl],
                                        start=True, stop=False)
                    def mmb(t, j=j, sl=sl, pp=ps_p1):
                        return t.matmul(pp[0:1, j * 512:(j + 1) * 512],
                                        ones_col[0:64, 0:1], kpe_sq[0:64, sl],
                                        start=False, stop=True)
                    QT.op_noinc(mma)
                    QT.op(mmb)
                QV.wait(tS, tS.n)
                last_red = QV.op(lambda v, half=half, pp=ps_p1: v.reduce_max(
                    kmx[0:1, half:half + 1], pp[0:1, 0:1024], AX.X))
            QV.op(lambda v: v.tensor_tensor(kmx[0:1, 2:3], kmx[0:1, 0:1], kmx[0:1, 1:2],
                                            ALU.max))
            v_kmx = vS.n
            QA.wait(vS, v_kmx)
            QA.op(lambda sc, hl=hl, grp=grp: sc.activation(
                kmaxn[0:1, grp * HG + hl:grp * HG + hl + 1], kmx[0:1, 2:3],
                AF.Sqrt, scale=1.0404))

        a_qn_prev = 0
        for hl in range(HG):
            QA.wait(tS, tS.n)
            QA.wait(vS, vS.n)
            a_q1 = QA.op(lambda sc, hl=hl: sc.activation(sqs[:, 0:NQ],
                                                         q_no[:, hl:hl + 1, :], AF.Square))
            QT.wait(aS, a_q1)
            QT.wait(aS, a_qn_prev)
            for j in range(2):
                def mmq(t, j=j, pp=ps_p1):
                    return t.matmul(pp[0:1, j * 512:(j + 1) * 512], ones_col[:, 0:1],
                                    sqs[:, j * 512:(j + 1) * 512], start=True, stop=False)
                QT.op(mmq) if j == 1 else QT.op_noinc(mmq)
            QA.wait(tS, tS.n)
            a_q2 = QA.op(lambda sc, hl=hl: sc.activation(
                sqs[0:64, 0:NQ], qpe_ext[0:64, hl * NQ:(hl + 1) * NQ], AF.Square))
            QT.wait(aS, a_q2)
            for j in range(2):
                def mmq2(t, j=j, pp=ps_p1):
                    return t.matmul(pp[0:1, j * 512:(j + 1) * 512], ones_col[0:64, 0:1],
                                    sqs[0:64, j * 512:(j + 1) * 512], start=False, stop=True)
                QT.op(mmq2) if j == 1 else QT.op_noinc(mmq2)
            QA.wait(tS, tS.n)
            a_qn_prev = QA.op(lambda sc, pp=ps_p1: sc.activation(qn_row[0:1, :], pp[0:1, 0:1024],
                                                                 AF.Sqrt))
            QV.wait(aS, a_qn_prev)
            QV.wait(dB, dB.n)
            QV.op(lambda v, hl=hl, grp=grp: v.tensor_scalar(
                bnd_row[0:1, :], qn_row[0:1, :],
                kmaxn[0:1, grp * HG + hl:grp * HG + hl + 1], None, ALU.mult))
            v_bnd = QV.op(lambda v: v.tensor_scalar(bnd_row[0:1, :], bnd_row[0:1, :],
                                                    -1.0, None, ALU.mult))
            QS.wait(vS, v_bnd)
            QS.dma(dB, qpe_ext[64:65, hl * NQ:(hl + 1) * NQ], bnd_row[0:1, :])

        if debug and grp == 0:
            QS.wait(vS, vS.n)
            QS.wait(aS, aS.n)
            QS.wait(dB, dB.n)
            QS.dma(dDbg, dbg["qno"], q_no[:, :, :])
            QS.dma(dDbg, dbg["qpe"], qpe_ext[:, :])
            QS.dma(dDbg, dbg["kno"], kno[:, :, :])
            QS.dma(dDbg, dbg["vtok"], vtok[:, :, :])
            QS.dma(dDbg, dbg["kmaxn"], kmaxn[:, :])
            QV.wait(dDbg, dDbg.n)
            QT.wait(dDbg, dDbg.n)
            QA.wait(dDbg, dDbg.n)

        al.free("ps_p2", "ps_p1", "ps_a1", "ps_a0")
        if grp != 0:
            al.free("wuv_sb", "wuk_sb", "wuq_sb")

        QT.wait(dB, dB.n)
        ps_sT = [al.ps("ps_sT0", [128, 512]), al.ps("ps_sT1", [128, 512])]
        ps_cx = [al.ps("ps_cx0", [128, 512]), al.ps("ps_cx1", [128, 512])]
        ps_sm = [al.ps("ps_sm0", [128, 512]), al.ps("ps_sm1", [128, 512])]

        QT.wait(vS, vS.n)
        QT.wait(aS, aS.n)
        for sl_i in range(2):
            nk = SLOT_K[sl_i]
            qs = sl_i * 512
            for hl in range(HG):
                gh = grp * HG + hl
                cb = gh % 2
                prev_evac = ctx_evac.get(gh - 2, 0)
                def emit_av(kc, dep_v, a_ex):
                    pbuf = kc % 3
                    if dep_v:
                        QT.wait(vS, dep_v)
                    else:
                        QT.wait(aS, a_ex)
                    if kc == 0:
                        QT.wait(vS, prev_evac)
                    def mm_av(t, hl=hl, kc=kc, pbuf=pbuf, nk=nk, pp=ps_cx[cb]):
                        return t.matmul(pp[:, :],
                                        vtok[:, kc:kc + 1, hl * VD:(hl + 1) * VD],
                                        p_sb[:, pbuf:pbuf + 1, :],
                                        start=(kc == 0), stop=(kc == nk - 1))
                    QT.op_noinc(mm_av)
                    def mm_sm(t, kc=kc, pbuf=pbuf, nk=nk, pp=ps_sm[cb]):
                        return t.matmul(pp[:, :], ones_mat[:, :],
                                        p_sb[:, pbuf:pbuf + 1, :],
                                        start=(kc == 0), stop=(kc == nk - 1))
                    last_av_pbuf[pbuf] = QT.op(mm_sm)

                pend = None
                for kc in range(nk):
                    pbuf = kc % 3
                    sbk = kc % 2
                    QT.wait(aS, last_exp_bank[sbk])
                    def mm_n(t, hl=hl, kc=kc, qs=qs, pp=ps_sT[sbk]):
                        return t.matmul(pp[:, :],
                                        kno[:, hl:hl + 1, kc * 128:(kc + 1) * 128],
                                        q_no[:, hl:hl + 1, qs:qs + 512],
                                        start=True, stop=False)
                    QT.op_noinc(mm_n)
                    def mm_p(t, hl=hl, kc=kc, qs=qs, pp=ps_sT[sbk]):
                        return t.matmul(pp[:, :],
                                        kpe_ext[0:65, kc * 128:(kc + 1) * 128],
                                        qpe_ext[0:65, hl * NQ + qs:hl * NQ + qs + 512],
                                        start=False, stop=True)
                    t_sc = QT.op(mm_p)
                    QA.wait(tS, t_sc)
                    QA.wait(tS, last_av_pbuf[pbuf])
                    a_ex = QA.op(lambda sc, pbuf=pbuf, pp=ps_sT[sbk]: sc.activation(
                        p_sb[:, pbuf:pbuf + 1, :], pp[:, :], AF.Exp, scale=SCALE))
                    last_exp_bank[sbk] = a_ex
                    midx = kc if sl_i == 0 else (8 + (kc - 8) if kc >= 8 else None)
                    dep_v = 0
                    if midx is not None:
                        QV.wait(aS, a_ex)
                        dep_v = QV.op(lambda v, pbuf=pbuf, midx=midx: v.tensor_mul(
                            p_sb[:, pbuf:pbuf + 1, :], p_sb[:, pbuf:pbuf + 1, :],
                            masks[:, midx:midx + 1, :]))
                    if pend is not None:
                        emit_av(*pend)
                    pend = (kc, dep_v, a_ex)
                emit_av(*pend)
                QV.wait(tS, last_av_pbuf[(nk - 1) % 3])
                QV.op(lambda v, pp=ps_sm[cb]: v.reciprocal(rec_sb[:, :], pp[:, :]))
                ctx_evac[gh] = QV.op(lambda v, gh=gh, qs=qs, pp=ps_cx[cb]: v.scalar_tensor_tensor(
                    ctx[:, gh:gh + 1, qs:qs + 512], pp[:, :], 1.0,
                    rec_sb[:, :], ALU.mult, ALU.mult))
        al.free("ps_sm1", "ps_sm0", "ps_cx1", "ps_cx0", "ps_sT1", "ps_sT0")

    v_ctx = vS.n
    if debug:
        QS.wait(vS, v_ctx)
        QS.dma(dDbg, dbg["ctx"], ctx[:, :, :])
        QV.wait(dDbg, dDbg.n)
        QT.wait(dDbg, dDbg.n)
        QA.wait(dDbg, dDbg.n)
    al.free("rpqbf", "rpq2", "rpq1", "rec_sb", "p_sb", "kmx", "bnd_row", "qn_row",
            "sqs", "vtok", "kno", "qpe_ext", "q_no")
    al.free("wuv0_sb", "wuk0_sb", "wuq0_sb")
    al.free("kpe_sq", "kpe_ext", "kvT", "cq_sb", "masks")

    wo_sb = al.sb("wo_sb", [128, 16, DM], BF16, side="right")
    QG.wait(tS, tS.n)
    QG.wait(vS, vS.n)
    QG.wait(aS, aS.n)
    QG.dma(dW, wo_sb[:, 0:8, :], w_o[0:1024, :].rearrange("(c p) f -> p c f", p=128))
    d_wo = QG.dma(dW, wo_sb[:, 8:16, :],
                  w_o[1024:2048, :].rearrange("(c p) f -> p c f", p=128))
    h2T = al.sb("h2T", [128, 16, NQ], BF16)
    xqst = al.sb("xqst", [128, 2, DM], F32, side="right")
    x1st = al.sb("x1st", [128, 2, DM], F32, side="right")
    h2st = al.sb("h2st", [128, 2, DM], BF16, side="right")
    sqstf = al.sb("sqstf", [128, DM], F32, side="right")
    st1f = al.sb("st1f", [128, 8], F32, side="right")
    ps_f = [al.ps("ps_f%d" % i, [128, 512]) for i in range(4)]

    f_evac = {}
    x1_wr = [0, 0]
    h2_rd = [0, 0]
    x_rd = [0, 0]
    QS.wait(tS, tS.n)
    QS.wait(vS, vS.n)
    QS.wait(aS, aS.n)
    for tt in range(8):
        buf = tt % 2
        QS.wait(vS, x_rd[buf])
        d_xq = QS.dma(dX[buf], xqst[:, buf:buf + 1, :], x_q[tt * 128:(tt + 1) * 128, :])
        QT.wait(vS, v_ctx)
        QT.wait(dW, d_wo)
        for ds in range(4):
            pb = ds
            QT.wait(vS, f_evac.get((tt - 1, ds), 0))
            for c in range(16):
                def mm(t, tt=tt, ds=ds, c=c, pb=pb):
                    return t.matmul(ps_f[pb][:, :],
                                    ctx[:, c:c + 1, tt * 128:(tt + 1) * 128],
                                    wo_sb[:, c:c + 1, ds * 512:(ds + 1) * 512],
                                    start=(c == 0), stop=(c == 15))
                QT.op(mm) if c == 15 else QT.op_noinc(mm)
            QV.wait(tS, tS.n)
            QV.wait(dX[buf], d_xq)
            QV.wait(dSt[buf], x1_wr[buf])
            f_evac[(tt, ds)] = QV.op(lambda v, tt=tt, ds=ds, pb=pb, buf=buf: v.tensor_add(
                x1st[:, buf:buf + 1, ds * 512:(ds + 1) * 512], ps_f[pb][:, :],
                xqst[:, buf:buf + 1, ds * 512:(ds + 1) * 512]))
        x_rd[buf] = f_evac[(tt, 3)]
        xa = x1st[:, buf:buf + 1, :]
        ha = h2st[:, buf:buf + 1, :]
        QS.wait(vS, f_evac[(tt, 3)])
        x1_wr[buf] = QS.dma(dSt[buf], x1_scr[tt * 128:(tt + 1) * 128, :], xa)
        QV.op(lambda v, xa=xa: v.reduce_sum(st1f[:, 0:1], xa, AX.X))
        QA.wait(vS, vS.n)
        QA.op(lambda sc, xa=xa: sc.activation(sqstf[:, :], xa, AF.Square, accum_out=st1f[:, 1:2]))
        QV.wait(aS, aS.n)
        QV.op(lambda v: v.tensor_scalar(st1f[:, 2:3], st1f[:, 0:1], 1.0 / DM, None, ALU.mult))
        QV.op(lambda v: v.tensor_scalar(st1f[:, 3:4], st1f[:, 1:2], 1.0 / DM, None, ALU.mult))
        QV.op(lambda v: v.tensor_mul(st1f[:, 4:5], st1f[:, 2:3], st1f[:, 2:3]))
        v_var = QV.op(lambda v: v.tensor_sub(st1f[:, 3:4], st1f[:, 3:4], st1f[:, 4:5]))
        QA.wait(vS, v_var)
        a_std = QA.op(lambda sc: sc.activation(st1f[:, 5:6], st1f[:, 3:4], AF.Sqrt, bias=epsln[:, 0:1]))
        QV.wait(aS, a_std)
        QV.op(lambda v: v.reciprocal(st1f[:, 6:7], st1f[:, 5:6]))
        v_nb = QV.op(lambda v: v.tensor_scalar(st1f[:, 7:8], st1f[:, 2:3], st1f[:, 6:7], -1.0,
                                               ALU.mult, ALU.mult))
        QA.wait(vS, v_nb)
        if h2_rd[buf]:
            QA.wait(dT[buf], h2_rd[buf])
        a_h = QA.op(lambda sc, ha=ha, xa=xa: sc.activation(ha, xa, AF.Identity,
                                                           bias=st1f[:, 7:8], scale=st1f[:, 6:7]))
        QS.wait(aS, a_h)
        h2_rd[buf] = QS.dma(dT[buf], h2T[:, :, tt * 128:(tt + 1) * 128], ha, transpose=True)
    d_h2T = (dT[0].n, dT[1].n)
    if debug:
        QS.wait(dT[0], d_h2T[0])
        QS.wait(dT[1], d_h2T[1])
        QS.wait(dSt[0], dSt[0].n)
        QS.wait(dSt[1], dSt[1].n)
        QS.dma(dDbg, dbg["h2T"], h2T[:, :, :])
        QS.dma(dDbg, dbg["x1"], x1_scr)
        QV.wait(dDbg, dDbg.n)
        QT.wait(dDbg, dDbg.n)
        QA.wait(dDbg, dDbg.n)
    al.free("ps_f3", "ps_f2", "ps_f1", "ps_f0")
    al.free("st1f", "sqstf", "h2st", "x1st", "xqst")
    al.free("wo_sb")
    al.free("ctx")

    sdbc = al.sb("sdbc", [128, DM], F32)
    nact = al.sb("nact", [1, NQ], BF16)
    x1r = al.sb("x1r", [128, 2, 512], F32)
    y1 = al.sb("y1", [128, 2, 512], F32)
    sdrow = al.sb("sdrow", [1, DM], F32)
    act_sb = al.sb("act_sb", [128, 44, NQ], BF16, side="right")
    slab = [al.sb("slab0", [128, 16, 512], BF16, side="right"),
            al.sb("slab1", [128, 16, 512], BF16, side="right")]
    gate_sb = al.sb("gate_sb", [128, 8, 512], BF16, side="right")
    dslab = [al.sb("dslab0", [128, 4, 512], BF16, side="right"),
             al.sb("dslab1", [128, 4, 512], BF16, side="right")]

    ps_g = [al.ps("ps_g0", [128, 512]), al.ps("ps_g1", [128, 512])]
    ps_u = [al.ps("ps_u0", [128, 512]), al.ps("ps_u1", [128, 512])]
    ps_as = al.ps("ps_as", [1, 1024])
    ps_sd = al.ps("ps_sd", [128, 512])

    d_sdr = QS.dma(dIn, sdrow[:, :], s_down_r)
    for j in range(4):
        QT.wait(dIn, d_sdr)
        QT.wait(vS, vS.n)
        t_b = QT.op(lambda t, j=j: t.matmul(ps_sd[:, :], ones_rf[0:1, 0:128],
                                            sdrow[0:1, j * 512:(j + 1) * 512],
                                            start=True, stop=True))
        QV.wait(tS, t_b)
        QV.op(lambda v, j=j: v.tensor_copy(sdbc[:, j * 512:(j + 1) * 512], ps_sd[:, :]))
    v_sdbc = vS.n
    QS.wait(vS, v_sdbc)
    al.free("sdrow")

    QS.wait(dSt[0], dSt[0].n)
    QS.wait(dSt[1], dSt[1].n)
    QS.wait(dT[0], d_h2T[0])
    QS.wait(dT[1], d_h2T[1])
    QV.wait(dT[0], d_h2T[0])
    QV.wait(dT[1], d_h2T[1])
    QA.wait(dT[0], d_h2T[0])
    QA.wait(dT[1], d_h2T[1])
    QV.wait(dSt[0], dSt[0].n)
    QV.wait(dSt[1], dSt[1].n)
    QA.wait(dSt[0], dSt[0].n)
    QA.wait(dSt[1], dSt[1].n)
    slab_user = [0, 0]
    silu_done = {}
    act_done = {}
    for sec in range(11):
        for which in range(2):
            l = 2 * sec + which
            sb_i = l % 2
            src = g_scr if which == 0 else u_scr
            QS.wait(dD2, d_d2d)
            QS.wait(tS, slab_user[sb_i])
            d_sl = QS.dma(dSl[sb_i], slab[sb_i][:, :, :],
                          src[sec * 512:(sec + 1) * 512, :], transpose=True)
            for hcl in range(4):
                hc = sec * 4 + hcl
                for nt in range(2):
                    pp = (ps_g if which == 0 else ps_u)[nt]
                    QT.wait(dSl[sb_i], d_sl)
                    QT.wait(dT[0], d_h2T[0])
                    QT.wait(dT[1], d_h2T[1])
                    if which == 0:
                        QT.wait(aS, silu_done.get((hc - 1, nt), 0))
                    else:
                        QT.wait(vS, act_done.get((hc - 1, nt), 0))
                    for c in range(16):
                        def mm(t, sb_i=sb_i, hcl=hcl, nt=nt, c=c, pp=pp):
                            return t.matmul(pp[:, :],
                                            slab[sb_i][:, c:c + 1, hcl * 128:(hcl + 1) * 128],
                                            h2T[:, c:c + 1, nt * 512:(nt + 1) * 512],
                                            start=(c == 0), stop=(c == 15))
                        t_mm = QT.op(mm) if c == 15 else QT.op_noinc(mm)
                    slab_user[sb_i] = tS.n
                    gi = hcl * 2 + nt
                    if which == 0:
                        QA.wait(tS, tS.n)
                        QA.wait(vS, act_done.get((hc - 4, nt), 0))
                        silu_done[(hc, nt)] = QA.op(lambda sc, hc=hc, gi=gi, pp=pp: sc.activation(
                            gate_sb[:, gi:gi + 1, :], pp[:, :], AF.Silu,
                            scale=sgate_c[:, hc:hc + 1]))
                    else:
                        QV.wait(tS, tS.n)
                        QV.wait(aS, silu_done[(hc, nt)])
                        a_d = QV.op(lambda v, hc=hc, nt=nt, gi=gi, pp=pp: v.scalar_tensor_tensor(
                            act_sb[:, hc:hc + 1, nt * 512:(nt + 1) * 512], pp[:, :],
                            sup_c[:, hc:hc + 1], gate_sb[:, gi:gi + 1, :],
                            ALU.mult, ALU.mult))
                        act_done[(hc, nt)] = a_d
                        QT.wait(vS, a_d)
                        def mm_as(t, hc=hc, nt=nt):
                            return t.matmul(ps_as[0:1, nt * 512:(nt + 1) * 512],
                                            ones_col[:, 0:1],
                                            act_sb[:, hc:hc + 1, nt * 512:(nt + 1) * 512],
                                            start=(hc == 0), stop=(hc == 43))
                        QT.op(mm_as) if hc == 43 else QT.op_noinc(mm_as)
    t_as = tS.n
    QA.wait(tS, t_as)
    a_nact = QA.op(lambda sc: sc.activation(nact[0:1, :], ps_as[0:1, :], AF.Copy,
                                            scale=-128.0))
    al.free("ps_sd", "ps_as", "ps_u1", "ps_u0", "ps_g1", "ps_g0")
    ps_d = [al.ps("ps_d%d" % i, [128, 512]) for i in range(8)]
    QT.wait(aS, aS.n)

    SECS = [(0, 4), (4, 4), (8, 4), (12, 4), (16, 4), (20, 4),
            (24, 4), (28, 4), (32, 4), (36, 4), (40, 4)]
    dslab_user = [0, 0]
    y_done = {}
    x1r_use = [0, 0]
    y1_use = [0, 0]
    QS.wait(dSt[0], dSt[0].n)
    QS.wait(dSt[1], dSt[1].n)
    for ds in range(4):
        for si, (h0, nch) in enumerate(SECS):
            sb_i = si % 2
            QS.wait(dD2, d_d2d)
            QS.wait(tS, dslab_user[sb_i])
            d_sl = QS.dma(dDl[sb_i], dslab[sb_i][:, 0:nch, :],
                          d_scr[ds * 512:(ds + 1) * 512, h0 * 128:(h0 + nch) * 128],
                          transpose=True)
            last = None
            for j in range(nch):
                hc = h0 + j
                for tt in range(8):
                    QT.wait(dDl[sb_i], d_sl)
                    if hc == 0:
                        QT.wait(vS, y_done.get((tt, ds - 1), 0))
                    def mm(t, sb_i=sb_i, j=j, hc=hc, tt=tt):
                        return t.matmul(ps_d[tt][:, :],
                                        act_sb[:, hc:hc + 1, tt * 128:(tt + 1) * 128],
                                        dslab[sb_i][:, j:j + 1, :],
                                        start=(hc == 0), stop=False)
                    if j == nch - 1 and tt == 7:
                        dslab_user[sb_i] = QT.op(mm)
                    else:
                        QT.op_noinc(mm)
        for tt in range(8):
            buf = tt % 2
            QT.wait(aS, a_nact)
            def mm_c(t, tt=tt):
                return t.matmul(ps_d[tt][:, :], nact[0:1, tt * 128:(tt + 1) * 128],
                                ones_rb[0:1, :], start=False, stop=True)
            t_c = QT.op(mm_c)
            QS.wait(vS, x1r_use[buf])
            d_x1 = QS.dma(dR[buf], x1r[:, buf:buf + 1, :],
                          x1_scr[tt * 128:(tt + 1) * 128, ds * 512:(ds + 1) * 512])
            QV.wait(tS, t_c)
            QV.wait(dR[buf], d_x1)
            QV.wait(vS, v_sdbc)
            QV.wait(dY[buf], y1_use[buf])
            QV.op(lambda v, ds=ds, buf=buf, tt=tt: v.tensor_mul(
                y1[:, buf:buf + 1, :], ps_d[tt][:, :],
                sdbc[:, ds * 512:(ds + 1) * 512]))
            y_done[(tt, ds)] = QV.op(lambda v, buf=buf: v.tensor_add(
                y1[:, buf:buf + 1, :], y1[:, buf:buf + 1, :], x1r[:, buf:buf + 1, :]))
            x1r_use[buf] = y_done[(tt, ds)]
            QS.wait(vS, y_done[(tt, ds)])
            y1_use[buf] = QS.dma(
                dY[buf], y_out[tt * 128:(tt + 1) * 128, ds * 512:(ds + 1) * 512],
                y1[:, buf:buf + 1, :])
    QS.wait(dY[0], dY[0].n)
    QS.wait(dY[1], dY[1].n)
    if debug:
        QS.wait(dDbg, dDbg.n)

    block_g = nc.Block()
    block = block_g.__enter__()

    def run_queue(q):
        def body(eng):
            for fn in q.ops:
                fn(eng)
        return body

    block.sync(run_queue(QS))
    block.tensor(run_queue(QT))
    block.vector(run_queue(QV))
    block.scalar(run_queue(QA))
    block.gpsimd(run_queue(QG))
    block_g.__exit__(None, None, None)

    for g in reversed(sem_guards):
        g.__exit__(None, None, None)
    return nc


_cache = {}


def kernel(**inputs):
    f = lambda k: np.ascontiguousarray(np.asarray(inputs[k], dtype=np.float32))
    x = f("L0_attn_x")
    w = {
        "w_dq": f("L0_attn_w_dq"),
        "w_uq": f("L0_attn_w_uq"),
        "w_dkv": f("L0_attn_w_dkv"),
        "w_kr": f("L0_attn_w_kr"),
        "w_uk": f("L0_attn_w_uk"),
        "w_uv": f("L0_attn_w_uv"),
        "w_o": f("L0_attn_w_o"),
        "g_q_t": np.ascontiguousarray(f("L0_attn_g_q").reshape(6, 128).T),
        "g_kv_t": np.ascontiguousarray(f("L0_attn_g_kv").reshape(4, 128).T),
        "s_gate_t": np.ascontiguousarray(f("L0_ffn_s_gate").reshape(44, 128).T),
        "s_up_t": np.ascontiguousarray(f("L0_ffn_s_up").reshape(44, 128).T),
        "s_down_r": np.ascontiguousarray(f("L0_ffn_s_down").reshape(1, DM)),
        "wq_gate": np.ascontiguousarray(np.asarray(inputs["L0_ffn_wq_gate"], dtype=np.int32)),
        "wq_up": np.ascontiguousarray(np.asarray(inputs["L0_ffn_wq_up"], dtype=np.int32)),
        "wq_down": np.ascontiguousarray(np.asarray(inputs["L0_ffn_wq_down"], dtype=np.int32)),
    }
    if "nc" not in _cache:
        _cache["nc"] = build_nc()
    nc = _cache["nc"]

    in_maps = []
    sels = []
    for core in range(8):
        b, half = core // 2, core % 2
        if half == 0:
            rows = np.r_[0:512, 1536:2048]
            g0, g1 = 0.0, 1536.0
        else:
            rows = np.r_[512:1536]
            g0, g1 = 512.0, 1024.0
        sels.append((b, rows))
        m = dict(w)
        m["x_full"] = x[b]
        m["x_q"] = np.ascontiguousarray(x[b][rows])
        m["pos_q"] = np.ascontiguousarray(rows.astype(np.float32).reshape(1, NQ))
        m["gmeta"] = np.array([[g0, g1]], dtype=np.float32)
        in_maps.append(m)

    res = run_bass_kernel_spmd(nc, in_maps, core_ids=list(range(8)))
    y = np.empty((4, S, DM), dtype=np.float32)
    for core in range(8):
        b, rows = sels[core]
        y[b][rows] = np.asarray(res.results[core]["y"], dtype=np.float32)
    return y


# version 11
# speedup vs baseline: 1.8985x; 1.0751x over previous
import math
import numpy as np
import ml_dtypes
import concourse.bass as bass
import concourse.mybir as mybir
from concourse.bass_utils import run_bass_kernel_spmd

F32 = mybir.dt.float32
BF16 = mybir.dt.bfloat16
I32 = mybir.dt.int32
AF = mybir.ActivationFunctionType
ALU = mybir.AluOpType
AX = mybir.AxisListType

S = 2048
DM = 2048
H = 16
NOPE = 128
RD = 64
VD = 128
QL = 768
KVL = 512
HID = 5632
NQ = 1024
HG = 4
NGRP = 4
SLOT_K = (8, 16)
SCALE = float(1.0 / math.sqrt(192.0))
LN_EPS = 1e-5
RMS_EPS = 1e-6
MAGIC = 8388608.0

C_HI = 6.28125
C_MID = float(np.float32(2 * np.pi - C_HI))
C_LO = float(np.float64(2 * np.pi) - np.float64(C_HI) - np.float64(np.float32(C_MID)))
INV2PI = float(1.0 / (2 * np.pi))
LOG1E4_32 = float(np.log(10000.0) / 32.0)


class Ctr:
    def __init__(self, handle):
        self.h = handle
        self.n = 0

    def inc(self, k=1):
        self.n += k
        return self.n


class EngQ:
    def __init__(self, ctr, chain):
        self.ops = []
        self.ctr = ctr
        self.chain = chain

    def wait(self, ctr, val):
        if val > 0:
            self.ops.append(lambda e, s=ctr.h, v=val: e.wait_ge(s, v))

    def op(self, fn):
        if self.chain and self.ctr.n > 0:
            self.ops.append(lambda e, s=self.ctr.h, v=self.ctr.n: e.wait_ge(s, v))
        val = self.ctr.inc()
        def run(e, fn=fn, s=self.ctr.h):
            fn(e).then_inc(s, 1)
        self.ops.append(run)
        return val

    def op_noinc(self, fn):
        self.ops.append(lambda e, fn=fn: fn(e))

    def dma(self, ctr, out, in_, transpose=False):
        val = ctr.inc(16)
        def run(e, out=out, in_=in_, s=ctr.h, tr=transpose):
            e.dma_start(out=out, in_=in_, transpose=tr).then_inc(s, 16)
        self.ops.append(run)
        return val


def build_nc(debug=False):
    nc = bass.Bass("TRN2", target_bir_lowering=False)

    x_full = nc.dram_tensor("x_full", [S, DM], F32, kind="ExternalInput").ap()
    x_q = nc.dram_tensor("x_q", [NQ, DM], F32, kind="ExternalInput").ap()
    pos_q = nc.dram_tensor("pos_q", [1, NQ], F32, kind="ExternalInput").ap()
    gmeta = nc.dram_tensor("gmeta", [1, 2], F32, kind="ExternalInput").ap()
    w_dq = nc.dram_tensor("w_dq", [DM, QL], F32, kind="ExternalInput").ap()
    g_q_t = nc.dram_tensor("g_q_t", [128, 6], F32, kind="ExternalInput").ap()
    w_uq = nc.dram_tensor("w_uq", [QL, H * 192], F32, kind="ExternalInput").ap()
    w_dkv = nc.dram_tensor("w_dkv", [DM, KVL], F32, kind="ExternalInput").ap()
    w_kr = nc.dram_tensor("w_kr", [DM, RD], F32, kind="ExternalInput").ap()
    g_kv_t = nc.dram_tensor("g_kv_t", [128, 4], F32, kind="ExternalInput").ap()
    w_uk = nc.dram_tensor("w_uk", [KVL, H * NOPE], F32, kind="ExternalInput").ap()
    w_uv = nc.dram_tensor("w_uv", [KVL, H * VD], F32, kind="ExternalInput").ap()
    w_o = nc.dram_tensor("w_o", [H * VD, DM], F32, kind="ExternalInput").ap()
    wq_gate = nc.dram_tensor("wq_gate", [HID, DM], I32, kind="ExternalInput").ap()
    s_gate_t = nc.dram_tensor("s_gate_t", [128, 44], F32, kind="ExternalInput").ap()
    wq_up = nc.dram_tensor("wq_up", [HID, DM], I32, kind="ExternalInput").ap()
    s_up_t = nc.dram_tensor("s_up_t", [128, 44], F32, kind="ExternalInput").ap()
    wq_down = nc.dram_tensor("wq_down", [DM, HID], I32, kind="ExternalInput").ap()
    s_down_r = nc.dram_tensor("s_down_r", [1, DM], F32, kind="ExternalInput").ap()
    y_out = nc.dram_tensor("y", [NQ, DM], F32, kind="ExternalOutput").ap()

    g_scr = nc.dram_tensor("g_scr", [HID, DM], BF16, kind="Internal").ap()
    u_scr = nc.dram_tensor("u_scr", [HID, DM], BF16, kind="Internal").ap()
    d_scr = nc.dram_tensor("d_scr", [DM, HID], BF16, kind="Internal").ap()
    x1_scr = nc.dram_tensor("x1_scr", [NQ, DM], F32, kind="Internal").ap()

    dbg = {}
    if debug:
        def dbg_t(name, shape, dt=F32):
            dbg[name] = nc.dram_tensor("dbg_" + name, shape, dt, kind="ExternalOutput").ap()
        dbg_t("cq", [128, 6, NQ], BF16)
        dbg_t("kvT", [128, 4, S], BF16)
        dbg_t("kpe", [65, S], BF16)
        dbg_t("qno", [128, HG, NQ], BF16)
        dbg_t("qpe", [65, HG * NQ], BF16)
        dbg_t("kno", [128, HG, S], BF16)
        dbg_t("vtok", [128, 16, HG * VD], BF16)
        dbg_t("ctx", [128, 16, NQ], BF16)
        dbg_t("x1", [NQ, DM], F32)
        dbg_t("h2T", [128, 16, NQ], BF16)
        dbg_t("masks", [128, 16, 512], BF16)
        dbg_t("cosq", [32, NQ], F32)
        dbg_t("kmaxn", [1, 16], F32)

    sem_guards = []
    def mksem(name):
        g = nc.semaphore(name)
        h = g.__enter__()
        sem_guards.append(g)
        return Ctr(h)

    dIn = mksem("dIn")
    dX = [mksem("dX0"), mksem("dX1")]
    dT = [mksem("dT0"), mksem("dT1")]
    dW = mksem("dW")
    dW0 = mksem("dW0")
    dD2 = mksem("dD2")
    dB = mksem("dB")
    dSt = [mksem("dSt0"), mksem("dSt1")]
    dSl = [mksem("dSl0"), mksem("dSl1")]
    dDl = [mksem("dDl0"), mksem("dDl1")]
    dR = [mksem("dR0"), mksem("dR1")]
    dY = [mksem("dY0"), mksem("dY1")]
    dDbg = mksem("dDbg")
    tS = mksem("tS")
    vS = mksem("vS")
    aS = mksem("aS")
    gS = mksem("gS")

    QT = EngQ(tS, chain=False)
    QV = EngQ(vS, chain=True)
    QA = EngQ(aS, chain=True)
    QG = EngQ(gS, chain=True)
    QS = EngQ(None, chain=False)

    class Alloc:
        def __init__(self):
            self.guards = {}
            self.order = {}
            self.k = 0

        def sb(self, name, shape, dt, side=None):
            g = nc.sbuf_tensor("%s__%d" % (name, self.k), shape, dt, side=side)
            t = g.__enter__()
            self.guards[name] = g
            self.order[name] = self.k
            self.k += 1
            return t

        def ps(self, name, shape):
            g = nc.psum_tensor("%s__%d" % (name, self.k), shape, F32)
            t = g.__enter__()
            self.guards[name] = g
            self.order[name] = self.k
            self.k += 1
            return t

        def free(self, *names):
            for name in sorted(names, key=lambda n: -self.order[n]):
                self.guards.pop(name).__exit__(None, None, None)
                self.order.pop(name)

    al = Alloc()

    ones_col = al.sb("ones_col", [128, 1], BF16)
    ones_cf = al.sb("ones_cf", [128, 1], F32)
    epsln = al.sb("epsln", [128, 1], F32)
    epsrms = al.sb("epsrms", [128, 1], F32)
    ones_mat = al.sb("ones_mat", [128, 128], BF16)
    ones_rf = al.sb("ones_rf", [1, 128], F32)
    ones_rb = al.sb("ones_rb", [1, 512], BF16)
    sgate_c = al.sb("sgate_c", [128, 44], F32)
    sup_c = al.sb("sup_c", [128, 44], F32)
    gq_c = al.sb("gq_c", [128, 6], F32)
    gkv_c = al.sb("gkv_c", [128, 4], F32)
    gmeta_sb = al.sb("gmeta_sb", [1, 2], F32)
    gcol = al.sb("gcol", [128, 2], F32)
    iota_pq = al.sb("iota_pq", [128, 512], F32)
    kmaxn = al.sb("kmaxn", [1, 16], F32)
    cosq = al.sb("cosq", [32, NQ], F32)
    sinq = al.sb("sinq", [32, NQ], F32)
    masks = al.sb("masks", [128, 16, 512], BF16)
    cq_sb = al.sb("cq_sb", [128, 6, NQ], BF16)
    kvT = al.sb("kvT", [128, 4, S], BF16)
    kpe_ext = al.sb("kpe_ext", [65, S], BF16)
    kpe_sq = al.sb("kpe_sq", [64, S], BF16)

    Wp = S + NQ
    cosk = al.sb("cosk", [32, S], F32, side="right")
    sink = al.sb("sink", [32, S], F32, side="right")
    posq_sb = al.sb("posq_sb", [1, NQ], F32, side="right")
    rt_ang = al.sb("rt_ang", [32, Wp], F32, side="right")
    rt_k = al.sb("rt_k", [32, Wp], F32, side="right")
    rt_t1 = al.sb("rt_t1", [32, Wp], F32, side="right")
    rt_t2 = al.sb("rt_t2", [32, Wp], F32, side="right")
    invf = al.sb("invf", [32, 1], F32, side="right")
    invfr = al.sb("invfr", [32, 1], F32, side="right")
    em = al.sb("em", [128, 2, 512], F32, side="right")

    for out_t, in_t in ((gmeta_sb[:, :], gmeta), (posq_sb[:, :], pos_q),
                        (sgate_c[:, :], s_gate_t), (sup_c[:, :], s_up_t),
                        (gq_c[:, :], g_q_t), (gkv_c[:, :], g_kv_t)):
        QS.dma(dIn, out_t, in_t)
    d_small = dIn.n

    QG.op(lambda g: g.iota(iota_pq[:, :], [[-1, 512]], base=0, channel_multiplier=1,
                           allow_small_or_imprecise_dtypes=True))
    QG.op(lambda g: g.iota(invfr[:, :], [[0, 1]], base=0, channel_multiplier=1,
                           allow_small_or_imprecise_dtypes=True))
    g_iota = QG.op(lambda g: g.iota(rt_ang[:, 0:S], [[1, S]], base=0, channel_multiplier=0,
                                    allow_small_or_imprecise_dtypes=True))

    QV.op(lambda v: v.memset(ones_col[:, :], 1.0))
    QV.op(lambda v: v.memset(ones_cf[:, :], 1.0))
    QV.op(lambda v: v.memset(epsln[:, :], LN_EPS))
    QV.op(lambda v: v.memset(epsrms[:, :], RMS_EPS))
    QV.op(lambda v: v.memset(kmaxn[:, :], 0.0))
    QV.op(lambda v: v.memset(ones_mat[:, :], 1.0))
    QV.op(lambda v: v.memset(ones_rb[:, :], 1.0))
    v_ones = QV.op(lambda v: v.memset(ones_rf[:, :], 1.0))

    QA.wait(gS, g_iota)
    a_invf = QA.op(lambda sc: sc.activation(invf[:, :], invfr[:, :], AF.Exp, scale=-LOG1E4_32))

    ps_b32 = al.ps("ps_b32", [32, 512])
    ps_b128 = al.ps("ps_b128", [128, 2])
    QT.wait(dIn, d_small)
    QT.wait(vS, v_ones)
    for i in range(NQ // 512):
        QT.wait(vS, vS.n)
        t_b = QT.op(lambda t, i=i: t.matmul(ps_b32[:, :], ones_rf[0:1, 0:32],
                                            posq_sb[0:1, i * 512:(i + 1) * 512],
                                            start=True, stop=True))
        QV.wait(tS, t_b)
        QV.wait(gS, g_iota)
        QV.op(lambda v, i=i: v.tensor_copy(rt_ang[:, S + i * 512:S + (i + 1) * 512],
                                           ps_b32[:, :]))
    t_g = QT.op(lambda t: t.matmul(ps_b128[:, :], ones_rf[0:1, 0:128], gmeta_sb[0:1, 0:2],
                                   start=True, stop=True))
    QV.wait(tS, t_g)
    QV.op(lambda v: v.tensor_copy(gcol[:, :], ps_b128[:, :]))

    QV.wait(aS, a_invf)
    QV.op(lambda v: v.tensor_scalar(rt_ang[:, :], rt_ang[:, :], invf[:, :], None, ALU.mult))
    QV.op(lambda v: v.tensor_scalar(rt_k[:, :], rt_ang[:, :], INV2PI, None, ALU.mult))
    QV.op(lambda v: v.tensor_scalar(rt_k[:, :], rt_k[:, :], MAGIC, None, ALU.add))
    QV.op(lambda v: v.tensor_scalar(rt_k[:, :], rt_k[:, :], -MAGIC, None, ALU.add))
    QV.op(lambda v: v.tensor_scalar(rt_t1[:, :], rt_k[:, :], C_HI, None, ALU.mult))
    QV.op(lambda v: v.tensor_sub(rt_ang[:, :], rt_ang[:, :], rt_t1[:, :]))
    QV.op(lambda v: v.tensor_scalar(rt_t1[:, :], rt_k[:, :], C_MID, None, ALU.mult))
    QV.op(lambda v: v.tensor_sub(rt_ang[:, :], rt_ang[:, :], rt_t1[:, :]))
    QV.op(lambda v: v.tensor_scalar(rt_t1[:, :], rt_k[:, :], C_LO, None, ALU.mult))
    v_red = QV.op(lambda v: v.tensor_sub(rt_ang[:, :], rt_ang[:, :], rt_t1[:, :]))
    QA.wait(vS, v_red)
    a_s4 = QA.op(lambda sc: sc.activation(rt_k[:, :], rt_ang[:, :], AF.Sin, scale=0.25))
    QV.wait(aS, a_s4)
    v_t4 = QV.op(lambda v: v.tensor_mul(rt_t1[:, :], rt_k[:, :], rt_k[:, :]))
    QA.wait(vS, v_t4)
    a_c4 = QA.op(lambda sc: sc.activation(rt_t2[:, :], rt_t1[:, :], AF.Sqrt, bias=1.0, scale=-1.0))
    QV.wait(aS, a_c4)
    QV.op(lambda v: v.tensor_mul(rt_k[:, :], rt_k[:, :], rt_t2[:, :]))
    QV.op(lambda v: v.tensor_scalar(rt_k[:, :], rt_k[:, :], 2.0, None, ALU.mult))
    QV.op(lambda v: v.tensor_scalar(rt_t2[:, :], rt_t1[:, :], -2.0, 1.0, ALU.mult, ALU.add))
    QV.op(lambda v: v.tensor_mul(rt_t1[:, :], rt_k[:, :], rt_t2[:, :]))
    QV.op(lambda v: v.tensor_scalar(rt_t1[:, :], rt_t1[:, :], 2.0, None, ALU.mult))
    QV.op(lambda v: v.tensor_copy(sink[:, :], rt_t1[:, 0:S]))
    QV.op(lambda v: v.tensor_copy(sinq[:, :], rt_t1[:, S:Wp]))
    QV.op(lambda v: v.tensor_mul(rt_t1[:, :], rt_k[:, :], rt_k[:, :]))
    QV.op(lambda v: v.tensor_scalar(rt_t1[:, :], rt_t1[:, :], -2.0, 1.0, ALU.mult, ALU.add))
    QV.op(lambda v: v.tensor_copy(cosk[:, :], rt_t1[:, 0:S]))
    v_trig = QV.op(lambda v: v.tensor_copy(cosq[:, :], rt_t1[:, S:Wp]))
    if debug:
        QS.wait(vS, v_trig)
        QS.dma(dDbg, dbg["cosq"], cosq[:, :])

    QV.op(lambda v: v.tensor_scalar(em[:, 0:1, :], iota_pq[:, :], gcol[:, 0:1], None, ALU.subtract))
    QV.op(lambda v: v.tensor_scalar(em[:, 1:2, :], iota_pq[:, :], gcol[:, 1:2], None, ALU.subtract))
    for s in range(2):
        for j in range(8):
            kc = j if s == 0 else 8 + j
            QV.op(lambda v, s=s, j=j, kc=kc: v.tensor_scalar(
                masks[:, s * 8 + j:s * 8 + j + 1, :], em[:, s:s + 1, :],
                float(-128.0 * kc), None, ALU.is_le))
    v_masks = vS.n
    if debug:
        QS.wait(vS, v_masks)
        QS.dma(dDbg, dbg["masks"], masks[:, :, :])
    al.free("em", "invfr", "invf", "rt_t2", "rt_t1", "rt_k", "rt_ang", "posq_sb")
    al.free("ps_b32", "ps_b128")

    xst = al.sb("xst", [128, 2, DM], F32, side="right")
    hst = al.sb("hst", [128, 2, DM], BF16, side="right")
    sqst = al.sb("sqst", [128, DM], F32, side="right")
    st1 = al.sb("st1", [128, 8], F32, side="right")
    hwin = al.sb("hwin", [128, 16, 512], BF16, side="right")
    rka = al.sb("rka", [32, 512], F32, side="right")
    rkb = al.sb("rkb", [32, 512], F32, side="right")
    rkbf = al.sb("rkbf", [32, 512], BF16, side="right")
    sqw = al.sb("sqw", [128, 512], F32, side="right")
    rrow = al.sb("rrow", [1, 512], F32, side="right")
    rbc = al.sb("rbc", [128, 512], F32, side="right")

    x_last_rd = [0, 0]
    h_last_rd = [0, 0]

    def ln_tile(src_rows, dst_col, buf):
        xa = xst[:, buf:buf + 1, :]
        ha = hst[:, buf:buf + 1, :]
        if x_last_rd[buf]:
            QS.wait(aS, x_last_rd[buf])
        d_x = QS.dma(dX[buf], xa, src_rows)
        QV.wait(dX[buf], d_x)
        QV.op(lambda v: v.reduce_sum(st1[:, 0:1], xa, AX.X))
        QA.wait(dX[buf], d_x)
        QA.op(lambda sc: sc.activation(sqst[:, :], xa, AF.Square, accum_out=st1[:, 1:2]))
        QV.wait(aS, aS.n)
        QV.op(lambda v: v.tensor_scalar(st1[:, 2:3], st1[:, 0:1], 1.0 / DM, None, ALU.mult))
        QV.op(lambda v: v.tensor_scalar(st1[:, 3:4], st1[:, 1:2], 1.0 / DM, None, ALU.mult))
        QV.op(lambda v: v.tensor_mul(st1[:, 4:5], st1[:, 2:3], st1[:, 2:3]))
        v_var = QV.op(lambda v: v.tensor_sub(st1[:, 3:4], st1[:, 3:4], st1[:, 4:5]))
        QA.wait(vS, v_var)
        a_std = QA.op(lambda sc: sc.activation(st1[:, 5:6], st1[:, 3:4], AF.Sqrt, bias=epsln[:, 0:1]))
        QV.wait(aS, a_std)
        QV.op(lambda v: v.reciprocal(st1[:, 6:7], st1[:, 5:6]))
        v_nb = QV.op(lambda v: v.tensor_scalar(st1[:, 7:8], st1[:, 2:3], st1[:, 6:7], -1.0,
                                               ALU.mult, ALU.mult))
        QA.wait(vS, v_nb)
        if h_last_rd[buf]:
            QA.wait(dT[buf], h_last_rd[buf])
        a_h = QA.op(lambda sc: sc.activation(ha, xa, AF.Identity, bias=st1[:, 7:8], scale=st1[:, 6:7]))
        x_last_rd[buf] = a_h
        QS.wait(aS, a_h)
        d_tr = QS.dma(dT[buf], hwin[:, :, dst_col:dst_col + 128], ha, transpose=True)
        h_last_rd[buf] = d_tr
        return d_tr

    wdkv_sb = al.sb("wdkv_sb", [128, 16, KVL], BF16, side="right")
    wkr_sb = al.sb("wkr_sb", [128, 16, RD], BF16, side="right")
    QG.wait(vS, vS.n)
    QG.wait(aS, aS.n)
    QG.dma(dW, wdkv_sb[:, :, :], w_dkv.rearrange("(c p) f -> p c f", p=128))
    d_wkr = QG.dma(dW, wkr_sb[:, :, :], w_kr.rearrange("(c p) f -> p c f", p=128))
    QS.wait(vS, vS.n)

    ps_r = al.ps("ps_r", [1, 512])
    ps_bw = al.ps("ps_bw", [128, 512])
    ps_c = [al.ps("ps_c%d" % i, [128, 512]) for i in range(4)]
    ps_kp1 = al.ps("ps_kp1", [32, 512])
    ps_kp2 = al.ps("ps_kp2", [32, 512])

    hwin_rd = 0
    for wdx in range(4):
        wc = wdx * 512
        QS.wait(tS, hwin_rd)
        for j in range(4):
            d_hw = ln_tile(x_full[(4 * wdx + j) * 128:(4 * wdx + j + 1) * 128, :],
                           j * 128, j % 2)
        QT.wait(dT[0], dT[0].n)
        QT.wait(dT[1], dT[1].n)
        QT.wait(dW, d_wkr)
        QT.wait(vS, vS.n)
        for m in range(4):
            for c in range(16):
                def mm(t, m=m, c=c):
                    return t.matmul(ps_c[m][:, :],
                                    wdkv_sb[:, c:c + 1, m * 128:(m + 1) * 128],
                                    hwin[:, c:c + 1, :],
                                    start=(c == 0), stop=(c == 15))
                QT.op(mm) if c == 15 else QT.op_noinc(mm)
        for c in range(16):
            def mm1(t, c=c):
                return t.matmul(ps_kp1[:, :], wkr_sb[:, c:c + 1, 0:32],
                                hwin[:, c:c + 1, :],
                                start=(c == 0), stop=(c == 15))
            def mm2(t, c=c):
                return t.matmul(ps_kp2[:, :], wkr_sb[:, c:c + 1, 32:64],
                                hwin[:, c:c + 1, :],
                                start=(c == 0), stop=(c == 15))
            QT.op_noinc(mm1)
            QT.op(mm2) if c == 15 else QT.op_noinc(mm2)
        hwin_rd = tS.n
        QA.wait(tS, hwin_rd)
        for m in range(4):
            a_s = QA.op(lambda sc, m=m: sc.activation(sqw[:, :], ps_c[m][:, :], AF.Square))
            QT.wait(aS, a_s)
            def mm(t, m=m):
                return t.matmul(ps_r[0:1, :], ones_cf[:, 0:1], sqw[:, :],
                                start=(m == 0), stop=(m == 3))
            t_sq = QT.op(mm)
            QA.wait(tS, t_sq)
        QA.wait(tS, tS.n)
        a_r = QA.op(lambda sc: sc.activation(rrow[0:1, :], ps_r[0:1, :], AF.Sqrt,
                                             scale=1.0 / KVL, bias=epsrms[0:1, 0:1]))
        QV.wait(aS, a_r)
        QV.op(lambda v: v.reciprocal(rrow[0:1, :], rrow[0:1, :]))
        QT.wait(vS, vS.n)
        t_b = QT.op(lambda t: t.matmul(ps_bw[:, :], ones_rf[0:1, 0:128], rrow[0:1, :],
                                       start=True, stop=True))
        QV.wait(tS, t_b)
        QV.op(lambda v: v.tensor_copy(rbc[:, :], ps_bw[:, :]))
        for m in range(4):
            QV.op(lambda v, m=m, wc=wc: v.scalar_tensor_tensor(
                kvT[:, m:m + 1, wc:wc + 512], ps_c[m][:, :],
                gkv_c[:, m:m + 1], rbc[:, :], ALU.mult, ALU.mult))
        csl = slice(wc, wc + 512)
        QV.op(lambda v, csl=csl: v.tensor_mul(rka[0:32, :], ps_kp1[:, :], cosk[:, csl]))
        QV.op(lambda v, csl=csl: v.tensor_mul(rkb[0:32, :], ps_kp2[:, :], sink[:, csl]))
        QV.op(lambda v: v.tensor_sub(rka[0:32, :], rka[0:32, :], rkb[0:32, :]))
        QV.op(lambda v, csl=csl: v.tensor_copy(kpe_ext[0:32, csl], rka[0:32, :]))
        QV.op(lambda v, csl=csl: v.tensor_mul(rka[0:32, :], ps_kp1[:, :], sink[:, csl]))
        QV.op(lambda v, csl=csl: v.tensor_mul(rkb[0:32, :], ps_kp2[:, :], cosk[:, csl]))
        QV.op(lambda v: v.tensor_add(rka[0:32, :], rka[0:32, :], rkb[0:32, :]))
        QV.wait(dB, dB.n)
        v_k2 = QV.op(lambda v: v.tensor_copy(rkbf[0:32, :], rka[0:32, :]))
        QS.wait(vS, v_k2)
        QS.dma(dB, kpe_ext[32:64, csl], rkbf[0:32, :])
    QV.wait(dB, dB.n)
    QV.op(lambda v: v.memset(kpe_ext[64:65, :], 1.0))
    v_kv = vS.n
    QA.wait(vS, v_kv)
    a_kpsq = QA.op(lambda sc: sc.activation(kpe_sq[0:64, :], kpe_ext[0:64, :], AF.Square))
    if debug:
        QS.wait(vS, v_kv)
        QS.dma(dDbg, dbg["kvT"], kvT[:, :, :])
        QS.dma(dDbg, dbg["kpe"], kpe_ext[:, :])
        QV.wait(dDbg, dDbg.n)
        QT.wait(dDbg, dDbg.n)
        QA.wait(dDbg, dDbg.n)
    al.free("ps_kp2", "ps_kp1", "ps_c3", "ps_c2", "ps_c1", "ps_c0")
    al.free("wkr_sb", "wdkv_sb")

    wdq_sb = al.sb("wdq_sb", [128, 16, QL], BF16, side="right")
    QG.wait(tS, tS.n)
    d_wdq = QG.dma(dW, wdq_sb[:, :, :], w_dq.rearrange("(c p) f -> p c f", p=128))
    ps_c2 = [al.ps("ps_c2%d" % i, [128, 512]) for i in range(6)]

    for wdx in range(2):
        wc = wdx * 512
        QS.wait(tS, hwin_rd)
        for j in range(4):
            d_hw = ln_tile(x_q[(4 * wdx + j) * 128:(4 * wdx + j + 1) * 128, :],
                           j * 128, j % 2)
        QT.wait(dT[0], dT[0].n)
        QT.wait(dT[1], dT[1].n)
        QT.wait(dW, d_wdq)
        QT.wait(vS, vS.n)
        for m in range(6):
            for c in range(16):
                def mm(t, m=m, c=c):
                    return t.matmul(ps_c2[m][:, :],
                                    wdq_sb[:, c:c + 1, m * 128:(m + 1) * 128],
                                    hwin[:, c:c + 1, :],
                                    start=(c == 0), stop=(c == 15))
                QT.op(mm) if c == 15 else QT.op_noinc(mm)
        hwin_rd = tS.n
        QA.wait(tS, hwin_rd)
        for m in range(6):
            a_s = QA.op(lambda sc, m=m: sc.activation(sqw[:, :], ps_c2[m][:, :], AF.Square))
            QT.wait(aS, a_s)
            def mm(t, m=m):
                return t.matmul(ps_r[0:1, :], ones_cf[:, 0:1], sqw[:, :],
                                start=(m == 0), stop=(m == 5))
            t_sq = QT.op(mm)
            QA.wait(tS, t_sq)
        QA.wait(tS, tS.n)
        a_r = QA.op(lambda sc: sc.activation(rrow[0:1, :], ps_r[0:1, :], AF.Sqrt,
                                             scale=1.0 / QL, bias=epsrms[0:1, 0:1]))
        QV.wait(aS, a_r)
        QV.op(lambda v: v.reciprocal(rrow[0:1, :], rrow[0:1, :]))
        QT.wait(vS, vS.n)
        t_b = QT.op(lambda t: t.matmul(ps_bw[:, :], ones_rf[0:1, 0:128], rrow[0:1, :],
                                       start=True, stop=True))
        QV.wait(tS, t_b)
        QV.op(lambda v: v.tensor_copy(rbc[:, :], ps_bw[:, :]))
        for m in range(6):
            QV.op(lambda v, m=m, wc=wc: v.scalar_tensor_tensor(
                cq_sb[:, m:m + 1, wc:wc + 512], ps_c2[m][:, :],
                gq_c[:, m:m + 1], rbc[:, :], ALU.mult, ALU.mult))
    v_cq = vS.n
    if debug:
        QS.wait(vS, v_cq)
        QS.dma(dDbg, dbg["cq"], cq_sb[:, :, :])
        QV.wait(dDbg, dDbg.n)
        QT.wait(dDbg, dDbg.n)
        QA.wait(dDbg, dDbg.n)
    al.free("ps_c25", "ps_c24", "ps_c23", "ps_c22", "ps_c21", "ps_c20", "ps_bw", "ps_r")
    al.free("wdq_sb")
    al.free("rbc", "rrow", "sqw", "rkbf", "rkb", "rka", "hwin", "st1", "sqst", "hst", "xst")
    al.free("sink", "cosk")

    wuq0_sb = al.sb("wuq0_sb", [128, 6, HG * 192], BF16)
    wuk0_sb = al.sb("wuk0_sb", [128, 4, HG * NOPE], BF16)
    wuv0_sb = al.sb("wuv0_sb", [128, 4, HG * VD], BF16)
    QG.wait(dX[0], dX[0].n)
    QG.wait(dX[1], dX[1].n)
    QG.dma(dW0, wuq0_sb[:, :, :],
           w_uq[:, 0:HG * 192].rearrange("(c p) f -> p c f", p=128))
    QG.dma(dW0, wuk0_sb[:, :, :],
           w_uk[:, 0:HG * NOPE].rearrange("(c p) f -> p c f", p=128))
    d_wg0 = QG.dma(dW0, wuv0_sb[:, :, :],
                   w_uv[:, 0:HG * VD].rearrange("(c p) f -> p c f", p=128))

    cast_pieces = []
    for j in range(4):
        cast_pieces.append((g_scr[j * 1408:(j + 1) * 1408, :],
                            wq_gate[j * 1408:(j + 1) * 1408, :]))
        cast_pieces.append((u_scr[j * 1408:(j + 1) * 1408, :],
                            wq_up[j * 1408:(j + 1) * 1408, :]))
        cast_pieces.append((d_scr[j * 512:(j + 1) * 512, :],
                            wq_down[j * 512:(j + 1) * 512, :]))

    q_no = al.sb("q_no", [128, HG, NQ], BF16)
    qpe_ext = al.sb("qpe_ext", [65, HG * NQ], BF16)
    kno = al.sb("kno", [128, HG, S], BF16)
    vtok = al.sb("vtok", [128, 16, HG * VD], BF16)
    sqs = al.sb("sqs", [128, S], BF16)
    qn_row = al.sb("qn_row", [1, NQ], F32)
    bnd_row = al.sb("bnd_row", [1, NQ], BF16)
    kmx = al.sb("kmx", [1, 8], F32)
    p_sb = al.sb("p_sb", [128, 3, 512], BF16)
    rec_sb = al.sb("rec_sb", [128, 512], F32)
    rpq1 = al.sb("rpq1", [32, NQ], F32)
    rpq2 = al.sb("rpq2", [32, NQ], F32)
    rpqbf = al.sb("rpqbf", [32, NQ], BF16)
    ctx = al.sb("ctx", [128, 16, NQ], BF16, side="right")

    ctx_evac = {}
    last_exp_bank = [0, 0]
    last_av_pbuf = [0, 0, 0]
    d_wo = 0
    wo_sb = None

    for grp in range(NGRP):
        if grp == 0:
            wuq_sb, wuk_sb, wuv_sb = wuq0_sb, wuk0_sb, wuv0_sb
            d_wg, s_wg = d_wg0, dW0
        else:
            wuq_sb = al.sb("wuq_sb", [128, 6, HG * 192], BF16, side="right")
            wuk_sb = al.sb("wuk_sb", [128, 4, HG * NOPE], BF16, side="right")
            wuv_sb = al.sb("wuv_sb", [128, 4, HG * VD], BF16, side="right")
            QG.wait(tS, t_dend)
            QG.dma(dW, wuq_sb[:, :, :],
                   w_uq[:, grp * HG * 192:(grp + 1) * HG * 192]
                   .rearrange("(c p) f -> p c f", p=128))
            QG.dma(dW, wuk_sb[:, :, :],
                   w_uk[:, grp * HG * NOPE:(grp + 1) * HG * NOPE]
                   .rearrange("(c p) f -> p c f", p=128))
            d_wg = QG.dma(dW, wuv_sb[:, :, :],
                          w_uv[:, grp * HG * VD:(grp + 1) * HG * VD]
                          .rearrange("(c p) f -> p c f", p=128))
            s_wg = dW

        ps_a = [al.ps("ps_a0", [128, 512]), al.ps("ps_a1", [128, 512])]
        ps_p1 = al.ps("ps_p1", [32, 1024])
        ps_p2 = al.ps("ps_p2", [32, 1024])

        QT.wait(s_wg, d_wg)
        QT.wait(vS, vS.n)
        QT.wait(aS, aS.n)
        evac_done = [vS.n, vS.n]
        for hl in range(HG):
            for nt in range(2):
                pb = (hl * 2 + nt) % 2
                QT.wait(vS, evac_done[pb])
                for c in range(6):
                    def mm(t, hl=hl, nt=nt, c=c, pa=ps_a[pb], w=wuq_sb):
                        return t.matmul(pa[:, :],
                                        w[:, c:c + 1, hl * 192:hl * 192 + 128],
                                        cq_sb[:, c:c + 1, nt * 512:(nt + 1) * 512],
                                        start=(c == 0), stop=(c == 5))
                    QT.op(mm) if c == 5 else QT.op_noinc(mm)
                QV.wait(tS, tS.n)
                evac_done[pb] = QV.op(lambda v, hl=hl, nt=nt, pa=ps_a[pb]: v.tensor_copy(
                    q_no[:, hl:hl + 1, nt * 512:(nt + 1) * 512], pa[:, :]))

        for hl in range(HG):
            co = hl * NQ
            QT.wait(vS, vS.n)
            for c in range(6):
                for nt in range(2):
                    def mm1(t, hl=hl, c=c, nt=nt, pp=ps_p1, w=wuq_sb):
                        return t.matmul(pp[:, nt * 512:(nt + 1) * 512],
                                        w[:, c:c + 1,
                                          hl * 192 + 128:hl * 192 + 160],
                                        cq_sb[:, c:c + 1, nt * 512:(nt + 1) * 512],
                                        start=(c == 0), stop=(c == 5))
                    def mm2(t, hl=hl, c=c, nt=nt, pp=ps_p2, w=wuq_sb):
                        return t.matmul(pp[:, nt * 512:(nt + 1) * 512],
                                        w[:, c:c + 1,
                                          hl * 192 + 160:hl * 192 + 192],
                                        cq_sb[:, c:c + 1, nt * 512:(nt + 1) * 512],
                                        start=(c == 0), stop=(c == 5))
                    QT.op_noinc(mm1)
                    QT.op(mm2) if (c == 5 and nt == 1) else QT.op_noinc(mm2)
            t_pe = tS.n
            QV.wait(tS, t_pe)
            QV.op(lambda v, pp=ps_p1: v.tensor_mul(rpq1[:, :], pp[:, :], cosq[:, :]))
            QV.op(lambda v, pp=ps_p2: v.tensor_mul(rpq2[:, :], pp[:, :], sinq[:, :]))
            QV.op(lambda v: v.tensor_sub(rpq1[:, :], rpq1[:, :], rpq2[:, :]))
            QV.op(lambda v, co=co: v.tensor_copy(qpe_ext[0:32, co:co + NQ], rpq1[:, :]))
            QV.op(lambda v, pp=ps_p1: v.tensor_mul(rpq1[:, :], pp[:, :], sinq[:, :]))
            QV.op(lambda v, pp=ps_p2: v.tensor_mul(rpq2[:, :], pp[:, :], cosq[:, :]))
            QV.op(lambda v: v.tensor_add(rpq1[:, :], rpq1[:, :], rpq2[:, :]))
            QV.wait(dB, dB.n)
            v_q2 = QV.op(lambda v: v.tensor_copy(rpqbf[:, :], rpq1[:, :]))
            QS.wait(vS, v_q2)
            QS.dma(dB, qpe_ext[32:64, co:co + NQ], rpqbf[:, :])

        QT.wait(vS, v_kv)
        for hl in range(HG):
            for nt in range(4):
                pb = (hl * 4 + nt) % 2
                QT.wait(vS, evac_done[pb])
                for c in range(4):
                    def mm(t, hl=hl, nt=nt, c=c, pa=ps_a[pb], w=wuk_sb):
                        return t.matmul(pa[:, :],
                                        w[:, c:c + 1, hl * 128:(hl + 1) * 128],
                                        kvT[:, c:c + 1, nt * 512:(nt + 1) * 512],
                                        start=(c == 0), stop=(c == 3))
                    QT.op(mm) if c == 3 else QT.op_noinc(mm)
                QV.wait(tS, tS.n)
                evac_done[pb] = QV.op(lambda v, hl=hl, nt=nt, pa=ps_a[pb]: v.tensor_copy(
                    kno[:, hl:hl + 1, nt * 512:(nt + 1) * 512], pa[:, :]))

        for kt in range(16):
            pb = kt % 2
            QT.wait(vS, evac_done[pb])
            for c in range(4):
                def mm(t, kt=kt, c=c, pa=ps_a[pb], w=wuv_sb):
                    return t.matmul(pa[:, :],
                                    kvT[:, c:c + 1, kt * 128:(kt + 1) * 128],
                                    w[:, c:c + 1, 0:HG * VD],
                                    start=(c == 0), stop=(c == 3))
                QT.op(mm) if c == 3 else QT.op_noinc(mm)
            QV.wait(tS, tS.n)
            evac_done[pb] = QV.op(lambda v, kt=kt, pa=ps_a[pb]: v.tensor_copy(
                vtok[:, kt:kt + 1, :], pa[:, :]))
        t_dend = tS.n

        last_red = 0
        for hl in range(HG):
            QA.wait(vS, vS.n)
            QA.wait(tS, tS.n)
            a_sq = QA.op(lambda sc, hl=hl: sc.activation(sqs[:, :], kno[:, hl:hl + 1, :],
                                                         AF.Square))
            for half in range(2):
                QT.wait(aS, a_sq)
                QT.wait(aS, a_kpsq)
                QT.wait(vS, last_red)
                for j in range(2):
                    sl = slice(half * 1024 + j * 512, half * 1024 + (j + 1) * 512)
                    def mma(t, j=j, sl=sl, pp=ps_p1):
                        return t.matmul(pp[0:1, j * 512:(j + 1) * 512],
                                        ones_col[:, 0:1], sqs[:, sl],
                                        start=True, stop=False)
                    def mmb(t, j=j, sl=sl, pp=ps_p1):
                        return t.matmul(pp[0:1, j * 512:(j + 1) * 512],
                                        ones_col[0:64, 0:1], kpe_sq[0:64, sl],
                                        start=False, stop=True)
                    QT.op_noinc(mma)
                    QT.op(mmb)
                QV.wait(tS, tS.n)
                last_red = QV.op(lambda v, half=half, pp=ps_p1: v.reduce_max(
                    kmx[0:1, half:half + 1], pp[0:1, 0:1024], AX.X))
            QV.op(lambda v: v.tensor_tensor(kmx[0:1, 2:3], kmx[0:1, 0:1], kmx[0:1, 1:2],
                                            ALU.max))
            v_kmx = vS.n
            QA.wait(vS, v_kmx)
            QA.op(lambda sc, hl=hl, grp=grp: sc.activation(
                kmaxn[0:1, grp * HG + hl:grp * HG + hl + 1], kmx[0:1, 2:3],
                AF.Sqrt, scale=1.0404))

        a_qn_prev = 0
        for hl in range(HG):
            QA.wait(tS, tS.n)
            QA.wait(vS, vS.n)
            a_q1 = QA.op(lambda sc, hl=hl: sc.activation(sqs[:, 0:NQ],
                                                         q_no[:, hl:hl + 1, :], AF.Square))
            QT.wait(aS, a_q1)
            QT.wait(aS, a_qn_prev)
            for j in range(2):
                def mmq(t, j=j, pp=ps_p1):
                    return t.matmul(pp[0:1, j * 512:(j + 1) * 512], ones_col[:, 0:1],
                                    sqs[:, j * 512:(j + 1) * 512], start=True, stop=False)
                QT.op(mmq) if j == 1 else QT.op_noinc(mmq)
            QA.wait(tS, tS.n)
            a_q2 = QA.op(lambda sc, hl=hl: sc.activation(
                sqs[0:64, 0:NQ], qpe_ext[0:64, hl * NQ:(hl + 1) * NQ], AF.Square))
            QT.wait(aS, a_q2)
            for j in range(2):
                def mmq2(t, j=j, pp=ps_p1):
                    return t.matmul(pp[0:1, j * 512:(j + 1) * 512], ones_col[0:64, 0:1],
                                    sqs[0:64, j * 512:(j + 1) * 512], start=False, stop=True)
                QT.op(mmq2) if j == 1 else QT.op_noinc(mmq2)
            QA.wait(tS, tS.n)
            a_qn_prev = QA.op(lambda sc, pp=ps_p1: sc.activation(qn_row[0:1, :], pp[0:1, 0:1024],
                                                                 AF.Sqrt))
            QV.wait(aS, a_qn_prev)
            QV.wait(dB, dB.n)
            QV.op(lambda v, hl=hl, grp=grp: v.tensor_scalar(
                bnd_row[0:1, :], qn_row[0:1, :],
                kmaxn[0:1, grp * HG + hl:grp * HG + hl + 1], None, ALU.mult))
            v_bnd = QV.op(lambda v: v.tensor_scalar(bnd_row[0:1, :], bnd_row[0:1, :],
                                                    -1.0, None, ALU.mult))
            QS.wait(vS, v_bnd)
            QS.dma(dB, qpe_ext[64:65, hl * NQ:(hl + 1) * NQ], bnd_row[0:1, :])

        if debug and grp == 0:
            QS.wait(vS, vS.n)
            QS.wait(aS, aS.n)
            QS.wait(dB, dB.n)
            QS.dma(dDbg, dbg["qno"], q_no[:, :, :])
            QS.dma(dDbg, dbg["qpe"], qpe_ext[:, :])
            QS.dma(dDbg, dbg["kno"], kno[:, :, :])
            QS.dma(dDbg, dbg["vtok"], vtok[:, :, :])
            QS.dma(dDbg, dbg["kmaxn"], kmaxn[:, :])
            QV.wait(dDbg, dDbg.n)
            QT.wait(dDbg, dDbg.n)
            QA.wait(dDbg, dDbg.n)

        al.free("ps_p2", "ps_p1", "ps_a1", "ps_a0")
        if grp != 0:
            al.free("wuv_sb", "wuk_sb", "wuq_sb")

        QG.wait(tS, tS.n)
        for dst_p, src_p in cast_pieces[grp * 3:(grp + 1) * 3]:
            QG.dma(dD2, dst_p, src_p)
        if grp == NGRP - 1:
            d_d2d = dD2.n
        QT.wait(dB, dB.n)
        ps_sT = [al.ps("ps_sT0", [128, 512]), al.ps("ps_sT1", [128, 512])]
        ps_cx = [al.ps("ps_cx0", [128, 512]), al.ps("ps_cx1", [128, 512])]
        ps_sm = [al.ps("ps_sm0", [128, 512]), al.ps("ps_sm1", [128, 512])]

        QT.wait(vS, vS.n)
        QT.wait(aS, aS.n)
        for sl_i in range(2):
            nk = SLOT_K[sl_i]
            qs = sl_i * 512
            for hl in range(HG):
                gh = grp * HG + hl
                cb = gh % 2
                prev_evac = ctx_evac.get(gh - 2, 0)
                def emit_av(kc, dep_v, a_ex):
                    pbuf = kc % 3
                    if dep_v:
                        QT.wait(vS, dep_v)
                    else:
                        QT.wait(aS, a_ex)
                    if kc == 0:
                        QT.wait(vS, prev_evac)
                    def mm_av(t, hl=hl, kc=kc, pbuf=pbuf, nk=nk, pp=ps_cx[cb]):
                        return t.matmul(pp[:, :],
                                        vtok[:, kc:kc + 1, hl * VD:(hl + 1) * VD],
                                        p_sb[:, pbuf:pbuf + 1, :],
                                        start=(kc == 0), stop=(kc == nk - 1))
                    QT.op_noinc(mm_av)
                    def mm_sm(t, kc=kc, pbuf=pbuf, nk=nk, pp=ps_sm[cb]):
                        return t.matmul(pp[:, :], ones_mat[:, :],
                                        p_sb[:, pbuf:pbuf + 1, :],
                                        start=(kc == 0), stop=(kc == nk - 1))
                    last_av_pbuf[pbuf] = QT.op(mm_sm)

                pend = None
                for kc in range(nk):
                    pbuf = kc % 3
                    sbk = kc % 2
                    QT.wait(aS, last_exp_bank[sbk])
                    def mm_n(t, hl=hl, kc=kc, qs=qs, pp=ps_sT[sbk]):
                        return t.matmul(pp[:, :],
                                        kno[:, hl:hl + 1, kc * 128:(kc + 1) * 128],
                                        q_no[:, hl:hl + 1, qs:qs + 512],
                                        start=True, stop=False)
                    QT.op_noinc(mm_n)
                    def mm_p(t, hl=hl, kc=kc, qs=qs, pp=ps_sT[sbk]):
                        return t.matmul(pp[:, :],
                                        kpe_ext[0:65, kc * 128:(kc + 1) * 128],
                                        qpe_ext[0:65, hl * NQ + qs:hl * NQ + qs + 512],
                                        start=False, stop=True)
                    t_sc = QT.op(mm_p)
                    QA.wait(tS, t_sc)
                    QA.wait(tS, last_av_pbuf[pbuf])
                    a_ex = QA.op(lambda sc, pbuf=pbuf, pp=ps_sT[sbk]: sc.activation(
                        p_sb[:, pbuf:pbuf + 1, :], pp[:, :], AF.Exp, scale=SCALE))
                    last_exp_bank[sbk] = a_ex
                    midx = kc if sl_i == 0 else (8 + (kc - 8) if kc >= 8 else None)
                    dep_v = 0
                    if midx is not None:
                        QV.wait(aS, a_ex)
                        dep_v = QV.op(lambda v, pbuf=pbuf, midx=midx: v.tensor_mul(
                            p_sb[:, pbuf:pbuf + 1, :], p_sb[:, pbuf:pbuf + 1, :],
                            masks[:, midx:midx + 1, :]))
                    if pend is not None:
                        emit_av(*pend)
                    pend = (kc, dep_v, a_ex)
                emit_av(*pend)
                QV.wait(tS, last_av_pbuf[(nk - 1) % 3])
                QV.op(lambda v, pp=ps_sm[cb]: v.reciprocal(rec_sb[:, :], pp[:, :]))
                ctx_evac[gh] = QV.op(lambda v, gh=gh, qs=qs, pp=ps_cx[cb]: v.scalar_tensor_tensor(
                    ctx[:, gh:gh + 1, qs:qs + 512], pp[:, :], 1.0,
                    rec_sb[:, :], ALU.mult, ALU.mult))
        al.free("ps_sm1", "ps_sm0", "ps_cx1", "ps_cx0", "ps_sT1", "ps_sT0")

    v_ctx = vS.n
    if debug:
        QS.wait(vS, v_ctx)
        QS.dma(dDbg, dbg["ctx"], ctx[:, :, :])
        QV.wait(dDbg, dDbg.n)
        QT.wait(dDbg, dDbg.n)
        QA.wait(dDbg, dDbg.n)
    al.free("rpqbf", "rpq2", "rpq1", "rec_sb", "p_sb", "kmx", "bnd_row", "qn_row",
            "sqs", "vtok", "kno", "qpe_ext", "q_no")
    al.free("wuv0_sb", "wuk0_sb", "wuq0_sb")
    al.free("kpe_sq", "kpe_ext", "kvT", "cq_sb", "masks")

    wo_sb = al.sb("wo_sb", [128, 16, DM], BF16, side="right")
    QG.wait(tS, tS.n)
    QG.wait(vS, vS.n)
    QG.wait(aS, aS.n)
    QG.dma(dW, wo_sb[:, 0:8, :], w_o[0:1024, :].rearrange("(c p) f -> p c f", p=128))
    d_wo = QG.dma(dW, wo_sb[:, 8:16, :],
                  w_o[1024:2048, :].rearrange("(c p) f -> p c f", p=128))
    h2T = al.sb("h2T", [128, 16, NQ], BF16)
    xqst = al.sb("xqst", [128, 2, DM], F32, side="right")
    x1st = al.sb("x1st", [128, 2, DM], F32, side="right")
    h2st = al.sb("h2st", [128, 2, DM], BF16, side="right")
    sqstf = al.sb("sqstf", [128, DM], F32, side="right")
    st1f = al.sb("st1f", [128, 8], F32, side="right")
    ps_f = [al.ps("ps_f%d" % i, [128, 512]) for i in range(4)]

    f_evac = {}
    x1_wr = [0, 0]
    h2_rd = [0, 0]
    x_rd = [0, 0]
    QS.wait(tS, tS.n)
    QS.wait(vS, vS.n)
    QS.wait(aS, aS.n)
    for tt in range(8):
        buf = tt % 2
        QS.wait(vS, x_rd[buf])
        d_xq = QS.dma(dX[buf], xqst[:, buf:buf + 1, :], x_q[tt * 128:(tt + 1) * 128, :])
        QT.wait(vS, v_ctx)
        QT.wait(dW, d_wo)
        for ds in range(4):
            pb = ds
            QT.wait(vS, f_evac.get((tt - 1, ds), 0))
            for c in range(16):
                def mm(t, tt=tt, ds=ds, c=c, pb=pb):
                    return t.matmul(ps_f[pb][:, :],
                                    ctx[:, c:c + 1, tt * 128:(tt + 1) * 128],
                                    wo_sb[:, c:c + 1, ds * 512:(ds + 1) * 512],
                                    start=(c == 0), stop=(c == 15))
                QT.op(mm) if c == 15 else QT.op_noinc(mm)
            QV.wait(tS, tS.n)
            QV.wait(dX[buf], d_xq)
            QV.wait(dSt[buf], x1_wr[buf])
            f_evac[(tt, ds)] = QV.op(lambda v, tt=tt, ds=ds, pb=pb, buf=buf: v.tensor_add(
                x1st[:, buf:buf + 1, ds * 512:(ds + 1) * 512], ps_f[pb][:, :],
                xqst[:, buf:buf + 1, ds * 512:(ds + 1) * 512]))
        x_rd[buf] = f_evac[(tt, 3)]
        xa = x1st[:, buf:buf + 1, :]
        ha = h2st[:, buf:buf + 1, :]
        QS.wait(vS, f_evac[(tt, 3)])
        x1_wr[buf] = QS.dma(dSt[buf], x1_scr[tt * 128:(tt + 1) * 128, :], xa)
        QV.op(lambda v, xa=xa: v.reduce_sum(st1f[:, 0:1], xa, AX.X))
        QA.wait(vS, vS.n)
        QA.op(lambda sc, xa=xa: sc.activation(sqstf[:, :], xa, AF.Square, accum_out=st1f[:, 1:2]))
        QV.wait(aS, aS.n)
        QV.op(lambda v: v.tensor_scalar(st1f[:, 2:3], st1f[:, 0:1], 1.0 / DM, None, ALU.mult))
        QV.op(lambda v: v.tensor_scalar(st1f[:, 3:4], st1f[:, 1:2], 1.0 / DM, None, ALU.mult))
        QV.op(lambda v: v.tensor_mul(st1f[:, 4:5], st1f[:, 2:3], st1f[:, 2:3]))
        v_var = QV.op(lambda v: v.tensor_sub(st1f[:, 3:4], st1f[:, 3:4], st1f[:, 4:5]))
        QA.wait(vS, v_var)
        a_std = QA.op(lambda sc: sc.activation(st1f[:, 5:6], st1f[:, 3:4], AF.Sqrt, bias=epsln[:, 0:1]))
        QV.wait(aS, a_std)
        QV.op(lambda v: v.reciprocal(st1f[:, 6:7], st1f[:, 5:6]))
        v_nb = QV.op(lambda v: v.tensor_scalar(st1f[:, 7:8], st1f[:, 2:3], st1f[:, 6:7], -1.0,
                                               ALU.mult, ALU.mult))
        QA.wait(vS, v_nb)
        if h2_rd[buf]:
            QA.wait(dT[buf], h2_rd[buf])
        a_h = QA.op(lambda sc, ha=ha, xa=xa: sc.activation(ha, xa, AF.Identity,
                                                           bias=st1f[:, 7:8], scale=st1f[:, 6:7]))
        QS.wait(aS, a_h)
        h2_rd[buf] = QS.dma(dT[buf], h2T[:, :, tt * 128:(tt + 1) * 128], ha, transpose=True)
    d_h2T = (dT[0].n, dT[1].n)
    if debug:
        QS.wait(dT[0], d_h2T[0])
        QS.wait(dT[1], d_h2T[1])
        QS.wait(dSt[0], dSt[0].n)
        QS.wait(dSt[1], dSt[1].n)
        QS.dma(dDbg, dbg["h2T"], h2T[:, :, :])
        QS.dma(dDbg, dbg["x1"], x1_scr)
        QV.wait(dDbg, dDbg.n)
        QT.wait(dDbg, dDbg.n)
        QA.wait(dDbg, dDbg.n)
    al.free("ps_f3", "ps_f2", "ps_f1", "ps_f0")
    al.free("st1f", "sqstf", "h2st", "x1st", "xqst")
    al.free("wo_sb")
    al.free("ctx")

    sdbc = al.sb("sdbc", [128, DM], F32)
    nact = al.sb("nact", [1, NQ], BF16)
    x1r = al.sb("x1r", [128, 2, 512], F32)
    y1 = al.sb("y1", [128, 2, 512], F32)
    sdrow = al.sb("sdrow", [1, DM], F32)
    act_sb = al.sb("act_sb", [128, 44, NQ], BF16, side="right")
    slab = [al.sb("slab0", [128, 16, 512], BF16, side="right"),
            al.sb("slab1", [128, 16, 512], BF16, side="right")]
    gate_sb = al.sb("gate_sb", [128, 8, 512], BF16, side="right")
    dslab = [al.sb("dslab0", [128, 4, 512], BF16, side="right"),
             al.sb("dslab1", [128, 4, 512], BF16, side="right")]

    ps_g = [al.ps("ps_g0", [128, 512]), al.ps("ps_g1", [128, 512])]
    ps_u = [al.ps("ps_u0", [128, 512]), al.ps("ps_u1", [128, 512])]
    ps_as = al.ps("ps_as", [1, 1024])
    ps_sd = al.ps("ps_sd", [128, 512])

    d_sdr = QS.dma(dIn, sdrow[:, :], s_down_r)
    for j in range(4):
        QT.wait(dIn, d_sdr)
        QT.wait(vS, vS.n)
        t_b = QT.op(lambda t, j=j: t.matmul(ps_sd[:, :], ones_rf[0:1, 0:128],
                                            sdrow[0:1, j * 512:(j + 1) * 512],
                                            start=True, stop=True))
        QV.wait(tS, t_b)
        QV.op(lambda v, j=j: v.tensor_copy(sdbc[:, j * 512:(j + 1) * 512], ps_sd[:, :]))
    v_sdbc = vS.n
    QS.wait(vS, v_sdbc)
    al.free("sdrow")

    QS.wait(dSt[0], dSt[0].n)
    QS.wait(dSt[1], dSt[1].n)
    QS.wait(dT[0], d_h2T[0])
    QS.wait(dT[1], d_h2T[1])
    QV.wait(dT[0], d_h2T[0])
    QV.wait(dT[1], d_h2T[1])
    QA.wait(dT[0], d_h2T[0])
    QA.wait(dT[1], d_h2T[1])
    QV.wait(dSt[0], dSt[0].n)
    QV.wait(dSt[1], dSt[1].n)
    QA.wait(dSt[0], dSt[0].n)
    QA.wait(dSt[1], dSt[1].n)
    slab_user = [0, 0]
    silu_done = {}
    act_done = {}
    for sec in range(11):
        for which in range(2):
            l = 2 * sec + which
            sb_i = l % 2
            src = g_scr if which == 0 else u_scr
            QS.wait(dD2, d_d2d)
            QS.wait(tS, slab_user[sb_i])
            d_sl = QS.dma(dSl[sb_i], slab[sb_i][:, :, :],
                          src[sec * 512:(sec + 1) * 512, :], transpose=True)
            for hcl in range(4):
                hc = sec * 4 + hcl
                for nt in range(2):
                    pp = (ps_g if which == 0 else ps_u)[nt]
                    QT.wait(dSl[sb_i], d_sl)
                    QT.wait(dT[0], d_h2T[0])
                    QT.wait(dT[1], d_h2T[1])
                    if which == 0:
                        QT.wait(aS, silu_done.get((hc - 1, nt), 0))
                    else:
                        QT.wait(vS, act_done.get((hc - 1, nt), 0))
                    for c in range(16):
                        def mm(t, sb_i=sb_i, hcl=hcl, nt=nt, c=c, pp=pp):
                            return t.matmul(pp[:, :],
                                            slab[sb_i][:, c:c + 1, hcl * 128:(hcl + 1) * 128],
                                            h2T[:, c:c + 1, nt * 512:(nt + 1) * 512],
                                            start=(c == 0), stop=(c == 15))
                        t_mm = QT.op(mm) if c == 15 else QT.op_noinc(mm)
                    slab_user[sb_i] = tS.n
                    gi = hcl * 2 + nt
                    if which == 0:
                        QA.wait(tS, tS.n)
                        QA.wait(vS, act_done.get((hc - 4, nt), 0))
                        silu_done[(hc, nt)] = QA.op(lambda sc, hc=hc, gi=gi, pp=pp: sc.activation(
                            gate_sb[:, gi:gi + 1, :], pp[:, :], AF.Silu,
                            scale=sgate_c[:, hc:hc + 1]))
                    else:
                        QV.wait(tS, tS.n)
                        QV.wait(aS, silu_done[(hc, nt)])
                        a_d = QV.op(lambda v, hc=hc, nt=nt, gi=gi, pp=pp: v.scalar_tensor_tensor(
                            act_sb[:, hc:hc + 1, nt * 512:(nt + 1) * 512], pp[:, :],
                            sup_c[:, hc:hc + 1], gate_sb[:, gi:gi + 1, :],
                            ALU.mult, ALU.mult))
                        act_done[(hc, nt)] = a_d
                        QT.wait(vS, a_d)
                        def mm_as(t, hc=hc, nt=nt):
                            return t.matmul(ps_as[0:1, nt * 512:(nt + 1) * 512],
                                            ones_col[:, 0:1],
                                            act_sb[:, hc:hc + 1, nt * 512:(nt + 1) * 512],
                                            start=(hc == 0), stop=(hc == 43))
                        QT.op(mm_as) if hc == 43 else QT.op_noinc(mm_as)
    t_as = tS.n
    QA.wait(tS, t_as)
    a_nact = QA.op(lambda sc: sc.activation(nact[0:1, :], ps_as[0:1, :], AF.Copy,
                                            scale=-128.0))
    al.free("ps_sd", "ps_as", "ps_u1", "ps_u0", "ps_g1", "ps_g0")
    ps_d = [al.ps("ps_d%d" % i, [128, 512]) for i in range(8)]
    QT.wait(aS, aS.n)

    SECS = [(0, 4), (4, 4), (8, 4), (12, 4), (16, 4), (20, 4),
            (24, 4), (28, 4), (32, 4), (36, 4), (40, 4)]
    dslab_user = [0, 0]
    y_done = {}
    x1r_use = [0, 0]
    y1_use = [0, 0]
    QS.wait(dSt[0], dSt[0].n)
    QS.wait(dSt[1], dSt[1].n)
    for ds in range(4):
        for si, (h0, nch) in enumerate(SECS):
            sb_i = si % 2
            QS.wait(dD2, d_d2d)
            QS.wait(tS, dslab_user[sb_i])
            d_sl = QS.dma(dDl[sb_i], dslab[sb_i][:, 0:nch, :],
                          d_scr[ds * 512:(ds + 1) * 512, h0 * 128:(h0 + nch) * 128],
                          transpose=True)
            last = None
            for j in range(nch):
                hc = h0 + j
                for tt in range(8):
                    QT.wait(dDl[sb_i], d_sl)
                    if hc == 0:
                        QT.wait(vS, y_done.get((tt, ds - 1), 0))
                    def mm(t, sb_i=sb_i, j=j, hc=hc, tt=tt):
                        return t.matmul(ps_d[tt][:, :],
                                        act_sb[:, hc:hc + 1, tt * 128:(tt + 1) * 128],
                                        dslab[sb_i][:, j:j + 1, :],
                                        start=(hc == 0), stop=False)
                    if j == nch - 1 and tt == 7:
                        dslab_user[sb_i] = QT.op(mm)
                    else:
                        QT.op_noinc(mm)
        for tt in range(8):
            buf = tt % 2
            QT.wait(aS, a_nact)
            def mm_c(t, tt=tt):
                return t.matmul(ps_d[tt][:, :], nact[0:1, tt * 128:(tt + 1) * 128],
                                ones_rb[0:1, :], start=False, stop=True)
            t_c = QT.op(mm_c)
            QS.wait(vS, x1r_use[buf])
            d_x1 = QS.dma(dR[buf], x1r[:, buf:buf + 1, :],
                          x1_scr[tt * 128:(tt + 1) * 128, ds * 512:(ds + 1) * 512])
            QV.wait(tS, t_c)
            QV.wait(dR[buf], d_x1)
            QV.wait(vS, v_sdbc)
            QV.wait(dY[buf], y1_use[buf])
            QV.op(lambda v, ds=ds, buf=buf, tt=tt: v.tensor_mul(
                y1[:, buf:buf + 1, :], ps_d[tt][:, :],
                sdbc[:, ds * 512:(ds + 1) * 512]))
            y_done[(tt, ds)] = QV.op(lambda v, buf=buf: v.tensor_add(
                y1[:, buf:buf + 1, :], y1[:, buf:buf + 1, :], x1r[:, buf:buf + 1, :]))
            x1r_use[buf] = y_done[(tt, ds)]
            QS.wait(vS, y_done[(tt, ds)])
            y1_use[buf] = QS.dma(
                dY[buf], y_out[tt * 128:(tt + 1) * 128, ds * 512:(ds + 1) * 512],
                y1[:, buf:buf + 1, :])
    QS.wait(dY[0], dY[0].n)
    QS.wait(dY[1], dY[1].n)
    if debug:
        QS.wait(dDbg, dDbg.n)

    block_g = nc.Block()
    block = block_g.__enter__()

    def run_queue(q):
        def body(eng):
            for fn in q.ops:
                fn(eng)
        return body

    block.sync(run_queue(QS))
    block.tensor(run_queue(QT))
    block.vector(run_queue(QV))
    block.scalar(run_queue(QA))
    block.gpsimd(run_queue(QG))
    block_g.__exit__(None, None, None)

    for g in reversed(sem_guards):
        g.__exit__(None, None, None)
    return nc


_cache = {}


def kernel(**inputs):
    f = lambda k: np.ascontiguousarray(np.asarray(inputs[k], dtype=np.float32))
    x = f("L0_attn_x")
    w = {
        "w_dq": f("L0_attn_w_dq"),
        "w_uq": f("L0_attn_w_uq"),
        "w_dkv": f("L0_attn_w_dkv"),
        "w_kr": f("L0_attn_w_kr"),
        "w_uk": f("L0_attn_w_uk"),
        "w_uv": f("L0_attn_w_uv"),
        "w_o": f("L0_attn_w_o"),
        "g_q_t": np.ascontiguousarray(f("L0_attn_g_q").reshape(6, 128).T),
        "g_kv_t": np.ascontiguousarray(f("L0_attn_g_kv").reshape(4, 128).T),
        "s_gate_t": np.ascontiguousarray(f("L0_ffn_s_gate").reshape(44, 128).T),
        "s_up_t": np.ascontiguousarray(f("L0_ffn_s_up").reshape(44, 128).T),
        "s_down_r": np.ascontiguousarray(f("L0_ffn_s_down").reshape(1, DM)),
        "wq_gate": np.ascontiguousarray(np.asarray(inputs["L0_ffn_wq_gate"], dtype=np.int32)),
        "wq_up": np.ascontiguousarray(np.asarray(inputs["L0_ffn_wq_up"], dtype=np.int32)),
        "wq_down": np.ascontiguousarray(np.asarray(inputs["L0_ffn_wq_down"], dtype=np.int32)),
    }
    if "nc" not in _cache:
        _cache["nc"] = build_nc()
    nc = _cache["nc"]

    in_maps = []
    sels = []
    for core in range(8):
        b, half = core // 2, core % 2
        if half == 0:
            rows = np.r_[0:512, 1536:2048]
            g0, g1 = 0.0, 1536.0
        else:
            rows = np.r_[512:1536]
            g0, g1 = 512.0, 1024.0
        sels.append((b, rows))
        m = dict(w)
        m["x_full"] = x[b]
        m["x_q"] = np.ascontiguousarray(x[b][rows])
        m["pos_q"] = np.ascontiguousarray(rows.astype(np.float32).reshape(1, NQ))
        m["gmeta"] = np.array([[g0, g1]], dtype=np.float32)
        in_maps.append(m)

    res = run_bass_kernel_spmd(nc, in_maps, core_ids=list(range(8)))
    y = np.empty((4, S, DM), dtype=np.float32)
    for core in range(8):
        b, rows = sels[core]
        y[b][rows] = np.asarray(res.results[core]["y"], dtype=np.float32)
    return y


# version 13
# speedup vs baseline: 1.9786x; 1.0422x over previous
import math
import numpy as np
import ml_dtypes
import concourse.bass as bass
import concourse.mybir as mybir
from concourse.bass_utils import run_bass_kernel_spmd

F32 = mybir.dt.float32
BF16 = mybir.dt.bfloat16
I32 = mybir.dt.int32
AF = mybir.ActivationFunctionType
ALU = mybir.AluOpType
AX = mybir.AxisListType

S = 2048
DM = 2048
H = 16
NOPE = 128
RD = 64
VD = 128
QL = 768
KVL = 512
HID = 5632
NQ = 1024
HG = 4
NGRP = 4
SLOT_K = (8, 16)
SCALE = float(1.0 / math.sqrt(192.0))
LN_EPS = 1e-5
RMS_EPS = 1e-6
MAGIC = 8388608.0

C_HI = 6.28125
C_MID = float(np.float32(2 * np.pi - C_HI))
C_LO = float(np.float64(2 * np.pi) - np.float64(C_HI) - np.float64(np.float32(C_MID)))
INV2PI = float(1.0 / (2 * np.pi))
LOG1E4_32 = float(np.log(10000.0) / 32.0)


class Ctr:
    def __init__(self, handle):
        self.h = handle
        self.n = 0

    def inc(self, k=1):
        self.n += k
        return self.n


class EngQ:
    def __init__(self, ctr, chain):
        self.ops = []
        self.ctr = ctr
        self.chain = chain

    def wait(self, ctr, val):
        if val > 0:
            self.ops.append(lambda e, s=ctr.h, v=val: e.wait_ge(s, v))

    def op(self, fn):
        if self.chain and self.ctr.n > 0:
            self.ops.append(lambda e, s=self.ctr.h, v=self.ctr.n: e.wait_ge(s, v))
        val = self.ctr.inc()
        def run(e, fn=fn, s=self.ctr.h):
            fn(e).then_inc(s, 1)
        self.ops.append(run)
        return val

    def op_noinc(self, fn):
        self.ops.append(lambda e, fn=fn: fn(e))

    def dma(self, ctr, out, in_, transpose=False):
        val = ctr.inc(16)
        def run(e, out=out, in_=in_, s=ctr.h, tr=transpose):
            e.dma_start(out=out, in_=in_, transpose=tr).then_inc(s, 16)
        self.ops.append(run)
        return val


def build_nc(debug=False):
    nc = bass.Bass("TRN2", target_bir_lowering=False)

    x_full = nc.dram_tensor("x_full", [S, DM], F32, kind="ExternalInput").ap()
    x_q = nc.dram_tensor("x_q", [NQ, DM], F32, kind="ExternalInput").ap()
    pos_q = nc.dram_tensor("pos_q", [1, NQ], F32, kind="ExternalInput").ap()
    gmeta = nc.dram_tensor("gmeta", [1, 2], F32, kind="ExternalInput").ap()
    w_dq = nc.dram_tensor("w_dq", [DM, QL], F32, kind="ExternalInput").ap()
    g_q_t = nc.dram_tensor("g_q_t", [128, 6], F32, kind="ExternalInput").ap()
    w_uq = nc.dram_tensor("w_uq", [QL, H * 192], F32, kind="ExternalInput").ap()
    w_dkv = nc.dram_tensor("w_dkv", [DM, KVL], F32, kind="ExternalInput").ap()
    w_kr = nc.dram_tensor("w_kr", [DM, RD], F32, kind="ExternalInput").ap()
    g_kv_t = nc.dram_tensor("g_kv_t", [128, 4], F32, kind="ExternalInput").ap()
    w_uk = nc.dram_tensor("w_uk", [KVL, H * NOPE], F32, kind="ExternalInput").ap()
    w_uv = nc.dram_tensor("w_uv", [KVL, H * VD], F32, kind="ExternalInput").ap()
    w_o = nc.dram_tensor("w_o", [H * VD, DM], F32, kind="ExternalInput").ap()
    wq_gate = nc.dram_tensor("wq_gate", [HID, DM], I32, kind="ExternalInput").ap()
    s_gate_t = nc.dram_tensor("s_gate_t", [128, 44], F32, kind="ExternalInput").ap()
    wq_up = nc.dram_tensor("wq_up", [HID, DM], I32, kind="ExternalInput").ap()
    s_up_t = nc.dram_tensor("s_up_t", [128, 44], F32, kind="ExternalInput").ap()
    wq_down = nc.dram_tensor("wq_down", [DM, HID], I32, kind="ExternalInput").ap()
    s_down_r = nc.dram_tensor("s_down_r", [1, DM], F32, kind="ExternalInput").ap()
    y_out = nc.dram_tensor("y", [NQ, DM], F32, kind="ExternalOutput").ap()

    g_scr = nc.dram_tensor("g_scr", [HID, DM], BF16, kind="Internal").ap()
    u_scr = nc.dram_tensor("u_scr", [HID, DM], BF16, kind="Internal").ap()
    d_scr = nc.dram_tensor("d_scr", [DM, HID], BF16, kind="Internal").ap()
    x1_scr = nc.dram_tensor("x1_scr", [NQ, DM], F32, kind="Internal").ap()

    dbg = {}
    if debug:
        def dbg_t(name, shape, dt=F32):
            dbg[name] = nc.dram_tensor("dbg_" + name, shape, dt, kind="ExternalOutput").ap()
        dbg_t("cq", [128, 6, NQ], BF16)
        dbg_t("kvT", [128, 4, S], BF16)
        dbg_t("kpe", [65, S], BF16)
        dbg_t("qno", [128, HG, NQ], BF16)
        dbg_t("qpe", [65, HG * NQ], BF16)
        dbg_t("kno", [128, HG, S], BF16)
        dbg_t("vtok", [128, 16, HG * VD], BF16)
        dbg_t("ctx", [128, 16, NQ], BF16)
        dbg_t("x1", [NQ, DM], F32)
        dbg_t("h2T", [128, 16, NQ], BF16)
        dbg_t("masks", [128, 16, 512], BF16)
        dbg_t("cosq", [32, NQ], F32)
        dbg_t("kmaxn", [1, 16], F32)

    sem_guards = []
    def mksem(name):
        g = nc.semaphore(name)
        h = g.__enter__()
        sem_guards.append(g)
        return Ctr(h)

    dIn = mksem("dIn")
    dX = [mksem("dX0"), mksem("dX1")]
    dT = [mksem("dT0"), mksem("dT1")]
    dW = mksem("dW")
    dW0 = mksem("dW0")
    dD2 = mksem("dD2")
    dB = mksem("dB")
    dSt = [mksem("dSt0"), mksem("dSt1")]
    dSl = [mksem("dSl0"), mksem("dSl1")]
    dDl = [mksem("dDl0"), mksem("dDl1")]
    dR = [mksem("dR0"), mksem("dR1")]
    dY = [mksem("dY0"), mksem("dY1")]
    dDbg = mksem("dDbg")
    tS = mksem("tS")
    vS = mksem("vS")
    aS = mksem("aS")
    gS = mksem("gS")

    QT = EngQ(tS, chain=False)
    QV = EngQ(vS, chain=True)
    QA = EngQ(aS, chain=True)
    QG = EngQ(gS, chain=True)
    QS = EngQ(None, chain=False)

    class Alloc:
        def __init__(self):
            self.guards = {}
            self.order = {}
            self.k = 0

        def sb(self, name, shape, dt, side=None):
            g = nc.sbuf_tensor("%s__%d" % (name, self.k), shape, dt, side=side)
            t = g.__enter__()
            self.guards[name] = g
            self.order[name] = self.k
            self.k += 1
            return t

        def ps(self, name, shape):
            g = nc.psum_tensor("%s__%d" % (name, self.k), shape, F32)
            t = g.__enter__()
            self.guards[name] = g
            self.order[name] = self.k
            self.k += 1
            return t

        def free(self, *names):
            for name in sorted(names, key=lambda n: -self.order[n]):
                self.guards.pop(name).__exit__(None, None, None)
                self.order.pop(name)

    al = Alloc()

    ones_col = al.sb("ones_col", [128, 1], BF16)
    ones_cf = al.sb("ones_cf", [128, 1], F32)
    epsln = al.sb("epsln", [128, 1], F32)
    epsrms = al.sb("epsrms", [128, 1], F32)
    ones_mat = al.sb("ones_mat", [128, 128], BF16)
    ones_rf = al.sb("ones_rf", [1, 128], F32)
    ones_rb = al.sb("ones_rb", [1, 512], BF16)
    sgate_c = al.sb("sgate_c", [128, 44], F32)
    sup_c = al.sb("sup_c", [128, 44], F32)
    gq_c = al.sb("gq_c", [128, 6], F32)
    gkv_c = al.sb("gkv_c", [128, 4], F32)
    gmeta_sb = al.sb("gmeta_sb", [1, 2], F32)
    gcol = al.sb("gcol", [128, 2], F32)
    iota_pq = al.sb("iota_pq", [128, 512], F32)
    kmaxn = al.sb("kmaxn", [1, 16], F32)
    cosq = al.sb("cosq", [32, NQ], F32)
    sinq = al.sb("sinq", [32, NQ], F32)
    masks = al.sb("masks", [128, 16, 512], BF16)
    cq_sb = al.sb("cq_sb", [128, 6, NQ], BF16)
    kvT = al.sb("kvT", [128, 4, S], BF16)
    kpe_ext = al.sb("kpe_ext", [65, S], BF16)
    kpe_sq = al.sb("kpe_sq", [64, S], BF16)

    Wp = S + NQ
    cosk = al.sb("cosk", [32, S], F32, side="right")
    sink = al.sb("sink", [32, S], F32, side="right")
    posq_sb = al.sb("posq_sb", [1, NQ], F32, side="right")
    rt_ang = al.sb("rt_ang", [32, Wp], F32, side="right")
    rt_k = al.sb("rt_k", [32, Wp], F32, side="right")
    rt_t1 = al.sb("rt_t1", [32, Wp], F32, side="right")
    rt_t2 = al.sb("rt_t2", [32, Wp], F32, side="right")
    invf = al.sb("invf", [32, 1], F32, side="right")
    invfr = al.sb("invfr", [32, 1], F32, side="right")
    em = al.sb("em", [128, 2, 512], F32, side="right")

    for out_t, in_t in ((gmeta_sb[:, :], gmeta), (posq_sb[:, :], pos_q),
                        (sgate_c[:, :], s_gate_t), (sup_c[:, :], s_up_t),
                        (gq_c[:, :], g_q_t), (gkv_c[:, :], g_kv_t)):
        QS.dma(dIn, out_t, in_t)
    d_small = dIn.n

    QG.op(lambda g: g.iota(iota_pq[:, :], [[-1, 512]], base=0, channel_multiplier=1,
                           allow_small_or_imprecise_dtypes=True))
    QG.op(lambda g: g.iota(invfr[:, :], [[0, 1]], base=0, channel_multiplier=1,
                           allow_small_or_imprecise_dtypes=True))
    g_iota = QG.op(lambda g: g.iota(rt_ang[:, 0:S], [[1, S]], base=0, channel_multiplier=0,
                                    allow_small_or_imprecise_dtypes=True))

    QV.op(lambda v: v.memset(ones_col[:, :], 1.0))
    QV.op(lambda v: v.memset(ones_cf[:, :], 1.0))
    QV.op(lambda v: v.memset(epsln[:, :], LN_EPS))
    QV.op(lambda v: v.memset(epsrms[:, :], RMS_EPS))
    QV.op(lambda v: v.memset(kmaxn[:, :], 0.0))
    QV.op(lambda v: v.memset(ones_mat[:, :], 1.0))
    QV.op(lambda v: v.memset(ones_rb[:, :], 1.0))
    v_ones = QV.op(lambda v: v.memset(ones_rf[:, :], 1.0))

    QA.wait(gS, g_iota)
    a_invf = QA.op(lambda sc: sc.activation(invf[:, :], invfr[:, :], AF.Exp, scale=-LOG1E4_32))

    ps_b32 = al.ps("ps_b32", [32, 512])
    ps_b128 = al.ps("ps_b128", [128, 2])
    QT.wait(dIn, d_small)
    QT.wait(vS, v_ones)
    for i in range(NQ // 512):
        QT.wait(vS, vS.n)
        t_b = QT.op(lambda t, i=i: t.matmul(ps_b32[:, :], ones_rf[0:1, 0:32],
                                            posq_sb[0:1, i * 512:(i + 1) * 512],
                                            start=True, stop=True))
        QV.wait(tS, t_b)
        QV.wait(gS, g_iota)
        QV.op(lambda v, i=i: v.tensor_copy(rt_ang[:, S + i * 512:S + (i + 1) * 512],
                                           ps_b32[:, :]))
    t_g = QT.op(lambda t: t.matmul(ps_b128[:, :], ones_rf[0:1, 0:128], gmeta_sb[0:1, 0:2],
                                   start=True, stop=True))
    QV.wait(tS, t_g)
    QV.op(lambda v: v.tensor_copy(gcol[:, :], ps_b128[:, :]))

    QV.wait(aS, a_invf)
    QV.op(lambda v: v.tensor_scalar(rt_ang[:, :], rt_ang[:, :], invf[:, :], None, ALU.mult))
    QV.op(lambda v: v.tensor_scalar(rt_k[:, :], rt_ang[:, :], INV2PI, None, ALU.mult))
    QV.op(lambda v: v.tensor_scalar(rt_k[:, :], rt_k[:, :], MAGIC, None, ALU.add))
    QV.op(lambda v: v.tensor_scalar(rt_k[:, :], rt_k[:, :], -MAGIC, None, ALU.add))
    QV.op(lambda v: v.tensor_scalar(rt_t1[:, :], rt_k[:, :], C_HI, None, ALU.mult))
    QV.op(lambda v: v.tensor_sub(rt_ang[:, :], rt_ang[:, :], rt_t1[:, :]))
    QV.op(lambda v: v.tensor_scalar(rt_t1[:, :], rt_k[:, :], C_MID, None, ALU.mult))
    QV.op(lambda v: v.tensor_sub(rt_ang[:, :], rt_ang[:, :], rt_t1[:, :]))
    QV.op(lambda v: v.tensor_scalar(rt_t1[:, :], rt_k[:, :], C_LO, None, ALU.mult))
    v_red = QV.op(lambda v: v.tensor_sub(rt_ang[:, :], rt_ang[:, :], rt_t1[:, :]))
    QA.wait(vS, v_red)
    a_s4 = QA.op(lambda sc: sc.activation(rt_k[:, :], rt_ang[:, :], AF.Sin, scale=0.25))
    QV.wait(aS, a_s4)
    v_t4 = QV.op(lambda v: v.tensor_mul(rt_t1[:, :], rt_k[:, :], rt_k[:, :]))
    QA.wait(vS, v_t4)
    a_c4 = QA.op(lambda sc: sc.activation(rt_t2[:, :], rt_t1[:, :], AF.Sqrt, bias=1.0, scale=-1.0))
    QV.wait(aS, a_c4)
    QV.op(lambda v: v.tensor_mul(rt_k[:, :], rt_k[:, :], rt_t2[:, :]))
    QV.op(lambda v: v.tensor_scalar(rt_k[:, :], rt_k[:, :], 2.0, None, ALU.mult))
    QV.op(lambda v: v.tensor_scalar(rt_t2[:, :], rt_t1[:, :], -2.0, 1.0, ALU.mult, ALU.add))
    QV.op(lambda v: v.tensor_mul(rt_t1[:, :], rt_k[:, :], rt_t2[:, :]))
    QV.op(lambda v: v.tensor_scalar(rt_t1[:, :], rt_t1[:, :], 2.0, None, ALU.mult))
    QV.op(lambda v: v.tensor_copy(sink[:, :], rt_t1[:, 0:S]))
    QV.op(lambda v: v.tensor_copy(sinq[:, :], rt_t1[:, S:Wp]))
    QV.op(lambda v: v.tensor_mul(rt_t1[:, :], rt_k[:, :], rt_k[:, :]))
    QV.op(lambda v: v.tensor_scalar(rt_t1[:, :], rt_t1[:, :], -2.0, 1.0, ALU.mult, ALU.add))
    QV.op(lambda v: v.tensor_copy(cosk[:, :], rt_t1[:, 0:S]))
    v_trig = QV.op(lambda v: v.tensor_copy(cosq[:, :], rt_t1[:, S:Wp]))
    if debug:
        QS.wait(vS, v_trig)
        QS.dma(dDbg, dbg["cosq"], cosq[:, :])

    QV.op(lambda v: v.tensor_scalar(em[:, 0:1, :], iota_pq[:, :], gcol[:, 0:1], None, ALU.subtract))
    QV.op(lambda v: v.tensor_scalar(em[:, 1:2, :], iota_pq[:, :], gcol[:, 1:2], None, ALU.subtract))
    for s in range(2):
        for j in range(8):
            kc = j if s == 0 else 8 + j
            QV.op(lambda v, s=s, j=j, kc=kc: v.tensor_scalar(
                masks[:, s * 8 + j:s * 8 + j + 1, :], em[:, s:s + 1, :],
                float(-128.0 * kc), None, ALU.is_le))
    v_masks = vS.n
    if debug:
        QS.wait(vS, v_masks)
        QS.dma(dDbg, dbg["masks"], masks[:, :, :])
    al.free("em", "invfr", "invf", "rt_t2", "rt_t1", "rt_k", "rt_ang", "posq_sb")
    al.free("ps_b32", "ps_b128")

    xst = al.sb("xst", [128, 2, DM], F32, side="right")
    hst = al.sb("hst", [128, 2, DM], BF16, side="right")
    sqst = al.sb("sqst", [128, DM], F32, side="right")
    st1 = al.sb("st1", [128, 8], F32, side="right")
    hwin = al.sb("hwin", [128, 32, 512], BF16, side="right")
    rka = al.sb("rka", [32, 512], F32, side="right")
    rkb = al.sb("rkb", [32, 512], F32, side="right")
    rkbf = al.sb("rkbf", [32, 512], BF16, side="right")
    sqw = al.sb("sqw", [128, 512], F32, side="right")
    rrow = al.sb("rrow", [1, 512], F32, side="right")
    rbc = al.sb("rbc", [128, 512], F32, side="right")

    x_last_rd = [0, 0]
    h_last_rd = [0, 0]

    def ln_tile(src_rows, dst_col, buf, hb):
        xa = xst[:, buf:buf + 1, :]
        ha = hst[:, buf:buf + 1, :]
        if x_last_rd[buf]:
            QS.wait(aS, x_last_rd[buf])
        d_x = QS.dma(dX[buf], xa, src_rows)
        QV.wait(dX[buf], d_x)
        QV.op(lambda v: v.reduce_sum(st1[:, 0:1], xa, AX.X))
        QA.wait(dX[buf], d_x)
        QA.op(lambda sc: sc.activation(sqst[:, :], xa, AF.Square, accum_out=st1[:, 1:2]))
        QV.wait(aS, aS.n)
        QV.op(lambda v: v.tensor_scalar(st1[:, 2:3], st1[:, 0:1], 1.0 / DM, None, ALU.mult))
        QV.op(lambda v: v.tensor_scalar(st1[:, 3:4], st1[:, 1:2], 1.0 / DM, None, ALU.mult))
        QV.op(lambda v: v.tensor_mul(st1[:, 4:5], st1[:, 2:3], st1[:, 2:3]))
        v_var = QV.op(lambda v: v.tensor_sub(st1[:, 3:4], st1[:, 3:4], st1[:, 4:5]))
        QA.wait(vS, v_var)
        a_std = QA.op(lambda sc: sc.activation(st1[:, 5:6], st1[:, 3:4], AF.Sqrt, bias=epsln[:, 0:1]))
        QV.wait(aS, a_std)
        QV.op(lambda v: v.reciprocal(st1[:, 6:7], st1[:, 5:6]))
        v_nb = QV.op(lambda v: v.tensor_scalar(st1[:, 7:8], st1[:, 2:3], st1[:, 6:7], -1.0,
                                               ALU.mult, ALU.mult))
        QA.wait(vS, v_nb)
        if h_last_rd[buf]:
            QA.wait(dT[buf], h_last_rd[buf])
        a_h = QA.op(lambda sc: sc.activation(ha, xa, AF.Identity, bias=st1[:, 7:8], scale=st1[:, 6:7]))
        x_last_rd[buf] = a_h
        QS.wait(aS, a_h)
        d_tr = QS.dma(dT[buf], hwin[:, hb * 16:(hb + 1) * 16, dst_col:dst_col + 128],
                      ha, transpose=True)
        h_last_rd[buf] = d_tr
        return d_tr

    wdkv_sb = al.sb("wdkv_sb", [128, 16, KVL], BF16, side="right")
    wkr_sb = al.sb("wkr_sb", [128, 16, RD], BF16, side="right")
    QG.wait(vS, vS.n)
    QG.wait(aS, aS.n)
    QG.dma(dW, wdkv_sb[:, :, :], w_dkv.rearrange("(c p) f -> p c f", p=128))
    d_wkr = QG.dma(dW, wkr_sb[:, :, :], w_kr.rearrange("(c p) f -> p c f", p=128))
    QS.wait(vS, vS.n)

    ps_r = al.ps("ps_r", [1, 512])
    ps_bw = al.ps("ps_bw", [128, 512])
    ps_c = [al.ps("ps_c%d" % i, [128, 512]) for i in range(4)]
    ps_kp1 = al.ps("ps_kp1", [32, 512])
    ps_kp2 = al.ps("ps_kp2", [32, 512])

    hwin_rd = {0: 0, 1: 0}

    def emit_ln4(src, tile0, hb2):
        QS.wait(tS, hwin_rd[hb2])
        for j in range(4):
            ln_tile(src[(tile0 + j) * 128:(tile0 + j + 1) * 128, :],
                    j * 128, j % 2, hb2)

    emit_ln4(x_full, 0, 0)
    for wdx in range(4):
        wc = wdx * 512
        hb = wdx % 2
        QT.wait(dT[0], dT[0].n)
        QT.wait(dT[1], dT[1].n)
        QT.wait(dW, d_wkr)
        QT.wait(vS, vS.n)
        for m in range(4):
            for c in range(16):
                def mm(t, m=m, c=c, hb=hb):
                    return t.matmul(ps_c[m][:, :],
                                    wdkv_sb[:, c:c + 1, m * 128:(m + 1) * 128],
                                    hwin[:, hb * 16 + c:hb * 16 + c + 1, :],
                                    start=(c == 0), stop=(c == 15))
                QT.op(mm) if c == 15 else QT.op_noinc(mm)
        for c in range(16):
            def mm1(t, c=c, hb=hb):
                return t.matmul(ps_kp1[:, :], wkr_sb[:, c:c + 1, 0:32],
                                hwin[:, hb * 16 + c:hb * 16 + c + 1, :],
                                start=(c == 0), stop=(c == 15))
            def mm2(t, c=c, hb=hb):
                return t.matmul(ps_kp2[:, :], wkr_sb[:, c:c + 1, 32:64],
                                hwin[:, hb * 16 + c:hb * 16 + c + 1, :],
                                start=(c == 0), stop=(c == 15))
            QT.op_noinc(mm1)
            QT.op(mm2) if c == 15 else QT.op_noinc(mm2)
        hwin_rd[hb] = tS.n
        if wdx < 3:
            emit_ln4(x_full, 4 * (wdx + 1), (wdx + 1) % 2)
        else:
            emit_ln4(x_q, 0, 0)
        QA.wait(tS, hwin_rd[hb])
        for m in range(4):
            a_s = QA.op(lambda sc, m=m: sc.activation(sqw[:, :], ps_c[m][:, :], AF.Square))
            QT.wait(aS, a_s)
            def mm(t, m=m):
                return t.matmul(ps_r[0:1, :], ones_cf[:, 0:1], sqw[:, :],
                                start=(m == 0), stop=(m == 3))
            t_sq = QT.op(mm)
            QA.wait(tS, t_sq)
        QA.wait(tS, tS.n)
        a_r = QA.op(lambda sc: sc.activation(rrow[0:1, :], ps_r[0:1, :], AF.Sqrt,
                                             scale=1.0 / KVL, bias=epsrms[0:1, 0:1]))
        QV.wait(aS, a_r)
        QV.op(lambda v: v.reciprocal(rrow[0:1, :], rrow[0:1, :]))
        QT.wait(vS, vS.n)
        t_b = QT.op(lambda t: t.matmul(ps_bw[:, :], ones_rf[0:1, 0:128], rrow[0:1, :],
                                       start=True, stop=True))
        QV.wait(tS, t_b)
        QV.op(lambda v: v.tensor_copy(rbc[:, :], ps_bw[:, :]))
        for m in range(4):
            QV.op(lambda v, m=m, wc=wc: v.scalar_tensor_tensor(
                kvT[:, m:m + 1, wc:wc + 512], ps_c[m][:, :],
                gkv_c[:, m:m + 1], rbc[:, :], ALU.mult, ALU.mult))
        csl = slice(wc, wc + 512)
        QV.op(lambda v, csl=csl: v.tensor_mul(rka[0:32, :], ps_kp1[:, :], cosk[:, csl]))
        QV.op(lambda v, csl=csl: v.tensor_mul(rkb[0:32, :], ps_kp2[:, :], sink[:, csl]))
        QV.op(lambda v: v.tensor_sub(rka[0:32, :], rka[0:32, :], rkb[0:32, :]))
        QV.op(lambda v, csl=csl: v.tensor_copy(kpe_ext[0:32, csl], rka[0:32, :]))
        QV.op(lambda v, csl=csl: v.tensor_mul(rka[0:32, :], ps_kp1[:, :], sink[:, csl]))
        QV.op(lambda v, csl=csl: v.tensor_mul(rkb[0:32, :], ps_kp2[:, :], cosk[:, csl]))
        QV.op(lambda v: v.tensor_add(rka[0:32, :], rka[0:32, :], rkb[0:32, :]))
        QV.wait(dB, dB.n)
        v_k2 = QV.op(lambda v: v.tensor_copy(rkbf[0:32, :], rka[0:32, :]))
        QS.wait(vS, v_k2)
        QS.dma(dB, kpe_ext[32:64, csl], rkbf[0:32, :])
    QV.wait(dB, dB.n)
    QV.op(lambda v: v.memset(kpe_ext[64:65, :], 1.0))
    v_kv = vS.n
    QA.wait(vS, v_kv)
    a_kpsq = QA.op(lambda sc: sc.activation(kpe_sq[0:64, :], kpe_ext[0:64, :], AF.Square))
    if debug:
        QS.wait(vS, v_kv)
        QS.dma(dDbg, dbg["kvT"], kvT[:, :, :])
        QS.dma(dDbg, dbg["kpe"], kpe_ext[:, :])
        QV.wait(dDbg, dDbg.n)
        QT.wait(dDbg, dDbg.n)
        QA.wait(dDbg, dDbg.n)
    al.free("ps_kp2", "ps_kp1", "ps_c3", "ps_c2", "ps_c1", "ps_c0")
    al.free("wkr_sb", "wdkv_sb")

    wdq_sb = al.sb("wdq_sb", [128, 16, QL], BF16, side="right")
    QG.wait(tS, tS.n)
    d_wdq = QG.dma(dW, wdq_sb[:, :, :], w_dq.rearrange("(c p) f -> p c f", p=128))
    ps_c2 = [al.ps("ps_c2%d" % i, [128, 512]) for i in range(6)]

    for wdx in range(2):
        wc = wdx * 512
        hb = wdx % 2
        QT.wait(dT[0], dT[0].n)
        QT.wait(dT[1], dT[1].n)
        QT.wait(dW, d_wdq)
        QT.wait(vS, vS.n)
        for m in range(6):
            for c in range(16):
                def mm(t, m=m, c=c, hb=hb):
                    return t.matmul(ps_c2[m][:, :],
                                    wdq_sb[:, c:c + 1, m * 128:(m + 1) * 128],
                                    hwin[:, hb * 16 + c:hb * 16 + c + 1, :],
                                    start=(c == 0), stop=(c == 15))
                QT.op(mm) if c == 15 else QT.op_noinc(mm)
        hwin_rd[hb] = tS.n
        if wdx == 0:
            emit_ln4(x_q, 4, 1)
        QA.wait(tS, hwin_rd[hb])
        for m in range(6):
            a_s = QA.op(lambda sc, m=m: sc.activation(sqw[:, :], ps_c2[m][:, :], AF.Square))
            QT.wait(aS, a_s)
            def mm(t, m=m):
                return t.matmul(ps_r[0:1, :], ones_cf[:, 0:1], sqw[:, :],
                                start=(m == 0), stop=(m == 5))
            t_sq = QT.op(mm)
            QA.wait(tS, t_sq)
        QA.wait(tS, tS.n)
        a_r = QA.op(lambda sc: sc.activation(rrow[0:1, :], ps_r[0:1, :], AF.Sqrt,
                                             scale=1.0 / QL, bias=epsrms[0:1, 0:1]))
        QV.wait(aS, a_r)
        QV.op(lambda v: v.reciprocal(rrow[0:1, :], rrow[0:1, :]))
        QT.wait(vS, vS.n)
        t_b = QT.op(lambda t: t.matmul(ps_bw[:, :], ones_rf[0:1, 0:128], rrow[0:1, :],
                                       start=True, stop=True))
        QV.wait(tS, t_b)
        QV.op(lambda v: v.tensor_copy(rbc[:, :], ps_bw[:, :]))
        for m in range(6):
            QV.op(lambda v, m=m, wc=wc: v.scalar_tensor_tensor(
                cq_sb[:, m:m + 1, wc:wc + 512], ps_c2[m][:, :],
                gq_c[:, m:m + 1], rbc[:, :], ALU.mult, ALU.mult))
    v_cq = vS.n
    if debug:
        QS.wait(vS, v_cq)
        QS.dma(dDbg, dbg["cq"], cq_sb[:, :, :])
        QV.wait(dDbg, dDbg.n)
        QT.wait(dDbg, dDbg.n)
        QA.wait(dDbg, dDbg.n)
    al.free("ps_c25", "ps_c24", "ps_c23", "ps_c22", "ps_c21", "ps_c20", "ps_bw", "ps_r")
    al.free("wdq_sb")
    al.free("rbc", "rrow", "sqw", "rkbf", "rkb", "rka", "hwin", "st1", "sqst", "hst", "xst")
    al.free("sink", "cosk")

    wuq0_sb = al.sb("wuq0_sb", [128, 6, HG * 192], BF16)
    wuk0_sb = al.sb("wuk0_sb", [128, 4, HG * NOPE], BF16)
    wuv0_sb = al.sb("wuv0_sb", [128, 4, HG * VD], BF16)
    QG.wait(dX[0], dX[0].n)
    QG.wait(dX[1], dX[1].n)
    QG.dma(dW0, wuq0_sb[:, :, :],
           w_uq[:, 0:HG * 192].rearrange("(c p) f -> p c f", p=128))
    QG.dma(dW0, wuk0_sb[:, :, :],
           w_uk[:, 0:HG * NOPE].rearrange("(c p) f -> p c f", p=128))
    d_wg0 = QG.dma(dW0, wuv0_sb[:, :, :],
                   w_uv[:, 0:HG * VD].rearrange("(c p) f -> p c f", p=128))

    cast_pieces = []
    for j in range(4):
        cast_pieces.append((g_scr[j * 1408:(j + 1) * 1408, :],
                            wq_gate[j * 1408:(j + 1) * 1408, :]))
        cast_pieces.append((u_scr[j * 1408:(j + 1) * 1408, :],
                            wq_up[j * 1408:(j + 1) * 1408, :]))
        cast_pieces.append((d_scr[j * 512:(j + 1) * 512, :],
                            wq_down[j * 512:(j + 1) * 512, :]))

    q_no = al.sb("q_no", [128, HG, NQ], BF16)
    qpe_ext = al.sb("qpe_ext", [65, HG * NQ], BF16)
    kno = al.sb("kno", [128, HG, S], BF16)
    vtok = al.sb("vtok", [128, 16, HG * VD], BF16)
    sqs = al.sb("sqs", [128, S], BF16)
    qn_row = al.sb("qn_row", [1, NQ], F32)
    bnd_row = al.sb("bnd_row", [1, NQ], BF16)
    kmx = al.sb("kmx", [1, 8], F32)
    p_sb = al.sb("p_sb", [128, 3, 512], BF16)
    rec_sb = al.sb("rec_sb", [128, 512], F32)
    rpq1 = al.sb("rpq1", [32, NQ], F32)
    rpq2 = al.sb("rpq2", [32, NQ], F32)
    rpqbf = al.sb("rpqbf", [32, NQ], BF16)
    ctx = al.sb("ctx", [128, 16, NQ], BF16, side="right")

    ctx_evac = {}
    last_exp_bank = [0, 0]
    last_av_pbuf = [0, 0, 0]
    d_wo = 0
    wo_sb = None

    for grp in range(NGRP):
        if grp == 0:
            wuq_sb, wuk_sb, wuv_sb = wuq0_sb, wuk0_sb, wuv0_sb
            d_wg, s_wg = d_wg0, dW0
        else:
            wuq_sb = al.sb("wuq_sb", [128, 6, HG * 192], BF16, side="right")
            wuk_sb = al.sb("wuk_sb", [128, 4, HG * NOPE], BF16, side="right")
            wuv_sb = al.sb("wuv_sb", [128, 4, HG * VD], BF16, side="right")
            QG.wait(tS, t_dend)
            QG.dma(dW, wuq_sb[:, :, :],
                   w_uq[:, grp * HG * 192:(grp + 1) * HG * 192]
                   .rearrange("(c p) f -> p c f", p=128))
            QG.dma(dW, wuk_sb[:, :, :],
                   w_uk[:, grp * HG * NOPE:(grp + 1) * HG * NOPE]
                   .rearrange("(c p) f -> p c f", p=128))
            d_wg = QG.dma(dW, wuv_sb[:, :, :],
                          w_uv[:, grp * HG * VD:(grp + 1) * HG * VD]
                          .rearrange("(c p) f -> p c f", p=128))
            s_wg = dW

        ps_a = [al.ps("ps_a0", [128, 512]), al.ps("ps_a1", [128, 512])]
        ps_p1 = al.ps("ps_p1", [32, 1024])
        ps_p2 = al.ps("ps_p2", [32, 1024])

        QT.wait(s_wg, d_wg)
        QT.wait(vS, vS.n)
        QT.wait(aS, aS.n)
        evac_done = [vS.n, vS.n]
        for hl in range(HG):
            for nt in range(2):
                pb = (hl * 2 + nt) % 2
                QT.wait(vS, evac_done[pb])
                for c in range(6):
                    def mm(t, hl=hl, nt=nt, c=c, pa=ps_a[pb], w=wuq_sb):
                        return t.matmul(pa[:, :],
                                        w[:, c:c + 1, hl * 192:hl * 192 + 128],
                                        cq_sb[:, c:c + 1, nt * 512:(nt + 1) * 512],
                                        start=(c == 0), stop=(c == 5))
                    QT.op(mm) if c == 5 else QT.op_noinc(mm)
                QV.wait(tS, tS.n)
                evac_done[pb] = QV.op(lambda v, hl=hl, nt=nt, pa=ps_a[pb]: v.tensor_copy(
                    q_no[:, hl:hl + 1, nt * 512:(nt + 1) * 512], pa[:, :]))

        for hl in range(HG):
            co = hl * NQ
            QT.wait(vS, vS.n)
            for c in range(6):
                for nt in range(2):
                    def mm1(t, hl=hl, c=c, nt=nt, pp=ps_p1, w=wuq_sb):
                        return t.matmul(pp[:, nt * 512:(nt + 1) * 512],
                                        w[:, c:c + 1,
                                          hl * 192 + 128:hl * 192 + 160],
                                        cq_sb[:, c:c + 1, nt * 512:(nt + 1) * 512],
                                        start=(c == 0), stop=(c == 5))
                    def mm2(t, hl=hl, c=c, nt=nt, pp=ps_p2, w=wuq_sb):
                        return t.matmul(pp[:, nt * 512:(nt + 1) * 512],
                                        w[:, c:c + 1,
                                          hl * 192 + 160:hl * 192 + 192],
                                        cq_sb[:, c:c + 1, nt * 512:(nt + 1) * 512],
                                        start=(c == 0), stop=(c == 5))
                    QT.op_noinc(mm1)
                    QT.op(mm2) if (c == 5 and nt == 1) else QT.op_noinc(mm2)
            t_pe = tS.n
            QV.wait(tS, t_pe)
            QV.op(lambda v, pp=ps_p1: v.tensor_mul(rpq1[:, :], pp[:, :], cosq[:, :]))
            QV.op(lambda v, pp=ps_p2: v.tensor_mul(rpq2[:, :], pp[:, :], sinq[:, :]))
            QV.op(lambda v: v.tensor_sub(rpq1[:, :], rpq1[:, :], rpq2[:, :]))
            QV.op(lambda v, co=co: v.tensor_copy(qpe_ext[0:32, co:co + NQ], rpq1[:, :]))
            QV.op(lambda v, pp=ps_p1: v.tensor_mul(rpq1[:, :], pp[:, :], sinq[:, :]))
            QV.op(lambda v, pp=ps_p2: v.tensor_mul(rpq2[:, :], pp[:, :], cosq[:, :]))
            QV.op(lambda v: v.tensor_add(rpq1[:, :], rpq1[:, :], rpq2[:, :]))
            QV.wait(dB, dB.n)
            v_q2 = QV.op(lambda v: v.tensor_copy(rpqbf[:, :], rpq1[:, :]))
            QS.wait(vS, v_q2)
            QS.dma(dB, qpe_ext[32:64, co:co + NQ], rpqbf[:, :])

        QT.wait(vS, v_kv)
        for hl in range(HG):
            for nt in range(4):
                pb = (hl * 4 + nt) % 2
                QT.wait(vS, evac_done[pb])
                for c in range(4):
                    def mm(t, hl=hl, nt=nt, c=c, pa=ps_a[pb], w=wuk_sb):
                        return t.matmul(pa[:, :],
                                        w[:, c:c + 1, hl * 128:(hl + 1) * 128],
                                        kvT[:, c:c + 1, nt * 512:(nt + 1) * 512],
                                        start=(c == 0), stop=(c == 3))
                    QT.op(mm) if c == 3 else QT.op_noinc(mm)
                QV.wait(tS, tS.n)
                evac_done[pb] = QV.op(lambda v, hl=hl, nt=nt, pa=ps_a[pb]: v.tensor_copy(
                    kno[:, hl:hl + 1, nt * 512:(nt + 1) * 512], pa[:, :]))

        for kt in range(16):
            pb = kt % 2
            QT.wait(vS, evac_done[pb])
            for c in range(4):
                def mm(t, kt=kt, c=c, pa=ps_a[pb], w=wuv_sb):
                    return t.matmul(pa[:, :],
                                    kvT[:, c:c + 1, kt * 128:(kt + 1) * 128],
                                    w[:, c:c + 1, 0:HG * VD],
                                    start=(c == 0), stop=(c == 3))
                QT.op(mm) if c == 3 else QT.op_noinc(mm)
            QV.wait(tS, tS.n)
            evac_done[pb] = QV.op(lambda v, kt=kt, pa=ps_a[pb]: v.tensor_copy(
                vtok[:, kt:kt + 1, :], pa[:, :]))
        t_dend = tS.n

        last_red = 0
        for hl in range(HG):
            QA.wait(vS, vS.n)
            QA.wait(tS, tS.n)
            a_sq = QA.op(lambda sc, hl=hl: sc.activation(sqs[:, :], kno[:, hl:hl + 1, :],
                                                         AF.Square))
            for half in range(2):
                QT.wait(aS, a_sq)
                QT.wait(aS, a_kpsq)
                QT.wait(vS, last_red)
                for j in range(2):
                    sl = slice(half * 1024 + j * 512, half * 1024 + (j + 1) * 512)
                    def mma(t, j=j, sl=sl, pp=ps_p1):
                        return t.matmul(pp[0:1, j * 512:(j + 1) * 512],
                                        ones_col[:, 0:1], sqs[:, sl],
                                        start=True, stop=False)
                    def mmb(t, j=j, sl=sl, pp=ps_p1):
                        return t.matmul(pp[0:1, j * 512:(j + 1) * 512],
                                        ones_col[0:64, 0:1], kpe_sq[0:64, sl],
                                        start=False, stop=True)
                    QT.op_noinc(mma)
                    QT.op(mmb)
                QV.wait(tS, tS.n)
                last_red = QV.op(lambda v, half=half, pp=ps_p1: v.reduce_max(
                    kmx[0:1, half:half + 1], pp[0:1, 0:1024], AX.X))
            QV.op(lambda v: v.tensor_tensor(kmx[0:1, 2:3], kmx[0:1, 0:1], kmx[0:1, 1:2],
                                            ALU.max))
            v_kmx = vS.n
            QA.wait(vS, v_kmx)
            QA.op(lambda sc, hl=hl, grp=grp: sc.activation(
                kmaxn[0:1, grp * HG + hl:grp * HG + hl + 1], kmx[0:1, 2:3],
                AF.Sqrt, scale=1.0404))

        a_qn_prev = 0
        for hl in range(HG):
            QA.wait(tS, tS.n)
            QA.wait(vS, vS.n)
            a_q1 = QA.op(lambda sc, hl=hl: sc.activation(sqs[:, 0:NQ],
                                                         q_no[:, hl:hl + 1, :], AF.Square))
            QT.wait(aS, a_q1)
            QT.wait(aS, a_qn_prev)
            for j in range(2):
                def mmq(t, j=j, pp=ps_p1):
                    return t.matmul(pp[0:1, j * 512:(j + 1) * 512], ones_col[:, 0:1],
                                    sqs[:, j * 512:(j + 1) * 512], start=True, stop=False)
                QT.op(mmq) if j == 1 else QT.op_noinc(mmq)
            QA.wait(tS, tS.n)
            a_q2 = QA.op(lambda sc, hl=hl: sc.activation(
                sqs[0:64, 0:NQ], qpe_ext[0:64, hl * NQ:(hl + 1) * NQ], AF.Square))
            QT.wait(aS, a_q2)
            for j in range(2):
                def mmq2(t, j=j, pp=ps_p1):
                    return t.matmul(pp[0:1, j * 512:(j + 1) * 512], ones_col[0:64, 0:1],
                                    sqs[0:64, j * 512:(j + 1) * 512], start=False, stop=True)
                QT.op(mmq2) if j == 1 else QT.op_noinc(mmq2)
            QA.wait(tS, tS.n)
            a_qn_prev = QA.op(lambda sc, pp=ps_p1: sc.activation(qn_row[0:1, :], pp[0:1, 0:1024],
                                                                 AF.Sqrt))
            QV.wait(aS, a_qn_prev)
            QV.wait(dB, dB.n)
            QV.op(lambda v, hl=hl, grp=grp: v.tensor_scalar(
                bnd_row[0:1, :], qn_row[0:1, :],
                kmaxn[0:1, grp * HG + hl:grp * HG + hl + 1], None, ALU.mult))
            v_bnd = QV.op(lambda v: v.tensor_scalar(bnd_row[0:1, :], bnd_row[0:1, :],
                                                    -1.0, None, ALU.mult))
            QS.wait(vS, v_bnd)
            QS.dma(dB, qpe_ext[64:65, hl * NQ:(hl + 1) * NQ], bnd_row[0:1, :])

        if debug and grp == 0:
            QS.wait(vS, vS.n)
            QS.wait(aS, aS.n)
            QS.wait(dB, dB.n)
            QS.dma(dDbg, dbg["qno"], q_no[:, :, :])
            QS.dma(dDbg, dbg["qpe"], qpe_ext[:, :])
            QS.dma(dDbg, dbg["kno"], kno[:, :, :])
            QS.dma(dDbg, dbg["vtok"], vtok[:, :, :])
            QS.dma(dDbg, dbg["kmaxn"], kmaxn[:, :])
            QV.wait(dDbg, dDbg.n)
            QT.wait(dDbg, dDbg.n)
            QA.wait(dDbg, dDbg.n)

        al.free("ps_p2", "ps_p1", "ps_a1", "ps_a0")
        if grp != 0:
            al.free("wuv_sb", "wuk_sb", "wuq_sb")

        QG.wait(tS, tS.n)
        for dst_p, src_p in cast_pieces[grp * 3:(grp + 1) * 3]:
            QG.dma(dD2, dst_p, src_p)
        if grp == NGRP - 1:
            d_d2d = dD2.n
        QT.wait(dB, dB.n)
        ps_sT = [al.ps("ps_sT0", [128, 512]), al.ps("ps_sT1", [128, 512])]
        ps_cx = [al.ps("ps_cx0", [128, 512]), al.ps("ps_cx1", [128, 512])]
        ps_sm = [al.ps("ps_sm0", [128, 512]), al.ps("ps_sm1", [128, 512])]

        QT.wait(vS, vS.n)
        QT.wait(aS, aS.n)
        for sl_i in range(2):
            nk = SLOT_K[sl_i]
            qs = sl_i * 512
            for hl in range(HG):
                gh = grp * HG + hl
                cb = gh % 2
                prev_evac = ctx_evac.get(gh - 2, 0)
                def emit_av(kc, dep_v, a_ex):
                    pbuf = kc % 3
                    if dep_v:
                        QT.wait(vS, dep_v)
                    else:
                        QT.wait(aS, a_ex)
                    if kc == 0:
                        QT.wait(vS, prev_evac)
                    def mm_av(t, hl=hl, kc=kc, pbuf=pbuf, nk=nk, pp=ps_cx[cb]):
                        return t.matmul(pp[:, :],
                                        vtok[:, kc:kc + 1, hl * VD:(hl + 1) * VD],
                                        p_sb[:, pbuf:pbuf + 1, :],
                                        start=(kc == 0), stop=(kc == nk - 1))
                    QT.op_noinc(mm_av)
                    def mm_sm(t, kc=kc, pbuf=pbuf, nk=nk, pp=ps_sm[cb]):
                        return t.matmul(pp[:, :], ones_mat[:, :],
                                        p_sb[:, pbuf:pbuf + 1, :],
                                        start=(kc == 0), stop=(kc == nk - 1))
                    last_av_pbuf[pbuf] = QT.op(mm_sm)

                pend = None
                for kc in range(nk):
                    pbuf = kc % 3
                    sbk = kc % 2
                    QT.wait(aS, last_exp_bank[sbk])
                    def mm_n(t, hl=hl, kc=kc, qs=qs, pp=ps_sT[sbk]):
                        return t.matmul(pp[:, :],
                                        kno[:, hl:hl + 1, kc * 128:(kc + 1) * 128],
                                        q_no[:, hl:hl + 1, qs:qs + 512],
                                        start=True, stop=False)
                    QT.op_noinc(mm_n)
                    def mm_p(t, hl=hl, kc=kc, qs=qs, pp=ps_sT[sbk]):
                        return t.matmul(pp[:, :],
                                        kpe_ext[0:65, kc * 128:(kc + 1) * 128],
                                        qpe_ext[0:65, hl * NQ + qs:hl * NQ + qs + 512],
                                        start=False, stop=True)
                    t_sc = QT.op(mm_p)
                    QA.wait(tS, t_sc)
                    QA.wait(tS, last_av_pbuf[pbuf])
                    a_ex = QA.op(lambda sc, pbuf=pbuf, pp=ps_sT[sbk]: sc.activation(
                        p_sb[:, pbuf:pbuf + 1, :], pp[:, :], AF.Exp, scale=SCALE))
                    last_exp_bank[sbk] = a_ex
                    midx = kc if sl_i == 0 else (8 + (kc - 8) if kc >= 8 else None)
                    dep_v = 0
                    if midx is not None:
                        QV.wait(aS, a_ex)
                        dep_v = QV.op(lambda v, pbuf=pbuf, midx=midx: v.tensor_mul(
                            p_sb[:, pbuf:pbuf + 1, :], p_sb[:, pbuf:pbuf + 1, :],
                            masks[:, midx:midx + 1, :]))
                    if pend is not None:
                        emit_av(*pend)
                    pend = (kc, dep_v, a_ex)
                emit_av(*pend)
                QV.wait(tS, last_av_pbuf[(nk - 1) % 3])
                QV.op(lambda v, pp=ps_sm[cb]: v.reciprocal(rec_sb[:, :], pp[:, :]))
                ctx_evac[gh] = QV.op(lambda v, gh=gh, qs=qs, pp=ps_cx[cb]: v.scalar_tensor_tensor(
                    ctx[:, gh:gh + 1, qs:qs + 512], pp[:, :], 1.0,
                    rec_sb[:, :], ALU.mult, ALU.mult))
        al.free("ps_sm1", "ps_sm0", "ps_cx1", "ps_cx0", "ps_sT1", "ps_sT0")

    v_ctx = vS.n
    if debug:
        QS.wait(vS, v_ctx)
        QS.dma(dDbg, dbg["ctx"], ctx[:, :, :])
        QV.wait(dDbg, dDbg.n)
        QT.wait(dDbg, dDbg.n)
        QA.wait(dDbg, dDbg.n)
    al.free("rpqbf", "rpq2", "rpq1", "rec_sb", "p_sb", "kmx", "bnd_row", "qn_row",
            "sqs", "vtok", "kno", "qpe_ext", "q_no")
    al.free("wuv0_sb", "wuk0_sb", "wuq0_sb")
    al.free("kpe_sq", "kpe_ext", "kvT", "cq_sb", "masks")

    wo_sb = al.sb("wo_sb", [128, 16, DM], BF16, side="right")
    QG.wait(tS, tS.n)
    QG.wait(vS, vS.n)
    QG.wait(aS, aS.n)
    QG.dma(dW, wo_sb[:, 0:8, :], w_o[0:1024, :].rearrange("(c p) f -> p c f", p=128))
    d_wo = QG.dma(dW, wo_sb[:, 8:16, :],
                  w_o[1024:2048, :].rearrange("(c p) f -> p c f", p=128))
    h2T = al.sb("h2T", [128, 16, NQ], BF16)
    xqst = al.sb("xqst", [128, 2, DM], F32, side="right")
    x1st = al.sb("x1st", [128, 2, DM], F32, side="right")
    h2st = al.sb("h2st", [128, 2, DM], BF16, side="right")
    sqstf = al.sb("sqstf", [128, DM], F32, side="right")
    st1f = al.sb("st1f", [128, 8], F32, side="right")
    ps_f = [al.ps("ps_f%d" % i, [128, 512]) for i in range(4)]

    f_evac = {}
    x1_wr = [0, 0]
    h2_rd = [0, 0]
    x_rd = [0, 0]
    QS.wait(tS, tS.n)
    QS.wait(vS, vS.n)
    QS.wait(aS, aS.n)
    for tt in range(8):
        buf = tt % 2
        QS.wait(vS, x_rd[buf])
        d_xq = QS.dma(dX[buf], xqst[:, buf:buf + 1, :], x_q[tt * 128:(tt + 1) * 128, :])
        QT.wait(vS, v_ctx)
        QT.wait(dW, d_wo)
        for ds in range(4):
            pb = ds
            QT.wait(vS, f_evac.get((tt - 1, ds), 0))
            for c in range(16):
                def mm(t, tt=tt, ds=ds, c=c, pb=pb):
                    return t.matmul(ps_f[pb][:, :],
                                    ctx[:, c:c + 1, tt * 128:(tt + 1) * 128],
                                    wo_sb[:, c:c + 1, ds * 512:(ds + 1) * 512],
                                    start=(c == 0), stop=(c == 15))
                QT.op(mm) if c == 15 else QT.op_noinc(mm)
            QV.wait(tS, tS.n)
            QV.wait(dX[buf], d_xq)
            QV.wait(dSt[buf], x1_wr[buf])
            f_evac[(tt, ds)] = QV.op(lambda v, tt=tt, ds=ds, pb=pb, buf=buf: v.tensor_add(
                x1st[:, buf:buf + 1, ds * 512:(ds + 1) * 512], ps_f[pb][:, :],
                xqst[:, buf:buf + 1, ds * 512:(ds + 1) * 512]))
        x_rd[buf] = f_evac[(tt, 3)]
        xa = x1st[:, buf:buf + 1, :]
        ha = h2st[:, buf:buf + 1, :]
        QS.wait(vS, f_evac[(tt, 3)])
        x1_wr[buf] = QS.dma(dSt[buf], x1_scr[tt * 128:(tt + 1) * 128, :], xa)
        QV.op(lambda v, xa=xa: v.reduce_sum(st1f[:, 0:1], xa, AX.X))
        QA.wait(vS, vS.n)
        QA.op(lambda sc, xa=xa: sc.activation(sqstf[:, :], xa, AF.Square, accum_out=st1f[:, 1:2]))
        QV.wait(aS, aS.n)
        QV.op(lambda v: v.tensor_scalar(st1f[:, 2:3], st1f[:, 0:1], 1.0 / DM, None, ALU.mult))
        QV.op(lambda v: v.tensor_scalar(st1f[:, 3:4], st1f[:, 1:2], 1.0 / DM, None, ALU.mult))
        QV.op(lambda v: v.tensor_mul(st1f[:, 4:5], st1f[:, 2:3], st1f[:, 2:3]))
        v_var = QV.op(lambda v: v.tensor_sub(st1f[:, 3:4], st1f[:, 3:4], st1f[:, 4:5]))
        QA.wait(vS, v_var)
        a_std = QA.op(lambda sc: sc.activation(st1f[:, 5:6], st1f[:, 3:4], AF.Sqrt, bias=epsln[:, 0:1]))
        QV.wait(aS, a_std)
        QV.op(lambda v: v.reciprocal(st1f[:, 6:7], st1f[:, 5:6]))
        v_nb = QV.op(lambda v: v.tensor_scalar(st1f[:, 7:8], st1f[:, 2:3], st1f[:, 6:7], -1.0,
                                               ALU.mult, ALU.mult))
        QA.wait(vS, v_nb)
        if h2_rd[buf]:
            QA.wait(dT[buf], h2_rd[buf])
        a_h = QA.op(lambda sc, ha=ha, xa=xa: sc.activation(ha, xa, AF.Identity,
                                                           bias=st1f[:, 7:8], scale=st1f[:, 6:7]))
        QS.wait(aS, a_h)
        h2_rd[buf] = QS.dma(dT[buf], h2T[:, :, tt * 128:(tt + 1) * 128], ha, transpose=True)
    d_h2T = (dT[0].n, dT[1].n)
    if debug:
        QS.wait(dT[0], d_h2T[0])
        QS.wait(dT[1], d_h2T[1])
        QS.wait(dSt[0], dSt[0].n)
        QS.wait(dSt[1], dSt[1].n)
        QS.dma(dDbg, dbg["h2T"], h2T[:, :, :])
        QS.dma(dDbg, dbg["x1"], x1_scr)
        QV.wait(dDbg, dDbg.n)
        QT.wait(dDbg, dDbg.n)
        QA.wait(dDbg, dDbg.n)
    al.free("ps_f3", "ps_f2", "ps_f1", "ps_f0")
    al.free("st1f", "sqstf", "h2st", "x1st", "xqst")
    al.free("wo_sb")
    al.free("ctx")

    sdbc = al.sb("sdbc", [128, DM], F32)
    nact = al.sb("nact", [1, NQ], BF16)
    x1r = al.sb("x1r", [128, 2, 512], F32)
    y1 = al.sb("y1", [128, 2, 512], F32)
    sdrow = al.sb("sdrow", [1, DM], F32)
    act_sb = al.sb("act_sb", [128, 44, NQ], BF16, side="right")
    slab = [al.sb("slab0", [128, 16, 512], BF16, side="right"),
            al.sb("slab1", [128, 16, 512], BF16, side="right")]
    gate_sb = al.sb("gate_sb", [128, 8, 512], BF16, side="right")
    dslab = [al.sb("dslab0", [128, 4, 512], BF16, side="right"),
             al.sb("dslab1", [128, 4, 512], BF16, side="right")]

    ps_g = [al.ps("ps_g0", [128, 512]), al.ps("ps_g1", [128, 512])]
    ps_u = [al.ps("ps_u0", [128, 512]), al.ps("ps_u1", [128, 512])]
    ps_as = al.ps("ps_as", [1, 1024])
    ps_sd = al.ps("ps_sd", [128, 512])

    d_sdr = QS.dma(dIn, sdrow[:, :], s_down_r)
    for j in range(4):
        QT.wait(dIn, d_sdr)
        QT.wait(vS, vS.n)
        t_b = QT.op(lambda t, j=j: t.matmul(ps_sd[:, :], ones_rf[0:1, 0:128],
                                            sdrow[0:1, j * 512:(j + 1) * 512],
                                            start=True, stop=True))
        QV.wait(tS, t_b)
        QV.op(lambda v, j=j: v.tensor_copy(sdbc[:, j * 512:(j + 1) * 512], ps_sd[:, :]))
    v_sdbc = vS.n
    QS.wait(vS, v_sdbc)
    al.free("sdrow")

    QS.wait(dSt[0], dSt[0].n)
    QS.wait(dSt[1], dSt[1].n)
    QS.wait(dT[0], d_h2T[0])
    QS.wait(dT[1], d_h2T[1])
    QV.wait(dT[0], d_h2T[0])
    QV.wait(dT[1], d_h2T[1])
    QA.wait(dT[0], d_h2T[0])
    QA.wait(dT[1], d_h2T[1])
    QV.wait(dSt[0], dSt[0].n)
    QV.wait(dSt[1], dSt[1].n)
    QA.wait(dSt[0], dSt[0].n)
    QA.wait(dSt[1], dSt[1].n)
    slab_user = [0, 0]
    silu_done = {}
    act_done = {}
    for sec in range(11):
        for which in range(2):
            l = 2 * sec + which
            sb_i = l % 2
            src = g_scr if which == 0 else u_scr
            QS.wait(dD2, d_d2d)
            QS.wait(tS, slab_user[sb_i])
            d_sl = QS.dma(dSl[sb_i], slab[sb_i][:, :, :],
                          src[sec * 512:(sec + 1) * 512, :], transpose=True)
            for hcl in range(4):
                hc = sec * 4 + hcl
                for nt in range(2):
                    pp = (ps_g if which == 0 else ps_u)[nt]
                    QT.wait(dSl[sb_i], d_sl)
                    QT.wait(dT[0], d_h2T[0])
                    QT.wait(dT[1], d_h2T[1])
                    if which == 0:
                        QT.wait(aS, silu_done.get((hc - 1, nt), 0))
                    else:
                        QT.wait(vS, act_done.get((hc - 1, nt), 0))
                    for c in range(16):
                        def mm(t, sb_i=sb_i, hcl=hcl, nt=nt, c=c, pp=pp):
                            return t.matmul(pp[:, :],
                                            slab[sb_i][:, c:c + 1, hcl * 128:(hcl + 1) * 128],
                                            h2T[:, c:c + 1, nt * 512:(nt + 1) * 512],
                                            start=(c == 0), stop=(c == 15))
                        t_mm = QT.op(mm) if c == 15 else QT.op_noinc(mm)
                    slab_user[sb_i] = tS.n
                    gi = hcl * 2 + nt
                    if which == 0:
                        QA.wait(tS, tS.n)
                        QA.wait(vS, act_done.get((hc - 4, nt), 0))
                        silu_done[(hc, nt)] = QA.op(lambda sc, hc=hc, gi=gi, pp=pp: sc.activation(
                            gate_sb[:, gi:gi + 1, :], pp[:, :], AF.Silu,
                            scale=sgate_c[:, hc:hc + 1]))
                    else:
                        QV.wait(tS, tS.n)
                        QV.wait(aS, silu_done[(hc, nt)])
                        a_d = QV.op(lambda v, hc=hc, nt=nt, gi=gi, pp=pp: v.scalar_tensor_tensor(
                            act_sb[:, hc:hc + 1, nt * 512:(nt + 1) * 512], pp[:, :],
                            sup_c[:, hc:hc + 1], gate_sb[:, gi:gi + 1, :],
                            ALU.mult, ALU.mult))
                        act_done[(hc, nt)] = a_d
                        QT.wait(vS, a_d)
                        def mm_as(t, hc=hc, nt=nt):
                            return t.matmul(ps_as[0:1, nt * 512:(nt + 1) * 512],
                                            ones_col[:, 0:1],
                                            act_sb[:, hc:hc + 1, nt * 512:(nt + 1) * 512],
                                            start=(hc == 0), stop=(hc == 43))
                        QT.op(mm_as) if hc == 43 else QT.op_noinc(mm_as)
    t_as = tS.n
    QA.wait(tS, t_as)
    a_nact = QA.op(lambda sc: sc.activation(nact[0:1, :], ps_as[0:1, :], AF.Copy,
                                            scale=-128.0))
    al.free("ps_sd", "ps_as", "ps_u1", "ps_u0", "ps_g1", "ps_g0")
    ps_d = [al.ps("ps_d%d" % i, [128, 512]) for i in range(8)]
    QT.wait(aS, aS.n)

    SECS = [(0, 4), (4, 4), (8, 4), (12, 4), (16, 4), (20, 4),
            (24, 4), (28, 4), (32, 4), (36, 4), (40, 4)]
    dslab_user = [0, 0]
    y_done = {}
    x1r_use = [0, 0]
    y1_use = [0, 0]
    QS.wait(dSt[0], dSt[0].n)
    QS.wait(dSt[1], dSt[1].n)
    for ds in range(4):
        for si, (h0, nch) in enumerate(SECS):
            sb_i = si % 2
            QS.wait(dD2, d_d2d)
            QS.wait(tS, dslab_user[sb_i])
            d_sl = QS.dma(dDl[sb_i], dslab[sb_i][:, 0:nch, :],
                          d_scr[ds * 512:(ds + 1) * 512, h0 * 128:(h0 + nch) * 128],
                          transpose=True)
            last = None
            for j in range(nch):
                hc = h0 + j
                for tt in range(8):
                    QT.wait(dDl[sb_i], d_sl)
                    if hc == 0:
                        QT.wait(vS, y_done.get((tt, ds - 1), 0))
                    def mm(t, sb_i=sb_i, j=j, hc=hc, tt=tt):
                        return t.matmul(ps_d[tt][:, :],
                                        act_sb[:, hc:hc + 1, tt * 128:(tt + 1) * 128],
                                        dslab[sb_i][:, j:j + 1, :],
                                        start=(hc == 0), stop=False)
                    if j == nch - 1 and tt == 7:
                        dslab_user[sb_i] = QT.op(mm)
                    else:
                        QT.op_noinc(mm)
        for tt in range(8):
            buf = tt % 2
            QT.wait(aS, a_nact)
            def mm_c(t, tt=tt):
                return t.matmul(ps_d[tt][:, :], nact[0:1, tt * 128:(tt + 1) * 128],
                                ones_rb[0:1, :], start=False, stop=True)
            t_c = QT.op(mm_c)
            QS.wait(vS, x1r_use[buf])
            d_x1 = QS.dma(dR[buf], x1r[:, buf:buf + 1, :],
                          x1_scr[tt * 128:(tt + 1) * 128, ds * 512:(ds + 1) * 512])
            QV.wait(tS, t_c)
            QV.wait(dR[buf], d_x1)
            QV.wait(vS, v_sdbc)
            QV.wait(dY[buf], y1_use[buf])
            QV.op(lambda v, ds=ds, buf=buf, tt=tt: v.tensor_mul(
                y1[:, buf:buf + 1, :], ps_d[tt][:, :],
                sdbc[:, ds * 512:(ds + 1) * 512]))
            y_done[(tt, ds)] = QV.op(lambda v, buf=buf: v.tensor_add(
                y1[:, buf:buf + 1, :], y1[:, buf:buf + 1, :], x1r[:, buf:buf + 1, :]))
            x1r_use[buf] = y_done[(tt, ds)]
            QS.wait(vS, y_done[(tt, ds)])
            y1_use[buf] = QS.dma(
                dY[buf], y_out[tt * 128:(tt + 1) * 128, ds * 512:(ds + 1) * 512],
                y1[:, buf:buf + 1, :])
    QS.wait(dY[0], dY[0].n)
    QS.wait(dY[1], dY[1].n)
    if debug:
        QS.wait(dDbg, dDbg.n)

    block_g = nc.Block()
    block = block_g.__enter__()

    def run_queue(q):
        def body(eng):
            for fn in q.ops:
                fn(eng)
        return body

    block.sync(run_queue(QS))
    block.tensor(run_queue(QT))
    block.vector(run_queue(QV))
    block.scalar(run_queue(QA))
    block.gpsimd(run_queue(QG))
    block_g.__exit__(None, None, None)

    for g in reversed(sem_guards):
        g.__exit__(None, None, None)
    return nc


_cache = {}


def kernel(**inputs):
    f = lambda k: np.ascontiguousarray(np.asarray(inputs[k], dtype=np.float32))
    x = f("L0_attn_x")
    w = {
        "w_dq": f("L0_attn_w_dq"),
        "w_uq": f("L0_attn_w_uq"),
        "w_dkv": f("L0_attn_w_dkv"),
        "w_kr": f("L0_attn_w_kr"),
        "w_uk": f("L0_attn_w_uk"),
        "w_uv": f("L0_attn_w_uv"),
        "w_o": f("L0_attn_w_o"),
        "g_q_t": np.ascontiguousarray(f("L0_attn_g_q").reshape(6, 128).T),
        "g_kv_t": np.ascontiguousarray(f("L0_attn_g_kv").reshape(4, 128).T),
        "s_gate_t": np.ascontiguousarray(f("L0_ffn_s_gate").reshape(44, 128).T),
        "s_up_t": np.ascontiguousarray(f("L0_ffn_s_up").reshape(44, 128).T),
        "s_down_r": np.ascontiguousarray(f("L0_ffn_s_down").reshape(1, DM)),
        "wq_gate": np.ascontiguousarray(np.asarray(inputs["L0_ffn_wq_gate"], dtype=np.int32)),
        "wq_up": np.ascontiguousarray(np.asarray(inputs["L0_ffn_wq_up"], dtype=np.int32)),
        "wq_down": np.ascontiguousarray(np.asarray(inputs["L0_ffn_wq_down"], dtype=np.int32)),
    }
    if "nc" not in _cache:
        _cache["nc"] = build_nc()
    nc = _cache["nc"]

    in_maps = []
    sels = []
    for core in range(8):
        b, half = core // 2, core % 2
        if half == 0:
            rows = np.r_[0:512, 1536:2048]
            g0, g1 = 0.0, 1536.0
        else:
            rows = np.r_[512:1536]
            g0, g1 = 512.0, 1024.0
        sels.append((b, rows))
        m = dict(w)
        m["x_full"] = x[b]
        m["x_q"] = np.ascontiguousarray(x[b][rows])
        m["pos_q"] = np.ascontiguousarray(rows.astype(np.float32).reshape(1, NQ))
        m["gmeta"] = np.array([[g0, g1]], dtype=np.float32)
        in_maps.append(m)

    res = run_bass_kernel_spmd(nc, in_maps, core_ids=list(range(8)))
    y = np.empty((4, S, DM), dtype=np.float32)
    for core in range(8):
        b, rows = sels[core]
        y[b][rows] = np.asarray(res.results[core]["y"], dtype=np.float32)
    return y
